# version 08
# baseline (speedup 1.0000x reference)
import numpy as np
import ml_dtypes
from contextlib import ExitStack

import concourse.bass as bass
import concourse.mybir as mybir
from concourse.bass_utils import run_bass_kernel_spmd

F32 = mybir.dt.float32
BF16 = mybir.dt.bfloat16
AX = mybir.AxisListType.X
AF = mybir.ActivationFunctionType
ALU = mybir.AluOpType

NT = 64
NQ = 32
KR = 4
D = 1024
EPS = 1e-5
ISQ = 1.0 / 32.0

ENGS = ("tensor", "vector", "scalar", "sync", "gpsimd")


class Sched:
    def __init__(self):
        self.prog = {e: [] for e in ENGS}
        self.cnt = {}
        self.lastwait = {e: {} for e in ENGS}

    CHAIN = {"vector": "v", "scalar": "s", "gpsimd": "g"}

    def add(self, eng, fn, waits=(), inc=None):
        sem0 = self.CHAIN.get(eng)
        if sem0 is not None:
            waits = ((sem0, self.cnt.get(sem0, 0)),) + tuple(waits)
            if inc is None:
                inc = (sem0, 1)
        for w in waits:
            if w is None:
                continue
            sem, val = w
            if isinstance(val, tuple):
                sem, val = val
            if val is None or val <= 0:
                continue
            if sem in ("ld", "st"):
                val = self.cnt.get(sem, 0)
            lw = self.lastwait[eng]
            if lw.get(sem, 0) >= val:
                continue
            lw[sem] = val
            self.prog[eng].append(("w", sem, val))
        self.prog[eng].append(("c", fn, inc))
        if inc is not None:
            sem, amt = inc
            self.cnt[sem] = self.cnt.get(sem, 0) + amt
            return self.cnt[sem]
        return None

    def count(self, sem):
        return self.cnt.get(sem, 0)

    def barrier(self):
        snap = dict(self.cnt)
        for e in ENGS:
            for sem, val in snap.items():
                if val > 0 and self.lastwait[e].get(sem, 0) < val:
                    self.lastwait[e][sem] = val
                    self.prog[e].append(("w", sem, val))


class _Cut(Exception):
    pass


def build(phases=3, dbg=False, cut=None):
    nc = bass.Bass("TRN2", target_bir_lowering=False)

    dI = lambda n, s, dt=F32: nc.dram_tensor(n, s, dt, kind="ExternalInput").ap()
    x_d = dI("x", [8192, 1024])
    wp0_d = dI("wp0", [1024, 1024])
    wdT_d = dI("wdT", [1024, 1024])
    wcT_d = dI("wcT", [1024, 1024])
    wrT_d = dI("wrT", [1024, 1024])
    wT_d = dI("wT", [1024, 1024])
    uaT_d = dI("uaT", [128, 1024])
    da_d = dI("da", [128, 1024])
    ldT_d = dI("ldT", [128, 4])
    luT_d = dI("luT", [128, 50])
    wqT_d = dI("wqT", [1024, 1024])
    wkT_d = dI("wkT", [1024, 1024])
    wvT_d = dI("wvT", [1024, 1024])
    rows_d = dI("rows", [8, 1024])
    bqT_d = dI("bqT", [128, 8])
    bkT_d = dI("bkT", [128, 8])
    m1_d = dI("m1c", [64, 1])
    m2n_d = dI("m2n", [64, 1])
    ident_d = dI("ident", [128, 128], BF16)
    tri128_d = dI("tri128", [128, 128], BF16)
    tri64_d = dI("tri64x", [64, 64])
    emsk_d = dI("emsk", [128, 256], BF16)
    o1r_d = dI("ones1r", [1, 128])
    o64_d = dI("ones64", [1, 64])
    o2r_d = dI("ones2r", [2, 128], BF16)

    sk = "ExternalOutput" if dbg else "Internal"
    dS = lambda n, s, dt=F32: nc.dram_tensor(n, s, dt, kind=sk).ap()
    sc_delta = dS("sc_delta", [NT, 128, 1024])
    sc_res = dS("sc_res", [NT, 128, 1024])
    sc_hi = dS("sc_hi", [64, 1024], BF16)
    sc_lo = dS("sc_lo", [64, 1024], BF16)
    sc_xn3 = dS("sc_xn3", [NT, 128, 1024], BF16)
    sc_x2 = dS("sc_x2", [NQ, 128, 1024])
    sc_qT = dS("sc_qT", [NQ, 128, 8, 128], BF16)
    sc_z = dS("sc_z", [NQ, 128, 1024])
    if dbg:
        dbg_tbl = nc.dram_tensor("dbg_tbl", [128, 1024], F32, kind="ExternalOutput").ap()
        dbg_bc = nc.dram_tensor("dbg_bc", [128, 4096], F32, kind="ExternalOutput").ap()
        sc_x1 = nc.dram_tensor("sc_x1", [NT, 128, 1024], F32, kind="ExternalOutput").ap()
        dbg_ml = nc.dram_tensor("dbg_ml", [128, 64], F32, kind="ExternalOutput").ap()
    out_d = nc.dram_tensor("out", [4096, 1024], F32, kind="ExternalOutput").ap()

    sch = Sched()

    def _ck(n):
        if cut is not None and n >= cut:
            raise _Cut()

    LD = lambda m: m if isinstance(m, tuple) else ("ld", m)
    VD = LD
    ST = lambda m: m if isinstance(m, tuple) else ("st", m)
    TM = lambda m: ("t", m)
    VM = lambda m: ("v", m)
    SM = lambda m: ("s", m)
    GM = lambda m: ("g", m)

    (es_glob, es_p23w, es_pqw, es_p1w, es_p1t, es_p2, es_p3,
     es_bc, es_wtmp, es_fold, es_p2a, es_prep, es_qsp) = (ExitStack() for _ in range(13))
    C = es_glob.enter_context
    R = lambda es, t: es.enter_context(t)

    s_id = C(nc.sbuf_tensor("s_id", [128, 128], BF16))
    s_tri128 = C(nc.sbuf_tensor("s_tri128", [128, 128], BF16))
    s_tri64 = C(nc.sbuf_tensor("s_tri64", [64, 64], F32))
    s_emsk = C(nc.sbuf_tensor("s_emsk", [128, 256], BF16))
    s_o1r = C(nc.sbuf_tensor("s_o1r", [1, 128], F32))
    s_o64 = C(nc.sbuf_tensor("s_o64", [1, 64], F32))
    s_o2r = C(nc.sbuf_tensor("s_o2r", [2, 128], BF16))
    s_bqT = C(nc.sbuf_tensor("s_bqT", [128, 8], F32))
    s_bkT = C(nc.sbuf_tensor("s_bkT", [128, 8], F32))
    s_m1 = C(nc.sbuf_tensor("s_m1", [64, 1], F32))
    s_m2n = C(nc.sbuf_tensor("s_m2n", [64, 1], F32))
    s_Dbc = C(nc.sbuf_tensor("s_Dbc", [128, 1], F32))
    s_bfbc = C(nc.sbuf_tensor("s_bfbc", [128, 1024], F32))
    s_bvbc = R(es_prep, nc.sbuf_tensor("s_bvbc", [128, 1024], F32, side="right"))
    W_eff = C(nc.sbuf_tensor("W_eff", [128, 8192], BF16))
    s_wtmp = [R(es_wtmp, nc.sbuf_tensor(f"s_wtmp{p}", [128, 1024], F32, side="right"))
              for p in range(2)]
    s_sqscr = C(nc.sbuf_tensor("s_sqscr", [128, 1024], BF16))
    s_cst = C(nc.sbuf_tensor("s_cst", [128, 4], F32))
    s_out = [C(nc.sbuf_tensor("s_out0", [128, 1024], F32))] * 2

    PA = C(nc.psum_tensor("PA", [128, 1024], F32))
    PB = C(nc.psum_tensor("PB", [128, 1024], F32))
    PC = C(nc.psum_tensor("PC", [128, 1024], F32))
    PT1 = C(nc.psum_tensor("PT1", [128, 8, 128], BF16))
    PT2 = C(nc.psum_tensor("PT2", [128, 8, 128], BF16))

    sems = {}

    def dma(q, outap, inap, waits=(), ch=None):
        if q == "vector":
            q = "sync"
        semn = ch if ch is not None else {"sync": "ld", "scalar": "st"}[q]
        if semn in ("ld", "st"):
            waits = tuple(waits) + ((semn, 1),)
        m = sch.add(q, lambda e, o=outap, i=inap: e.dma_start(out=o, in_=i),
                    waits, inc=(semn, 16))
        return (semn, m)

    def transposes8(pt, src_bf, waits=()):
        m = None
        for c in range(8):
            m = sch.add("tensor",
                        lambda e, o=pt[:, c, :], i_=src_bf[:, c * 128:(c + 1) * 128]:
                        e.transpose(o, i_, s_id[:, :]),
                        waits if c == 0 else (), inc=("t", 1) if c == 7 else None)
        return m

    def ln(xin, xn_bf, stats, w_x):
        mv = sch.add("vector", lambda e: e.reduce_sum(stats[:, 0:1], xin[:, :], AX), w_x,
                     inc=("v", 1))
        msq = sch.add("scalar", lambda e: e.activation(s_sqscr[:, :], xin[:, :], AF.Square,
                                                       accum_out=stats[:, 1:2]), w_x,
                      inc=("s", 1))
        sch.add("gpsimd", lambda e: e.tensor_mul(stats[:, 2:3], stats[:, 0:1], s_cst[:, 0:1]),
                (VM(mv),), inc=("g", 1))
        sch.add("gpsimd", lambda e: e.tensor_mul(stats[:, 3:4], stats[:, 1:2], s_cst[:, 1:2]),
                (SM(msq),), inc=("g", 1))
        sch.add("gpsimd", lambda e: e.tensor_mul(stats[:, 4:5], stats[:, 2:3], stats[:, 2:3]),
                inc=("g", 1))
        sch.add("gpsimd", lambda e: e.tensor_sub(stats[:, 5:6], stats[:, 3:4], stats[:, 4:5]),
                inc=("g", 1))
        mg6 = sch.add("gpsimd", lambda e: e.tensor_add(stats[:, 6:7], stats[:, 5:6],
                                                       s_cst[:, 2:3]), inc=("g", 1))
        mv3 = sch.add("vector", lambda e: e.reciprocal(stats[:, 7:8], stats[:, 6:7]),
                      (GM(mg6),), inc=("v", 1))
        ms3 = sch.add("scalar", lambda e: e.activation(stats[:, 8:9], stats[:, 7:8], AF.Sqrt),
                      (VM(mv3),), inc=("s", 1))
        mg9 = sch.add("gpsimd", lambda e: e.tensor_mul(stats[:, 9:10], stats[:, 2:3],
                                                       stats[:, 8:9]), (SM(ms3),), inc=("g", 1))
        mxn = sch.add("scalar", lambda e: e.activation(xn_bf[:, :], xin[:, :], AF.Identity,
                                                       bias=stats[:, 9:10], scale=stats[:, 8:9]),
                      (GM(mg9),), inc=("s", 1))
        return mxn

    wtmp_prev = {0: None, 1: None}

    def load_weight(dst_bf, src_d):
        last = None
        for k in range(8):
            p = k % 2
            ml = dma("sync", s_wtmp[p][:, :], src_d[k * 128:(k + 1) * 128, :],
                     (wtmp_prev[p],), ch=f"lw{p}")
            mg = sch.add("gpsimd",
                         lambda e, kk=k, pp=p: e.tensor_copy(
                             dst_bf[:, kk * 1024:(kk + 1) * 1024], s_wtmp[pp][:, :]),
                         (LD(ml),), inc=("g", 1))
            wtmp_prev[p] = GM(mg)
            last = mg
        return last

    WkT = R(es_p23w, nc.sbuf_tensor("WkT", [128, 8192], BF16))
    WvT = R(es_p23w, nc.sbuf_tensor("WvT", [128, 8192], BF16))
    WqT = R(es_pqw, nc.sbuf_tensor("WqT", [128, 8192], BF16))
    W_dd = R(es_p1w, nc.sbuf_tensor("W_dd", [128, 8192], BF16))
    WcT = R(es_p1w, nc.sbuf_tensor("WcT", [128, 8192], BF16))
    WrT = R(es_p1w, nc.sbuf_tensor("WrT", [128, 8192], BF16))
    Rb = lambda t: R(es_bc, t)
    s_A1bc = Rb(nc.sbuf_tensor("s_A1bc", [128, 1024], F32, side="right"))
    s_B1bc = Rb(nc.sbuf_tensor("s_B1bc", [128, 1024], F32, side="right"))
    s_bdbc = Rb(nc.sbuf_tensor("s_bdbc", [128, 1024], F32, side="right"))
    s_Csbc = Rb(nc.sbuf_tensor("s_Csbc", [128, 1024], F32, side="right"))
    s_R2bc = Rb(nc.sbuf_tensor("s_R2bc", [128, 1024], F32, side="right"))
    Cf = lambda t: R(es_fold, t)
    s_row1 = Cf(nc.sbuf_tensor("s_row1", [1, 1024], F32, side="right"))
    Wp0 = Cf(nc.sbuf_tensor("Wp0", [128, 8192], BF16, side="right"))
    WdT = Cf(nc.sbuf_tensor("WdT", [128, 8192], BF16, side="right"))
    s_ldT = Cf(nc.sbuf_tensor("s_ldT", [128, 4], BF16, side="right"))
    s_luT = Cf(nc.sbuf_tensor("s_luT", [128, 50], BF16, side="right"))
    s_uaT = Cf(nc.sbuf_tensor("s_uaT", [128, 1024], BF16, side="right"))
    s_da = Cf(nc.sbuf_tensor("s_da", [128, 1024], BF16, side="right"))
    s_t1 = Cf(nc.sbuf_tensor("s_t1", [128, 1024], BF16, side="right"))
    s_t2 = Cf(nc.sbuf_tensor("s_t2", [128, 1024], BF16, side="right"))
    s_lt = s_wtmp[0]

    try:
        for outap, inap in ((s_id[:, :], ident_d[:, :]), (s_tri128[:, :], tri128_d[:, :]),
                            (s_tri64[:, :], tri64_d[:, :]), (s_emsk[:, :], emsk_d[:, :]),
                            (s_o1r[:, :], o1r_d[:, :]), (s_o64[:, :], o64_d[:, :]),
                            (s_o2r[:, :], o2r_d[:, :]),
                            (s_bqT[:, :], bqT_d[:, :]), (s_bkT[:, :], bkT_d[:, :]),
                            (s_m1[:, :], m1_d[:, :]), (s_m2n[:, :], m2n_d[:, :])):
            m_startld = dma("sync", outap, inap)
        _ck(1)

        ml = dma("sync", s_lt[:, 0:4], ldT_d[:, :], (wtmp_prev[0],), ch="lw0")
        mg1 = sch.add("gpsimd", lambda e: e.tensor_copy(s_ldT[:, :], s_lt[:, 0:4]),
                      (LD(ml),), inc=("g", 1))
        ml = dma("sync", s_lt[:, 0:50], luT_d[:, :], (GM(mg1),), ch="lw0")
        mg2 = sch.add("gpsimd", lambda e: e.tensor_copy(s_luT[:, :], s_lt[:, 0:50]),
                      (LD(ml),), inc=("g", 1))
        ml = dma("sync", s_lt[:, :], uaT_d[:, :], (GM(mg2),), ch="lw0")
        mg3 = sch.add("gpsimd", lambda e: e.tensor_copy(s_uaT[:, :], s_lt[:, :]),
                      (LD(ml),), inc=("g", 1))
        ml = dma("sync", s_lt[:, :], da_d[:, :], (GM(mg3),), ch="lw0")
        mg4 = sch.add("gpsimd", lambda e: e.tensor_copy(s_da[:, :], s_lt[:, :]),
                      (LD(ml),), inc=("g", 1))
        wtmp_prev[0] = GM(mg4)
        mz1 = sch.add("gpsimd", lambda e: e.memzero(s_t1[:, :]), inc=("g", 1))
        mz2 = sch.add("gpsimd", lambda e: e.memzero(s_t2[:, :]), inc=("g", 1))
        sch.add("gpsimd", lambda e: e.memset(s_cst[:, 0:1], -1.0 / 1024.0), inc=("g", 1))
        sch.add("gpsimd", lambda e: e.memset(s_cst[:, 1:2], 1.0 / 1024.0), inc=("g", 1))
        sch.add("gpsimd", lambda e: e.memset(s_cst[:, 2:3], EPS), inc=("g", 1))
        _ck(2)

        mwp0 = load_weight(Wp0, wp0_d)
        mwdt = load_weight(WdT, wdT_d)
        _ck(3)

        mt1 = None
        for n in range(2):
            mt1 = sch.add("tensor", lambda e, nn=n: e.matmul(
                PA[0:4, nn * 512:(nn + 1) * 512], s_ldT[:, :], s_da[:, nn * 512:(nn + 1) * 512],
                start=True, stop=True), (GM(mg4), GM(mg1)) if n == 0 else (),
                inc=("t", 1) if n == 1 else None)
        ms = sch.add("scalar", lambda e: e.copy(s_t1[0:4, :], PA[0:4, :]),
                     (TM(mt1), GM(mz1)), inc=("s", 1))
        mt2 = None
        for n in range(2):
            mt2 = sch.add("tensor", lambda e, nn=n: e.matmul(
                PB[0:50, nn * 512:(nn + 1) * 512], s_luT[:, :], s_t1[:, nn * 512:(nn + 1) * 512],
                start=True, stop=True), (SM(ms), GM(mg2)) if n == 0 else (),
                inc=("t", 1) if n == 1 else None)
        ms2 = sch.add("scalar", lambda e: e.copy(s_t2[0:50, :], PB[0:50, :]),
                      (TM(mt2), GM(mz2)), inc=("s", 1))
        _ck(4)
        m_weff_v = None
        for mch in range(8):
            p = mch % 2
            wld = dma("sync", s_wtmp[p][:, :], wT_d[mch * 128:(mch + 1) * 128, :],
                      (wtmp_prev[p],), ch=f"lw{p}")
            mml = None
            for n in range(2):
                mml = sch.add("tensor", lambda e, nn=n, m_=mch: e.matmul(
                    PA[:, nn * 512:(nn + 1) * 512], s_t2[:, m_ * 128:(m_ + 1) * 128],
                    s_uaT[:, nn * 512:(nn + 1) * 512], start=True, stop=True),
                    ((SM(ms2), GM(mg3)) if mch == 0 else (VM(m_weff_v),))
                    if n == 0 else (),
                    inc=("t", 1) if n == 1 else None)
            m_weff_v = sch.add("vector", lambda e, m_=mch, pp=p: e.tensor_add(
                W_eff[:, m_ * 1024:(m_ + 1) * 1024], PA[:, :], s_wtmp[pp][:, :]),
                (TM(mml), LD(wld)), inc=("v", 1))
            wtmp_prev[p] = VM(m_weff_v)

        _ck(5)
        m_wdd_s = None
        for mch in range(8):
            mml = None
            for n in range(2):
                for k in range(8):
                    mml = sch.add("tensor", lambda e, nn=n, kk=k, m_=mch: e.matmul(
                        PB[:, nn * 512:(nn + 1) * 512],
                        Wp0[:, kk * 1024 + m_ * 128: kk * 1024 + (m_ + 1) * 128],
                        WdT[:, kk * 1024 + nn * 512: kk * 1024 + nn * 512 + 512],
                        start=(kk == 0), stop=(kk == 7)),
                        ((GM(mwp0), GM(mwdt), SM(ms2)) if mch == 0 else (SM(m_wdd_s),))
                        if (n == 0 and k == 0) else (),
                        inc=("t", 1) if (n == 1 and k == 7) else None)
            m_wdd_s = sch.add("scalar", lambda e, m_=mch: e.copy(
                W_dd[:, m_ * 1024:(m_ + 1) * 1024], PB[:, :]), (TM(mml),), inc=("s", 1))

        _ck(6)
        mwc = load_weight(WcT, wcT_d)
        mwr = load_weight(WrT, wrT_d)

        _ck(7)
        m_prevbc = 0

        def bcast_row(dst, rowidx, dcol=False):
            nonlocal_marks = {}
            mrl = dma("sync", s_row1[:, :], rows_d[rowidx:rowidx + 1, :],
                      (TM(m_prevbc), LD(m_startld)))
            mb = None
            for n in range(2):
                mb = sch.add("tensor", lambda e, nn=n: e.matmul(
                    PC[:, nn * 512:(nn + 1) * 512], s_o1r[:, :],
                    s_row1[0:1, nn * 512:(nn + 1) * 512], start=True, stop=True),
                    (LD(mrl), VM(sch.count("v"))) if n == 0 else (),
                    inc=("t", 1) if n == 1 else None)
            if dcol:
                mv = sch.add("vector", lambda e: e.tensor_copy(s_Dbc[:, :], PC[:, 0:1]),
                             (TM(mb),), inc=("v", 1))
            else:
                mv = sch.add("vector", lambda e: e.tensor_copy(dst[:, :], PC[:, :]), (TM(mb),),
                             inc=("v", 1))
            return mb

        m_prevbc = bcast_row(s_A1bc, 0)
        m_prevbc = bcast_row(s_B1bc, 1)
        m_prevbc = bcast_row(s_bdbc, 2)
        m_prevbc = bcast_row(s_bfbc, 3)
        m_prevbc = bcast_row(s_bvbc, 4)
        m_prevbc = bcast_row(None, 5, dcol=True)

        _ck(8)
        sch.barrier()
        es_fold.close()

        if phases >= 1:
            Cp = es_p1t.enter_context
            xb = [Cp(nc.sbuf_tensor(f"xb{p}", [128, 1024], F32)) for p in range(2)]
            xn_bf = [Cp(nc.sbuf_tensor(f"xn_bf{p}", [128, 1024], BF16)) for p in range(3)]
            xnT = [Cp(nc.sbuf_tensor(f"xnT{p}", [128, 8, 128], BF16)) for p in range(3)]
            delta_f = [Cp(nc.sbuf_tensor(f"delta_f{p}", [128, 1024], F32)) for p in range(2)]
            e1_f = [Cp(nc.sbuf_tensor("e1_f0", [128, 1024], F32))] * 2
            res_f = [Cp(nc.sbuf_tensor(f"res_f{p}", [128, 1024], F32)) for p in range(2)]
            cs_bf = [Cp(nc.sbuf_tensor(f"cs_bf{p}", [128, 1024], BF16)) for p in range(2)]
            g0_bf = [Cp(nc.sbuf_tensor("g0_bf0", [128, 1024], BF16))] * 2
            m_bf = [Cp(nc.sbuf_tensor("m_bf0", [128, 1024], BF16))] * 2
            st1 = [Cp(nc.sbuf_tensor(f"st1_{p}", [128, 16], F32)) for p in range(3)]


            mk = {}

            def p1A(i):
                p = i % 2
                q3 = i % 3
                PTt = PT1 if p == 0 else PT2
                w_x = [LD(mk[("lx", i)]), SM(mk.get(("xn_c", i - 3))),
                       GM(mk.get(("m", i - 3))), TM(mk.get(("tp", i - 3)))]
                if i == 0:
                    w_x += [SM(m_wdd_s), GM(mwc), GM(mwr), VM(sch.count("v"))]
                mxn = ln(xb[p], xn_bf[q3], st1[q3], tuple(w_x))
                mk[("xn_c", i)] = mxn
                mtp = transposes8(PTt, xn_bf[q3], (SM(mxn), VM(mk.get(("xnTev", i - 2), 0))))
                mk[("tp", i)] = mtp
                mev = sch.add("vector", lambda e, qq=q3, pt=PTt: e.tensor_copy(
                    xnT[qq][:, :, :], pt[:, :, :]), (TM(mtp),), inc=("v", 1))
                mk[("xnTev", i)] = mev

            def p1B(i):
                p = i % 2
                q3 = i % 3
                mev = mk[("xnTev", i)]
                mdp = None
                for n in range(2):
                    for k in range(8):
                        mdp = sch.add("tensor", lambda e, nn=n, kk=k, qq=q3: e.matmul(
                            PA[:, nn * 512:(nn + 1) * 512], xnT[qq][:, kk, :],
                            W_dd[:, kk * 1024 + nn * 512: kk * 1024 + nn * 512 + 512],
                            start=(kk == 0), stop=(kk == 7)),
                            (VM(mev), VM(mk.get(("resev", i - 1), 0)))
                            if (n == 0 and k == 0) else (),
                            inc=("t", 1) if (n == 1 and k == 7) else None)
                mba = sch.add("vector", lambda e, pp=p: e.tensor_add(
                    delta_f[pp][:, :], PA[:, :], s_bdbc[:, :]),
                    (TM(mdp), GM(mk.get(("g0", i - 2), 0)), mk.get(("dst", i - 2))),
                    inc=("v", 1))
                mcs = None
                for n in range(2):
                    for k in range(8):
                        mcs = sch.add("tensor", lambda e, nn=n, kk=k, qq=q3: e.matmul(
                            PB[:, nn * 512:(nn + 1) * 512], xnT[qq][:, kk, :],
                            WcT[:, kk * 1024 + nn * 512: kk * 1024 + nn * 512 + 512],
                            start=(kk == 0), stop=(kk == 7)),
                            (SM(mk.get(("csbf", i - 1), 0)),) if (n == 0 and k == 0) else (),
                            inc=("t", 1) if (n == 1 and k == 7) else None)
                mex = sch.add("scalar", lambda e, pp=p: e.activation(
                    e1_f[pp][:, :], delta_f[pp][:, :], AF.Exp), (VM(mba),), inc=("s", 1))
                mln = sch.add("scalar", lambda e, pp=p: e.activation(
                    delta_f[pp][:, :], e1_f[pp][:, :], AF.Ln, bias=1.0), (SM(mex),),
                    inc=("s", 1))
                mk[("sp", i)] = mln
                mk[("dst", i)] = dma("scalar", sc_delta[i, :, :], delta_f[p][:, :], ch=f"sdl{p}")
                mcb = sch.add("scalar", lambda e, pp=p: e.copy(cs_bf[pp][:, :], PB[:, :]),
                              (TM(mcs), GM(mk.get(("m", i - 2), 0))), inc=("s", 1))
                mk[("csbf", i)] = mcb
                mg0 = sch.add("gpsimd", lambda e, pp=p: e.tensor_mul(
                    g0_bf[pp][:, :], delta_f[pp][:, :], s_B1bc[:, :]),
                    (SM(mln), TM(mk.get(("colsum", i - 1), 0))), inc=("g", 1))
                mk[("g0", i)] = mg0
                mmm = sch.add("gpsimd", lambda e, pp=p, qq=q3: e.tensor_mul(
                    m_bf[pp][:, :], xn_bf[qq][:, :], cs_bf[pp][:, :]),
                    (SM(mcb), TM(mk.get(("colsum", i - 1), 0))), inc=("g", 1))
                mk[("m", i)] = mmm
                mrs = None
                for n in range(2):
                    for k in range(8):
                        mrs = sch.add("tensor", lambda e, nn=n, kk=k, qq=q3: e.matmul(
                            PA[:, nn * 512:(nn + 1) * 512], xnT[qq][:, kk, :],
                            WrT[:, kk * 1024 + nn * 512: kk * 1024 + nn * 512 + 512],
                            start=(kk == 0), stop=(kk == 7)),
                            (VM(mba),) if (n == 0 and k == 0) else (),
                            inc=("t", 1) if (n == 1 and k == 7) else None)
                mrev = sch.add("vector", lambda e, pp=p: e.tensor_copy(
                    res_f[pp][:, :], PA[:, :]),
                    (TM(mrs), mk.get(("rst", i - 2))), inc=("v", 1))
                mk[("resev", i)] = mrev
                mk[("rst", i)] = dma("scalar", sc_res[i, :, :], res_f[p][:, :], (VM(mrev),),
                                     ch=f"sre{p}")
                mcl = None
                for n in range(2):
                    mcl = sch.add("tensor", lambda e, nn=n, ii=i, pp=p: e.matmul(
                        PC[0:128, nn * 512:(nn + 1) * 512], s_emsk[:, 65 - ii:193 - ii],
                        g0_bf[pp][:, nn * 512:(nn + 1) * 512], start=(ii == 0), stop=False),
                        (GM(mg0),) if n == 0 else ())
                for n in range(2):
                    mcl = sch.add("tensor", lambda e, nn=n, ii=i, pp=p: e.matmul(
                        PC[0:128, nn * 512:(nn + 1) * 512], s_emsk[:, 1:129],
                        m_bf[pp][:, nn * 512:(nn + 1) * 512], start=False, stop=False),
                        (GM(mmm),) if n == 0 else ())
                for n in range(2):
                    mcl = sch.add("tensor", lambda e, nn=n, ii=i, pp=p: e.matmul(
                        PC[0:128, nn * 512:(nn + 1) * 512], s_emsk[:, 0:128],
                        cs_bf[pp][:, nn * 512:(nn + 1) * 512], start=False,
                        stop=(ii == NT - 1)),
                        (), inc=("t", 1) if n == 1 else None)
                mk[("colsum", i)] = mcl

            for i in range(2):
                mk[("lx", i)] = dma("sync", xb[i][:, :], x_d[i * 128:(i + 1) * 128, :],
                                    ch=f"lxb{i % 2}")
            p1A(0)
            p1A(1)
            mk[("lx", 2)] = dma("sync", xb[0][:, :], x_d[2 * 128:3 * 128, :],
                                (SM(mk[("xn_c", 0)]),), ch="lxb0")
            p1A(2)
            for i in range(NT):
                p1B(i)
                if i + 3 < NT:
                    mk[("lx", i + 3)] = dma("sync", xb[(i + 3) % 2][:, :],
                                            x_d[(i + 3) * 128:(i + 4) * 128, :],
                                            (SM(mk[("xn_c", i + 1)]),), ch=f"lxb{(i + 3) % 2}")
                    p1A(i + 3)

            sch.barrier()
            es_p1t.close()
            es_p1w.close()

            C2a = lambda t: R(es_p2a, t)
            tbl_sb = C2a(nc.sbuf_tensor("tbl_sb", [128, 1024], F32, side="right"))
            car0_sb = C2a(nc.sbuf_tensor("car0_sb", [64, 1024], F32, side="right"))
            l1bc_sb = C2a(nc.sbuf_tensor("l1bc_sb", [64, 1024], F32, side="right"))
            tadd_sb = C2a(nc.sbuf_tensor("tadd_sb", [64, 1024], F32, side="right"))
            hi_bf = C2a(nc.sbuf_tensor("hi_bf", [64, 1024], BF16, side="right"))
            lo_bf = C2a(nc.sbuf_tensor("lo_bf", [64, 1024], BF16, side="right"))
            hi32_sb = C2a(nc.sbuf_tensor("hi32_sb", [64, 1024], F32, side="right"))
            b0_sb = C2a(nc.sbuf_tensor("b0_sb", [1, 6144], F32, side="right"))
            ucbc_sb = C2a(nc.sbuf_tensor("ucbc_sb", [128, 1024], F32, side="right"))

            mtb = sch.add("vector", lambda e: e.tensor_copy(tbl_sb[0:66, :], PC[0:66, :]), (),
                          inc=("v", 1))
            mu1 = dma("sync", b0_sb[0:1, 0:1024], tbl_sb[64:65, :], (VM(mtb),))
            mu2 = dma("sync", b0_sb[0:1, 1024:2048], tbl_sb[65:66, :])
            mpf = None
            for n in range(2):
                mpf = sch.add("tensor", lambda e, nn=n: e.matmul(
                    PA[0:64, nn * 512:(nn + 1) * 512], s_tri64[:, :],
                    tbl_sb[0:64, nn * 512:(nn + 1) * 512], start=True, stop=True),
                    (VM(mtb),) if n == 0 else (), inc=("t", 1) if n == 1 else None)
            mc0 = sch.add("vector", lambda e: e.tensor_copy(car0_sb[:, :], PA[0:64, :]), (TM(mpf),),
                          inc=("v", 1))
            ml1 = dma("sync", b0_sb[0:1, 2048:3072], car0_sb[32:33, :], (VM(mc0),))
            ml2 = dma("sync", b0_sb[0:1, 3072:4096], car0_sb[63:64, :])
            ml3 = dma("sync", b0_sb[0:1, 4096:5120], tbl_sb[63:64, :])
            sch.add("vector", lambda e: e.tensor_add(b0_sb[0:1, 5120:6144], b0_sb[0:1, 3072:4096],
                                                     b0_sb[0:1, 4096:5120]), (LD(ml3),), inc=("v", 1))
            mt_ = sch.add("vector", lambda e: e.tensor_sub(b0_sb[0:1, 5120:6144],
                                                           b0_sb[0:1, 5120:6144],
                                                           b0_sb[0:1, 2048:3072]), inc=("v", 1))
            mb1 = None
            for n in range(2):
                mb1 = sch.add("tensor", lambda e, nn=n: e.matmul(
                    PB[0:64, nn * 512:(nn + 1) * 512], s_o64[:, :],
                    b0_sb[0:1, 2048 + nn * 512:2048 + (nn + 1) * 512], start=True, stop=True),
                    (VM(mt_),) if n == 0 else (), inc=("t", 1) if n == 1 else None)
            mevl1 = sch.add("vector", lambda e: e.tensor_copy(l1bc_sb[:, :], PB[0:64, :]), (TM(mb1),),
                            inc=("v", 1))
            mb2 = None
            for n in range(2):
                mb2 = sch.add("tensor", lambda e, nn=n: e.matmul(
                    PB[0:64, nn * 512:(nn + 1) * 512], s_o64[:, :],
                    b0_sb[0:1, 5120 + nn * 512:5120 + (nn + 1) * 512], start=True, stop=True),
                    (VM(mevl1),) if n == 0 else (), inc=("t", 1) if n == 1 else None)
            mta = sch.add("vector", lambda e: e.scalar_tensor_tensor(
                tadd_sb[:, :], PB[0:64, :], s_m1[:, :], car0_sb[:, :], ALU.mult, ALU.add),
                (TM(mb2),), inc=("v", 1))
            mta = sch.add("vector", lambda e: e.scalar_tensor_tensor(
                tadd_sb[:, :], l1bc_sb[:, :], s_m2n[:, :], tadd_sb[:, :], ALU.mult, ALU.add),
                inc=("v", 1))
            mhi = sch.add("scalar", lambda e: e.copy(hi_bf[:, :], tadd_sb[:, :]), (VM(mta),),
                          inc=("s", 1))
            mhi2 = sch.add("scalar", lambda e: e.copy(hi32_sb[:, :], hi_bf[:, :]), (SM(mhi),),
                           inc=("s", 1))
            mlo = sch.add("vector", lambda e: e.tensor_sub(lo_bf[:, :], tadd_sb[:, :], hi32_sb[:, :]),
                          (SM(mhi2),), inc=("v", 1))
            mst_hi = dma("scalar", sc_hi[:, :], hi_bf[:, :], (SM(mhi),))
            mst_lo = dma("scalar", sc_lo[:, :], lo_bf[:, :], (VM(mlo),))
            mcb_ = None
            for n in range(2):
                mcb_ = sch.add("tensor", lambda e, nn=n: e.matmul(
                    PC[:, nn * 512:(nn + 1) * 512], s_o1r[:, :],
                    b0_sb[0:1, 1024 + nn * 512:1024 + (nn + 1) * 512], start=True, stop=True),
                    (LD(mu2), VM(mtb)) if n == 0 else (), inc=("t", 1) if n == 1 else None)
            mcsbc = sch.add("vector", lambda e: e.tensor_copy(s_Csbc[:, :], PC[:, :]), (TM(mcb_),),
                            inc=("v", 1))
            mub_ = None
            for n in range(2):
                mub_ = sch.add("tensor", lambda e, nn=n: e.matmul(
                    PC[:, nn * 512:(nn + 1) * 512], s_o1r[:, :],
                    b0_sb[0:1, 0 + nn * 512:0 + (nn + 1) * 512], start=True, stop=True),
                    (VM(mcsbc), LD(mu1)) if n == 0 else (), inc=("t", 1) if n == 1 else None)
            mucbc = sch.add("vector", lambda e: e.tensor_copy(ucbc_sb[:, :], PC[:, :]), (TM(mub_),),
                            inc=("v", 1))
            mr2 = sch.add("gpsimd", lambda e: e.tensor_mul(s_R2bc[:, :], s_B1bc[:, :], ucbc_sb[:, :]),
                          (VM(mucbc),), inc=("g", 1))
            if dbg:
                dma("scalar", dbg_tbl[0:66, :], tbl_sb[0:66, :], (VM(mtb),))
                dma("scalar", dbg_bc[:, 0:1024], s_Csbc[:, :], (VM(mcsbc),))
                dma("scalar", dbg_bc[:, 1024:2048], ucbc_sb[:, :], (VM(mucbc),))
                dma("scalar", dbg_bc[:, 2048:3072], s_R2bc[:, :], (GM(mr2),))
                dma("scalar", dbg_bc[0:64, 3072:4096], car0_sb[0:64, :], (VM(mc0),))

            sch.barrier()
            es_p2a.close()
            es_p1t_dummy = None


        if phases >= 2:
            C2 = es_p2.enter_context
            A_ = [C2(nc.sbuf_tensor(f"p2A{p}", [128, 1024], F32)) for p in range(2)]
            B_ = [C2(nc.sbuf_tensor(f"p2B{p}", [128, 1024], F32)) for p in range(2)]
            Cc_ = [C2(nc.sbuf_tensor(f"p2C{p}", [128, 1024], F32)) for p in range(2)]
            D_ = [C2(nc.sbuf_tensor(f"p2D{p}", [128, 1024], F32)) for p in range(2)]
            E_ = [C2(nc.sbuf_tensor(f"p2E{p}", [128, 1024], F32)) for p in range(2)]
            F_ = [C2(nc.sbuf_tensor(f"p2F{p}", [128, 1024], F32)) for p in range(3)]
            G_ = [C2(nc.sbuf_tensor(f"p2G{p}", [128, 1024], F32)) for p in range(3)]
            H_ = [C2(nc.sbuf_tensor(f"p2H{p}", [128, 1024], F32)) for p in range(2)]
            g0b = [C2(nc.sbuf_tensor(f"p2g0{p}", [128, 1024], BF16)) for p in range(2)]
            hilo = [C2(nc.sbuf_tensor(f"hilo{p}", [2, 1024], BF16)) for p in range(2)]
            xn2b = [C2(nc.sbuf_tensor(f"xn2b{p}", [128, 1024], BF16)) for p in range(2)]
            xn2T = [C2(nc.sbuf_tensor(f"xn2T{p}", [128, 8, 128], BF16)) for p in range(2)]
            xn3b = [C2(nc.sbuf_tensor(f"xn3b{p}", [128, 1024], BF16)) for p in range(2)]
            st2 = [C2(nc.sbuf_tensor(f"st2_{p}", [128, 16], F32)) for p in range(2)]
            st3 = [C2(nc.sbuf_tensor(f"st3_{p}", [128, 16], F32)) for p in range(2)]

            mwq = load_weight(WqT, wqT_d)
            mwk = load_weight(WkT, wkT_d)
            mwv = load_weight(WvT, wvT_d)

            mk2 = {}
            for i in range(2):
                mk2[("lx", i)] = dma("sync", A_[i][:, :], x_d[i * 128:(i + 1) * 128, :],
                                     ch=f"lA{i}")
                mk2[("lhi", i)] = dma("sync", hilo[i][0:1, :], sc_hi[i:i + 1, :],
                                      (mst_hi, mst_lo), ch=f"lh{i}")
                mk2[("llo", i)] = dma("sync", hilo[i][1:2, :], sc_lo[i:i + 1, :], ch=f"lh{i}")
                mk2[("ldl", i)] = dma("vector", B_[i][:, :], sc_delta[i, :, :],
                                      (mk[("dst", i)],), ch=f"lB{i}")
                mk2[("lr", i)] = dma("vector", Cc_[i][:, :], sc_res[i, :, :],
                                     (mk[("rst", i)],), ch=f"lC{i}")


            def p2A(i):
                p = i % 2
                q3 = i % 3
                PD = PA if p == 0 else PC
                mg0 = sch.add("gpsimd", lambda e, pp=p: e.tensor_mul(
                    g0b[pp][:, :], B_[pp][:, :], s_B1bc[:, :]),
                    (VD(mk2[("ldl", i)]), TM(mk2.get(("tri", i - 2), 0)),
                     GM(mr2) if i == 0 else None), inc=("g", 1))
                mtr = None
                for n in range(2):
                    mtr = sch.add("tensor", lambda e, nn=n, pp=p, PD_=PD: e.matmul(
                        PD_[:, nn * 512:(nn + 1) * 512], s_tri128[:, :],
                        g0b[pp][:, nn * 512:(nn + 1) * 512], start=True, stop=False),
                        (GM(mg0), VM(mk2.get(("ydb", i - 2), 0))) if n == 0 else ())
                for n in range(2):
                    mtr = sch.add("tensor", lambda e, nn=n, pp=p, PD_=PD: e.matmul(
                        PD_[:, nn * 512:(nn + 1) * 512], s_o2r[:, :],
                        hilo[pp][:, nn * 512:(nn + 1) * 512], start=False, stop=True),
                        (LD(mk2[("llo", i)]),) if n == 0 else (),
                        inc=("t", 1) if n == 1 else None)
                mk2[("tri", i)] = mtr
                mei = sch.add("gpsimd", lambda e, pp=p: e.tensor_mul(
                    D_[pp][:, :], B_[pp][:, :], s_A1bc[:, :]), (), inc=("g", 1))
                mk2[("eIn", i)] = mei
                mev_ = sch.add("scalar", lambda e, pp=p: e.activation(
                    E_[pp][:, :], D_[pp][:, :], AF.Exp), (GM(mei),), inc=("s", 1))
                mydb = sch.add("vector", lambda e, pp=q3, PD_=PD: e.tensor_mul(
                    G_[pp][:, :], PD_[:, :], s_Csbc[:, :]),
                    (TM(mtr), GM(mk2.get(("x2", i - 3), 0))), inc=("v", 1))
                mk2[("ydb", i)] = mydb
                my1 = sch.add("gpsimd", lambda e, pp=p, qq=q3: e.tensor_mul(
                    F_[qq][:, :], E_[pp][:, :], s_R2bc[:, :]),
                    (SM(mev_), VM(mk2.get(("x2a", i - 3), 0))), inc=("g", 1))
                my12 = sch.add("gpsimd", lambda e, pp=p, qq=q3: e.tensor_add(
                    D_[pp][:, :], G_[qq][:, :], F_[qq][:, :]), (VM(mydb),), inc=("g", 1))
                mx1a = sch.add("vector", lambda e, pp=p: e.scalar_tensor_tensor(
                    E_[pp][:, :], Cc_[pp][:, :], s_Dbc[:, :], D_[pp][:, :], ALU.mult, ALU.add),
                    (VD(mk2[("lr", i)]), GM(my12), SM(mev_)), inc=("v", 1))
                mk2[("x1a", i)] = mx1a
                mx1 = sch.add("gpsimd", lambda e, pp=p, qq=q3: e.tensor_add(
                    F_[qq][:, :], E_[pp][:, :], A_[pp][:, :]),
                    (LD(mk2[("lx", i)]), VM(mx1a)), inc=("g", 1))
                mk2[("x1", i)] = mx1
                if dbg:
                    dma("scalar", sc_x1[i, :, :], F_[q3][:, :], (GM(mx1),))

            def p2B(i):
                p = i % 2
                q3 = i % 3
                PTx = PT1 if p == 0 else PT2
                mxn2 = ln(F_[q3], xn2b[p], st2[p], (GM(mk2[("x1", i)]),
                                                   TM(mk2.get(("tp2", i - 2), 0))))
                mtp2 = transposes8(PTx, xn2b[p], (SM(mxn2), SM(mk2.get(("ev2", i - 2), 0))))
                mk2[("tp2", i)] = mtp2
                mev2 = sch.add("scalar", lambda e, pp=p, pt=PTx: e.copy(
                    xn2T[pp][:, :, :], pt[:, :, :]),
                               (TM(mtp2),), inc=("s", 1))
                mk2[("ev2", i)] = mev2
                mf1 = None
                for n in range(2):
                    for k in range(8):
                        mf1 = sch.add("tensor", lambda e, nn=n, kk=k, pp=p: e.matmul(
                            PB[:, nn * 512:(nn + 1) * 512], xn2T[pp][:, kk, :],
                            W_eff[:, kk * 1024 + nn * 512: kk * 1024 + nn * 512 + 512],
                            start=(kk == 0), stop=(kk == 7)),
                            (SM(mev2), VM(mk2.get(("x2a", i - 1), 0)), VM(m_weff_v))
                            if (n == 0 and k == 0) else (),
                            inc=("t", 1) if (n == 1 and k == 7) else None)
                mx2a = sch.add("vector", lambda e, qq=q3: e.tensor_add(
                    G_[qq][:, :], PB[:, :], s_bfbc[:, :]), (TM(mf1),), inc=("v", 1))
                mk2[("x2a", i)] = mx2a
                mx2 = sch.add("gpsimd", lambda e, pp=p, qq=q3: e.tensor_add(
                    H_[pp][:, :], G_[qq][:, :], F_[qq][:, :]),
                    (VM(mx2a), mk2.get(("sx2", i - 2))), inc=("g", 1))
                mk2[("x2", i)] = mx2

            def p2C(i):
                p = i % 2
                mxn3 = ln(H_[p], xn3b[p], st3[p], (GM(mk2[("x2", i)]),
                                                   mk2.get(("sxn3", i - 2))))
                mk2[("sxn3", i)] = dma("scalar", sc_xn3[i, :, :], xn3b[p][:, :], ch=f"sx3{p}")
                if i < NQ:
                    mk2[("sx2", i)] = dma("scalar", sc_x2[i, :, :], H_[p][:, :],
                                          (GM(mk2[("x2", i)]),), ch=f"sx2{p}")

            def p2loads(i):
                p = i % 2
                mk2[("lx", i)] = dma("sync", A_[p][:, :], x_d[i * 128:(i + 1) * 128, :],
                                     (GM(mk2[("x1", i - 2)]),), ch=f"lA{p}")
                mk2[("lhi", i)] = dma("sync", hilo[p][0:1, :], sc_hi[i:i + 1, :],
                                      (TM(mk2[("tri", i - 2)]),), ch=f"lh{p}")
                mk2[("llo", i)] = dma("sync", hilo[p][1:2, :], sc_lo[i:i + 1, :], ch=f"lh{p}")
                mk2[("ldl", i)] = dma("vector", B_[p][:, :], sc_delta[i, :, :],
                                      (GM(mk2[("eIn", i - 2)]), mk[("dst", i)]), ch=f"lB{p}")
                mk2[("lr", i)] = dma("vector", Cc_[p][:, :], sc_res[i, :, :],
                                     (VM(mk2[("x1a", i - 2)]), mk[("rst", i)]), ch=f"lC{p}")

            p2A(0)
            p2A(1)
            p2loads(2)
            p2A(2)
            p2B(0)
            for i in range(NT):
                p2C(i)
                if i + 3 < NT:
                    p2loads(i + 3)
                    p2A(i + 3)
                if i + 1 < NT:
                    p2B(i + 1)

            sch.barrier()
            es_p2.close()
            es_bc.close()
            es_wtmp.close()

        if phases >= 3:
            xg3T = [R(es_prep, nc.sbuf_tensor(f"xg3T{p}", [128, 8, 512], BF16, side="right"))
                    for p in range(2)]
            xq_in = [R(es_prep, nc.sbuf_tensor(f"xq_in{p}", [128, 1024], BF16, side="right"))
                     for p in range(2)]
            qsp = [R(es_qsp, nc.sbuf_tensor(f"qsp{p}", [128, 8, 512], BF16, side="right"))
                   for p in range(2)]

            mk3 = {"tcnt": 0, "vevlast": 0}

            def prep_load(ti, tc):
                mk3[("gld", tc)] = dma("sync", xq_in[tc % 2][:, :], sc_xn3[ti, :, :],
                                       (TM(mk3.get(("gtp", tc - 2), 0)), mk2[("sxn3", ti)]),
                                       ch=f"lxq{tc % 2}")

            def prep_group(tiles, pg):
                last = None
                tiles = list(tiles)
                for j, ti in enumerate(tiles):
                    tc = mk3["tcnt"]
                    tl = tc % 4
                    if ("gld", tc) not in mk3:
                        prep_load(ti, tc)
                    if j + 1 < len(tiles) and ("gld", tc + 1) not in mk3:
                        prep_load(tiles[j + 1], tc + 1)
                    PTp = PT2 if tc % 2 == 0 else PT1
                    mtp = transposes8(PTp, xq_in[tc % 2],
                                      (LD(mk3[("gld", tc)]), SM(mk3.get(("gev", tc - 2), 0))))
                    mk3[("gtp", tc)] = mtp
                    last = sch.add("scalar",
                                   lambda e, tt=tl, pp=pg, pt=PTp: e.copy(
                                       xg3T[pp][:, :, tt * 128:(tt + 1) * 128], pt[:, :, :]),
                                   (TM(mtp),), inc=("s", 1))
                    mk3[("gev", tc)] = last
                    mk3["tcnt"] = tc + 1
                return last

            for g in range(8):
                pg = g % 2
                mev = prep_group(range(4 * g, 4 * g + 4), pg)
                for c in range(8):
                    half = c % 2
                    mq = None
                    for k in range(8):
                        mq = sch.add("tensor", lambda e, cc=c, kk=k, hh=half, pp=pg: e.matmul(
                            PA[:, hh * 512:(hh + 1) * 512],
                            WqT[:, kk * 1024 + cc * 128: kk * 1024 + (cc + 1) * 128],
                            xg3T[pp][:, kk, :], start=(kk == 0), stop=(kk == 7)),
                            ((SM(mev), GM(mwq)) if c == 0 else
                             (SM(mk3.get(("qev", g, c - 2), 0)),)) if k == 0 else (),
                            inc=("t", 1) if k == 7 else None)
                    mqe = sch.add("scalar", lambda e, cc=c, hh=half, pp=pg: e.activation(
                        qsp[pp][:, cc, :], PA[:, hh * 512:(hh + 1) * 512],
                        AF.Identity, bias=s_bqT[:, cc:cc + 1], scale=1.0),
                        (TM(mq), mk3.get(("qst", g - 2))), inc=("s", 1))
                    mk3[("qev", g, c)] = mqe
                for t in range(4):
                    mk3[("qst", g)] = dma("scalar", sc_qT[4 * g + t, :, :, :],
                                          qsp[pg][:, :, t * 128:(t + 1) * 128],
                                          (SM(mk3[("qev", g, 7)]),) if t == 0 else (),
                                          ch=f"sq{pg}")
            sch.barrier()
            es_qsp.close()
            es_pqw.close()
            C3 = es_p3.enter_context
            kT_res = C3(nc.sbuf_tensor("kT_res", [128, 8, 2048], BF16))
            v_res = C3(nc.sbuf_tensor("v_res", [128, 16384], BF16))
            qT_t = [C3(nc.sbuf_tensor(f"qT_t{p}", [128, 8, 128], BF16)) for p in range(2)]
            P_bf = [C3(nc.sbuf_tensor(f"P_bf{p}", [128, 2048], BF16)) for p in range(2)]
            pt_sb = [C3(nc.sbuf_tensor(f"pt_sb{p}", [128, 16, 128], BF16))
                     for p in range(2)]
            zs = [C3(nc.sbuf_tensor(f"zs{p}", [128, 1024], F32)) for p in range(2)]
            zl = [C3(nc.sbuf_tensor(f"zl{p}", [128, 1024], F32)) for p in range(3)]
            x2l = [C3(nc.sbuf_tensor(f"x2l{p}", [128, 1024], F32)) for p in range(2)]
            x3b = [C3(nc.sbuf_tensor(f"x3b{p}", [128, 1024], F32)) for p in range(2)]
            xn4b = [C3(nc.sbuf_tensor("xn4b0", [128, 1024], BF16))] * 2
            xn4T = [pt_sb[0][:, 0:8, :], pt_sb[1][:, 0:8, :]]
            st4 = [C3(nc.sbuf_tensor(f"st4_{p}", [128, 16], F32)) for p in range(2)]
            m_run = C3(nc.sbuf_tensor("m_run", [128, 32], F32))
            l_run = C3(nc.sbuf_tensor("l_run", [128, 32], F32))
            cmax = C3(nc.sbuf_tensor("cmax", [128, 8], F32))
            lbuf = C3(nc.sbuf_tensor("lbuf", [128, 8], F32))
            pstat = C3(nc.sbuf_tensor("pstat", [128, 8], F32))

            for r in range(KR):
                sch.barrier()
                for gg in range(4):
                    pg = (8 + r * 4 + gg) % 2
                    mev = prep_group(range(16 * r + 4 * gg, 16 * r + 4 * gg + 4), pg)
                    for c in range(8):
                        half = c % 2
                        mq = None
                        for k in range(8):
                            mq = sch.add("tensor", lambda e, cc=c, kk=k, hh=half, pp=pg: e.matmul(
                                PA[:, hh * 512:(hh + 1) * 512],
                                WkT[:, kk * 1024 + cc * 128: kk * 1024 + (cc + 1) * 128],
                                xg3T[pp][:, kk, :], start=(kk == 0), stop=(kk == 7)),
                                ((SM(mev), GM(mwk)) if c == 0 else
                                 (SM(mk3.get(("kev", gg, c - 2), 0)),)) if k == 0 else (),
                                inc=("t", 1) if k == 7 else None)
                        mqe = sch.add("scalar", lambda e, cc=c, hh=half, go=gg: e.activation(
                            kT_res[:, cc, go * 512:(go + 1) * 512],
                            PA[:, hh * 512:(hh + 1) * 512],
                            AF.Identity, bias=s_bkT[:, cc:cc + 1], scale=1.0),
                            (TM(mq),), inc=("s", 1))
                        mk3[("kev", gg, c)] = mqe
                    for t in range(4):
                        mv = None
                        for n in range(2):
                            for k in range(8):
                                mv = sch.add("tensor", lambda e, nn=n, kk=k, tt=t, pp=pg: e.matmul(
                                    PB[:, nn * 512:(nn + 1) * 512],
                                    xg3T[pp][:, kk, tt * 128:(tt + 1) * 128],
                                    WvT[:, kk * 1024 + nn * 512: kk * 1024 + nn * 512 + 512],
                                    start=(kk == 0), stop=(kk == 7)),
                                    ((GM(mwv), VM(mk3["vevlast"]))
                                     if (n == 0 and k == 0) else ()),
                                    inc=("t", 1) if (n == 1 and k == 7) else None)
                        mve = sch.add("vector", lambda e, tt=t, go=gg: e.tensor_tensor(
                            v_res[:, (4 * go + tt) * 1024:(4 * go + tt + 1) * 1024],
                            PB[:, :], s_bvbc[:, :], ALU.add), (TM(mv),), inc=("v", 1))
                        mk3["vevlast"] = mve
                sch.barrier()


                def qS(step):
                    qi = step - r * NQ
                    p = step % 2
                    ac = 3 + 3 * (step % 2)
                    mlq = dma("vector", qT_t[p][:, :, :], sc_qT[qi, :, :, :],
                              (TM(mk3.get(("smm", step - 2), 0)), mk3[("qst", qi // 4)]),
                              ch=f"lqt{p}")
                    if r > 0:
                        mk3[("lz", step)] = dma("vector", zl[step % 3][:, :], sc_z[qi, :, :],
                                                (VM(mk3.get(("zco", step - 3), 0)),
                                                 mk3[("zst", (r - 1) * NQ + qi)]),
                                                ch=f"lzl{step % 3}")
                    if r == KR - 1:
                        mk3[("lx2", step)] = dma("vector", x2l[p][:, :], sc_x2[qi, :, :],
                                                 (VM(mk3.get(("x3", step - 2), 0)),
                                                  mk2[("sx2", qi)]), ch=f"lx2{p}")
                    msmm = None
                    for n in range(4):
                        ps_ = PA if n < 2 else PB
                        off = (n % 2) * 512
                        for c in range(8):
                            msmm = sch.add("tensor", lambda e, nn=n, cc=c, o=ps_, of=off, pp=p:
                                           e.matmul(o[:, of:of + 512], qT_t[pp][:, cc, :],
                                                    kT_res[:, cc, nn * 512:(nn + 1) * 512],
                                                    start=(cc == 0), stop=(cc == 7)),
                                           ((VD(mlq), SM(mk3.get(("exp", step - 1, n), 0)),
                                             VM(mk3.get(("rmx", step - 1, n), 0)))
                                            if c == 0 else ()),
                                           inc=("t", 1) if c == 7 else None)
                        mk3[("smmn", step, n)] = msmm
                        mrx = sch.add("vector", lambda e, nn=n, o=ps_, of=off: e.reduce_max(
                            cmax[:, nn:nn + 1], o[:, of:of + 512], AX), (TM(msmm),),
                            inc=("v", 1))
                        mk3[("rmx", step, n)] = mrx
                    mk3[("smm", step)] = msmm
                    mmx = sch.add("vector", lambda e: e.reduce_max(pstat[:, 0:1], cmax[:, 0:4],
                                                                   AX),
                                  (VM(mk3[("rmx", step, 3)]),), inc=("v", 1))
                    if r > 0:
                        mmx = sch.add("vector", lambda e, q_=qi: e.tensor_max(
                            pstat[:, 0:1], pstat[:, 0:1], m_run[:, q_:q_ + 1]), inc=("v", 1))
                        mdm = sch.add("vector", lambda e, q_=qi: e.tensor_sub(
                            pstat[:, 2:3], m_run[:, q_:q_ + 1], pstat[:, 0:1]), inc=("v", 1))
                        mal = sch.add("scalar", lambda e, a_=ac: e.activation(
                            pstat[:, a_:a_ + 1], pstat[:, 2:3], AF.Exp, scale=ISQ),
                            (VM(mdm),), inc=("s", 1))
                        mk3[("al", step)] = mal
                    mng = sch.add("scalar", lambda e: e.activation(pstat[:, 1:2], pstat[:, 0:1],
                                                                   AF.Identity, scale=-ISQ),
                                  (VM(mmx),), inc=("s", 1))
                    sch.add("scalar", lambda e, q_=qi: e.activation(
                        m_run[:, q_:q_ + 1], pstat[:, 0:1], AF.Identity, scale=1.0),
                        (SM(mng),), inc=("s", 1))
                    for n in range(4):
                        ps_ = PA if n < 2 else PB
                        off = (n % 2) * 512
                        mex = sch.add("scalar", lambda e, nn=n, o=ps_, of=off, pp=p:
                                      e.activation(
                                          P_bf[pp][:, nn * 512:(nn + 1) * 512],
                                          o[:, of:of + 512], AF.Exp,
                                          bias=pstat[:, 1:2], scale=ISQ,
                                          accum_out=lbuf[:, nn:nn + 1]),
                                      (TM(mk3.get(("ptp", step - 2), 0)), SM(mng),
                                       VM(mk3.get(("la", step - 1), 0))) if n == 0 else (),
                                      inc=("s", 1))
                        mk3[("exp", step, n)] = mex
                    mla = sch.add("vector", lambda e: e.reduce_sum(pstat[:, 4:5], lbuf[:, 0:4],
                                                                   AX),
                                  (SM(mk3[("exp", step, 3)]),), inc=("v", 1))
                    mk3[("la", step)] = mla
                    if r == 0:
                        sch.add("scalar", lambda e, q_=qi: e.activation(
                            l_run[:, q_:q_ + 1], pstat[:, 4:5], AF.Identity, scale=1.0),
                            (VM(mla),), inc=("s", 1))
                        mk3[("lr", step)] = ("v", mla)
                    else:
                        sch.add("vector", lambda e, q_=qi, a_=ac: e.tensor_mul(
                            l_run[:, q_:q_ + 1], l_run[:, q_:q_ + 1], pstat[:, a_:a_ + 1]),
                            (SM(mk3[("al", step)]),), inc=("v", 1))
                        mlr = sch.add("vector", lambda e, q_=qi: e.tensor_add(
                            l_run[:, q_:q_ + 1], l_run[:, q_:q_ + 1], pstat[:, 4:5]),
                            (VM(mla),), inc=("v", 1))
                        mk3[("lr", step)] = ("v", mlr)

                def qB(step):
                    p = step % 2
                    mpt = None
                    for j in range(16):
                        pt = PT1 if j < 8 else PT2
                        mpt = sch.add("tensor", lambda e, jj=j, pt_=pt, pp=p: e.transpose(
                            pt_[:, jj % 8, :], P_bf[pp][:, jj * 128:(jj + 1) * 128],
                            s_id[:, :]),
                            ((SM(mk3[("exp", step, 3)]), SM(mk3.get(("pev", step - 1), 0)),
                              SM(mk3.get(("ev4", step - 1), 0)))
                             if j == 0 else ()),
                            inc=("t", 1) if j == 15 else None)
                    mk3[("ptp", step)] = mpt
                    sch.add("scalar", lambda e, pp=p: e.copy(pt_sb[pp][:, 0:8, :],
                                                             PT1[:, :, :]),
                            (TM(mpt),), inc=("s", 1))
                    mpe = sch.add("scalar", lambda e, pp=p: e.copy(pt_sb[pp][:, 8:16, :],
                                                                   PT2[:, :, :]),
                                  inc=("s", 1))
                    mk3[("pev", step)] = mpe

                def qC(step):
                    qi = step - r * NQ
                    p = step % 2
                    ac = 3 + 3 * (step % 2)
                    mpe = mk3[("pev", step)]
                    mpv = None
                    for j in range(16):
                        for n in range(2):
                            mpv = sch.add("tensor", lambda e, jj=j, nn=n, pp=p: e.matmul(
                                PC[:, nn * 512:(nn + 1) * 512], pt_sb[pp][:, jj, :],
                                v_res[:, jj * 1024 + nn * 512: jj * 1024 + nn * 512 + 512],
                                start=(jj == 0), stop=(jj == 15)),
                                ((SM(mpe), VM(mk3.get(("zco", step - 1), 0)),
                                  VM(mk3.get(("mo1", step - 1), 0)))
                                 if (j == 0 and n == 0) else ()),
                                inc=("t", 1) if (j == 15 and n == 1) else None)
                    if r == 0:
                        mz = sch.add("vector", lambda e, pp=p: e.tensor_copy(zs[pp][:, :],
                                                                             PC[:, :]),
                                     (TM(mpv), mk3.get(("zst", step - 2))), inc=("v", 1))
                    else:
                        mz = sch.add("vector", lambda e, pp=p, z3=step % 3, a_=ac:
                                     e.scalar_tensor_tensor(
                            zs[pp][:, :], zl[z3][:, :], pstat[:, a_:a_ + 1], PC[:, :],
                            ALU.mult, ALU.add),
                            (TM(mpv), mk3.get(("lz", step)), SM(mk3[("al", step)]),
                             mk3.get(("zst", step - 2)),
                             GM(mk3.get(("moo", step - 2), 0))),
                            inc=("v", 1))
                    mk3[("zco", step)] = mz
                    if r < KR - 1:
                        mk3[("zst", step)] = dma("scalar", sc_z[qi, :, :], zs[p][:, :],
                                                 (VM(mz),), ch=f"sz{p}")
                    else:
                        mli = sch.add("vector", lambda e, q_=qi: e.reciprocal(
                            pstat[:, 5:6], l_run[:, q_:q_ + 1]),
                            (mk3[("lr", step)],), inc=("v", 1))
                        mx3 = sch.add("vector", lambda e, pp=p: e.scalar_tensor_tensor(
                            x3b[pp][:, :], zs[pp][:, :], pstat[:, 5:6], x2l[pp][:, :],
                            ALU.mult, ALU.add),
                            (VM(mli), mk3.get(("lx2", step)),
                             GM(mk3.get(("moo", step - 2), 0))), inc=("v", 1))
                        mk3[("x3", step)] = mx3
                        if dbg:
                            mk3[("zdbg", step)] = dma("scalar", sc_z[qi, :, :], zs[p][:, :],
                                                      (VM(mx3),), ch=f"sz{p}")
                        mxn4 = ln(x3b[p], xn4b[p], st4[p],
                                  (VM(mx3), TM(mk3.get(("tp4", step - 1), 0))))
                        mtp4 = transposes8(PT1, xn4b[p], (SM(mxn4),))
                        mk3[("tp4", step)] = mtp4
                        mev4 = sch.add("scalar", lambda e, pp=p: e.copy(xn4T[pp],
                                                                        PT1[:, :, :]),
                                       (TM(mtp4),), inc=("s", 1))
                        mk3[("ev4", step)] = mev4
                        mf2 = None
                        for n in range(2):
                            for k in range(8):
                                mf2 = sch.add("tensor", lambda e, nn=n, kk=k, pp=p: e.matmul(
                                    PC[:, nn * 512:(nn + 1) * 512], xn4T[pp].tensor[:, kk, :],
                                    W_eff[:, kk * 1024 + nn * 512: kk * 1024 + nn * 512 + 512],
                                    start=(kk == 0), stop=(kk == 7)),
                                    (SM(mev4), VM(mk3[("zco", step)])) if (n == 0 and k == 0)
                                    else (),
                                    inc=("t", 1) if (n == 1 and k == 7) else None)
                        mo1 = sch.add("vector", lambda e, pp=p: e.tensor_add(
                            zs[pp][:, :], PC[:, :], s_bfbc[:, :]),
                            (TM(mf2), GM(mk3.get(("moo", step - 2), 0)),
                             mk3.get(("zdbg", step))), inc=("v", 1))
                        mk3[("mo1", step)] = mo1
                        moo = sch.add("gpsimd", lambda e, pp=p: e.tensor_add(
                            s_out[pp][:, :], zs[pp][:, :], x3b[pp][:, :]),
                            (VM(mo1), mk3.get(("ost", step - 1))), inc=("g", 1))
                        mk3[("moo", step)] = moo
                        mk3[("ost", step)] = dma("scalar", out_d[qi * 128:(qi + 1) * 128, :],
                                                 s_out[p][:, :], (GM(moo),), ch=f"so{p}")

                qS(r * NQ)
                qS(r * NQ + 1)
                qB(r * NQ)
                if r < KR - 1:
                    for qi in range(NQ):
                        step = r * NQ + qi
                        if qi + 2 < NQ:
                            qS(step + 2)
                        if qi + 1 < NQ:
                            qB(step + 1)
                        qC(step)
                else:
                    for qi in range(NQ):
                        step = r * NQ + qi
                        qC(step)
                        if qi + 2 < NQ:
                            qS(step + 2)
                        if qi + 1 < NQ:
                            qB(step + 1)

        pass
    except _Cut:
        pass

    if dbg and phases >= 3:
        dma("scalar", dbg_ml[:, 0:32], m_run[:, :], (VM(sch.count("v")),))
        dma("scalar", dbg_ml[:, 32:64], l_run[:, :], (VM(sch.count("v")),))
    _fw = [(snm, sch.count(snm)) for snm in list(sch.cnt.keys())
           if snm not in ("t", "v", "s", "g")]
    _fw += [TM(sch.count("t")), VM(sch.count("v")), GM(sch.count("g"))]
    sch.add("scalar", lambda e: e.activation(s_sqscr[:, 0:1], s_sqscr[:, 0:1], AF.Identity,
                                             scale=1.0), tuple(_fw), inc=("s", 1))

    for n in sorted(set(list(sch.cnt.keys()) + ["t", "v", "s", "g", "ld", "st"])):
        sems[n] = C(nc.semaphore(n))

    def run_prog(name):
        def f(eng):
            for item in sch.prog[name]:
                if item[0] == "w":
                    eng.wait_ge(sems[item[1]], item[2])
                else:
                    _, fn, inc = item
                    ins = fn(eng)
                    if inc is not None:
                        ins.then_inc(sems[inc[0]], inc[1])
        return f

    with nc.Block() as blk:
        blk.tensor(run_prog("tensor"))
        blk.vector(run_prog("vector"))
        blk.scalar(run_prog("scalar"))
        blk.sync(run_prog("sync"))
        blk.gpsimd(run_prog("gpsimd"))

    for es in (es_p3, es_p2, es_p1t, es_p1w, es_pqw, es_qsp, es_prep,
               es_p2a, es_fold, es_bc, es_wtmp, es_p23w, es_glob):
        es.close()
    return nc


def make_inmaps(inputs):
    f32 = np.float32
    bf = ml_dtypes.bfloat16
    x = np.asarray(inputs["L0_attn_a_x"], f32)
    Wp = np.asarray(inputs["L0_attn_a_W_proj"], f32)
    A = np.asarray(inputs["L0_attn_a_A"], f32).reshape(1, 1024)
    Bp = np.asarray(inputs["L0_attn_a_Bp"], f32).reshape(1, 1024)
    Dsc = np.asarray(inputs["L0_attn_a_D"], f32).reshape(1)
    Wd = np.asarray(inputs["L0_attn_a_W_delta"], f32)
    bd = np.asarray(inputs["L0_attn_a_b_delta"], f32).reshape(1, 1024)
    Wq = np.asarray(inputs["L0_attn_b_Wq"], f32)
    bq = np.asarray(inputs["L0_attn_b_bq"], f32)
    Wk = np.asarray(inputs["L0_attn_b_Wk"], f32)
    bk = np.asarray(inputs["L0_attn_b_bk"], f32)
    Wv = np.asarray(inputs["L0_attn_b_Wv"], f32)
    bv = np.asarray(inputs["L0_attn_b_bv"], f32).reshape(1, 1024)
    W = np.asarray(inputs["L0_ffn_W"], f32)
    b = np.asarray(inputs["L0_ffn_b"], f32).reshape(1, 1024)
    ld_ = np.asarray(inputs["L0_ffn_lora_down"], f32)
    lu_ = np.asarray(inputs["L0_ffn_lora_up"], f32)
    da_ = np.asarray(inputs["L0_ffn_down_aux"], f32)
    ua_ = np.asarray(inputs["L0_ffn_up_aux"], f32)

    ct = np.ascontiguousarray

    def _pad128(a):
        out = np.zeros((128,) + a.shape[1:], f32)
        out[:a.shape[0]] = a
        return out

    rows = np.zeros((8, 1024), f32)
    rows[0] = A
    rows[1] = Bp
    rows[2] = bd
    rows[3] = b
    rows[4] = bv
    rows[5, 0] = Dsc[0]
    emsk = np.zeros((128, 256), bf)
    emsk[:, 65] = 1
    shared = dict(
        wp0=ct(Wp[0:1024, :]), wdT=ct(Wd.T), wcT=ct(Wp[2048:3072, :].T),
        wrT=ct(Wp[3072:4096, :].T), wT=ct(W.T),
        uaT=_pad128(ua_.T), da=_pad128(da_), ldT=_pad128(ld_.T), luT=_pad128(lu_.T),
        wqT=ct(Wq.T), wkT=ct(Wk.T), wvT=ct(Wv.T),
        rows=rows, bqT=ct(bq.reshape(8, 128).T), bkT=ct(bk.reshape(8, 128).T),
        ident=np.eye(128, dtype=bf), tri128=np.triu(np.ones((128, 128))).astype(bf),
        tri64x=np.triu(np.ones((64, 64), f32), 1), emsk=emsk,
        ones1r=np.ones((1, 128), f32), ones64=np.ones((1, 64), f32),
        ones2r=np.ones((2, 128), bf),
    )
    in_maps = []
    for c in range(8):
        bidx, h = c // 2, c % 2
        xl = x[bidx] if h == 0 else np.concatenate([x[bidx, 4096:], x[bidx, :4096]], axis=0)
        m1 = np.zeros((64, 1), f32)
        m2n = np.zeros((64, 1), f32)
        if h == 1:
            m1[0:32] = 1.0
            m2n[32:64] = -1.0
        im = dict(shared)
        im["x"] = ct(xl)
        im["m1c"] = m1
        im["m2n"] = m2n
        in_maps.append(im)
    return in_maps


def kernel(**inputs):
    in_maps = make_inmaps(inputs)
    nc = build(phases=3, dbg=False)
    res = run_bass_kernel_spmd(nc, in_maps, core_ids=list(range(8)))
    out = np.empty((4, 8192, 1024), np.float32)
    for c in range(8):
        bidx, h = c // 2, c % 2
        out[bidx, h * 4096:(h + 1) * 4096] = np.asarray(res.results[c]["out"], np.float32)
    return out


# version 09
# speedup vs baseline: 1.0351x; 1.0351x over previous
import numpy as np
import ml_dtypes
from contextlib import ExitStack

import concourse.bass as bass
import concourse.mybir as mybir
from concourse.bass_utils import run_bass_kernel_spmd

F32 = mybir.dt.float32
BF16 = mybir.dt.bfloat16
AX = mybir.AxisListType.X
AF = mybir.ActivationFunctionType
ALU = mybir.AluOpType

NT = 64
NQ = 32
KR = 4
D = 1024
EPS = 1e-5
ISQ = 1.0 / 32.0

ENGS = ("tensor", "vector", "scalar", "sync", "gpsimd")


class Sched:
    def __init__(self):
        self.prog = {e: [] for e in ENGS}
        self.cnt = {}
        self.lastwait = {e: {} for e in ENGS}

    CHAIN = {"vector": "v", "scalar": "s", "gpsimd": "g"}

    def add(self, eng, fn, waits=(), inc=None):
        sem0 = self.CHAIN.get(eng)
        if sem0 is not None:
            waits = ((sem0, self.cnt.get(sem0, 0)),) + tuple(waits)
            if inc is None:
                inc = (sem0, 1)
        for w in waits:
            if w is None:
                continue
            sem, val = w
            if isinstance(val, tuple):
                sem, val = val
            if val is None or val <= 0:
                continue
            if sem in ("ld", "st"):
                val = self.cnt.get(sem, 0)
            lw = self.lastwait[eng]
            if lw.get(sem, 0) >= val:
                continue
            lw[sem] = val
            self.prog[eng].append(("w", sem, val))
        self.prog[eng].append(("c", fn, inc))
        if inc is not None:
            sem, amt = inc
            self.cnt[sem] = self.cnt.get(sem, 0) + amt
            return self.cnt[sem]
        return None

    def count(self, sem):
        return self.cnt.get(sem, 0)

    def barrier(self):
        snap = dict(self.cnt)
        for e in ENGS:
            for sem, val in snap.items():
                if val > 0 and self.lastwait[e].get(sem, 0) < val:
                    self.lastwait[e][sem] = val
                    self.prog[e].append(("w", sem, val))


class _Cut(Exception):
    pass


def build(phases=3, dbg=False, cut=None):
    nc = bass.Bass("TRN2", target_bir_lowering=False)

    dI = lambda n, s, dt=F32: nc.dram_tensor(n, s, dt, kind="ExternalInput").ap()
    x_d = dI("x", [8192, 1024])
    wp0_d = dI("wp0", [1024, 1024])
    wdT_d = dI("wdT", [1024, 1024])
    wcT_d = dI("wcT", [1024, 1024])
    wrT_d = dI("wrT", [1024, 1024])
    wT_d = dI("wT", [1024, 1024])
    uaT_d = dI("uaT", [128, 1024])
    da_d = dI("da", [128, 1024])
    ldT_d = dI("ldT", [128, 4])
    luT_d = dI("luT", [128, 50])
    wqT_d = dI("wqT", [1024, 1024])
    wkT_d = dI("wkT", [1024, 1024])
    wvT_d = dI("wvT", [1024, 1024])
    rows_d = dI("rows", [8, 1024])
    bqT_d = dI("bqT", [128, 8])
    bkT_d = dI("bkT", [128, 8])
    m1_d = dI("m1c", [64, 1])
    m2n_d = dI("m2n", [64, 1])
    ident_d = dI("ident", [128, 128], BF16)
    tri128_d = dI("tri128", [128, 128], BF16)
    tri64_d = dI("tri64x", [64, 64])
    emsk_d = dI("emsk", [128, 256], BF16)
    o1r_d = dI("ones1r", [1, 128])
    o64_d = dI("ones64", [1, 64])
    o2r_d = dI("ones2r", [2, 128], BF16)

    sk = "ExternalOutput" if dbg else "Internal"
    dS = lambda n, s, dt=F32: nc.dram_tensor(n, s, dt, kind=sk).ap()
    sc_delta = dS("sc_delta", [NT, 128, 1024])
    sc_res = dS("sc_res", [NT, 128, 1024])
    sc_hi = dS("sc_hi", [64, 1024], BF16)
    sc_lo = dS("sc_lo", [64, 1024], BF16)
    sc_xn3 = dS("sc_xn3", [NT, 128, 1024], BF16)
    sc_x2 = dS("sc_x2", [NQ, 128, 1024])
    sc_qT = dS("sc_qT", [NQ, 128, 8, 128], BF16)
    sc_z = dS("sc_z", [NQ, 128, 1024])
    if dbg:
        dbg_tbl = nc.dram_tensor("dbg_tbl", [128, 1024], F32, kind="ExternalOutput").ap()
        dbg_bc = nc.dram_tensor("dbg_bc", [128, 4096], F32, kind="ExternalOutput").ap()
        sc_x1 = nc.dram_tensor("sc_x1", [NT, 128, 1024], F32, kind="ExternalOutput").ap()
        dbg_ml = nc.dram_tensor("dbg_ml", [128, 64], F32, kind="ExternalOutput").ap()
    out_d = nc.dram_tensor("out", [4096, 1024], F32, kind="ExternalOutput").ap()

    sch = Sched()

    def _ck(n):
        if cut is not None and n >= cut:
            raise _Cut()

    LD = lambda m: m if isinstance(m, tuple) else ("ld", m)
    VD = LD
    ST = lambda m: m if isinstance(m, tuple) else ("st", m)
    TM = lambda m: ("t", m)
    VM = lambda m: ("v", m)
    SM = lambda m: ("s", m)
    GM = lambda m: ("g", m)

    (es_glob, es_p23w, es_pqw, es_p1w, es_p1t, es_p2, es_p3,
     es_bc, es_wtmp, es_fold, es_p2a, es_prep, es_qsp) = (ExitStack() for _ in range(13))
    C = es_glob.enter_context
    R = lambda es, t: es.enter_context(t)

    s_id = C(nc.sbuf_tensor("s_id", [128, 128], BF16))
    s_tri128 = C(nc.sbuf_tensor("s_tri128", [128, 128], BF16))
    s_tri64 = C(nc.sbuf_tensor("s_tri64", [64, 64], F32))
    s_emsk = C(nc.sbuf_tensor("s_emsk", [128, 256], BF16))
    s_o1r = C(nc.sbuf_tensor("s_o1r", [1, 128], F32))
    s_o64 = C(nc.sbuf_tensor("s_o64", [1, 64], F32))
    s_o2r = C(nc.sbuf_tensor("s_o2r", [2, 128], BF16))
    s_bqT = C(nc.sbuf_tensor("s_bqT", [128, 8], F32))
    s_bkT = C(nc.sbuf_tensor("s_bkT", [128, 8], F32))
    s_m1 = C(nc.sbuf_tensor("s_m1", [64, 1], F32))
    s_m2n = C(nc.sbuf_tensor("s_m2n", [64, 1], F32))
    s_Dbc = C(nc.sbuf_tensor("s_Dbc", [128, 1], F32))
    s_bfbc = C(nc.sbuf_tensor("s_bfbc", [128, 1024], F32))
    s_bvbc = R(es_prep, nc.sbuf_tensor("s_bvbc", [128, 1024], F32, side="right"))
    W_eff = C(nc.sbuf_tensor("W_eff", [128, 8192], BF16))
    s_wtmp = [R(es_wtmp, nc.sbuf_tensor(f"s_wtmp{p}", [128, 1024], F32, side="right"))
              for p in range(2)]
    s_sqscr = C(nc.sbuf_tensor("s_sqscr", [128, 1024], BF16))
    s_out = [C(nc.sbuf_tensor("s_out0", [128, 1024], F32))] * 2

    PA = C(nc.psum_tensor("PA", [128, 1024], F32))
    PB = C(nc.psum_tensor("PB", [128, 1024], F32))
    PC = C(nc.psum_tensor("PC", [128, 1024], F32))
    PT1 = C(nc.psum_tensor("PT1", [128, 8, 128], BF16))
    PT2 = C(nc.psum_tensor("PT2", [128, 8, 128], BF16))

    sems = {}

    def dma(q, outap, inap, waits=(), ch=None):
        if q == "vector":
            q = "sync"
        semn = ch if ch is not None else {"sync": "ld", "scalar": "st"}[q]
        if semn in ("ld", "st"):
            waits = tuple(waits) + ((semn, 1),)
        m = sch.add(q, lambda e, o=outap, i=inap: e.dma_start(out=o, in_=i),
                    waits, inc=(semn, 16))
        return (semn, m)

    def transposes8(pt, src_bf, waits=()):
        m = None
        for c in range(8):
            m = sch.add("tensor",
                        lambda e, o=pt[:, c, :], i_=src_bf[:, c * 128:(c + 1) * 128]:
                        e.transpose(o, i_, s_id[:, :]),
                        waits if c == 0 else (), inc=("t", 1) if c == 7 else None)
        return m

    def ln(xin, xn_bf, stats, w_x):
        mv = sch.add("vector", lambda e: e.reduce_sum(stats[:, 0:1], xin[:, :], AX), w_x,
                     inc=("v", 1))
        msq = sch.add("scalar", lambda e: e.activation(s_sqscr[:, :], xin[:, :], AF.Square,
                                                       accum_out=stats[:, 1:2]), w_x,
                      inc=("s", 1))
        mng = sch.add("scalar", lambda e: e.activation(stats[:, 2:3], stats[:, 0:1], AF.Identity,
                                                       scale=-1.0 / 1024.0), (VM(mv),), inc=("s", 1))
        mvh = sch.add("scalar", lambda e: e.activation(stats[:, 3:4], stats[:, 1:2], AF.Identity,
                                                       scale=1.0 / 1024.0), (SM(msq),), inc=("s", 1))
        ms = sch.add("scalar", lambda e: e.activation(stats[:, 4:5], stats[:, 2:3], AF.Square),
                     (SM(mng),), inc=("s", 1))
        mv2 = sch.add("vector", lambda e: e.tensor_sub(stats[:, 5:6], stats[:, 3:4],
                                                       stats[:, 4:5]), (SM(ms), SM(mvh)), inc=("v", 1))
        ms2 = sch.add("scalar", lambda e: e.activation(stats[:, 6:7], stats[:, 5:6], AF.Copy,
                                                       bias=EPS), (VM(mv2),), inc=("s", 1))
        mv3 = sch.add("vector", lambda e: e.reciprocal(stats[:, 7:8], stats[:, 6:7]),
                      (SM(ms2),), inc=("v", 1))
        ms3 = sch.add("scalar", lambda e: e.activation(stats[:, 8:9], stats[:, 7:8], AF.Sqrt),
                      (VM(mv3),), inc=("s", 1))
        mv4 = sch.add("vector", lambda e: e.tensor_mul(stats[:, 9:10], stats[:, 2:3],
                                                       stats[:, 8:9]), (SM(ms3),), inc=("v", 1))
        mxn = sch.add("scalar", lambda e: e.activation(xn_bf[:, :], xin[:, :], AF.Identity,
                                                       bias=stats[:, 9:10], scale=stats[:, 8:9]),
                      (VM(mv4), SM(ms3)), inc=("s", 1))
        return mxn

    wtmp_prev = {0: None, 1: None}

    def load_weight(dst_bf, src_d):
        last = None
        for k in range(8):
            p = k % 2
            ml = dma("sync", s_wtmp[p][:, :], src_d[k * 128:(k + 1) * 128, :],
                     (wtmp_prev[p],), ch=f"lw{p}")
            mg = sch.add("gpsimd",
                         lambda e, kk=k, pp=p: e.tensor_copy(
                             dst_bf[:, kk * 1024:(kk + 1) * 1024], s_wtmp[pp][:, :]),
                         (LD(ml),), inc=("g", 1))
            wtmp_prev[p] = GM(mg)
            last = mg
        return last

    WkT = R(es_p23w, nc.sbuf_tensor("WkT", [128, 8192], BF16))
    WvT = R(es_p23w, nc.sbuf_tensor("WvT", [128, 8192], BF16))
    WqT = R(es_pqw, nc.sbuf_tensor("WqT", [128, 8192], BF16))
    W_dd = R(es_p1w, nc.sbuf_tensor("W_dd", [128, 8192], BF16))
    WcT = R(es_p1w, nc.sbuf_tensor("WcT", [128, 8192], BF16))
    WrT = R(es_p1w, nc.sbuf_tensor("WrT", [128, 8192], BF16))
    Rb = lambda t: R(es_bc, t)
    s_A1bc = Rb(nc.sbuf_tensor("s_A1bc", [128, 1024], F32, side="right"))
    s_B1bc = Rb(nc.sbuf_tensor("s_B1bc", [128, 1024], F32, side="right"))
    s_bdbc = Rb(nc.sbuf_tensor("s_bdbc", [128, 1024], F32, side="right"))
    s_Csbc = Rb(nc.sbuf_tensor("s_Csbc", [128, 1024], F32, side="right"))
    s_R2bc = Rb(nc.sbuf_tensor("s_R2bc", [128, 1024], F32, side="right"))
    Cf = lambda t: R(es_fold, t)
    s_row1 = Cf(nc.sbuf_tensor("s_row1", [1, 1024], F32, side="right"))
    Wp0 = Cf(nc.sbuf_tensor("Wp0", [128, 8192], BF16, side="right"))
    WdT = Cf(nc.sbuf_tensor("WdT", [128, 8192], BF16, side="right"))
    s_ldT = Cf(nc.sbuf_tensor("s_ldT", [128, 4], BF16, side="right"))
    s_luT = Cf(nc.sbuf_tensor("s_luT", [128, 50], BF16, side="right"))
    s_uaT = Cf(nc.sbuf_tensor("s_uaT", [128, 1024], BF16, side="right"))
    s_da = Cf(nc.sbuf_tensor("s_da", [128, 1024], BF16, side="right"))
    s_t1 = Cf(nc.sbuf_tensor("s_t1", [128, 1024], BF16, side="right"))
    s_t2 = Cf(nc.sbuf_tensor("s_t2", [128, 1024], BF16, side="right"))
    s_lt = s_wtmp[0]

    try:
        for outap, inap in ((s_id[:, :], ident_d[:, :]), (s_tri128[:, :], tri128_d[:, :]),
                            (s_tri64[:, :], tri64_d[:, :]), (s_emsk[:, :], emsk_d[:, :]),
                            (s_o1r[:, :], o1r_d[:, :]), (s_o64[:, :], o64_d[:, :]),
                            (s_o2r[:, :], o2r_d[:, :]),
                            (s_bqT[:, :], bqT_d[:, :]), (s_bkT[:, :], bkT_d[:, :]),
                            (s_m1[:, :], m1_d[:, :]), (s_m2n[:, :], m2n_d[:, :])):
            m_startld = dma("sync", outap, inap)
        _ck(1)

        ml = dma("sync", s_lt[:, 0:4], ldT_d[:, :], (wtmp_prev[0],), ch="lw0")
        mg1 = sch.add("gpsimd", lambda e: e.tensor_copy(s_ldT[:, :], s_lt[:, 0:4]),
                      (LD(ml),), inc=("g", 1))
        ml = dma("sync", s_lt[:, 0:50], luT_d[:, :], (GM(mg1),), ch="lw0")
        mg2 = sch.add("gpsimd", lambda e: e.tensor_copy(s_luT[:, :], s_lt[:, 0:50]),
                      (LD(ml),), inc=("g", 1))
        ml = dma("sync", s_lt[:, :], uaT_d[:, :], (GM(mg2),), ch="lw0")
        mg3 = sch.add("gpsimd", lambda e: e.tensor_copy(s_uaT[:, :], s_lt[:, :]),
                      (LD(ml),), inc=("g", 1))
        ml = dma("sync", s_lt[:, :], da_d[:, :], (GM(mg3),), ch="lw0")
        mg4 = sch.add("gpsimd", lambda e: e.tensor_copy(s_da[:, :], s_lt[:, :]),
                      (LD(ml),), inc=("g", 1))
        wtmp_prev[0] = GM(mg4)
        mz1 = sch.add("gpsimd", lambda e: e.memzero(s_t1[:, :]), inc=("g", 1))
        mz2 = sch.add("gpsimd", lambda e: e.memzero(s_t2[:, :]), inc=("g", 1))
        _ck(2)

        mwp0 = load_weight(Wp0, wp0_d)
        mwdt = load_weight(WdT, wdT_d)
        _ck(3)

        mt1 = None
        for n in range(2):
            mt1 = sch.add("tensor", lambda e, nn=n: e.matmul(
                PA[0:4, nn * 512:(nn + 1) * 512], s_ldT[:, :], s_da[:, nn * 512:(nn + 1) * 512],
                start=True, stop=True), (GM(mg4), GM(mg1)) if n == 0 else (),
                inc=("t", 1) if n == 1 else None)
        ms = sch.add("scalar", lambda e: e.copy(s_t1[0:4, :], PA[0:4, :]),
                     (TM(mt1), GM(mz1)), inc=("s", 1))
        mt2 = None
        for n in range(2):
            mt2 = sch.add("tensor", lambda e, nn=n: e.matmul(
                PB[0:50, nn * 512:(nn + 1) * 512], s_luT[:, :], s_t1[:, nn * 512:(nn + 1) * 512],
                start=True, stop=True), (SM(ms), GM(mg2)) if n == 0 else (),
                inc=("t", 1) if n == 1 else None)
        ms2 = sch.add("scalar", lambda e: e.copy(s_t2[0:50, :], PB[0:50, :]),
                      (TM(mt2), GM(mz2)), inc=("s", 1))
        _ck(4)
        m_weff_v = None
        for mch in range(8):
            p = mch % 2
            wld = dma("sync", s_wtmp[p][:, :], wT_d[mch * 128:(mch + 1) * 128, :],
                      (wtmp_prev[p],), ch=f"lw{p}")
            mml = None
            for n in range(2):
                mml = sch.add("tensor", lambda e, nn=n, m_=mch: e.matmul(
                    PA[:, nn * 512:(nn + 1) * 512], s_t2[:, m_ * 128:(m_ + 1) * 128],
                    s_uaT[:, nn * 512:(nn + 1) * 512], start=True, stop=True),
                    ((SM(ms2), GM(mg3)) if mch == 0 else (VM(m_weff_v),))
                    if n == 0 else (),
                    inc=("t", 1) if n == 1 else None)
            m_weff_v = sch.add("vector", lambda e, m_=mch, pp=p: e.tensor_add(
                W_eff[:, m_ * 1024:(m_ + 1) * 1024], PA[:, :], s_wtmp[pp][:, :]),
                (TM(mml), LD(wld)), inc=("v", 1))
            wtmp_prev[p] = VM(m_weff_v)

        _ck(5)
        m_wdd_s = None
        for mch in range(8):
            mml = None
            for n in range(2):
                for k in range(8):
                    mml = sch.add("tensor", lambda e, nn=n, kk=k, m_=mch: e.matmul(
                        PB[:, nn * 512:(nn + 1) * 512],
                        Wp0[:, kk * 1024 + m_ * 128: kk * 1024 + (m_ + 1) * 128],
                        WdT[:, kk * 1024 + nn * 512: kk * 1024 + nn * 512 + 512],
                        start=(kk == 0), stop=(kk == 7)),
                        ((GM(mwp0), GM(mwdt), SM(ms2)) if mch == 0 else (SM(m_wdd_s),))
                        if (n == 0 and k == 0) else (),
                        inc=("t", 1) if (n == 1 and k == 7) else None)
            m_wdd_s = sch.add("scalar", lambda e, m_=mch: e.copy(
                W_dd[:, m_ * 1024:(m_ + 1) * 1024], PB[:, :]), (TM(mml),), inc=("s", 1))

        _ck(6)
        mwc = load_weight(WcT, wcT_d)
        mwr = load_weight(WrT, wrT_d)

        _ck(7)
        m_prevbc = 0

        def bcast_row(dst, rowidx, dcol=False):
            nonlocal_marks = {}
            mrl = dma("sync", s_row1[:, :], rows_d[rowidx:rowidx + 1, :],
                      (TM(m_prevbc), LD(m_startld)))
            mb = None
            for n in range(2):
                mb = sch.add("tensor", lambda e, nn=n: e.matmul(
                    PC[:, nn * 512:(nn + 1) * 512], s_o1r[:, :],
                    s_row1[0:1, nn * 512:(nn + 1) * 512], start=True, stop=True),
                    (LD(mrl), VM(sch.count("v"))) if n == 0 else (),
                    inc=("t", 1) if n == 1 else None)
            if dcol:
                mv = sch.add("vector", lambda e: e.tensor_copy(s_Dbc[:, :], PC[:, 0:1]),
                             (TM(mb),), inc=("v", 1))
            else:
                mv = sch.add("vector", lambda e: e.tensor_copy(dst[:, :], PC[:, :]), (TM(mb),),
                             inc=("v", 1))
            return mb

        m_prevbc = bcast_row(s_A1bc, 0)
        m_prevbc = bcast_row(s_B1bc, 1)
        m_prevbc = bcast_row(s_bdbc, 2)
        m_prevbc = bcast_row(s_bfbc, 3)
        m_prevbc = bcast_row(s_bvbc, 4)
        m_prevbc = bcast_row(None, 5, dcol=True)

        _ck(8)
        sch.barrier()
        es_fold.close()

        if phases >= 1:
            Cp = es_p1t.enter_context
            xb = [Cp(nc.sbuf_tensor(f"xb{p}", [128, 1024], F32)) for p in range(3)]
            xn_bf = [Cp(nc.sbuf_tensor(f"xn_bf{p}", [128, 1024], BF16)) for p in range(3)]
            xnT = [Cp(nc.sbuf_tensor(f"xnT{p}", [128, 8, 128], BF16)) for p in range(3)]
            delta_f = [Cp(nc.sbuf_tensor(f"delta_f{p}", [128, 1024], F32)) for p in range(2)]
            e1_f = [Cp(nc.sbuf_tensor("e1_f0", [128, 1024], F32))] * 2
            res_f = [Cp(nc.sbuf_tensor(f"res_f{p}", [128, 1024], F32)) for p in range(2)]
            cs_bf = [Cp(nc.sbuf_tensor("cs_bf0", [128, 1024], BF16))] * 2
            g0_bf = [Cp(nc.sbuf_tensor("g0_bf0", [128, 1024], BF16))] * 2
            m_bf = [Cp(nc.sbuf_tensor("m_bf0", [128, 1024], BF16))] * 2
            st1 = [Cp(nc.sbuf_tensor(f"st1_{p}", [128, 16], F32)) for p in range(3)]


            mk = {}

            def p1A(i):
                p = i % 3
                q3 = i % 3
                PTt = PT1 if i % 2 == 0 else PT2
                w_x = [LD(mk[("lx", i)]), SM(mk.get(("xn_c", i - 3))),
                       GM(mk.get(("m", i - 3))), TM(mk.get(("tp", i - 3)))]
                if i == 0:
                    w_x += [SM(m_wdd_s), GM(mwc), GM(mwr), VM(sch.count("v"))]
                mxn = ln(xb[p], xn_bf[q3], st1[q3], tuple(w_x))
                mk[("xn_c", i)] = mxn
                mtp = transposes8(PTt, xn_bf[q3], (SM(mxn), VM(mk.get(("xnTev", i - 2), 0))))
                mk[("tp", i)] = mtp
                mev = sch.add("vector", lambda e, qq=q3, pt=PTt: e.tensor_copy(
                    xnT[qq][:, :, :], pt[:, :, :]), (TM(mtp),), inc=("v", 1))
                mk[("xnTev", i)] = mev

            def p1B(i):
                p = i % 2
                q3 = i % 3
                mev = mk[("xnTev", i)]
                mdp = None
                for n in range(2):
                    for k in range(8):
                        mdp = sch.add("tensor", lambda e, nn=n, kk=k, qq=q3: e.matmul(
                            PA[:, nn * 512:(nn + 1) * 512], xnT[qq][:, kk, :],
                            W_dd[:, kk * 1024 + nn * 512: kk * 1024 + nn * 512 + 512],
                            start=(kk == 0), stop=(kk == 7)),
                            (VM(mev), VM(mk.get(("resev", i - 1), 0)))
                            if (n == 0 and k == 0) else (),
                            inc=("t", 1) if (n == 1 and k == 7) else None)
                mba = sch.add("vector", lambda e, pp=p: e.tensor_add(
                    delta_f[pp][:, :], PA[:, :], s_bdbc[:, :]),
                    (TM(mdp), GM(mk.get(("g0", i - 2), 0)), mk.get(("dst", i - 2))),
                    inc=("v", 1))
                mcs = None
                for n in range(2):
                    for k in range(8):
                        mcs = sch.add("tensor", lambda e, nn=n, kk=k, qq=q3: e.matmul(
                            PB[:, nn * 512:(nn + 1) * 512], xnT[qq][:, kk, :],
                            WcT[:, kk * 1024 + nn * 512: kk * 1024 + nn * 512 + 512],
                            start=(kk == 0), stop=(kk == 7)),
                            (SM(mk.get(("csbf", i - 1), 0)),) if (n == 0 and k == 0) else (),
                            inc=("t", 1) if (n == 1 and k == 7) else None)
                mex = sch.add("scalar", lambda e, pp=p: e.activation(
                    e1_f[pp][:, :], delta_f[pp][:, :], AF.Exp), (VM(mba),), inc=("s", 1))
                mln = sch.add("scalar", lambda e, pp=p: e.activation(
                    delta_f[pp][:, :], e1_f[pp][:, :], AF.Ln, bias=1.0), (SM(mex),),
                    inc=("s", 1))
                mk[("sp", i)] = mln
                mk[("dst", i)] = dma("scalar", sc_delta[i, :, :], delta_f[p][:, :], ch=f"sdl{p}")
                mcb = sch.add("scalar", lambda e, pp=p: e.copy(cs_bf[pp][:, :], PB[:, :]),
                              (TM(mcs), GM(mk.get(("m", i - 1), 0)),
                               TM(mk.get(("colsum", i - 1), 0))), inc=("s", 1))
                mk[("csbf", i)] = mcb
                mg0 = sch.add("gpsimd", lambda e, pp=p: e.tensor_mul(
                    g0_bf[pp][:, :], delta_f[pp][:, :], s_B1bc[:, :]),
                    (SM(mln), TM(mk.get(("colsum", i - 1), 0))), inc=("g", 1))
                mk[("g0", i)] = mg0
                mmm = sch.add("gpsimd", lambda e, pp=p, qq=q3: e.tensor_mul(
                    m_bf[pp][:, :], xn_bf[qq][:, :], cs_bf[pp][:, :]),
                    (SM(mcb), TM(mk.get(("colsum", i - 1), 0))), inc=("g", 1))
                mk[("m", i)] = mmm
                mrs = None
                for n in range(2):
                    for k in range(8):
                        mrs = sch.add("tensor", lambda e, nn=n, kk=k, qq=q3: e.matmul(
                            PA[:, nn * 512:(nn + 1) * 512], xnT[qq][:, kk, :],
                            WrT[:, kk * 1024 + nn * 512: kk * 1024 + nn * 512 + 512],
                            start=(kk == 0), stop=(kk == 7)),
                            (VM(mba),) if (n == 0 and k == 0) else (),
                            inc=("t", 1) if (n == 1 and k == 7) else None)
                mrev = sch.add("vector", lambda e, pp=p: e.tensor_copy(
                    res_f[pp][:, :], PA[:, :]),
                    (TM(mrs), mk.get(("rst", i - 2))), inc=("v", 1))
                mk[("resev", i)] = mrev
                mk[("rst", i)] = dma("scalar", sc_res[i, :, :], res_f[p][:, :], (VM(mrev),),
                                     ch=f"sre{p}")
                mcl = None
                for n in range(2):
                    mcl = sch.add("tensor", lambda e, nn=n, ii=i, pp=p: e.matmul(
                        PC[0:128, nn * 512:(nn + 1) * 512], s_emsk[:, 65 - ii:193 - ii],
                        g0_bf[pp][:, nn * 512:(nn + 1) * 512], start=(ii == 0), stop=False),
                        (GM(mg0),) if n == 0 else ())
                for n in range(2):
                    mcl = sch.add("tensor", lambda e, nn=n, ii=i, pp=p: e.matmul(
                        PC[0:128, nn * 512:(nn + 1) * 512], s_emsk[:, 1:129],
                        m_bf[pp][:, nn * 512:(nn + 1) * 512], start=False, stop=False),
                        (GM(mmm),) if n == 0 else ())
                for n in range(2):
                    mcl = sch.add("tensor", lambda e, nn=n, ii=i, pp=p: e.matmul(
                        PC[0:128, nn * 512:(nn + 1) * 512], s_emsk[:, 0:128],
                        cs_bf[pp][:, nn * 512:(nn + 1) * 512], start=False,
                        stop=(ii == NT - 1)),
                        (), inc=("t", 1) if n == 1 else None)
                mk[("colsum", i)] = mcl

            for i in range(5):
                if i < NT:
                    mk[("lx", i)] = dma("sync", xb[i % 3][:, :], x_d[i * 128:(i + 1) * 128, :],
                                        (SM(mk.get(("xn_c", i - 3))),), ch=f"lxb{i % 3}")
                if i < 3:
                    p1A(i)
            for i in range(NT):
                p1B(i)
                if i + 5 < NT:
                    mk[("lx", i + 5)] = dma("sync", xb[(i + 5) % 3][:, :],
                                            x_d[(i + 5) * 128:(i + 6) * 128, :],
                                            (SM(mk[("xn_c", i + 2)]),), ch=f"lxb{(i + 5) % 3}")
                if i + 3 < NT:
                    p1A(i + 3)

            sch.barrier()
            es_p1t.close()
            es_p1w.close()

            C2a = lambda t: R(es_p2a, t)
            tbl_sb = C2a(nc.sbuf_tensor("tbl_sb", [128, 1024], F32, side="right"))
            car0_sb = C2a(nc.sbuf_tensor("car0_sb", [64, 1024], F32, side="right"))
            l1bc_sb = C2a(nc.sbuf_tensor("l1bc_sb", [64, 1024], F32, side="right"))
            tadd_sb = C2a(nc.sbuf_tensor("tadd_sb", [64, 1024], F32, side="right"))
            hi_bf = C2a(nc.sbuf_tensor("hi_bf", [64, 1024], BF16, side="right"))
            lo_bf = C2a(nc.sbuf_tensor("lo_bf", [64, 1024], BF16, side="right"))
            hi32_sb = C2a(nc.sbuf_tensor("hi32_sb", [64, 1024], F32, side="right"))
            b0_sb = C2a(nc.sbuf_tensor("b0_sb", [1, 6144], F32, side="right"))
            ucbc_sb = C2a(nc.sbuf_tensor("ucbc_sb", [128, 1024], F32, side="right"))

            mtb = sch.add("vector", lambda e: e.tensor_copy(tbl_sb[0:66, :], PC[0:66, :]), (),
                          inc=("v", 1))
            mu1 = dma("sync", b0_sb[0:1, 0:1024], tbl_sb[64:65, :], (VM(mtb),))
            mu2 = dma("sync", b0_sb[0:1, 1024:2048], tbl_sb[65:66, :])
            mpf = None
            for n in range(2):
                mpf = sch.add("tensor", lambda e, nn=n: e.matmul(
                    PA[0:64, nn * 512:(nn + 1) * 512], s_tri64[:, :],
                    tbl_sb[0:64, nn * 512:(nn + 1) * 512], start=True, stop=True),
                    (VM(mtb),) if n == 0 else (), inc=("t", 1) if n == 1 else None)
            mc0 = sch.add("vector", lambda e: e.tensor_copy(car0_sb[:, :], PA[0:64, :]), (TM(mpf),),
                          inc=("v", 1))
            ml1 = dma("sync", b0_sb[0:1, 2048:3072], car0_sb[32:33, :], (VM(mc0),))
            ml2 = dma("sync", b0_sb[0:1, 3072:4096], car0_sb[63:64, :])
            ml3 = dma("sync", b0_sb[0:1, 4096:5120], tbl_sb[63:64, :])
            sch.add("vector", lambda e: e.tensor_add(b0_sb[0:1, 5120:6144], b0_sb[0:1, 3072:4096],
                                                     b0_sb[0:1, 4096:5120]), (LD(ml3),), inc=("v", 1))
            mt_ = sch.add("vector", lambda e: e.tensor_sub(b0_sb[0:1, 5120:6144],
                                                           b0_sb[0:1, 5120:6144],
                                                           b0_sb[0:1, 2048:3072]), inc=("v", 1))
            mb1 = None
            for n in range(2):
                mb1 = sch.add("tensor", lambda e, nn=n: e.matmul(
                    PB[0:64, nn * 512:(nn + 1) * 512], s_o64[:, :],
                    b0_sb[0:1, 2048 + nn * 512:2048 + (nn + 1) * 512], start=True, stop=True),
                    (VM(mt_),) if n == 0 else (), inc=("t", 1) if n == 1 else None)
            mevl1 = sch.add("vector", lambda e: e.tensor_copy(l1bc_sb[:, :], PB[0:64, :]), (TM(mb1),),
                            inc=("v", 1))
            mb2 = None
            for n in range(2):
                mb2 = sch.add("tensor", lambda e, nn=n: e.matmul(
                    PB[0:64, nn * 512:(nn + 1) * 512], s_o64[:, :],
                    b0_sb[0:1, 5120 + nn * 512:5120 + (nn + 1) * 512], start=True, stop=True),
                    (VM(mevl1),) if n == 0 else (), inc=("t", 1) if n == 1 else None)
            mta = sch.add("vector", lambda e: e.scalar_tensor_tensor(
                tadd_sb[:, :], PB[0:64, :], s_m1[:, :], car0_sb[:, :], ALU.mult, ALU.add),
                (TM(mb2),), inc=("v", 1))
            mta = sch.add("vector", lambda e: e.scalar_tensor_tensor(
                tadd_sb[:, :], l1bc_sb[:, :], s_m2n[:, :], tadd_sb[:, :], ALU.mult, ALU.add),
                inc=("v", 1))
            mhi = sch.add("scalar", lambda e: e.copy(hi_bf[:, :], tadd_sb[:, :]), (VM(mta),),
                          inc=("s", 1))
            mhi2 = sch.add("scalar", lambda e: e.copy(hi32_sb[:, :], hi_bf[:, :]), (SM(mhi),),
                           inc=("s", 1))
            mlo = sch.add("vector", lambda e: e.tensor_sub(lo_bf[:, :], tadd_sb[:, :], hi32_sb[:, :]),
                          (SM(mhi2),), inc=("v", 1))
            mst_hi = dma("scalar", sc_hi[:, :], hi_bf[:, :], (SM(mhi),))
            mst_lo = dma("scalar", sc_lo[:, :], lo_bf[:, :], (VM(mlo),))
            mcb_ = None
            for n in range(2):
                mcb_ = sch.add("tensor", lambda e, nn=n: e.matmul(
                    PC[:, nn * 512:(nn + 1) * 512], s_o1r[:, :],
                    b0_sb[0:1, 1024 + nn * 512:1024 + (nn + 1) * 512], start=True, stop=True),
                    (LD(mu2), VM(mtb)) if n == 0 else (), inc=("t", 1) if n == 1 else None)
            mcsbc = sch.add("vector", lambda e: e.tensor_copy(s_Csbc[:, :], PC[:, :]), (TM(mcb_),),
                            inc=("v", 1))
            mub_ = None
            for n in range(2):
                mub_ = sch.add("tensor", lambda e, nn=n: e.matmul(
                    PC[:, nn * 512:(nn + 1) * 512], s_o1r[:, :],
                    b0_sb[0:1, 0 + nn * 512:0 + (nn + 1) * 512], start=True, stop=True),
                    (VM(mcsbc), LD(mu1)) if n == 0 else (), inc=("t", 1) if n == 1 else None)
            mucbc = sch.add("vector", lambda e: e.tensor_copy(ucbc_sb[:, :], PC[:, :]), (TM(mub_),),
                            inc=("v", 1))
            mr2 = sch.add("gpsimd", lambda e: e.tensor_mul(s_R2bc[:, :], s_B1bc[:, :], ucbc_sb[:, :]),
                          (VM(mucbc),), inc=("g", 1))
            if dbg:
                dma("scalar", dbg_tbl[0:66, :], tbl_sb[0:66, :], (VM(mtb),))
                dma("scalar", dbg_bc[:, 0:1024], s_Csbc[:, :], (VM(mcsbc),))
                dma("scalar", dbg_bc[:, 1024:2048], ucbc_sb[:, :], (VM(mucbc),))
                dma("scalar", dbg_bc[:, 2048:3072], s_R2bc[:, :], (GM(mr2),))
                dma("scalar", dbg_bc[0:64, 3072:4096], car0_sb[0:64, :], (VM(mc0),))

            sch.barrier()
            es_p2a.close()
            es_p1t_dummy = None


        if phases >= 2:
            C2 = es_p2.enter_context
            A_ = [C2(nc.sbuf_tensor(f"p2A{p}", [128, 1024], F32)) for p in range(2)]
            B_ = [C2(nc.sbuf_tensor(f"p2B{p}", [128, 1024], F32)) for p in range(2)]
            Cc_ = [C2(nc.sbuf_tensor(f"p2C{p}", [128, 1024], F32)) for p in range(2)]
            D_ = [C2(nc.sbuf_tensor(f"p2D{p}", [128, 1024], F32)) for p in range(2)]
            E_ = [C2(nc.sbuf_tensor(f"p2E{p}", [128, 1024], F32)) for p in range(2)]
            F_ = [C2(nc.sbuf_tensor(f"p2F{p}", [128, 1024], F32)) for p in range(3)]
            G_ = [C2(nc.sbuf_tensor(f"p2G{p}", [128, 1024], F32)) for p in range(3)]
            H_ = [C2(nc.sbuf_tensor(f"p2H{p}", [128, 1024], F32)) for p in range(2)]
            g0b = [C2(nc.sbuf_tensor(f"p2g0{p}", [128, 1024], BF16)) for p in range(2)]
            hilo = [C2(nc.sbuf_tensor(f"hilo{p}", [2, 1024], BF16)) for p in range(2)]
            xn2b = [C2(nc.sbuf_tensor(f"xn2b{p}", [128, 1024], BF16)) for p in range(2)]
            xn2T = [C2(nc.sbuf_tensor(f"xn2T{p}", [128, 8, 128], BF16)) for p in range(2)]
            xn3b = [C2(nc.sbuf_tensor(f"xn3b{p}", [128, 1024], BF16)) for p in range(2)]
            st2 = [C2(nc.sbuf_tensor(f"st2_{p}", [128, 16], F32)) for p in range(2)]
            st3 = [C2(nc.sbuf_tensor(f"st3_{p}", [128, 16], F32)) for p in range(2)]

            mwq = load_weight(WqT, wqT_d)
            mwk = load_weight(WkT, wkT_d)
            mwv = load_weight(WvT, wvT_d)

            mk2 = {}
            for i in range(2):
                mk2[("lx", i)] = dma("sync", A_[i][:, :], x_d[i * 128:(i + 1) * 128, :],
                                     ch=f"lA{i}")
                mk2[("lhi", i)] = dma("sync", hilo[i][0:1, :], sc_hi[i:i + 1, :],
                                      (mst_hi, mst_lo), ch=f"lh{i}")
                mk2[("llo", i)] = dma("sync", hilo[i][1:2, :], sc_lo[i:i + 1, :], ch=f"lh{i}")
                mk2[("ldl", i)] = dma("vector", B_[i][:, :], sc_delta[i, :, :],
                                      (mk[("dst", i)],), ch=f"lB{i}")
                mk2[("lr", i)] = dma("vector", Cc_[i][:, :], sc_res[i, :, :],
                                     (mk[("rst", i)],), ch=f"lC{i}")


            def p2A(i):
                p = i % 2
                q3 = i % 3
                PD = PA if p == 0 else PC
                mg0 = sch.add("gpsimd", lambda e, pp=p: e.tensor_mul(
                    g0b[pp][:, :], B_[pp][:, :], s_B1bc[:, :]),
                    (VD(mk2[("ldl", i)]), TM(mk2.get(("tri", i - 2), 0)),
                     GM(mr2) if i == 0 else None), inc=("g", 1))
                mtr = None
                for n in range(2):
                    mtr = sch.add("tensor", lambda e, nn=n, pp=p, PD_=PD: e.matmul(
                        PD_[:, nn * 512:(nn + 1) * 512], s_tri128[:, :],
                        g0b[pp][:, nn * 512:(nn + 1) * 512], start=True, stop=False),
                        (GM(mg0), VM(mk2.get(("ydb", i - 2), 0))) if n == 0 else ())
                for n in range(2):
                    mtr = sch.add("tensor", lambda e, nn=n, pp=p, PD_=PD: e.matmul(
                        PD_[:, nn * 512:(nn + 1) * 512], s_o2r[:, :],
                        hilo[pp][:, nn * 512:(nn + 1) * 512], start=False, stop=True),
                        (LD(mk2[("llo", i)]),) if n == 0 else (),
                        inc=("t", 1) if n == 1 else None)
                mk2[("tri", i)] = mtr
                mei = sch.add("gpsimd", lambda e, pp=p: e.tensor_mul(
                    D_[pp][:, :], B_[pp][:, :], s_A1bc[:, :]), (), inc=("g", 1))
                mk2[("eIn", i)] = mei
                mev_ = sch.add("scalar", lambda e, pp=p: e.activation(
                    E_[pp][:, :], D_[pp][:, :], AF.Exp), (GM(mei),), inc=("s", 1))
                mydb = sch.add("vector", lambda e, pp=q3, PD_=PD: e.tensor_mul(
                    G_[pp][:, :], PD_[:, :], s_Csbc[:, :]),
                    (TM(mtr), GM(mk2.get(("x2", i - 3), 0))), inc=("v", 1))
                mk2[("ydb", i)] = mydb
                my1 = sch.add("gpsimd", lambda e, pp=p, qq=q3: e.tensor_mul(
                    F_[qq][:, :], E_[pp][:, :], s_R2bc[:, :]),
                    (SM(mev_), VM(mk2.get(("x2a", i - 3), 0))), inc=("g", 1))
                my12 = sch.add("gpsimd", lambda e, pp=p, qq=q3: e.tensor_add(
                    D_[pp][:, :], G_[qq][:, :], F_[qq][:, :]), (VM(mydb),), inc=("g", 1))
                mx1a = sch.add("vector", lambda e, pp=p: e.scalar_tensor_tensor(
                    E_[pp][:, :], Cc_[pp][:, :], s_Dbc[:, :], D_[pp][:, :], ALU.mult, ALU.add),
                    (VD(mk2[("lr", i)]), GM(my12), SM(mev_)), inc=("v", 1))
                mk2[("x1a", i)] = mx1a
                mx1 = sch.add("gpsimd", lambda e, pp=p, qq=q3: e.tensor_add(
                    F_[qq][:, :], E_[pp][:, :], A_[pp][:, :]),
                    (LD(mk2[("lx", i)]), VM(mx1a)), inc=("g", 1))
                mk2[("x1", i)] = mx1
                if dbg:
                    dma("scalar", sc_x1[i, :, :], F_[q3][:, :], (GM(mx1),))

            def p2B(i):
                p = i % 2
                q3 = i % 3
                PTx = PT1 if p == 0 else PT2
                mxn2 = ln(F_[q3], xn2b[p], st2[p], (GM(mk2[("x1", i)]),
                                                   TM(mk2.get(("tp2", i - 2), 0))))
                mtp2 = transposes8(PTx, xn2b[p], (SM(mxn2), SM(mk2.get(("ev2", i - 2), 0))))
                mk2[("tp2", i)] = mtp2
                mev2 = sch.add("scalar", lambda e, pp=p, pt=PTx: e.copy(
                    xn2T[pp][:, :, :], pt[:, :, :]),
                               (TM(mtp2),), inc=("s", 1))
                mk2[("ev2", i)] = mev2
                mf1 = None
                for n in range(2):
                    for k in range(8):
                        mf1 = sch.add("tensor", lambda e, nn=n, kk=k, pp=p: e.matmul(
                            PB[:, nn * 512:(nn + 1) * 512], xn2T[pp][:, kk, :],
                            W_eff[:, kk * 1024 + nn * 512: kk * 1024 + nn * 512 + 512],
                            start=(kk == 0), stop=(kk == 7)),
                            (SM(mev2), VM(mk2.get(("x2a", i - 1), 0)), VM(m_weff_v))
                            if (n == 0 and k == 0) else (),
                            inc=("t", 1) if (n == 1 and k == 7) else None)
                mx2a = sch.add("vector", lambda e, qq=q3: e.tensor_add(
                    G_[qq][:, :], PB[:, :], s_bfbc[:, :]), (TM(mf1),), inc=("v", 1))
                mk2[("x2a", i)] = mx2a
                mx2 = sch.add("gpsimd", lambda e, pp=p, qq=q3: e.tensor_add(
                    H_[pp][:, :], G_[qq][:, :], F_[qq][:, :]),
                    (VM(mx2a), mk2.get(("sx2", i - 2))), inc=("g", 1))
                mk2[("x2", i)] = mx2

            def p2C(i):
                p = i % 2
                mxn3 = ln(H_[p], xn3b[p], st3[p], (GM(mk2[("x2", i)]),
                                                   mk2.get(("sxn3", i - 2))))
                mk2[("sxn3", i)] = dma("scalar", sc_xn3[i, :, :], xn3b[p][:, :], ch=f"sx3{p}")
                if i < NQ:
                    mk2[("sx2", i)] = dma("scalar", sc_x2[i, :, :], H_[p][:, :],
                                          (GM(mk2[("x2", i)]),), ch=f"sx2{p}")

            def p2loads(i):
                p = i % 2
                mk2[("lx", i)] = dma("sync", A_[p][:, :], x_d[i * 128:(i + 1) * 128, :],
                                     (GM(mk2[("x1", i - 2)]),), ch=f"lA{p}")
                mk2[("lhi", i)] = dma("sync", hilo[p][0:1, :], sc_hi[i:i + 1, :],
                                      (TM(mk2[("tri", i - 2)]),), ch=f"lh{p}")
                mk2[("llo", i)] = dma("sync", hilo[p][1:2, :], sc_lo[i:i + 1, :], ch=f"lh{p}")
                mk2[("ldl", i)] = dma("vector", B_[p][:, :], sc_delta[i, :, :],
                                      (GM(mk2[("eIn", i - 2)]), mk[("dst", i)]), ch=f"lB{p}")
                mk2[("lr", i)] = dma("vector", Cc_[p][:, :], sc_res[i, :, :],
                                     (VM(mk2[("x1a", i - 2)]), mk[("rst", i)]), ch=f"lC{p}")

            p2A(0)
            p2A(1)
            p2loads(2)
            p2A(2)
            p2B(0)
            for i in range(NT):
                p2C(i)
                if i + 3 < NT:
                    p2loads(i + 3)
                    p2A(i + 3)
                if i + 1 < NT:
                    p2B(i + 1)

            sch.barrier()
            es_p2.close()
            es_bc.close()
            es_wtmp.close()

        if phases >= 3:
            xg3T = [R(es_prep, nc.sbuf_tensor(f"xg3T{p}", [128, 8, 512], BF16, side="right"))
                    for p in range(2)]
            xq_in = [R(es_prep, nc.sbuf_tensor(f"xq_in{p}", [128, 1024], BF16, side="right"))
                     for p in range(2)]
            qsp = [R(es_qsp, nc.sbuf_tensor(f"qsp{p}", [128, 8, 512], BF16, side="right"))
                   for p in range(2)]

            mk3 = {"tcnt": 0, "vevlast": 0}

            def prep_load(ti, tc):
                mk3[("gld", tc)] = dma("sync", xq_in[tc % 2][:, :], sc_xn3[ti, :, :],
                                       (TM(mk3.get(("gtp", tc - 2), 0)), mk2[("sxn3", ti)]),
                                       ch=f"lxq{tc % 2}")

            def prep_group(tiles, pg):
                last = None
                tiles = list(tiles)
                seq = mk3.get("prepseq", {})
                for j, ti in enumerate(tiles):
                    tc = mk3["tcnt"]
                    tl = tc % 4
                    for tq in (tc, tc + 1):
                        if ("gld", tq) not in mk3 and tq in seq:
                            prep_load(seq[tq], tq)
                    PTp = PT2 if tc % 2 == 0 else PT1
                    mtp = transposes8(PTp, xq_in[tc % 2],
                                      (LD(mk3[("gld", tc)]), SM(mk3.get(("gev", tc - 2), 0))))
                    mk3[("gtp", tc)] = mtp
                    last = sch.add("scalar",
                                   lambda e, tt=tl, pp=pg, pt=PTp: e.copy(
                                       xg3T[pp][:, :, tt * 128:(tt + 1) * 128], pt[:, :, :]),
                                   (TM(mtp),), inc=("s", 1))
                    mk3[("gev", tc)] = last
                    mk3["tcnt"] = tc + 1
                    if ("gld", tc + 2) not in mk3 and (tc + 2) in seq:
                        prep_load(seq[tc + 2], tc + 2)
                return last

            mk3["prepseq"] = {j: j for j in range(32)}
            for g in range(8):
                pg = g % 2
                mev = prep_group(range(4 * g, 4 * g + 4), pg)
                for c in range(8):
                    half = c % 2
                    mq = None
                    for k in range(8):
                        mq = sch.add("tensor", lambda e, cc=c, kk=k, hh=half, pp=pg: e.matmul(
                            PA[:, hh * 512:(hh + 1) * 512],
                            WqT[:, kk * 1024 + cc * 128: kk * 1024 + (cc + 1) * 128],
                            xg3T[pp][:, kk, :], start=(kk == 0), stop=(kk == 7)),
                            ((SM(mev), GM(mwq)) if c == 0 else
                             (SM(mk3.get(("qev", g, c - 2), 0)),)) if k == 0 else (),
                            inc=("t", 1) if k == 7 else None)
                    mqe = sch.add("scalar", lambda e, cc=c, hh=half, pp=pg: e.activation(
                        qsp[pp][:, cc, :], PA[:, hh * 512:(hh + 1) * 512],
                        AF.Identity, bias=s_bqT[:, cc:cc + 1], scale=1.0),
                        (TM(mq), mk3.get(("qst", g - 2))), inc=("s", 1))
                    mk3[("qev", g, c)] = mqe
                for t in range(4):
                    mk3[("qst", g)] = dma("scalar", sc_qT[4 * g + t, :, :, :],
                                          qsp[pg][:, :, t * 128:(t + 1) * 128],
                                          (SM(mk3[("qev", g, 7)]),) if t == 0 else (),
                                          ch=f"sq{pg}")
            sch.barrier()
            es_qsp.close()
            es_pqw.close()
            C3 = es_p3.enter_context
            kT_res = C3(nc.sbuf_tensor("kT_res", [128, 8, 2048], BF16))
            v_res = C3(nc.sbuf_tensor("v_res", [128, 16384], BF16))
            qT_t = [C3(nc.sbuf_tensor(f"qT_t{p}", [128, 8, 128], BF16)) for p in range(2)]
            P_bf = [C3(nc.sbuf_tensor(f"P_bf{p}", [128, 2048], BF16)) for p in range(2)]
            pt_sb = [C3(nc.sbuf_tensor(f"pt_sb{p}", [128, 16, 128], BF16))
                     for p in range(2)]
            zs = [C3(nc.sbuf_tensor(f"zs{p}", [128, 1024], F32)) for p in range(2)]
            zl = [C3(nc.sbuf_tensor(f"zl{p}", [128, 1024], F32)) for p in range(3)]
            x2l = [C3(nc.sbuf_tensor(f"x2l{p}", [128, 1024], F32)) for p in range(2)]
            x3b = [C3(nc.sbuf_tensor(f"x3b{p}", [128, 1024], F32)) for p in range(2)]
            xn4b = [C3(nc.sbuf_tensor("xn4b0", [128, 1024], BF16))] * 2
            xn4T = [pt_sb[0][:, 0:8, :], pt_sb[1][:, 0:8, :]]
            st4 = [C3(nc.sbuf_tensor(f"st4_{p}", [128, 16], F32)) for p in range(2)]
            m_run = C3(nc.sbuf_tensor("m_run", [128, 32], F32))
            l_run = C3(nc.sbuf_tensor("l_run", [128, 32], F32))
            cmax = C3(nc.sbuf_tensor("cmax", [128, 8], F32))
            lbuf = C3(nc.sbuf_tensor("lbuf", [128, 8], F32))
            pstat = C3(nc.sbuf_tensor("pstat", [128, 8], F32))

            for r in range(KR):
                sch.barrier()
                mk3["prepseq"] = {mk3["tcnt"] + j: 16 * r + j for j in range(16)}
                for gg in range(4):
                    pg = (8 + r * 4 + gg) % 2
                    mev = prep_group(range(16 * r + 4 * gg, 16 * r + 4 * gg + 4), pg)
                    for c in range(8):
                        half = c % 2
                        mq = None
                        for k in range(8):
                            mq = sch.add("tensor", lambda e, cc=c, kk=k, hh=half, pp=pg: e.matmul(
                                PA[:, hh * 512:(hh + 1) * 512],
                                WkT[:, kk * 1024 + cc * 128: kk * 1024 + (cc + 1) * 128],
                                xg3T[pp][:, kk, :], start=(kk == 0), stop=(kk == 7)),
                                ((SM(mev), GM(mwk)) if c == 0 else
                                 (SM(mk3.get(("kev", gg, c - 2), 0)),)) if k == 0 else (),
                                inc=("t", 1) if k == 7 else None)
                        mqe = sch.add("scalar", lambda e, cc=c, hh=half, go=gg: e.activation(
                            kT_res[:, cc, go * 512:(go + 1) * 512],
                            PA[:, hh * 512:(hh + 1) * 512],
                            AF.Identity, bias=s_bkT[:, cc:cc + 1], scale=1.0),
                            (TM(mq),), inc=("s", 1))
                        mk3[("kev", gg, c)] = mqe
                    for t in range(4):
                        mv = None
                        for n in range(2):
                            for k in range(8):
                                mv = sch.add("tensor", lambda e, nn=n, kk=k, tt=t, pp=pg: e.matmul(
                                    PB[:, nn * 512:(nn + 1) * 512],
                                    xg3T[pp][:, kk, tt * 128:(tt + 1) * 128],
                                    WvT[:, kk * 1024 + nn * 512: kk * 1024 + nn * 512 + 512],
                                    start=(kk == 0), stop=(kk == 7)),
                                    ((GM(mwv), VM(mk3["vevlast"]))
                                     if (n == 0 and k == 0) else ()),
                                    inc=("t", 1) if (n == 1 and k == 7) else None)
                        mve = sch.add("vector", lambda e, tt=t, go=gg: e.tensor_tensor(
                            v_res[:, (4 * go + tt) * 1024:(4 * go + tt + 1) * 1024],
                            PB[:, :], s_bvbc[:, :], ALU.add), (TM(mv),), inc=("v", 1))
                        mk3["vevlast"] = mve
                sch.barrier()


                def qS(step):
                    qi = step - r * NQ
                    p = step % 2
                    ac = 3 + 3 * (step % 2)
                    mlq = dma("vector", qT_t[p][:, :, :], sc_qT[qi, :, :, :],
                              (TM(mk3.get(("smm", step - 2), 0)), mk3[("qst", qi // 4)]),
                              ch=f"lqt{p}")
                    if r > 0:
                        mk3[("lz", step)] = dma("vector", zl[step % 3][:, :], sc_z[qi, :, :],
                                                (VM(mk3.get(("zco", step - 3), 0)),
                                                 mk3[("zst", (r - 1) * NQ + qi)]),
                                                ch=f"lzl{step % 3}")
                    if r == KR - 1:
                        mk3[("lx2", step)] = dma("vector", x2l[p][:, :], sc_x2[qi, :, :],
                                                 (VM(mk3.get(("x3", step - 2), 0)),
                                                  mk2[("sx2", qi)]), ch=f"lx2{p}")
                    msmm = None
                    for n in range(4):
                        ps_ = PA if n < 2 else PB
                        off = (n % 2) * 512
                        for c in range(8):
                            msmm = sch.add("tensor", lambda e, nn=n, cc=c, o=ps_, of=off, pp=p:
                                           e.matmul(o[:, of:of + 512], qT_t[pp][:, cc, :],
                                                    kT_res[:, cc, nn * 512:(nn + 1) * 512],
                                                    start=(cc == 0), stop=(cc == 7)),
                                           ((VD(mlq), SM(mk3.get(("exp", step - 1, n), 0)),
                                             VM(mk3.get(("rmx", step - 1, n), 0)))
                                            if c == 0 else ()),
                                           inc=("t", 1) if c == 7 else None)
                        mk3[("smmn", step, n)] = msmm
                        mrx = sch.add("vector", lambda e, nn=n, o=ps_, of=off: e.reduce_max(
                            cmax[:, nn:nn + 1], o[:, of:of + 512], AX), (TM(msmm),),
                            inc=("v", 1))
                        mk3[("rmx", step, n)] = mrx
                    mk3[("smm", step)] = msmm
                    mmx = sch.add("vector", lambda e: e.reduce_max(pstat[:, 0:1], cmax[:, 0:4],
                                                                   AX),
                                  (VM(mk3[("rmx", step, 3)]),), inc=("v", 1))
                    if r > 0:
                        mmx = sch.add("vector", lambda e, q_=qi: e.tensor_max(
                            pstat[:, 0:1], pstat[:, 0:1], m_run[:, q_:q_ + 1]), inc=("v", 1))
                        mdm = sch.add("vector", lambda e, q_=qi: e.tensor_sub(
                            pstat[:, 2:3], m_run[:, q_:q_ + 1], pstat[:, 0:1]), inc=("v", 1))
                        mal = sch.add("scalar", lambda e, a_=ac: e.activation(
                            pstat[:, a_:a_ + 1], pstat[:, 2:3], AF.Exp, scale=ISQ),
                            (VM(mdm),), inc=("s", 1))
                        mk3[("al", step)] = mal
                    mng = sch.add("scalar", lambda e: e.activation(pstat[:, 1:2], pstat[:, 0:1],
                                                                   AF.Identity, scale=-ISQ),
                                  (VM(mmx),), inc=("s", 1))
                    sch.add("scalar", lambda e, q_=qi: e.activation(
                        m_run[:, q_:q_ + 1], pstat[:, 0:1], AF.Identity, scale=1.0),
                        (SM(mng),), inc=("s", 1))
                    for n in range(4):
                        ps_ = PA if n < 2 else PB
                        off = (n % 2) * 512
                        mex = sch.add("scalar", lambda e, nn=n, o=ps_, of=off, pp=p:
                                      e.activation(
                                          P_bf[pp][:, nn * 512:(nn + 1) * 512],
                                          o[:, of:of + 512], AF.Exp,
                                          bias=pstat[:, 1:2], scale=ISQ,
                                          accum_out=lbuf[:, nn:nn + 1]),
                                      (TM(mk3.get(("ptp", step - 2), 0)), SM(mng),
                                       VM(mk3.get(("la", step - 1), 0))) if n == 0 else (),
                                      inc=("s", 1))
                        mk3[("exp", step, n)] = mex
                    mla = sch.add("vector", lambda e: e.reduce_sum(pstat[:, 4:5], lbuf[:, 0:4],
                                                                   AX),
                                  (SM(mk3[("exp", step, 3)]),), inc=("v", 1))
                    mk3[("la", step)] = mla
                    if r == 0:
                        sch.add("scalar", lambda e, q_=qi: e.activation(
                            l_run[:, q_:q_ + 1], pstat[:, 4:5], AF.Identity, scale=1.0),
                            (VM(mla),), inc=("s", 1))
                        mk3[("lr", step)] = ("v", mla)
                    else:
                        sch.add("vector", lambda e, q_=qi, a_=ac: e.tensor_mul(
                            l_run[:, q_:q_ + 1], l_run[:, q_:q_ + 1], pstat[:, a_:a_ + 1]),
                            (SM(mk3[("al", step)]),), inc=("v", 1))
                        mlr = sch.add("vector", lambda e, q_=qi: e.tensor_add(
                            l_run[:, q_:q_ + 1], l_run[:, q_:q_ + 1], pstat[:, 4:5]),
                            (VM(mla),), inc=("v", 1))
                        mk3[("lr", step)] = ("v", mlr)

                def qB(step):
                    p = step % 2
                    mpt = None
                    for j in range(16):
                        pt = PT1 if j < 8 else PT2
                        mpt = sch.add("tensor", lambda e, jj=j, pt_=pt, pp=p: e.transpose(
                            pt_[:, jj % 8, :], P_bf[pp][:, jj * 128:(jj + 1) * 128],
                            s_id[:, :]),
                            ((SM(mk3[("exp", step, 3)]), SM(mk3.get(("pev", step - 1), 0)),
                              SM(mk3.get(("ev4", step - 1), 0)))
                             if j == 0 else ()),
                            inc=("t", 1) if j == 15 else None)
                    mk3[("ptp", step)] = mpt
                    sch.add("scalar", lambda e, pp=p: e.copy(pt_sb[pp][:, 0:8, :],
                                                             PT1[:, :, :]),
                            (TM(mpt),), inc=("s", 1))
                    mpe = sch.add("scalar", lambda e, pp=p: e.copy(pt_sb[pp][:, 8:16, :],
                                                                   PT2[:, :, :]),
                                  inc=("s", 1))
                    mk3[("pev", step)] = mpe

                def qC(step):
                    qi = step - r * NQ
                    p = step % 2
                    ac = 3 + 3 * (step % 2)
                    mpe = mk3[("pev", step)]
                    mpv = None
                    for j in range(16):
                        for n in range(2):
                            mpv = sch.add("tensor", lambda e, jj=j, nn=n, pp=p: e.matmul(
                                PC[:, nn * 512:(nn + 1) * 512], pt_sb[pp][:, jj, :],
                                v_res[:, jj * 1024 + nn * 512: jj * 1024 + nn * 512 + 512],
                                start=(jj == 0), stop=(jj == 15)),
                                ((SM(mpe), VM(mk3.get(("zco", step - 1), 0)),
                                  VM(mk3.get(("mo1", step - 1), 0)))
                                 if (j == 0 and n == 0) else ()),
                                inc=("t", 1) if (j == 15 and n == 1) else None)
                    if r == 0:
                        mz = sch.add("vector", lambda e, pp=p: e.tensor_copy(zs[pp][:, :],
                                                                             PC[:, :]),
                                     (TM(mpv), mk3.get(("zst", step - 2))), inc=("v", 1))
                    else:
                        mz = sch.add("vector", lambda e, pp=p, z3=step % 3, a_=ac:
                                     e.scalar_tensor_tensor(
                            zs[pp][:, :], zl[z3][:, :], pstat[:, a_:a_ + 1], PC[:, :],
                            ALU.mult, ALU.add),
                            (TM(mpv), mk3.get(("lz", step)), SM(mk3[("al", step)]),
                             mk3.get(("zst", step - 2)),
                             GM(mk3.get(("moo", step - 2), 0))),
                            inc=("v", 1))
                    mk3[("zco", step)] = mz
                    if r < KR - 1:
                        mk3[("zst", step)] = dma("scalar", sc_z[qi, :, :], zs[p][:, :],
                                                 (VM(mz),), ch=f"sz{p}")
                    else:
                        mli = sch.add("vector", lambda e, q_=qi: e.reciprocal(
                            pstat[:, 5:6], l_run[:, q_:q_ + 1]),
                            (mk3[("lr", step)],), inc=("v", 1))
                        mx3 = sch.add("vector", lambda e, pp=p: e.scalar_tensor_tensor(
                            x3b[pp][:, :], zs[pp][:, :], pstat[:, 5:6], x2l[pp][:, :],
                            ALU.mult, ALU.add),
                            (VM(mli), mk3.get(("lx2", step)),
                             GM(mk3.get(("moo", step - 2), 0))), inc=("v", 1))
                        mk3[("x3", step)] = mx3
                        if dbg:
                            mk3[("zdbg", step)] = dma("scalar", sc_z[qi, :, :], zs[p][:, :],
                                                      (VM(mx3),), ch=f"sz{p}")
                        mxn4 = ln(x3b[p], xn4b[p], st4[p],
                                  (VM(mx3), TM(mk3.get(("tp4", step - 1), 0))))
                        mtp4 = transposes8(PT1, xn4b[p], (SM(mxn4),))
                        mk3[("tp4", step)] = mtp4
                        mev4 = sch.add("scalar", lambda e, pp=p: e.copy(xn4T[pp],
                                                                        PT1[:, :, :]),
                                       (TM(mtp4),), inc=("s", 1))
                        mk3[("ev4", step)] = mev4
                        mf2 = None
                        for n in range(2):
                            for k in range(8):
                                mf2 = sch.add("tensor", lambda e, nn=n, kk=k, pp=p: e.matmul(
                                    PC[:, nn * 512:(nn + 1) * 512], xn4T[pp].tensor[:, kk, :],
                                    W_eff[:, kk * 1024 + nn * 512: kk * 1024 + nn * 512 + 512],
                                    start=(kk == 0), stop=(kk == 7)),
                                    (SM(mev4), VM(mk3[("zco", step)])) if (n == 0 and k == 0)
                                    else (),
                                    inc=("t", 1) if (n == 1 and k == 7) else None)
                        mo1 = sch.add("vector", lambda e, pp=p: e.tensor_add(
                            zs[pp][:, :], PC[:, :], s_bfbc[:, :]),
                            (TM(mf2), GM(mk3.get(("moo", step - 2), 0)),
                             mk3.get(("zdbg", step))), inc=("v", 1))
                        mk3[("mo1", step)] = mo1
                        moo = sch.add("gpsimd", lambda e, pp=p: e.tensor_add(
                            s_out[pp][:, :], zs[pp][:, :], x3b[pp][:, :]),
                            (VM(mo1), mk3.get(("ost", step - 1))), inc=("g", 1))
                        mk3[("moo", step)] = moo
                        mk3[("ost", step)] = dma("scalar", out_d[qi * 128:(qi + 1) * 128, :],
                                                 s_out[p][:, :], (GM(moo),), ch=f"so{p}")

                qS(r * NQ)
                qS(r * NQ + 1)
                qB(r * NQ)
                if r < KR - 1:
                    for qi in range(NQ):
                        step = r * NQ + qi
                        if qi + 2 < NQ:
                            qS(step + 2)
                        if qi + 1 < NQ:
                            qB(step + 1)
                        qC(step)
                else:
                    for qi in range(NQ):
                        step = r * NQ + qi
                        qC(step)
                        if qi + 2 < NQ:
                            qS(step + 2)
                        if qi + 1 < NQ:
                            qB(step + 1)

        pass
    except _Cut:
        pass

    if dbg and phases >= 3:
        dma("scalar", dbg_ml[:, 0:32], m_run[:, :], (VM(sch.count("v")),))
        dma("scalar", dbg_ml[:, 32:64], l_run[:, :], (VM(sch.count("v")),))
    _fw = [(snm, sch.count(snm)) for snm in list(sch.cnt.keys())
           if snm not in ("t", "v", "s", "g")]
    _fw += [TM(sch.count("t")), VM(sch.count("v")), GM(sch.count("g"))]
    sch.add("scalar", lambda e: e.activation(s_sqscr[:, 0:1], s_sqscr[:, 0:1], AF.Identity,
                                             scale=1.0), tuple(_fw), inc=("s", 1))

    for n in sorted(set(list(sch.cnt.keys()) + ["t", "v", "s", "g", "ld", "st"])):
        sems[n] = C(nc.semaphore(n))

    def run_prog(name):
        def f(eng):
            for item in sch.prog[name]:
                if item[0] == "w":
                    eng.wait_ge(sems[item[1]], item[2])
                else:
                    _, fn, inc = item
                    ins = fn(eng)
                    if inc is not None:
                        ins.then_inc(sems[inc[0]], inc[1])
        return f

    with nc.Block() as blk:
        blk.tensor(run_prog("tensor"))
        blk.vector(run_prog("vector"))
        blk.scalar(run_prog("scalar"))
        blk.sync(run_prog("sync"))
        blk.gpsimd(run_prog("gpsimd"))

    for es in (es_p3, es_p2, es_p1t, es_p1w, es_pqw, es_qsp, es_prep,
               es_p2a, es_fold, es_bc, es_wtmp, es_p23w, es_glob):
        es.close()
    return nc


def make_inmaps(inputs):
    f32 = np.float32
    bf = ml_dtypes.bfloat16
    x = np.asarray(inputs["L0_attn_a_x"], f32)
    Wp = np.asarray(inputs["L0_attn_a_W_proj"], f32)
    A = np.asarray(inputs["L0_attn_a_A"], f32).reshape(1, 1024)
    Bp = np.asarray(inputs["L0_attn_a_Bp"], f32).reshape(1, 1024)
    Dsc = np.asarray(inputs["L0_attn_a_D"], f32).reshape(1)
    Wd = np.asarray(inputs["L0_attn_a_W_delta"], f32)
    bd = np.asarray(inputs["L0_attn_a_b_delta"], f32).reshape(1, 1024)
    Wq = np.asarray(inputs["L0_attn_b_Wq"], f32)
    bq = np.asarray(inputs["L0_attn_b_bq"], f32)
    Wk = np.asarray(inputs["L0_attn_b_Wk"], f32)
    bk = np.asarray(inputs["L0_attn_b_bk"], f32)
    Wv = np.asarray(inputs["L0_attn_b_Wv"], f32)
    bv = np.asarray(inputs["L0_attn_b_bv"], f32).reshape(1, 1024)
    W = np.asarray(inputs["L0_ffn_W"], f32)
    b = np.asarray(inputs["L0_ffn_b"], f32).reshape(1, 1024)
    ld_ = np.asarray(inputs["L0_ffn_lora_down"], f32)
    lu_ = np.asarray(inputs["L0_ffn_lora_up"], f32)
    da_ = np.asarray(inputs["L0_ffn_down_aux"], f32)
    ua_ = np.asarray(inputs["L0_ffn_up_aux"], f32)

    ct = np.ascontiguousarray

    def _pad128(a):
        out = np.zeros((128,) + a.shape[1:], f32)
        out[:a.shape[0]] = a
        return out

    rows = np.zeros((8, 1024), f32)
    rows[0] = A
    rows[1] = Bp
    rows[2] = bd
    rows[3] = b
    rows[4] = bv
    rows[5, 0] = Dsc[0]
    emsk = np.zeros((128, 256), bf)
    emsk[:, 65] = 1
    shared = dict(
        wp0=ct(Wp[0:1024, :]), wdT=ct(Wd.T), wcT=ct(Wp[2048:3072, :].T),
        wrT=ct(Wp[3072:4096, :].T), wT=ct(W.T),
        uaT=_pad128(ua_.T), da=_pad128(da_), ldT=_pad128(ld_.T), luT=_pad128(lu_.T),
        wqT=ct(Wq.T), wkT=ct(Wk.T), wvT=ct(Wv.T),
        rows=rows, bqT=ct(bq.reshape(8, 128).T), bkT=ct(bk.reshape(8, 128).T),
        ident=np.eye(128, dtype=bf), tri128=np.triu(np.ones((128, 128))).astype(bf),
        tri64x=np.triu(np.ones((64, 64), f32), 1), emsk=emsk,
        ones1r=np.ones((1, 128), f32), ones64=np.ones((1, 64), f32),
        ones2r=np.ones((2, 128), bf),
    )
    in_maps = []
    for c in range(8):
        bidx, h = c // 2, c % 2
        xl = x[bidx] if h == 0 else np.concatenate([x[bidx, 4096:], x[bidx, :4096]], axis=0)
        m1 = np.zeros((64, 1), f32)
        m2n = np.zeros((64, 1), f32)
        if h == 1:
            m1[0:32] = 1.0
            m2n[32:64] = -1.0
        im = dict(shared)
        im["x"] = ct(xl)
        im["m1c"] = m1
        im["m2n"] = m2n
        in_maps.append(im)
    return in_maps


def kernel(**inputs):
    in_maps = make_inmaps(inputs)
    nc = build(phases=3, dbg=False)
    res = run_bass_kernel_spmd(nc, in_maps, core_ids=list(range(8)))
    out = np.empty((4, 8192, 1024), np.float32)
    for c in range(8):
        bidx, h = c // 2, c % 2
        out[bidx, h * 4096:(h + 1) * 4096] = np.asarray(res.results[c]["out"], np.float32)
    return out


# version 10
# speedup vs baseline: 1.0362x; 1.0011x over previous
import numpy as np
import ml_dtypes
from contextlib import ExitStack

import concourse.bass as bass
import concourse.mybir as mybir
from concourse.bass_utils import run_bass_kernel_spmd

F32 = mybir.dt.float32
BF16 = mybir.dt.bfloat16
AX = mybir.AxisListType.X
AF = mybir.ActivationFunctionType
ALU = mybir.AluOpType

NT = 64
NQ = 32
KR = 4
D = 1024
EPS = 1e-5
ISQ = 1.0 / 32.0

ENGS = ("tensor", "vector", "scalar", "sync", "gpsimd")


class Sched:
    def __init__(self):
        self.prog = {e: [] for e in ENGS}
        self.cnt = {}
        self.lastwait = {e: {} for e in ENGS}

    CHAIN = {"vector": "v", "scalar": "s", "gpsimd": "g"}

    def add(self, eng, fn, waits=(), inc=None):
        sem0 = self.CHAIN.get(eng)
        if sem0 is not None:
            waits = ((sem0, self.cnt.get(sem0, 0)),) + tuple(waits)
            if inc is None:
                inc = (sem0, 1)
        for w in waits:
            if w is None:
                continue
            sem, val = w
            if isinstance(val, tuple):
                sem, val = val
            if val is None or val <= 0:
                continue
            if sem in ("ld", "st"):
                val = self.cnt.get(sem, 0)
            lw = self.lastwait[eng]
            if lw.get(sem, 0) >= val:
                continue
            lw[sem] = val
            self.prog[eng].append(("w", sem, val))
        self.prog[eng].append(("c", fn, inc))
        if inc is not None:
            sem, amt = inc
            self.cnt[sem] = self.cnt.get(sem, 0) + amt
            return self.cnt[sem]
        return None

    def count(self, sem):
        return self.cnt.get(sem, 0)

    def barrier(self):
        snap = dict(self.cnt)
        for e in ENGS:
            for sem, val in snap.items():
                if val > 0 and self.lastwait[e].get(sem, 0) < val:
                    self.lastwait[e][sem] = val
                    self.prog[e].append(("w", sem, val))


class _Cut(Exception):
    pass


def build(phases=3, dbg=False, cut=None):
    nc = bass.Bass("TRN2", target_bir_lowering=False)

    dI = lambda n, s, dt=F32: nc.dram_tensor(n, s, dt, kind="ExternalInput").ap()
    x_d = dI("x", [8192, 1024])
    wp0_d = dI("wp0", [1024, 1024])
    wdT_d = dI("wdT", [1024, 1024])
    wcT_d = dI("wcT", [1024, 1024])
    wrT_d = dI("wrT", [1024, 1024])
    wT_d = dI("wT", [1024, 1024])
    uaT_d = dI("uaT", [128, 1024])
    da_d = dI("da", [128, 1024])
    ldT_d = dI("ldT", [128, 4])
    luT_d = dI("luT", [128, 50])
    wqT_d = dI("wqT", [1024, 1024])
    wkT_d = dI("wkT", [1024, 1024])
    wvT_d = dI("wvT", [1024, 1024])
    rows_d = dI("rows", [8, 1024])
    bqT_d = dI("bqT", [128, 8])
    bkT_d = dI("bkT", [128, 8])
    m1_d = dI("m1c", [64, 1])
    m2n_d = dI("m2n", [64, 1])
    ident_d = dI("ident", [128, 128], BF16)
    tri128_d = dI("tri128", [128, 128], BF16)
    tri64_d = dI("tri64x", [64, 64])
    emsk_d = dI("emsk", [128, 256], BF16)
    o1r_d = dI("ones1r", [1, 128])
    o64_d = dI("ones64", [1, 64])
    o2r_d = dI("ones2r", [2, 128], BF16)

    sk = "ExternalOutput" if dbg else "Internal"
    dS = lambda n, s, dt=F32: nc.dram_tensor(n, s, dt, kind=sk).ap()
    sc_delta = dS("sc_delta", [NT, 128, 1024])
    sc_res = dS("sc_res", [NT, 128, 1024])
    sc_hi = dS("sc_hi", [64, 1024], BF16)
    sc_lo = dS("sc_lo", [64, 1024], BF16)
    sc_xn3 = dS("sc_xn3", [NT, 128, 1024], BF16)
    sc_x2 = dS("sc_x2", [NQ, 128, 1024])
    sc_qT = dS("sc_qT", [NQ, 128, 8, 128], BF16)
    sc_z = dS("sc_z", [NQ, 128, 1024])
    if dbg:
        dbg_tbl = nc.dram_tensor("dbg_tbl", [128, 1024], F32, kind="ExternalOutput").ap()
        dbg_bc = nc.dram_tensor("dbg_bc", [128, 4096], F32, kind="ExternalOutput").ap()
        sc_x1 = nc.dram_tensor("sc_x1", [NT, 128, 1024], F32, kind="ExternalOutput").ap()
        dbg_ml = nc.dram_tensor("dbg_ml", [128, 64], F32, kind="ExternalOutput").ap()
    out_d = nc.dram_tensor("out", [4096, 1024], F32, kind="ExternalOutput").ap()

    sch = Sched()

    def _ck(n):
        if cut is not None and n >= cut:
            raise _Cut()

    LD = lambda m: m if isinstance(m, tuple) else ("ld", m)
    VD = LD
    ST = lambda m: m if isinstance(m, tuple) else ("st", m)
    TM = lambda m: ("t", m)
    VM = lambda m: ("v", m)
    SM = lambda m: ("s", m)
    GM = lambda m: ("g", m)

    (es_glob, es_p23w, es_pqw, es_p1w, es_p1t, es_p2, es_p3,
     es_bc, es_wtmp, es_fold, es_p2a, es_prep, es_qsp) = (ExitStack() for _ in range(13))
    C = es_glob.enter_context
    R = lambda es, t: es.enter_context(t)

    s_id = C(nc.sbuf_tensor("s_id", [128, 128], BF16))
    s_tri128 = C(nc.sbuf_tensor("s_tri128", [128, 128], BF16))
    s_tri64 = C(nc.sbuf_tensor("s_tri64", [64, 64], F32))
    s_emsk = C(nc.sbuf_tensor("s_emsk", [128, 256], BF16))
    s_o1r = C(nc.sbuf_tensor("s_o1r", [1, 128], F32))
    s_o64 = C(nc.sbuf_tensor("s_o64", [1, 64], F32))
    s_o2r = C(nc.sbuf_tensor("s_o2r", [2, 128], BF16))
    s_bqT = C(nc.sbuf_tensor("s_bqT", [128, 8], F32))
    s_bkT = C(nc.sbuf_tensor("s_bkT", [128, 8], F32))
    s_m1 = C(nc.sbuf_tensor("s_m1", [64, 1], F32))
    s_m2n = C(nc.sbuf_tensor("s_m2n", [64, 1], F32))
    s_Dbc = C(nc.sbuf_tensor("s_Dbc", [128, 1], F32))
    s_bfbc = C(nc.sbuf_tensor("s_bfbc", [128, 1024], F32))
    s_bvbc = R(es_prep, nc.sbuf_tensor("s_bvbc", [128, 1024], F32, side="right"))
    W_eff = C(nc.sbuf_tensor("W_eff", [128, 8192], BF16))
    s_wtmp = [R(es_wtmp, nc.sbuf_tensor(f"s_wtmp{p}", [128, 1024], F32, side="right"))
              for p in range(2)]
    s_sqscr = C(nc.sbuf_tensor("s_sqscr", [128, 1024], BF16))
    s_out = [C(nc.sbuf_tensor("s_out0", [128, 1024], F32))] * 2

    PA = C(nc.psum_tensor("PA", [128, 1024], F32))
    PB = C(nc.psum_tensor("PB", [128, 1024], F32))
    PC = C(nc.psum_tensor("PC", [128, 1024], F32))
    PT1 = C(nc.psum_tensor("PT1", [128, 8, 128], BF16))
    PT2 = C(nc.psum_tensor("PT2", [128, 8, 128], BF16))

    sems = {}

    def dma(q, outap, inap, waits=(), ch=None):
        if q == "vector":
            q = "sync"
        semn = ch if ch is not None else {"sync": "ld", "scalar": "st"}[q]
        if semn in ("ld", "st"):
            waits = tuple(waits) + ((semn, 1),)
        m = sch.add(q, lambda e, o=outap, i=inap: e.dma_start(out=o, in_=i),
                    waits, inc=(semn, 16))
        return (semn, m)

    def transposes8(pt, src_bf, waits=()):
        m = None
        for c in range(8):
            m = sch.add("tensor",
                        lambda e, o=pt[:, c, :], i_=src_bf[:, c * 128:(c + 1) * 128]:
                        e.transpose(o, i_, s_id[:, :]),
                        waits if c == 0 else (), inc=("t", 1) if c == 7 else None)
        return m

    def ln(xin, xn_bf, stats, w_x):
        mv = sch.add("vector", lambda e: e.reduce_sum(stats[:, 0:1], xin[:, :], AX), w_x,
                     inc=("v", 1))
        msq = sch.add("scalar", lambda e: e.activation(s_sqscr[:, :], xin[:, :], AF.Square,
                                                       accum_out=stats[:, 1:2]), w_x,
                      inc=("s", 1))
        mng = sch.add("scalar", lambda e: e.activation(stats[:, 2:3], stats[:, 0:1], AF.Identity,
                                                       scale=-1.0 / 1024.0), (VM(mv),), inc=("s", 1))
        mvh = sch.add("scalar", lambda e: e.activation(stats[:, 3:4], stats[:, 1:2], AF.Identity,
                                                       scale=1.0 / 1024.0), (SM(msq),), inc=("s", 1))
        ms = sch.add("scalar", lambda e: e.activation(stats[:, 4:5], stats[:, 2:3], AF.Square),
                     (SM(mng),), inc=("s", 1))
        mv2 = sch.add("vector", lambda e: e.tensor_sub(stats[:, 5:6], stats[:, 3:4],
                                                       stats[:, 4:5]), (SM(ms), SM(mvh)), inc=("v", 1))
        ms2 = sch.add("scalar", lambda e: e.activation(stats[:, 6:7], stats[:, 5:6], AF.Copy,
                                                       bias=EPS), (VM(mv2),), inc=("s", 1))
        mv3 = sch.add("vector", lambda e: e.reciprocal(stats[:, 7:8], stats[:, 6:7]),
                      (SM(ms2),), inc=("v", 1))
        ms3 = sch.add("scalar", lambda e: e.activation(stats[:, 8:9], stats[:, 7:8], AF.Sqrt),
                      (VM(mv3),), inc=("s", 1))
        mv4 = sch.add("vector", lambda e: e.tensor_mul(stats[:, 9:10], stats[:, 2:3],
                                                       stats[:, 8:9]), (SM(ms3),), inc=("v", 1))
        mxn = sch.add("scalar", lambda e: e.activation(xn_bf[:, :], xin[:, :], AF.Identity,
                                                       bias=stats[:, 9:10], scale=stats[:, 8:9]),
                      (VM(mv4), SM(ms3)), inc=("s", 1))
        return mxn

    wtmp_prev = {0: None, 1: None}

    def load_weight(dst_bf, src_d):
        last = None
        for k in range(8):
            p = k % 2
            ml = dma("sync", s_wtmp[p][:, :], src_d[k * 128:(k + 1) * 128, :],
                     (wtmp_prev[p],), ch=f"lw{p}")
            mg = sch.add("gpsimd",
                         lambda e, kk=k, pp=p: e.tensor_copy(
                             dst_bf[:, kk * 1024:(kk + 1) * 1024], s_wtmp[pp][:, :]),
                         (LD(ml),), inc=("g", 1))
            wtmp_prev[p] = GM(mg)
            last = mg
        return last

    WkT = R(es_p23w, nc.sbuf_tensor("WkT", [128, 8192], BF16))
    WvT = R(es_p23w, nc.sbuf_tensor("WvT", [128, 8192], BF16))
    WqT = R(es_pqw, nc.sbuf_tensor("WqT", [128, 8192], BF16))
    W_dd = R(es_p1w, nc.sbuf_tensor("W_dd", [128, 8192], BF16))
    WcT = R(es_p1w, nc.sbuf_tensor("WcT", [128, 8192], BF16))
    WrT = R(es_p1w, nc.sbuf_tensor("WrT", [128, 8192], BF16))
    Rb = lambda t: R(es_bc, t)
    s_A1bc = Rb(nc.sbuf_tensor("s_A1bc", [128, 1024], F32, side="right"))
    s_B1bc = Rb(nc.sbuf_tensor("s_B1bc", [128, 1024], F32, side="right"))
    s_bdbc = Rb(nc.sbuf_tensor("s_bdbc", [128, 1024], F32, side="right"))
    s_Csbc = Rb(nc.sbuf_tensor("s_Csbc", [128, 1024], F32, side="right"))
    s_R2bc = Rb(nc.sbuf_tensor("s_R2bc", [128, 1024], F32, side="right"))
    Cf = lambda t: R(es_fold, t)
    s_row1 = Cf(nc.sbuf_tensor("s_row1", [1, 1024], F32, side="right"))
    Wp0 = Cf(nc.sbuf_tensor("Wp0", [128, 8192], BF16, side="right"))
    WdT = Cf(nc.sbuf_tensor("WdT", [128, 8192], BF16, side="right"))
    s_ldT = Cf(nc.sbuf_tensor("s_ldT", [128, 4], BF16, side="right"))
    s_luT = Cf(nc.sbuf_tensor("s_luT", [128, 50], BF16, side="right"))
    s_uaT = Cf(nc.sbuf_tensor("s_uaT", [128, 1024], BF16, side="right"))
    s_da = Cf(nc.sbuf_tensor("s_da", [128, 1024], BF16, side="right"))
    s_t1 = Cf(nc.sbuf_tensor("s_t1", [128, 1024], BF16, side="right"))
    s_t2 = Cf(nc.sbuf_tensor("s_t2", [128, 1024], BF16, side="right"))
    s_lt = s_wtmp[0]

    try:
        for outap, inap in ((s_id[:, :], ident_d[:, :]), (s_tri128[:, :], tri128_d[:, :]),
                            (s_tri64[:, :], tri64_d[:, :]), (s_emsk[:, :], emsk_d[:, :]),
                            (s_o1r[:, :], o1r_d[:, :]), (s_o64[:, :], o64_d[:, :]),
                            (s_o2r[:, :], o2r_d[:, :]),
                            (s_bqT[:, :], bqT_d[:, :]), (s_bkT[:, :], bkT_d[:, :]),
                            (s_m1[:, :], m1_d[:, :]), (s_m2n[:, :], m2n_d[:, :])):
            m_startld = dma("sync", outap, inap)
        _ck(1)

        ml = dma("sync", s_lt[:, 0:4], ldT_d[:, :], (wtmp_prev[0],), ch="lw0")
        mg1 = sch.add("gpsimd", lambda e: e.tensor_copy(s_ldT[:, :], s_lt[:, 0:4]),
                      (LD(ml),), inc=("g", 1))
        ml = dma("sync", s_lt[:, 0:50], luT_d[:, :], (GM(mg1),), ch="lw0")
        mg2 = sch.add("gpsimd", lambda e: e.tensor_copy(s_luT[:, :], s_lt[:, 0:50]),
                      (LD(ml),), inc=("g", 1))
        ml = dma("sync", s_lt[:, :], uaT_d[:, :], (GM(mg2),), ch="lw0")
        mg3 = sch.add("gpsimd", lambda e: e.tensor_copy(s_uaT[:, :], s_lt[:, :]),
                      (LD(ml),), inc=("g", 1))
        ml = dma("sync", s_lt[:, :], da_d[:, :], (GM(mg3),), ch="lw0")
        mg4 = sch.add("gpsimd", lambda e: e.tensor_copy(s_da[:, :], s_lt[:, :]),
                      (LD(ml),), inc=("g", 1))
        wtmp_prev[0] = GM(mg4)
        mz1 = sch.add("gpsimd", lambda e: e.memzero(s_t1[:, :]), inc=("g", 1))
        mz2 = sch.add("gpsimd", lambda e: e.memzero(s_t2[:, :]), inc=("g", 1))
        _ck(2)

        mwp0 = load_weight(Wp0, wp0_d)
        mwdt = load_weight(WdT, wdT_d)
        _ck(3)

        mt1 = None
        for n in range(2):
            mt1 = sch.add("tensor", lambda e, nn=n: e.matmul(
                PA[0:4, nn * 512:(nn + 1) * 512], s_ldT[:, :], s_da[:, nn * 512:(nn + 1) * 512],
                start=True, stop=True), (GM(mg4), GM(mg1)) if n == 0 else (),
                inc=("t", 1) if n == 1 else None)
        ms = sch.add("scalar", lambda e: e.copy(s_t1[0:4, :], PA[0:4, :]),
                     (TM(mt1), GM(mz1)), inc=("s", 1))
        mt2 = None
        for n in range(2):
            mt2 = sch.add("tensor", lambda e, nn=n: e.matmul(
                PB[0:50, nn * 512:(nn + 1) * 512], s_luT[:, :], s_t1[:, nn * 512:(nn + 1) * 512],
                start=True, stop=True), (SM(ms), GM(mg2)) if n == 0 else (),
                inc=("t", 1) if n == 1 else None)
        ms2 = sch.add("scalar", lambda e: e.copy(s_t2[0:50, :], PB[0:50, :]),
                      (TM(mt2), GM(mz2)), inc=("s", 1))
        _ck(4)
        m_weff_v = None
        for mch in range(8):
            p = mch % 2
            wld = dma("sync", s_wtmp[p][:, :], wT_d[mch * 128:(mch + 1) * 128, :],
                      (wtmp_prev[p],), ch=f"lw{p}")
            mml = None
            for n in range(2):
                mml = sch.add("tensor", lambda e, nn=n, m_=mch: e.matmul(
                    PA[:, nn * 512:(nn + 1) * 512], s_t2[:, m_ * 128:(m_ + 1) * 128],
                    s_uaT[:, nn * 512:(nn + 1) * 512], start=True, stop=True),
                    ((SM(ms2), GM(mg3)) if mch == 0 else (VM(m_weff_v),))
                    if n == 0 else (),
                    inc=("t", 1) if n == 1 else None)
            m_weff_v = sch.add("vector", lambda e, m_=mch, pp=p: e.tensor_add(
                W_eff[:, m_ * 1024:(m_ + 1) * 1024], PA[:, :], s_wtmp[pp][:, :]),
                (TM(mml), LD(wld)), inc=("v", 1))
            wtmp_prev[p] = VM(m_weff_v)

        _ck(5)
        m_wdd_s = None
        for mch in range(8):
            mml = None
            for n in range(2):
                for k in range(8):
                    mml = sch.add("tensor", lambda e, nn=n, kk=k, m_=mch: e.matmul(
                        PB[:, nn * 512:(nn + 1) * 512],
                        Wp0[:, kk * 1024 + m_ * 128: kk * 1024 + (m_ + 1) * 128],
                        WdT[:, kk * 1024 + nn * 512: kk * 1024 + nn * 512 + 512],
                        start=(kk == 0), stop=(kk == 7)),
                        ((GM(mwp0), GM(mwdt), SM(ms2)) if mch == 0 else (SM(m_wdd_s),))
                        if (n == 0 and k == 0) else (),
                        inc=("t", 1) if (n == 1 and k == 7) else None)
            m_wdd_s = sch.add("scalar", lambda e, m_=mch: e.copy(
                W_dd[:, m_ * 1024:(m_ + 1) * 1024], PB[:, :]), (TM(mml),), inc=("s", 1))

        _ck(6)
        mwc = load_weight(WcT, wcT_d)
        mwr = load_weight(WrT, wrT_d)

        _ck(7)
        m_prevbc = 0

        def bcast_row(dst, rowidx, dcol=False):
            nonlocal_marks = {}
            mrl = dma("sync", s_row1[:, :], rows_d[rowidx:rowidx + 1, :],
                      (TM(m_prevbc), LD(m_startld)))
            mb = None
            for n in range(2):
                mb = sch.add("tensor", lambda e, nn=n: e.matmul(
                    PC[:, nn * 512:(nn + 1) * 512], s_o1r[:, :],
                    s_row1[0:1, nn * 512:(nn + 1) * 512], start=True, stop=True),
                    (LD(mrl), VM(sch.count("v"))) if n == 0 else (),
                    inc=("t", 1) if n == 1 else None)
            if dcol:
                mv = sch.add("vector", lambda e: e.tensor_copy(s_Dbc[:, :], PC[:, 0:1]),
                             (TM(mb),), inc=("v", 1))
            else:
                mv = sch.add("vector", lambda e: e.tensor_copy(dst[:, :], PC[:, :]), (TM(mb),),
                             inc=("v", 1))
            return mb

        m_prevbc = bcast_row(s_A1bc, 0)
        m_prevbc = bcast_row(s_B1bc, 1)
        m_prevbc = bcast_row(s_bdbc, 2)
        m_prevbc = bcast_row(s_bfbc, 3)
        m_prevbc = bcast_row(s_bvbc, 4)
        m_prevbc = bcast_row(None, 5, dcol=True)

        _ck(8)
        sch.barrier()
        es_fold.close()

        if phases >= 1:
            Cp = es_p1t.enter_context
            xb = [Cp(nc.sbuf_tensor(f"xb{p}", [128, 1024], F32)) for p in range(3)]
            xn_bf = [Cp(nc.sbuf_tensor(f"xn_bf{p}", [128, 1024], BF16)) for p in range(3)]
            xnT = [Cp(nc.sbuf_tensor(f"xnT{p}", [128, 8, 128], BF16)) for p in range(3)]
            delta_f = [Cp(nc.sbuf_tensor(f"delta_f{p}", [128, 1024], F32)) for p in range(2)]
            e1_f = [Cp(nc.sbuf_tensor("e1_f0", [128, 1024], F32))] * 2
            res_f = [Cp(nc.sbuf_tensor(f"res_f{p}", [128, 1024], F32)) for p in range(2)]
            cs_bf = [Cp(nc.sbuf_tensor("cs_bf0", [128, 1024], BF16))] * 2
            g0_bf = [Cp(nc.sbuf_tensor("g0_bf0", [128, 1024], BF16))] * 2
            m_bf = [Cp(nc.sbuf_tensor("m_bf0", [128, 1024], BF16))] * 2
            st1 = [Cp(nc.sbuf_tensor(f"st1_{p}", [128, 16], F32)) for p in range(3)]


            mk = {}

            def p1A(i):
                p = i % 3
                q3 = i % 3
                PTt = PT1 if i % 2 == 0 else PT2
                w_x = [LD(mk[("lx", i)]), SM(mk.get(("xn_c", i - 3))),
                       GM(mk.get(("m", i - 3))), TM(mk.get(("tp", i - 3)))]
                if i == 0:
                    w_x += [SM(m_wdd_s), GM(mwc), GM(mwr), VM(sch.count("v"))]
                mxn = ln(xb[p], xn_bf[q3], st1[q3], tuple(w_x))
                mk[("xn_c", i)] = mxn
                mtp = transposes8(PTt, xn_bf[q3], (SM(mxn), VM(mk.get(("xnTev", i - 2), 0))))
                mk[("tp", i)] = mtp
                mev = sch.add("vector", lambda e, qq=q3, pt=PTt: e.tensor_copy(
                    xnT[qq][:, :, :], pt[:, :, :]), (TM(mtp),), inc=("v", 1))
                mk[("xnTev", i)] = mev

            def p1B(i):
                p = i % 2
                q3 = i % 3
                mev = mk[("xnTev", i)]
                mdp = None
                for n in range(2):
                    for k in range(8):
                        mdp = sch.add("tensor", lambda e, nn=n, kk=k, qq=q3: e.matmul(
                            PA[:, nn * 512:(nn + 1) * 512], xnT[qq][:, kk, :],
                            W_dd[:, kk * 1024 + nn * 512: kk * 1024 + nn * 512 + 512],
                            start=(kk == 0), stop=(kk == 7)),
                            (VM(mev), VM(mk.get(("resev", i - 1), 0)))
                            if (n == 0 and k == 0) else (),
                            inc=("t", 1) if (n == 1 and k == 7) else None)
                mba = sch.add("vector", lambda e, pp=p: e.tensor_add(
                    delta_f[pp][:, :], PA[:, :], s_bdbc[:, :]),
                    (TM(mdp), GM(mk.get(("g0", i - 2), 0)), mk.get(("dst", i - 2))),
                    inc=("v", 1))
                mcs = None
                for n in range(2):
                    for k in range(8):
                        mcs = sch.add("tensor", lambda e, nn=n, kk=k, qq=q3: e.matmul(
                            PB[:, nn * 512:(nn + 1) * 512], xnT[qq][:, kk, :],
                            WcT[:, kk * 1024 + nn * 512: kk * 1024 + nn * 512 + 512],
                            start=(kk == 0), stop=(kk == 7)),
                            (SM(mk.get(("csbf", i - 1), 0)),) if (n == 0 and k == 0) else (),
                            inc=("t", 1) if (n == 1 and k == 7) else None)
                mex = sch.add("scalar", lambda e, pp=p: e.activation(
                    e1_f[pp][:, :], delta_f[pp][:, :], AF.Exp), (VM(mba),), inc=("s", 1))
                mln = sch.add("scalar", lambda e, pp=p: e.activation(
                    delta_f[pp][:, :], e1_f[pp][:, :], AF.Ln, bias=1.0), (SM(mex),),
                    inc=("s", 1))
                mk[("sp", i)] = mln
                mk[("dst", i)] = dma("scalar", sc_delta[i, :, :], delta_f[p][:, :], ch=f"sdl{p}")
                mcb = sch.add("scalar", lambda e, pp=p: e.copy(cs_bf[pp][:, :], PB[:, :]),
                              (TM(mcs), GM(mk.get(("m", i - 1), 0)),
                               TM(mk.get(("colsum", i - 1), 0))), inc=("s", 1))
                mk[("csbf", i)] = mcb
                mg0 = sch.add("gpsimd", lambda e, pp=p: e.tensor_mul(
                    g0_bf[pp][:, :], delta_f[pp][:, :], s_B1bc[:, :]),
                    (SM(mln), TM(mk.get(("colsum", i - 1), 0))), inc=("g", 1))
                mk[("g0", i)] = mg0
                mmm = sch.add("gpsimd", lambda e, pp=p, qq=q3: e.tensor_mul(
                    m_bf[pp][:, :], xn_bf[qq][:, :], cs_bf[pp][:, :]),
                    (SM(mcb), TM(mk.get(("colsum", i - 1), 0))), inc=("g", 1))
                mk[("m", i)] = mmm
                mrs = None
                for n in range(2):
                    for k in range(8):
                        mrs = sch.add("tensor", lambda e, nn=n, kk=k, qq=q3: e.matmul(
                            PA[:, nn * 512:(nn + 1) * 512], xnT[qq][:, kk, :],
                            WrT[:, kk * 1024 + nn * 512: kk * 1024 + nn * 512 + 512],
                            start=(kk == 0), stop=(kk == 7)),
                            (VM(mba),) if (n == 0 and k == 0) else (),
                            inc=("t", 1) if (n == 1 and k == 7) else None)
                mrev = sch.add("vector", lambda e, pp=p: e.tensor_copy(
                    res_f[pp][:, :], PA[:, :]),
                    (TM(mrs), mk.get(("rst", i - 2))), inc=("v", 1))
                mk[("resev", i)] = mrev
                mk[("rst", i)] = dma("scalar", sc_res[i, :, :], res_f[p][:, :], (VM(mrev),),
                                     ch=f"sre{p}")
                mcl = None
                for n in range(2):
                    mcl = sch.add("tensor", lambda e, nn=n, ii=i, pp=p: e.matmul(
                        PC[0:128, nn * 512:(nn + 1) * 512], s_emsk[:, 65 - ii:193 - ii],
                        g0_bf[pp][:, nn * 512:(nn + 1) * 512], start=(ii == 0), stop=False),
                        (GM(mg0),) if n == 0 else ())
                for n in range(2):
                    mcl = sch.add("tensor", lambda e, nn=n, ii=i, pp=p: e.matmul(
                        PC[0:128, nn * 512:(nn + 1) * 512], s_emsk[:, 1:129],
                        m_bf[pp][:, nn * 512:(nn + 1) * 512], start=False, stop=False),
                        (GM(mmm),) if n == 0 else ())
                for n in range(2):
                    mcl = sch.add("tensor", lambda e, nn=n, ii=i, pp=p: e.matmul(
                        PC[0:128, nn * 512:(nn + 1) * 512], s_emsk[:, 0:128],
                        cs_bf[pp][:, nn * 512:(nn + 1) * 512], start=False,
                        stop=(ii == NT - 1)),
                        (), inc=("t", 1) if n == 1 else None)
                mk[("colsum", i)] = mcl

            for i in range(5):
                if i < NT:
                    mk[("lx", i)] = dma("sync", xb[i % 3][:, :], x_d[i * 128:(i + 1) * 128, :],
                                        (SM(mk.get(("xn_c", i - 3))),), ch=f"lxb{i % 3}")
                if i < 3:
                    p1A(i)
            for i in range(NT):
                p1B(i)
                if i + 5 < NT:
                    mk[("lx", i + 5)] = dma("sync", xb[(i + 5) % 3][:, :],
                                            x_d[(i + 5) * 128:(i + 6) * 128, :],
                                            (SM(mk[("xn_c", i + 2)]),), ch=f"lxb{(i + 5) % 3}")
                if i + 3 < NT:
                    p1A(i + 3)

            sch.barrier()
            es_p1t.close()
            es_p1w.close()

            C2a = lambda t: R(es_p2a, t)
            tbl_sb = C2a(nc.sbuf_tensor("tbl_sb", [128, 1024], F32, side="right"))
            car0_sb = C2a(nc.sbuf_tensor("car0_sb", [64, 1024], F32, side="right"))
            l1bc_sb = C2a(nc.sbuf_tensor("l1bc_sb", [64, 1024], F32, side="right"))
            tadd_sb = C2a(nc.sbuf_tensor("tadd_sb", [64, 1024], F32, side="right"))
            hi_bf = C2a(nc.sbuf_tensor("hi_bf", [64, 1024], BF16, side="right"))
            lo_bf = C2a(nc.sbuf_tensor("lo_bf", [64, 1024], BF16, side="right"))
            hi32_sb = C2a(nc.sbuf_tensor("hi32_sb", [64, 1024], F32, side="right"))
            b0_sb = C2a(nc.sbuf_tensor("b0_sb", [1, 6144], F32, side="right"))
            ucbc_sb = C2a(nc.sbuf_tensor("ucbc_sb", [128, 1024], F32, side="right"))

            mtb = sch.add("vector", lambda e: e.tensor_copy(tbl_sb[0:66, :], PC[0:66, :]), (),
                          inc=("v", 1))
            mu1 = dma("sync", b0_sb[0:1, 0:1024], tbl_sb[64:65, :], (VM(mtb),))
            mu2 = dma("sync", b0_sb[0:1, 1024:2048], tbl_sb[65:66, :])
            mpf = None
            for n in range(2):
                mpf = sch.add("tensor", lambda e, nn=n: e.matmul(
                    PA[0:64, nn * 512:(nn + 1) * 512], s_tri64[:, :],
                    tbl_sb[0:64, nn * 512:(nn + 1) * 512], start=True, stop=True),
                    (VM(mtb),) if n == 0 else (), inc=("t", 1) if n == 1 else None)
            mc0 = sch.add("vector", lambda e: e.tensor_copy(car0_sb[:, :], PA[0:64, :]), (TM(mpf),),
                          inc=("v", 1))
            ml1 = dma("sync", b0_sb[0:1, 2048:3072], car0_sb[32:33, :], (VM(mc0),))
            ml2 = dma("sync", b0_sb[0:1, 3072:4096], car0_sb[63:64, :])
            ml3 = dma("sync", b0_sb[0:1, 4096:5120], tbl_sb[63:64, :])
            sch.add("vector", lambda e: e.tensor_add(b0_sb[0:1, 5120:6144], b0_sb[0:1, 3072:4096],
                                                     b0_sb[0:1, 4096:5120]), (LD(ml3),), inc=("v", 1))
            mt_ = sch.add("vector", lambda e: e.tensor_sub(b0_sb[0:1, 5120:6144],
                                                           b0_sb[0:1, 5120:6144],
                                                           b0_sb[0:1, 2048:3072]), inc=("v", 1))
            mb1 = None
            for n in range(2):
                mb1 = sch.add("tensor", lambda e, nn=n: e.matmul(
                    PB[0:64, nn * 512:(nn + 1) * 512], s_o64[:, :],
                    b0_sb[0:1, 2048 + nn * 512:2048 + (nn + 1) * 512], start=True, stop=True),
                    (VM(mt_),) if n == 0 else (), inc=("t", 1) if n == 1 else None)
            mevl1 = sch.add("vector", lambda e: e.tensor_copy(l1bc_sb[:, :], PB[0:64, :]), (TM(mb1),),
                            inc=("v", 1))
            mb2 = None
            for n in range(2):
                mb2 = sch.add("tensor", lambda e, nn=n: e.matmul(
                    PB[0:64, nn * 512:(nn + 1) * 512], s_o64[:, :],
                    b0_sb[0:1, 5120 + nn * 512:5120 + (nn + 1) * 512], start=True, stop=True),
                    (VM(mevl1),) if n == 0 else (), inc=("t", 1) if n == 1 else None)
            mta = sch.add("vector", lambda e: e.scalar_tensor_tensor(
                tadd_sb[:, :], PB[0:64, :], s_m1[:, :], car0_sb[:, :], ALU.mult, ALU.add),
                (TM(mb2),), inc=("v", 1))
            mta = sch.add("vector", lambda e: e.scalar_tensor_tensor(
                tadd_sb[:, :], l1bc_sb[:, :], s_m2n[:, :], tadd_sb[:, :], ALU.mult, ALU.add),
                inc=("v", 1))
            mhi = sch.add("scalar", lambda e: e.copy(hi_bf[:, :], tadd_sb[:, :]), (VM(mta),),
                          inc=("s", 1))
            mhi2 = sch.add("scalar", lambda e: e.copy(hi32_sb[:, :], hi_bf[:, :]), (SM(mhi),),
                           inc=("s", 1))
            mlo = sch.add("vector", lambda e: e.tensor_sub(lo_bf[:, :], tadd_sb[:, :], hi32_sb[:, :]),
                          (SM(mhi2),), inc=("v", 1))
            mst_hi = dma("scalar", sc_hi[:, :], hi_bf[:, :], (SM(mhi),))
            mst_lo = dma("scalar", sc_lo[:, :], lo_bf[:, :], (VM(mlo),))
            mcb_ = None
            for n in range(2):
                mcb_ = sch.add("tensor", lambda e, nn=n: e.matmul(
                    PC[:, nn * 512:(nn + 1) * 512], s_o1r[:, :],
                    b0_sb[0:1, 1024 + nn * 512:1024 + (nn + 1) * 512], start=True, stop=True),
                    (LD(mu2), VM(mtb)) if n == 0 else (), inc=("t", 1) if n == 1 else None)
            mcsbc = sch.add("vector", lambda e: e.tensor_copy(s_Csbc[:, :], PC[:, :]), (TM(mcb_),),
                            inc=("v", 1))
            mub_ = None
            for n in range(2):
                mub_ = sch.add("tensor", lambda e, nn=n: e.matmul(
                    PC[:, nn * 512:(nn + 1) * 512], s_o1r[:, :],
                    b0_sb[0:1, 0 + nn * 512:0 + (nn + 1) * 512], start=True, stop=True),
                    (VM(mcsbc), LD(mu1)) if n == 0 else (), inc=("t", 1) if n == 1 else None)
            mucbc = sch.add("vector", lambda e: e.tensor_copy(ucbc_sb[:, :], PC[:, :]), (TM(mub_),),
                            inc=("v", 1))
            mr2 = sch.add("gpsimd", lambda e: e.tensor_mul(s_R2bc[:, :], s_B1bc[:, :], ucbc_sb[:, :]),
                          (VM(mucbc),), inc=("g", 1))
            if dbg:
                dma("scalar", dbg_tbl[0:66, :], tbl_sb[0:66, :], (VM(mtb),))
                dma("scalar", dbg_bc[:, 0:1024], s_Csbc[:, :], (VM(mcsbc),))
                dma("scalar", dbg_bc[:, 1024:2048], ucbc_sb[:, :], (VM(mucbc),))
                dma("scalar", dbg_bc[:, 2048:3072], s_R2bc[:, :], (GM(mr2),))
                dma("scalar", dbg_bc[0:64, 3072:4096], car0_sb[0:64, :], (VM(mc0),))

            sch.barrier()
            es_p2a.close()
            es_p1t_dummy = None


        if phases >= 2:
            C2 = es_p2.enter_context
            A_ = [C2(nc.sbuf_tensor(f"p2A{p}", [128, 1024], F32)) for p in range(2)]
            B_ = [C2(nc.sbuf_tensor(f"p2B{p}", [128, 1024], F32)) for p in range(2)]
            Cc_ = [C2(nc.sbuf_tensor(f"p2C{p}", [128, 1024], F32)) for p in range(2)]
            D_ = [C2(nc.sbuf_tensor(f"p2D{p}", [128, 1024], F32)) for p in range(2)]
            E_ = [C2(nc.sbuf_tensor(f"p2E{p}", [128, 1024], F32)) for p in range(2)]
            F_ = [C2(nc.sbuf_tensor(f"p2F{p}", [128, 1024], F32)) for p in range(3)]
            G_ = [C2(nc.sbuf_tensor(f"p2G{p}", [128, 1024], F32)) for p in range(3)]
            H_ = [C2(nc.sbuf_tensor(f"p2H{p}", [128, 1024], F32)) for p in range(2)]
            g0b = [C2(nc.sbuf_tensor(f"p2g0{p}", [128, 1024], BF16)) for p in range(2)]
            hilo = [C2(nc.sbuf_tensor(f"hilo{p}", [2, 1024], BF16)) for p in range(2)]
            xn2b = [C2(nc.sbuf_tensor(f"xn2b{p}", [128, 1024], BF16)) for p in range(2)]
            xn2T = [C2(nc.sbuf_tensor(f"xn2T{p}", [128, 8, 128], BF16)) for p in range(2)]
            xn3b = [C2(nc.sbuf_tensor(f"xn3b{p}", [128, 1024], BF16)) for p in range(2)]
            st2 = [C2(nc.sbuf_tensor(f"st2_{p}", [128, 16], F32)) for p in range(2)]
            st3 = [C2(nc.sbuf_tensor(f"st3_{p}", [128, 16], F32)) for p in range(2)]

            mwq = load_weight(WqT, wqT_d)
            mwk = load_weight(WkT, wkT_d)
            mwv = load_weight(WvT, wvT_d)

            mk2 = {}
            for i in range(2):
                mk2[("lx", i)] = dma("sync", A_[i][:, :], x_d[i * 128:(i + 1) * 128, :],
                                     ch=f"lA{i}")
                mk2[("lhi", i)] = dma("sync", hilo[i][0:1, :], sc_hi[i:i + 1, :],
                                      (mst_hi, mst_lo), ch=f"lh{i}")
                mk2[("llo", i)] = dma("sync", hilo[i][1:2, :], sc_lo[i:i + 1, :], ch=f"lh{i}")
                mk2[("ldl", i)] = dma("vector", B_[i][:, :], sc_delta[i, :, :],
                                      (mk[("dst", i)],), ch=f"lB{i}")
                mk2[("lr", i)] = dma("vector", Cc_[i][:, :], sc_res[i, :, :],
                                     (mk[("rst", i)],), ch=f"lC{i}")


            def p2A(i):
                p = i % 2
                q3 = i % 3
                PD = PA if p == 0 else PC
                mg0 = sch.add("gpsimd", lambda e, pp=p: e.tensor_mul(
                    g0b[pp][:, :], B_[pp][:, :], s_B1bc[:, :]),
                    (VD(mk2[("ldl", i)]), TM(mk2.get(("tri", i - 2), 0)),
                     GM(mr2) if i == 0 else None), inc=("g", 1))
                mtr = None
                for n in range(2):
                    mtr = sch.add("tensor", lambda e, nn=n, pp=p, PD_=PD: e.matmul(
                        PD_[:, nn * 512:(nn + 1) * 512], s_tri128[:, :],
                        g0b[pp][:, nn * 512:(nn + 1) * 512], start=True, stop=False),
                        (GM(mg0), VM(mk2.get(("ydb", i - 2), 0))) if n == 0 else ())
                for n in range(2):
                    mtr = sch.add("tensor", lambda e, nn=n, pp=p, PD_=PD: e.matmul(
                        PD_[:, nn * 512:(nn + 1) * 512], s_o2r[:, :],
                        hilo[pp][:, nn * 512:(nn + 1) * 512], start=False, stop=True),
                        (LD(mk2[("llo", i)]),) if n == 0 else (),
                        inc=("t", 1) if n == 1 else None)
                mk2[("tri", i)] = mtr
                mei = sch.add("gpsimd", lambda e, pp=p: e.tensor_mul(
                    D_[pp][:, :], B_[pp][:, :], s_A1bc[:, :]), (), inc=("g", 1))
                mk2[("eIn", i)] = mei
                mev_ = sch.add("scalar", lambda e, pp=p: e.activation(
                    E_[pp][:, :], D_[pp][:, :], AF.Exp), (GM(mei),), inc=("s", 1))
                mydb = sch.add("vector", lambda e, pp=q3, PD_=PD: e.tensor_mul(
                    G_[pp][:, :], PD_[:, :], s_Csbc[:, :]),
                    (TM(mtr), GM(mk2.get(("x2", i - 3), 0))), inc=("v", 1))
                mk2[("ydb", i)] = mydb
                my1 = sch.add("gpsimd", lambda e, pp=p, qq=q3: e.tensor_mul(
                    F_[qq][:, :], E_[pp][:, :], s_R2bc[:, :]),
                    (SM(mev_), VM(mk2.get(("x2a", i - 3), 0))), inc=("g", 1))
                my12 = sch.add("gpsimd", lambda e, pp=p, qq=q3: e.tensor_add(
                    D_[pp][:, :], G_[qq][:, :], F_[qq][:, :]), (VM(mydb),), inc=("g", 1))
                mx1a = sch.add("vector", lambda e, pp=p: e.scalar_tensor_tensor(
                    E_[pp][:, :], Cc_[pp][:, :], s_Dbc[:, :], D_[pp][:, :], ALU.mult, ALU.add),
                    (VD(mk2[("lr", i)]), GM(my12), SM(mev_)), inc=("v", 1))
                mk2[("x1a", i)] = mx1a
                mx1 = sch.add("gpsimd", lambda e, pp=p, qq=q3: e.tensor_add(
                    F_[qq][:, :], E_[pp][:, :], A_[pp][:, :]),
                    (LD(mk2[("lx", i)]), VM(mx1a)), inc=("g", 1))
                mk2[("x1", i)] = mx1
                if dbg:
                    dma("scalar", sc_x1[i, :, :], F_[q3][:, :], (GM(mx1),))

            def p2B(i):
                p = i % 2
                q3 = i % 3
                PTx = PT1 if p == 0 else PT2
                mxn2 = ln(F_[q3], xn2b[p], st2[p], (GM(mk2[("x1", i)]),
                                                   TM(mk2.get(("tp2", i - 2), 0))))
                mtp2 = transposes8(PTx, xn2b[p], (SM(mxn2), SM(mk2.get(("ev2", i - 2), 0))))
                mk2[("tp2", i)] = mtp2
                mev2 = sch.add("scalar", lambda e, pp=p, pt=PTx: e.copy(
                    xn2T[pp][:, :, :], pt[:, :, :]),
                               (TM(mtp2),), inc=("s", 1))
                mk2[("ev2", i)] = mev2
                mf1 = None
                for n in range(2):
                    for k in range(8):
                        mf1 = sch.add("tensor", lambda e, nn=n, kk=k, pp=p: e.matmul(
                            PB[:, nn * 512:(nn + 1) * 512], xn2T[pp][:, kk, :],
                            W_eff[:, kk * 1024 + nn * 512: kk * 1024 + nn * 512 + 512],
                            start=(kk == 0), stop=(kk == 7)),
                            (SM(mev2), VM(mk2.get(("x2a", i - 1), 0)), VM(m_weff_v))
                            if (n == 0 and k == 0) else (),
                            inc=("t", 1) if (n == 1 and k == 7) else None)
                mx2a = sch.add("vector", lambda e, qq=q3: e.tensor_add(
                    G_[qq][:, :], PB[:, :], s_bfbc[:, :]), (TM(mf1),), inc=("v", 1))
                mk2[("x2a", i)] = mx2a
                mx2 = sch.add("gpsimd", lambda e, pp=p, qq=q3: e.tensor_add(
                    H_[pp][:, :], G_[qq][:, :], F_[qq][:, :]),
                    (VM(mx2a), mk2.get(("sx2", i - 2))), inc=("g", 1))
                mk2[("x2", i)] = mx2

            def p2C(i):
                p = i % 2
                mxn3 = ln(H_[p], xn3b[p], st3[p], (GM(mk2[("x2", i)]),
                                                   mk2.get(("sxn3", i - 2))))
                mk2[("sxn3", i)] = dma("scalar", sc_xn3[i, :, :], xn3b[p][:, :], ch=f"sx3{p}")
                if i < NQ:
                    mk2[("sx2", i)] = dma("scalar", sc_x2[i, :, :], H_[p][:, :],
                                          (GM(mk2[("x2", i)]),), ch=f"sx2{p}")

            def p2loads(i):
                p = i % 2
                mk2[("lx", i)] = dma("sync", A_[p][:, :], x_d[i * 128:(i + 1) * 128, :],
                                     (GM(mk2[("x1", i - 2)]),), ch=f"lA{p}")
                mk2[("lhi", i)] = dma("sync", hilo[p][0:1, :], sc_hi[i:i + 1, :],
                                      (TM(mk2[("tri", i - 2)]),), ch=f"lh{p}")
                mk2[("llo", i)] = dma("sync", hilo[p][1:2, :], sc_lo[i:i + 1, :], ch=f"lh{p}")
                mk2[("ldl", i)] = dma("vector", B_[p][:, :], sc_delta[i, :, :],
                                      (GM(mk2[("eIn", i - 2)]), mk[("dst", i)]), ch=f"lB{p}")
                mk2[("lr", i)] = dma("vector", Cc_[p][:, :], sc_res[i, :, :],
                                     (VM(mk2[("x1a", i - 2)]), mk[("rst", i)]), ch=f"lC{p}")

            p2A(0)
            p2A(1)
            p2loads(2)
            p2A(2)
            p2B(0)
            for i in range(NT):
                p2C(i)
                if i + 3 < NT:
                    p2loads(i + 3)
                    p2A(i + 3)
                if i + 1 < NT:
                    p2B(i + 1)

            sch.barrier()
            es_p2.close()
            es_bc.close()
            es_wtmp.close()

        if phases >= 3:
            xg3T = [R(es_prep, nc.sbuf_tensor(f"xg3T{p}", [128, 8, 512], BF16, side="right"))
                    for p in range(2)]
            xq_in = [R(es_prep, nc.sbuf_tensor(f"xq_in{p}", [128, 1024], BF16, side="right"))
                     for p in range(2)]
            qsp = [R(es_qsp, nc.sbuf_tensor(f"qsp{p}", [128, 8, 512], BF16, side="right"))
                   for p in range(2)]

            mk3 = {"tcnt": 0, "vevlast": 0}

            def prep_load(ti, tc):
                mk3[("gld", tc)] = dma("sync", xq_in[tc % 2][:, :], sc_xn3[ti, :, :],
                                       (TM(mk3.get(("gtp", tc - 2), 0)), mk2[("sxn3", ti)]),
                                       ch=f"lxq{tc % 2}")

            def prep_group(tiles, pg):
                last = None
                tiles = list(tiles)
                seq = mk3.get("prepseq", {})
                for j, ti in enumerate(tiles):
                    tc = mk3["tcnt"]
                    tl = tc % 4
                    for tq in (tc, tc + 1):
                        if ("gld", tq) not in mk3 and tq in seq:
                            prep_load(seq[tq], tq)
                    PTp = PT2 if tc % 2 == 0 else PT1
                    mtp = transposes8(PTp, xq_in[tc % 2],
                                      (LD(mk3[("gld", tc)]), SM(mk3.get(("gev", tc - 2), 0)),
                                       SM(mk3.get("pevlast", 0)),
                                       SM(mk3.get("ev4last", 0))))
                    mk3[("gtp", tc)] = mtp
                    last = sch.add("scalar",
                                   lambda e, tt=tl, pp=pg, pt=PTp: e.copy(
                                       xg3T[pp][:, :, tt * 128:(tt + 1) * 128], pt[:, :, :]),
                                   (TM(mtp),), inc=("s", 1))
                    mk3[("gev", tc)] = last
                    mk3["tcnt"] = tc + 1
                    if ("gld", tc + 2) not in mk3 and (tc + 2) in seq:
                        prep_load(seq[tc + 2], tc + 2)
                return last

            mk3["prepseq"] = {j: j for j in range(32)}
            for g in range(8):
                pg = g % 2
                mev = prep_group(range(4 * g, 4 * g + 4), pg)
                for c in range(8):
                    half = c % 2
                    mq = None
                    for k in range(8):
                        mq = sch.add("tensor", lambda e, cc=c, kk=k, hh=half, pp=pg: e.matmul(
                            PA[:, hh * 512:(hh + 1) * 512],
                            WqT[:, kk * 1024 + cc * 128: kk * 1024 + (cc + 1) * 128],
                            xg3T[pp][:, kk, :], start=(kk == 0), stop=(kk == 7)),
                            ((SM(mev), GM(mwq)) if c == 0 else
                             (SM(mk3.get(("qev", g, c - 2), 0)),)) if k == 0 else (),
                            inc=("t", 1) if k == 7 else None)
                    mqe = sch.add("scalar", lambda e, cc=c, hh=half, pp=pg: e.activation(
                        qsp[pp][:, cc, :], PA[:, hh * 512:(hh + 1) * 512],
                        AF.Identity, bias=s_bqT[:, cc:cc + 1], scale=1.0),
                        (TM(mq), mk3.get(("qst", g - 2))), inc=("s", 1))
                    mk3[("qev", g, c)] = mqe
                for t in range(4):
                    mk3[("qst", g)] = dma("scalar", sc_qT[4 * g + t, :, :, :],
                                          qsp[pg][:, :, t * 128:(t + 1) * 128],
                                          (SM(mk3[("qev", g, 7)]),) if t == 0 else (),
                                          ch=f"sq{pg}")
            sch.barrier()
            es_qsp.close()
            es_pqw.close()
            C3 = es_p3.enter_context
            kT_res = C3(nc.sbuf_tensor("kT_res", [128, 8, 2048], BF16))
            v_res = C3(nc.sbuf_tensor("v_res", [128, 16384], BF16))
            qT_t = [C3(nc.sbuf_tensor(f"qT_t{p}", [128, 8, 128], BF16)) for p in range(2)]
            P_bf = [C3(nc.sbuf_tensor(f"P_bf{p}", [128, 2048], BF16)) for p in range(2)]
            pt_sb = [C3(nc.sbuf_tensor(f"pt_sb{p}", [128, 16, 128], BF16))
                     for p in range(2)]
            zs = [C3(nc.sbuf_tensor(f"zs{p}", [128, 1024], F32)) for p in range(2)]
            zl = [C3(nc.sbuf_tensor(f"zl{p}", [128, 1024], F32)) for p in range(3)]
            x2l = [C3(nc.sbuf_tensor(f"x2l{p}", [128, 1024], F32)) for p in range(2)]
            x3b = [C3(nc.sbuf_tensor(f"x3b{p}", [128, 1024], F32)) for p in range(2)]
            xn4b = [C3(nc.sbuf_tensor("xn4b0", [128, 1024], BF16))] * 2
            xn4T = [pt_sb[0][:, 0:8, :], pt_sb[1][:, 0:8, :]]
            st4 = [C3(nc.sbuf_tensor(f"st4_{p}", [128, 16], F32)) for p in range(2)]
            m_run = C3(nc.sbuf_tensor("m_run", [128, 32], F32))
            l_run = C3(nc.sbuf_tensor("l_run", [128, 32], F32))
            cmax = C3(nc.sbuf_tensor("cmax", [128, 8], F32))
            lbuf = C3(nc.sbuf_tensor("lbuf", [128, 8], F32))
            pstat = C3(nc.sbuf_tensor("pstat", [128, 8], F32))

            for r in range(KR):
                lastq = r * NQ - 1
                mk3["prepseq"] = {mk3["tcnt"] + j: 16 * r + j for j in range(16)}
                for gg in range(4):
                    pg = (8 + r * 4 + gg) % 2
                    mev = prep_group(range(16 * r + 4 * gg, 16 * r + 4 * gg + 4), pg)
                    for c in range(8):
                        half = c % 2
                        mq = None
                        for k in range(8):
                            mq = sch.add("tensor", lambda e, cc=c, kk=k, hh=half, pp=pg: e.matmul(
                                PA[:, hh * 512:(hh + 1) * 512],
                                WkT[:, kk * 1024 + cc * 128: kk * 1024 + (cc + 1) * 128],
                                xg3T[pp][:, kk, :], start=(kk == 0), stop=(kk == 7)),
                                ((SM(mev), GM(mwk),
                                  SM(mk3.get(("exp", lastq, 3), 0)),
                                  VM(mk3.get(("rmx", lastq, 3), 0))) if c == 0 else
                                 (SM(mk3.get(("kev", gg, c - 2), 0)),)) if k == 0 else (),
                                inc=("t", 1) if k == 7 else None)
                        mqe = sch.add("scalar", lambda e, cc=c, hh=half, go=gg: e.activation(
                            kT_res[:, cc, go * 512:(go + 1) * 512],
                            PA[:, hh * 512:(hh + 1) * 512],
                            AF.Identity, bias=s_bkT[:, cc:cc + 1], scale=1.0),
                            (TM(mq),), inc=("s", 1))
                        mk3[("kev", gg, c)] = mqe
                    for t in range(4):
                        mv = None
                        for n in range(2):
                            for k in range(8):
                                mv = sch.add("tensor", lambda e, nn=n, kk=k, tt=t, pp=pg: e.matmul(
                                    PB[:, nn * 512:(nn + 1) * 512],
                                    xg3T[pp][:, kk, tt * 128:(tt + 1) * 128],
                                    WvT[:, kk * 1024 + nn * 512: kk * 1024 + nn * 512 + 512],
                                    start=(kk == 0), stop=(kk == 7)),
                                    ((GM(mwv), VM(mk3["vevlast"]),
                                      SM(mk3.get(("exp", lastq, 3), 0)),
                                      VM(mk3.get(("rmx", lastq, 3), 0)))
                                     if (n == 0 and k == 0) else ()),
                                    inc=("t", 1) if (n == 1 and k == 7) else None)
                        mve = sch.add("vector", lambda e, tt=t, go=gg: e.tensor_tensor(
                            v_res[:, (4 * go + tt) * 1024:(4 * go + tt + 1) * 1024],
                            PB[:, :], s_bvbc[:, :], ALU.add), (TM(mv),), inc=("v", 1))
                        mk3["vevlast"] = mve
                mk3["kevlast"] = mk3[("kev", 3, 7)]
                mk3["vevlast2"] = mk3["vevlast"]
                mk3["gevlast"] = mk3[("gev", mk3["tcnt"] - 1)]


                def qS(step):
                    qi = step - r * NQ
                    p = step % 2
                    ac = 3 + 3 * (step % 2)
                    mlq = dma("vector", qT_t[p][:, :, :], sc_qT[qi, :, :, :],
                              (TM(mk3.get(("smm", step - 2), 0)), mk3[("qst", qi // 4)]),
                              ch=f"lqt{p}")
                    if r > 0:
                        mk3[("lz", step)] = dma("vector", zl[step % 3][:, :], sc_z[qi, :, :],
                                                (VM(mk3.get(("zco", step - 3), 0)),
                                                 mk3[("zst", (r - 1) * NQ + qi)]),
                                                ch=f"lzl{step % 3}")
                    if r == KR - 1:
                        mk3[("lx2", step)] = dma("vector", x2l[p][:, :], sc_x2[qi, :, :],
                                                 (VM(mk3.get(("x3", step - 2), 0)),
                                                  mk2[("sx2", qi)]), ch=f"lx2{p}")
                    msmm = None
                    for n in range(4):
                        ps_ = PA if n < 2 else PB
                        off = (n % 2) * 512
                        for c in range(8):
                            msmm = sch.add("tensor", lambda e, nn=n, cc=c, o=ps_, of=off, pp=p:
                                           e.matmul(o[:, of:of + 512], qT_t[pp][:, cc, :],
                                                    kT_res[:, cc, nn * 512:(nn + 1) * 512],
                                                    start=(cc == 0), stop=(cc == 7)),
                                           ((VD(mlq), SM(mk3.get(("exp", step - 1, n), 0)),
                                             VM(mk3.get(("rmx", step - 1, n), 0)),
                                             SM(mk3.get("kevlast", 0)),
                                             VM(mk3.get("vevlast2", 0)))
                                            if c == 0 else ()),
                                           inc=("t", 1) if c == 7 else None)
                        mk3[("smmn", step, n)] = msmm
                        mrx = sch.add("vector", lambda e, nn=n, o=ps_, of=off: e.reduce_max(
                            cmax[:, nn:nn + 1], o[:, of:of + 512], AX), (TM(msmm),),
                            inc=("v", 1))
                        mk3[("rmx", step, n)] = mrx
                    mk3[("smm", step)] = msmm
                    mmx = sch.add("vector", lambda e: e.reduce_max(pstat[:, 0:1], cmax[:, 0:4],
                                                                   AX),
                                  (VM(mk3[("rmx", step, 3)]),), inc=("v", 1))
                    if r > 0:
                        mmx = sch.add("vector", lambda e, q_=qi: e.tensor_max(
                            pstat[:, 0:1], pstat[:, 0:1], m_run[:, q_:q_ + 1]), inc=("v", 1))
                        mdm = sch.add("vector", lambda e, q_=qi: e.tensor_sub(
                            pstat[:, 2:3], m_run[:, q_:q_ + 1], pstat[:, 0:1]), inc=("v", 1))
                        mal = sch.add("scalar", lambda e, a_=ac: e.activation(
                            pstat[:, a_:a_ + 1], pstat[:, 2:3], AF.Exp, scale=ISQ),
                            (VM(mdm),), inc=("s", 1))
                        mk3[("al", step)] = mal
                    mng = sch.add("scalar", lambda e: e.activation(pstat[:, 1:2], pstat[:, 0:1],
                                                                   AF.Identity, scale=-ISQ),
                                  (VM(mmx),), inc=("s", 1))
                    sch.add("scalar", lambda e, q_=qi: e.activation(
                        m_run[:, q_:q_ + 1], pstat[:, 0:1], AF.Identity, scale=1.0),
                        (SM(mng),), inc=("s", 1))
                    for n in range(4):
                        ps_ = PA if n < 2 else PB
                        off = (n % 2) * 512
                        mex = sch.add("scalar", lambda e, nn=n, o=ps_, of=off, pp=p:
                                      e.activation(
                                          P_bf[pp][:, nn * 512:(nn + 1) * 512],
                                          o[:, of:of + 512], AF.Exp,
                                          bias=pstat[:, 1:2], scale=ISQ,
                                          accum_out=lbuf[:, nn:nn + 1]),
                                      (TM(mk3.get(("ptp", step - 2), 0)), SM(mng),
                                       VM(mk3.get(("la", step - 1), 0))) if n == 0 else (),
                                      inc=("s", 1))
                        mk3[("exp", step, n)] = mex
                    mla = sch.add("vector", lambda e: e.reduce_sum(pstat[:, 4:5], lbuf[:, 0:4],
                                                                   AX),
                                  (SM(mk3[("exp", step, 3)]),), inc=("v", 1))
                    mk3[("la", step)] = mla
                    if r == 0:
                        sch.add("scalar", lambda e, q_=qi: e.activation(
                            l_run[:, q_:q_ + 1], pstat[:, 4:5], AF.Identity, scale=1.0),
                            (VM(mla),), inc=("s", 1))
                        mk3[("lr", step)] = ("v", mla)
                    else:
                        sch.add("vector", lambda e, q_=qi, a_=ac: e.tensor_mul(
                            l_run[:, q_:q_ + 1], l_run[:, q_:q_ + 1], pstat[:, a_:a_ + 1]),
                            (SM(mk3[("al", step)]),), inc=("v", 1))
                        mlr = sch.add("vector", lambda e, q_=qi: e.tensor_add(
                            l_run[:, q_:q_ + 1], l_run[:, q_:q_ + 1], pstat[:, 4:5]),
                            (VM(mla),), inc=("v", 1))
                        mk3[("lr", step)] = ("v", mlr)

                def qB(step):
                    p = step % 2
                    mpt = None
                    for j in range(16):
                        pt = PT1 if j < 8 else PT2
                        mpt = sch.add("tensor", lambda e, jj=j, pt_=pt, pp=p: e.transpose(
                            pt_[:, jj % 8, :], P_bf[pp][:, jj * 128:(jj + 1) * 128],
                            s_id[:, :]),
                            ((SM(mk3[("exp", step, 3)]), SM(mk3.get(("pev", step - 1), 0)),
                              SM(mk3.get(("ev4", step - 1), 0)), SM(mk3.get("gevlast", 0)))
                             if j == 0 else ()),
                            inc=("t", 1) if j == 15 else None)
                    mk3[("ptp", step)] = mpt
                    sch.add("scalar", lambda e, pp=p: e.copy(pt_sb[pp][:, 0:8, :],
                                                             PT1[:, :, :]),
                            (TM(mpt),), inc=("s", 1))
                    mpe = sch.add("scalar", lambda e, pp=p: e.copy(pt_sb[pp][:, 8:16, :],
                                                                   PT2[:, :, :]),
                                  inc=("s", 1))
                    mk3[("pev", step)] = mpe

                def qC(step):
                    qi = step - r * NQ
                    p = step % 2
                    ac = 3 + 3 * (step % 2)
                    mpe = mk3[("pev", step)]
                    mpv = None
                    for j in range(16):
                        for n in range(2):
                            mpv = sch.add("tensor", lambda e, jj=j, nn=n, pp=p: e.matmul(
                                PC[:, nn * 512:(nn + 1) * 512], pt_sb[pp][:, jj, :],
                                v_res[:, jj * 1024 + nn * 512: jj * 1024 + nn * 512 + 512],
                                start=(jj == 0), stop=(jj == 15)),
                                ((SM(mpe), VM(mk3.get(("zco", step - 1), 0)),
                                  VM(mk3.get(("mo1", step - 1), 0)))
                                 if (j == 0 and n == 0) else ()),
                                inc=("t", 1) if (j == 15 and n == 1) else None)
                    if r == 0:
                        mz = sch.add("vector", lambda e, pp=p: e.tensor_copy(zs[pp][:, :],
                                                                             PC[:, :]),
                                     (TM(mpv), mk3.get(("zst", step - 2))), inc=("v", 1))
                    else:
                        mz = sch.add("vector", lambda e, pp=p, z3=step % 3, a_=ac:
                                     e.scalar_tensor_tensor(
                            zs[pp][:, :], zl[z3][:, :], pstat[:, a_:a_ + 1], PC[:, :],
                            ALU.mult, ALU.add),
                            (TM(mpv), mk3.get(("lz", step)), SM(mk3[("al", step)]),
                             mk3.get(("zst", step - 2)),
                             GM(mk3.get(("moo", step - 2), 0))),
                            inc=("v", 1))
                    mk3[("zco", step)] = mz
                    if r < KR - 1:
                        mk3[("zst", step)] = dma("scalar", sc_z[qi, :, :], zs[p][:, :],
                                                 (VM(mz),), ch=f"sz{p}")
                    else:
                        mli = sch.add("vector", lambda e, q_=qi: e.reciprocal(
                            pstat[:, 5:6], l_run[:, q_:q_ + 1]),
                            (mk3[("lr", step)],), inc=("v", 1))
                        mx3 = sch.add("vector", lambda e, pp=p: e.scalar_tensor_tensor(
                            x3b[pp][:, :], zs[pp][:, :], pstat[:, 5:6], x2l[pp][:, :],
                            ALU.mult, ALU.add),
                            (VM(mli), mk3.get(("lx2", step)),
                             GM(mk3.get(("moo", step - 2), 0))), inc=("v", 1))
                        mk3[("x3", step)] = mx3
                        if dbg:
                            mk3[("zdbg", step)] = dma("scalar", sc_z[qi, :, :], zs[p][:, :],
                                                      (VM(mx3),), ch=f"sz{p}")
                        mxn4 = ln(x3b[p], xn4b[p], st4[p],
                                  (VM(mx3), TM(mk3.get(("tp4", step - 1), 0))))
                        mtp4 = transposes8(PT1, xn4b[p], (SM(mxn4),))
                        mk3[("tp4", step)] = mtp4
                        mev4 = sch.add("scalar", lambda e, pp=p: e.copy(xn4T[pp],
                                                                        PT1[:, :, :]),
                                       (TM(mtp4),), inc=("s", 1))
                        mk3[("ev4", step)] = mev4
                        mf2 = None
                        for n in range(2):
                            for k in range(8):
                                mf2 = sch.add("tensor", lambda e, nn=n, kk=k, pp=p: e.matmul(
                                    PC[:, nn * 512:(nn + 1) * 512], xn4T[pp].tensor[:, kk, :],
                                    W_eff[:, kk * 1024 + nn * 512: kk * 1024 + nn * 512 + 512],
                                    start=(kk == 0), stop=(kk == 7)),
                                    (SM(mev4), VM(mk3[("zco", step)])) if (n == 0 and k == 0)
                                    else (),
                                    inc=("t", 1) if (n == 1 and k == 7) else None)
                        mo1 = sch.add("vector", lambda e, pp=p: e.tensor_add(
                            zs[pp][:, :], PC[:, :], s_bfbc[:, :]),
                            (TM(mf2), GM(mk3.get(("moo", step - 2), 0)),
                             mk3.get(("zdbg", step))), inc=("v", 1))
                        mk3[("mo1", step)] = mo1
                        moo = sch.add("gpsimd", lambda e, pp=p: e.tensor_add(
                            s_out[pp][:, :], zs[pp][:, :], x3b[pp][:, :]),
                            (VM(mo1), mk3.get(("ost", step - 1))), inc=("g", 1))
                        mk3[("moo", step)] = moo
                        mk3[("ost", step)] = dma("scalar", out_d[qi * 128:(qi + 1) * 128, :],
                                                 s_out[p][:, :], (GM(moo),), ch=f"so{p}")

                qS(r * NQ)
                qS(r * NQ + 1)
                qB(r * NQ)
                if r < KR - 1:
                    for qi in range(NQ):
                        step = r * NQ + qi
                        if qi + 2 < NQ:
                            qS(step + 2)
                        if qi + 1 < NQ:
                            qB(step + 1)
                        qC(step)
                else:
                    for qi in range(NQ):
                        step = r * NQ + qi
                        qC(step)
                        if qi + 2 < NQ:
                            qS(step + 2)
                        if qi + 1 < NQ:
                            qB(step + 1)
                mk3["pevlast"] = mk3[("pev", r * NQ + NQ - 1)]
                mk3["ev4last"] = mk3.get(("ev4", r * NQ + NQ - 1), 0)

        pass
    except _Cut:
        pass

    if dbg and phases >= 3:
        dma("scalar", dbg_ml[:, 0:32], m_run[:, :], (VM(sch.count("v")),))
        dma("scalar", dbg_ml[:, 32:64], l_run[:, :], (VM(sch.count("v")),))
    _fw = [(snm, sch.count(snm)) for snm in list(sch.cnt.keys())
           if snm not in ("t", "v", "s", "g")]
    _fw += [TM(sch.count("t")), VM(sch.count("v")), GM(sch.count("g"))]
    sch.add("scalar", lambda e: e.activation(s_sqscr[:, 0:1], s_sqscr[:, 0:1], AF.Identity,
                                             scale=1.0), tuple(_fw), inc=("s", 1))

    for n in sorted(set(list(sch.cnt.keys()) + ["t", "v", "s", "g", "ld", "st"])):
        sems[n] = C(nc.semaphore(n))

    def run_prog(name):
        def f(eng):
            for item in sch.prog[name]:
                if item[0] == "w":
                    eng.wait_ge(sems[item[1]], item[2])
                else:
                    _, fn, inc = item
                    ins = fn(eng)
                    if inc is not None:
                        ins.then_inc(sems[inc[0]], inc[1])
        return f

    with nc.Block() as blk:
        blk.tensor(run_prog("tensor"))
        blk.vector(run_prog("vector"))
        blk.scalar(run_prog("scalar"))
        blk.sync(run_prog("sync"))
        blk.gpsimd(run_prog("gpsimd"))

    for es in (es_p3, es_p2, es_p1t, es_p1w, es_pqw, es_qsp, es_prep,
               es_p2a, es_fold, es_bc, es_wtmp, es_p23w, es_glob):
        es.close()
    return nc


def make_inmaps(inputs):
    f32 = np.float32
    bf = ml_dtypes.bfloat16
    x = np.asarray(inputs["L0_attn_a_x"], f32)
    Wp = np.asarray(inputs["L0_attn_a_W_proj"], f32)
    A = np.asarray(inputs["L0_attn_a_A"], f32).reshape(1, 1024)
    Bp = np.asarray(inputs["L0_attn_a_Bp"], f32).reshape(1, 1024)
    Dsc = np.asarray(inputs["L0_attn_a_D"], f32).reshape(1)
    Wd = np.asarray(inputs["L0_attn_a_W_delta"], f32)
    bd = np.asarray(inputs["L0_attn_a_b_delta"], f32).reshape(1, 1024)
    Wq = np.asarray(inputs["L0_attn_b_Wq"], f32)
    bq = np.asarray(inputs["L0_attn_b_bq"], f32)
    Wk = np.asarray(inputs["L0_attn_b_Wk"], f32)
    bk = np.asarray(inputs["L0_attn_b_bk"], f32)
    Wv = np.asarray(inputs["L0_attn_b_Wv"], f32)
    bv = np.asarray(inputs["L0_attn_b_bv"], f32).reshape(1, 1024)
    W = np.asarray(inputs["L0_ffn_W"], f32)
    b = np.asarray(inputs["L0_ffn_b"], f32).reshape(1, 1024)
    ld_ = np.asarray(inputs["L0_ffn_lora_down"], f32)
    lu_ = np.asarray(inputs["L0_ffn_lora_up"], f32)
    da_ = np.asarray(inputs["L0_ffn_down_aux"], f32)
    ua_ = np.asarray(inputs["L0_ffn_up_aux"], f32)

    ct = np.ascontiguousarray

    def _pad128(a):
        out = np.zeros((128,) + a.shape[1:], f32)
        out[:a.shape[0]] = a
        return out

    rows = np.zeros((8, 1024), f32)
    rows[0] = A
    rows[1] = Bp
    rows[2] = bd
    rows[3] = b
    rows[4] = bv
    rows[5, 0] = Dsc[0]
    emsk = np.zeros((128, 256), bf)
    emsk[:, 65] = 1
    shared = dict(
        wp0=ct(Wp[0:1024, :]), wdT=ct(Wd.T), wcT=ct(Wp[2048:3072, :].T),
        wrT=ct(Wp[3072:4096, :].T), wT=ct(W.T),
        uaT=_pad128(ua_.T), da=_pad128(da_), ldT=_pad128(ld_.T), luT=_pad128(lu_.T),
        wqT=ct(Wq.T), wkT=ct(Wk.T), wvT=ct(Wv.T),
        rows=rows, bqT=ct(bq.reshape(8, 128).T), bkT=ct(bk.reshape(8, 128).T),
        ident=np.eye(128, dtype=bf), tri128=np.triu(np.ones((128, 128))).astype(bf),
        tri64x=np.triu(np.ones((64, 64), f32), 1), emsk=emsk,
        ones1r=np.ones((1, 128), f32), ones64=np.ones((1, 64), f32),
        ones2r=np.ones((2, 128), bf),
    )
    in_maps = []
    for c in range(8):
        bidx, h = c // 2, c % 2
        xl = x[bidx] if h == 0 else np.concatenate([x[bidx, 4096:], x[bidx, :4096]], axis=0)
        m1 = np.zeros((64, 1), f32)
        m2n = np.zeros((64, 1), f32)
        if h == 1:
            m1[0:32] = 1.0
            m2n[32:64] = -1.0
        im = dict(shared)
        im["x"] = ct(xl)
        im["m1c"] = m1
        im["m2n"] = m2n
        in_maps.append(im)
    return in_maps


def kernel(**inputs):
    in_maps = make_inmaps(inputs)
    nc = build(phases=3, dbg=False)
    res = run_bass_kernel_spmd(nc, in_maps, core_ids=list(range(8)))
    out = np.empty((4, 8192, 1024), np.float32)
    for c in range(8):
        bidx, h = c // 2, c % 2
        out[bidx, h * 4096:(h + 1) * 4096] = np.asarray(res.results[c]["out"], np.float32)
    return out


# version 11
# speedup vs baseline: 1.1065x; 1.0678x over previous
import numpy as np
import ml_dtypes
from contextlib import ExitStack

import concourse.bass as bass
import concourse.mybir as mybir
from concourse.bass_utils import run_bass_kernel_spmd

F32 = mybir.dt.float32
BF16 = mybir.dt.bfloat16
AX = mybir.AxisListType.X
AF = mybir.ActivationFunctionType
ALU = mybir.AluOpType

NT = 64
NQ = 32
KR = 4
D = 1024
EPS = 1e-5
ISQ = 1.0 / 32.0

ENGS = ("tensor", "vector", "scalar", "sync", "gpsimd")


class Sched:
    def __init__(self):
        self.prog = {e: [] for e in ENGS}
        self.cnt = {}
        self.lastwait = {e: {} for e in ENGS}

    CHAIN = {"vector": "v", "scalar": "s", "gpsimd": "g"}

    def add(self, eng, fn, waits=(), inc=None):
        sem0 = self.CHAIN.get(eng)
        if sem0 is not None:
            waits = ((sem0, self.cnt.get(sem0, 0)),) + tuple(waits)
            if inc is None:
                inc = (sem0, 1)
        for w in waits:
            if w is None:
                continue
            sem, val = w
            if isinstance(val, tuple):
                sem, val = val
            if val is None or val <= 0:
                continue
            if sem in ("ld", "st"):
                val = self.cnt.get(sem, 0)
            lw = self.lastwait[eng]
            if lw.get(sem, 0) >= val:
                continue
            lw[sem] = val
            self.prog[eng].append(("w", sem, val))
        self.prog[eng].append(("c", fn, inc))
        if inc is not None:
            sem, amt = inc
            self.cnt[sem] = self.cnt.get(sem, 0) + amt
            return self.cnt[sem]
        return None

    def count(self, sem):
        return self.cnt.get(sem, 0)

    def barrier(self):
        snap = dict(self.cnt)
        for e in ENGS:
            for sem, val in snap.items():
                if val > 0 and self.lastwait[e].get(sem, 0) < val:
                    self.lastwait[e][sem] = val
                    self.prog[e].append(("w", sem, val))


class _Cut(Exception):
    pass


def build(phases=3, dbg=False, cut=None):
    nc = bass.Bass("TRN2", target_bir_lowering=False)

    dI = lambda n, s, dt=F32: nc.dram_tensor(n, s, dt, kind="ExternalInput").ap()
    x_d = dI("x", [8192, 1024])
    wp0_d = dI("wp0", [1024, 1024])
    wdT_d = dI("wdT", [1024, 1024])
    wcT_d = dI("wcT", [1024, 1024])
    wrT_d = dI("wrT", [1024, 1024])
    wT_d = dI("wT", [1024, 1024])
    uaT_d = dI("uaT", [128, 1024])
    da_d = dI("da", [128, 1024])
    ldT_d = dI("ldT", [128, 4])
    luT_d = dI("luT", [128, 50])
    wqT_d = dI("wqT", [1024, 1024])
    wkT_d = dI("wkT", [1024, 1024])
    wvT_d = dI("wvT", [1024, 1024])
    rows_d = dI("rows", [8, 1024])
    bqT_d = dI("bqT", [128, 8])
    bkT_d = dI("bkT", [128, 8])
    m1_d = dI("m1c", [64, 1])
    m2n_d = dI("m2n", [64, 1])
    ident_d = dI("ident", [128, 128], BF16)
    tri128_d = dI("tri128", [128, 128], BF16)
    tri64_d = dI("tri64x", [64, 64])
    emsk_d = dI("emsk", [128, 256], BF16)
    o1r_d = dI("ones1r", [1, 128])
    o64_d = dI("ones64", [1, 64])
    o2r_d = dI("ones2r", [2, 128], BF16)

    sk = "ExternalOutput" if dbg else "Internal"
    dS = lambda n, s, dt=F32: nc.dram_tensor(n, s, dt, kind=sk).ap()
    sc_delta = dS("sc_delta", [NT, 128, 1024])
    sc_res = dS("sc_res", [NT, 128, 1024])
    sc_hi = dS("sc_hi", [64, 1024], BF16)
    sc_lo = dS("sc_lo", [64, 1024], BF16)
    sc_xn3 = dS("sc_xn3", [NT, 128, 1024], BF16)
    sc_x2 = dS("sc_x2", [NQ, 128, 1024])
    sc_qT = dS("sc_qT", [NQ, 128, 8, 128], BF16)
    sc_z = dS("sc_z", [NQ, 128, 1024])
    if dbg:
        dbg_tbl = nc.dram_tensor("dbg_tbl", [128, 1024], F32, kind="ExternalOutput").ap()
        dbg_bc = nc.dram_tensor("dbg_bc", [128, 4096], F32, kind="ExternalOutput").ap()
        sc_x1 = nc.dram_tensor("sc_x1", [NT, 128, 1024], F32, kind="ExternalOutput").ap()
        dbg_ml = nc.dram_tensor("dbg_ml", [128, 64], F32, kind="ExternalOutput").ap()
    out_d = nc.dram_tensor("out", [4096, 1024], F32, kind="ExternalOutput").ap()

    sch = Sched()

    def _ck(n):
        if cut is not None and n >= cut:
            raise _Cut()

    LD = lambda m: m if isinstance(m, tuple) else ("ld", m)
    VD = LD
    ST = lambda m: m if isinstance(m, tuple) else ("st", m)
    TM = lambda m: ("t", m)
    VM = lambda m: ("v", m)
    SM = lambda m: ("s", m)
    GM = lambda m: ("g", m)

    (es_glob, es_p23w, es_pqw, es_p1w, es_p1t, es_p2, es_p3,
     es_bc, es_wtmp, es_fold, es_p2a, es_prep, es_qsp) = (ExitStack() for _ in range(13))
    C = es_glob.enter_context
    R = lambda es, t: es.enter_context(t)

    s_id = C(nc.sbuf_tensor("s_id", [128, 128], BF16))
    s_tri128 = C(nc.sbuf_tensor("s_tri128", [128, 128], BF16))
    s_tri64 = C(nc.sbuf_tensor("s_tri64", [64, 64], F32))
    s_emsk = C(nc.sbuf_tensor("s_emsk", [128, 256], BF16))
    s_o1r = C(nc.sbuf_tensor("s_o1r", [1, 128], F32))
    s_o64 = C(nc.sbuf_tensor("s_o64", [1, 64], F32))
    s_o2r = C(nc.sbuf_tensor("s_o2r", [2, 128], BF16))
    s_bqT = C(nc.sbuf_tensor("s_bqT", [128, 8], F32))
    s_bkT = C(nc.sbuf_tensor("s_bkT", [128, 8], F32))
    s_m1 = C(nc.sbuf_tensor("s_m1", [64, 1], F32))
    s_m2n = C(nc.sbuf_tensor("s_m2n", [64, 1], F32))
    s_Dbc = C(nc.sbuf_tensor("s_Dbc", [128, 1], F32))
    s_bfbc = C(nc.sbuf_tensor("s_bfbc", [128, 1024], F32))
    s_bvbc = R(es_prep, nc.sbuf_tensor("s_bvbc", [128, 1024], F32, side="right"))
    W_eff = C(nc.sbuf_tensor("W_eff", [128, 8192], BF16))
    s_wtmp = [R(es_wtmp, nc.sbuf_tensor(f"s_wtmp{p}", [128, 1024], F32, side="right"))
              for p in range(2)]
    s_sqscr = C(nc.sbuf_tensor("s_sqscr", [128, 1024], BF16))
    s_out = [C(nc.sbuf_tensor("s_out0", [128, 1024], F32))] * 2

    PA = C(nc.psum_tensor("PA", [128, 1024], F32))
    PB = C(nc.psum_tensor("PB", [128, 1024], F32))
    PC = C(nc.psum_tensor("PC", [128, 1024], F32))
    PT1 = C(nc.psum_tensor("PT1", [128, 8, 128], BF16))
    PT2 = C(nc.psum_tensor("PT2", [128, 8, 128], BF16))

    sems = {}

    def dma(q, outap, inap, waits=(), ch=None):
        if q == "vector":
            q = "sync"
        semn = ch if ch is not None else {"sync": "ld", "scalar": "st"}[q]
        if semn in ("ld", "st"):
            waits = tuple(waits) + ((semn, 1),)
        m = sch.add(q, lambda e, o=outap, i=inap: e.dma_start(out=o, in_=i),
                    waits, inc=(semn, 16))
        return (semn, m)

    def transposes8(pt, src_bf, waits=()):
        m = None
        for c in range(8):
            m = sch.add("tensor",
                        lambda e, o=pt[:, c, :], i_=src_bf[:, c * 128:(c + 1) * 128]:
                        e.transpose(o, i_, s_id[:, :]),
                        waits if c == 0 else (), inc=("t", 1) if c == 7 else None)
        return m

    def ln(xin, xn_bf, stats, w_x):
        mv = sch.add("vector", lambda e: e.reduce_sum(stats[:, 0:1], xin[:, :], AX), w_x,
                     inc=("v", 1))
        msq = sch.add("scalar", lambda e: e.activation(s_sqscr[:, :], xin[:, :], AF.Square,
                                                       accum_out=stats[:, 1:2]), w_x,
                      inc=("s", 1))
        mng = sch.add("scalar", lambda e: e.activation(stats[:, 2:3], stats[:, 0:1], AF.Identity,
                                                       scale=-1.0 / 1024.0), (VM(mv),), inc=("s", 1))
        mvh = sch.add("scalar", lambda e: e.activation(stats[:, 3:4], stats[:, 1:2], AF.Identity,
                                                       scale=1.0 / 1024.0), (SM(msq),), inc=("s", 1))
        ms = sch.add("scalar", lambda e: e.activation(stats[:, 4:5], stats[:, 2:3], AF.Square),
                     (SM(mng),), inc=("s", 1))
        mv2 = sch.add("vector", lambda e: e.tensor_sub(stats[:, 5:6], stats[:, 3:4],
                                                       stats[:, 4:5]), (SM(ms), SM(mvh)), inc=("v", 1))
        ms2 = sch.add("scalar", lambda e: e.activation(stats[:, 6:7], stats[:, 5:6], AF.Copy,
                                                       bias=EPS), (VM(mv2),), inc=("s", 1))
        mv3 = sch.add("vector", lambda e: e.reciprocal(stats[:, 7:8], stats[:, 6:7]),
                      (SM(ms2),), inc=("v", 1))
        ms3 = sch.add("scalar", lambda e: e.activation(stats[:, 8:9], stats[:, 7:8], AF.Sqrt),
                      (VM(mv3),), inc=("s", 1))
        mv4 = sch.add("vector", lambda e: e.tensor_mul(stats[:, 9:10], stats[:, 2:3],
                                                       stats[:, 8:9]), (SM(ms3),), inc=("v", 1))
        mxn = sch.add("scalar", lambda e: e.activation(xn_bf[:, :], xin[:, :], AF.Identity,
                                                       bias=stats[:, 9:10], scale=stats[:, 8:9]),
                      (VM(mv4), SM(ms3)), inc=("s", 1))
        return mxn

    wtmp_prev = {0: None, 1: None}

    def load_weight(dst_bf, src_d):
        last = None
        for k in range(8):
            p = k % 2
            ml = dma("sync", s_wtmp[p][:, :], src_d[k * 128:(k + 1) * 128, :],
                     (wtmp_prev[p],), ch=f"lw{p}")
            mg = sch.add("gpsimd",
                         lambda e, kk=k, pp=p: e.tensor_copy(
                             dst_bf[:, kk * 1024:(kk + 1) * 1024], s_wtmp[pp][:, :]),
                         (LD(ml),), inc=("g", 1))
            wtmp_prev[p] = GM(mg)
            last = mg
        return last

    WkT = R(es_p23w, nc.sbuf_tensor("WkT", [128, 8192], BF16))
    WvT = R(es_p23w, nc.sbuf_tensor("WvT", [128, 8192], BF16))
    WqT = R(es_pqw, nc.sbuf_tensor("WqT", [128, 8192], BF16))
    W_dd = R(es_p1w, nc.sbuf_tensor("W_dd", [128, 8192], BF16))
    WcT = R(es_p1w, nc.sbuf_tensor("WcT", [128, 8192], BF16))
    WrT = R(es_p1w, nc.sbuf_tensor("WrT", [128, 8192], BF16))
    Rb = lambda t: R(es_bc, t)
    s_A1bc = Rb(nc.sbuf_tensor("s_A1bc", [128, 1024], F32, side="right"))
    s_B1bc = Rb(nc.sbuf_tensor("s_B1bc", [128, 1024], F32, side="right"))
    s_bdbc = Rb(nc.sbuf_tensor("s_bdbc", [128, 1024], F32, side="right"))
    s_Csbc = Rb(nc.sbuf_tensor("s_Csbc", [128, 1024], F32, side="right"))
    s_R2bc = Rb(nc.sbuf_tensor("s_R2bc", [128, 1024], F32, side="right"))
    Cf = lambda t: R(es_fold, t)
    s_row1 = Cf(nc.sbuf_tensor("s_row1", [1, 1024], F32, side="right"))
    Wp0 = Cf(nc.sbuf_tensor("Wp0", [128, 8192], BF16, side="right"))
    WdT = Cf(nc.sbuf_tensor("WdT", [128, 8192], BF16, side="right"))
    s_ldT = Cf(nc.sbuf_tensor("s_ldT", [128, 4], BF16, side="right"))
    s_luT = Cf(nc.sbuf_tensor("s_luT", [128, 50], BF16, side="right"))
    s_uaT = Cf(nc.sbuf_tensor("s_uaT", [128, 1024], BF16, side="right"))
    s_da = Cf(nc.sbuf_tensor("s_da", [128, 1024], BF16, side="right"))
    s_t1 = Cf(nc.sbuf_tensor("s_t1", [128, 1024], BF16, side="right"))
    s_t2 = Cf(nc.sbuf_tensor("s_t2", [128, 1024], BF16, side="right"))
    s_lt = s_wtmp[0]

    try:
        for outap, inap in ((s_id[:, :], ident_d[:, :]), (s_tri128[:, :], tri128_d[:, :]),
                            (s_tri64[:, :], tri64_d[:, :]), (s_emsk[:, :], emsk_d[:, :]),
                            (s_o1r[:, :], o1r_d[:, :]), (s_o64[:, :], o64_d[:, :]),
                            (s_o2r[:, :], o2r_d[:, :]),
                            (s_bqT[:, :], bqT_d[:, :]), (s_bkT[:, :], bkT_d[:, :]),
                            (s_m1[:, :], m1_d[:, :]), (s_m2n[:, :], m2n_d[:, :])):
            m_startld = dma("sync", outap, inap)
        _ck(1)

        ml = dma("sync", s_lt[:, 0:4], ldT_d[:, :], (wtmp_prev[0],), ch="lw0")
        mg1 = sch.add("gpsimd", lambda e: e.tensor_copy(s_ldT[:, :], s_lt[:, 0:4]),
                      (LD(ml),), inc=("g", 1))
        ml = dma("sync", s_lt[:, 0:50], luT_d[:, :], (GM(mg1),), ch="lw0")
        mg2 = sch.add("gpsimd", lambda e: e.tensor_copy(s_luT[:, :], s_lt[:, 0:50]),
                      (LD(ml),), inc=("g", 1))
        ml = dma("sync", s_lt[:, :], uaT_d[:, :], (GM(mg2),), ch="lw0")
        mg3 = sch.add("gpsimd", lambda e: e.tensor_copy(s_uaT[:, :], s_lt[:, :]),
                      (LD(ml),), inc=("g", 1))
        ml = dma("sync", s_lt[:, :], da_d[:, :], (GM(mg3),), ch="lw0")
        mg4 = sch.add("gpsimd", lambda e: e.tensor_copy(s_da[:, :], s_lt[:, :]),
                      (LD(ml),), inc=("g", 1))
        wtmp_prev[0] = GM(mg4)
        mz1 = sch.add("gpsimd", lambda e: e.memzero(s_t1[:, :]), inc=("g", 1))
        mz2 = sch.add("gpsimd", lambda e: e.memzero(s_t2[:, :]), inc=("g", 1))
        _ck(2)

        mwp0 = load_weight(Wp0, wp0_d)
        mwdt = load_weight(WdT, wdT_d)
        _ck(3)

        mt1 = None
        for n in range(2):
            mt1 = sch.add("tensor", lambda e, nn=n: e.matmul(
                PA[0:4, nn * 512:(nn + 1) * 512], s_ldT[:, :], s_da[:, nn * 512:(nn + 1) * 512],
                start=True, stop=True), (GM(mg4), GM(mg1)) if n == 0 else (),
                inc=("t", 1) if n == 1 else None)
        ms = sch.add("scalar", lambda e: e.copy(s_t1[0:4, :], PA[0:4, :]),
                     (TM(mt1), GM(mz1)), inc=("s", 1))
        mt2 = None
        for n in range(2):
            mt2 = sch.add("tensor", lambda e, nn=n: e.matmul(
                PB[0:50, nn * 512:(nn + 1) * 512], s_luT[:, :], s_t1[:, nn * 512:(nn + 1) * 512],
                start=True, stop=True), (SM(ms), GM(mg2)) if n == 0 else (),
                inc=("t", 1) if n == 1 else None)
        ms2 = sch.add("scalar", lambda e: e.copy(s_t2[0:50, :], PB[0:50, :]),
                      (TM(mt2), GM(mz2)), inc=("s", 1))
        _ck(4)
        m_weff_v = None
        for mch in range(8):
            p = mch % 2
            wld = dma("sync", s_wtmp[p][:, :], wT_d[mch * 128:(mch + 1) * 128, :],
                      (wtmp_prev[p],), ch=f"lw{p}")
            mml = None
            for n in range(2):
                mml = sch.add("tensor", lambda e, nn=n, m_=mch: e.matmul(
                    PA[:, nn * 512:(nn + 1) * 512], s_t2[:, m_ * 128:(m_ + 1) * 128],
                    s_uaT[:, nn * 512:(nn + 1) * 512], start=True, stop=True),
                    ((SM(ms2), GM(mg3)) if mch == 0 else (VM(m_weff_v),))
                    if n == 0 else (),
                    inc=("t", 1) if n == 1 else None)
            m_weff_v = sch.add("vector", lambda e, m_=mch, pp=p: e.tensor_add(
                W_eff[:, m_ * 1024:(m_ + 1) * 1024], PA[:, :], s_wtmp[pp][:, :]),
                (TM(mml), LD(wld)), inc=("v", 1))
            wtmp_prev[p] = VM(m_weff_v)

        _ck(5)
        m_wdd_s = None
        for mch in range(8):
            mml = None
            for n in range(2):
                for k in range(8):
                    mml = sch.add("tensor", lambda e, nn=n, kk=k, m_=mch: e.matmul(
                        PB[:, nn * 512:(nn + 1) * 512],
                        Wp0[:, kk * 1024 + m_ * 128: kk * 1024 + (m_ + 1) * 128],
                        WdT[:, kk * 1024 + nn * 512: kk * 1024 + nn * 512 + 512],
                        start=(kk == 0), stop=(kk == 7)),
                        ((GM(mwp0), GM(mwdt), SM(ms2)) if mch == 0 else (SM(m_wdd_s),))
                        if (n == 0 and k == 0) else (),
                        inc=("t", 1) if (n == 1 and k == 7) else None)
            m_wdd_s = sch.add("scalar", lambda e, m_=mch: e.copy(
                W_dd[:, m_ * 1024:(m_ + 1) * 1024], PB[:, :]), (TM(mml),), inc=("s", 1))

        _ck(6)
        mwc = load_weight(WcT, wcT_d)
        mwr = load_weight(WrT, wrT_d)

        _ck(7)
        m_prevbc = 0

        def bcast_row(dst, rowidx, dcol=False):
            nonlocal_marks = {}
            mrl = dma("sync", s_row1[:, :], rows_d[rowidx:rowidx + 1, :],
                      (TM(m_prevbc), LD(m_startld)))
            mb = None
            for n in range(2):
                mb = sch.add("tensor", lambda e, nn=n: e.matmul(
                    PC[:, nn * 512:(nn + 1) * 512], s_o1r[:, :],
                    s_row1[0:1, nn * 512:(nn + 1) * 512], start=True, stop=True),
                    (LD(mrl), VM(sch.count("v"))) if n == 0 else (),
                    inc=("t", 1) if n == 1 else None)
            if dcol:
                mv = sch.add("vector", lambda e: e.tensor_copy(s_Dbc[:, :], PC[:, 0:1]),
                             (TM(mb),), inc=("v", 1))
            else:
                mv = sch.add("vector", lambda e: e.tensor_copy(dst[:, :], PC[:, :]), (TM(mb),),
                             inc=("v", 1))
            return mb

        m_prevbc = bcast_row(s_A1bc, 0)
        m_prevbc = bcast_row(s_B1bc, 1)
        m_prevbc = bcast_row(s_bdbc, 2)
        m_prevbc = bcast_row(s_bfbc, 3)
        m_prevbc = bcast_row(s_bvbc, 4)
        m_prevbc = bcast_row(None, 5, dcol=True)

        _ck(8)
        sch.barrier()
        es_fold.close()

        if phases >= 1:
            Cp = es_p1t.enter_context
            xb = [Cp(nc.sbuf_tensor(f"xb{p}", [128, 1024], F32)) for p in range(3)]
            xn_bf = [Cp(nc.sbuf_tensor(f"xn_bf{p}", [128, 1024], BF16)) for p in range(3)]
            xnT = [Cp(nc.sbuf_tensor(f"xnT{p}", [128, 8, 128], BF16)) for p in range(3)]
            delta_f = [Cp(nc.sbuf_tensor(f"delta_f{p}", [128, 1024], F32)) for p in range(2)]
            e1_f = [Cp(nc.sbuf_tensor("e1_f0", [128, 1024], F32))] * 2
            res_f = [Cp(nc.sbuf_tensor(f"res_f{p}", [128, 1024], F32)) for p in range(2)]
            cs_bf = [Cp(nc.sbuf_tensor("cs_bf0", [128, 1024], BF16))] * 2
            g0_bf = [Cp(nc.sbuf_tensor("g0_bf0", [128, 1024], BF16))] * 2
            m_bf = [Cp(nc.sbuf_tensor("m_bf0", [128, 1024], BF16))] * 2
            st1 = [Cp(nc.sbuf_tensor(f"st1_{p}", [128, 16], F32)) for p in range(3)]


            mk = {}

            def p1Aln(i):
                p = i % 3
                q3 = i % 3
                w_x = [LD(mk[("lx", i)]), SM(mk.get(("xn_c", i - 3))),
                       GM(mk.get(("m", i - 3))), TM(mk.get(("tp", i - 3)))]
                if i == 0:
                    w_x += [SM(m_wdd_s), GM(mwc), GM(mwr), VM(sch.count("v"))]
                mxn = ln(xb[p], xn_bf[q3], st1[q3], tuple(w_x))
                mk[("xn_c", i)] = mxn

            def p1Atp(i):
                q3 = i % 3
                PTt = PT1 if i % 2 == 0 else PT2
                mtp = transposes8(PTt, xn_bf[q3],
                                  (SM(mk[("xn_c", i)]), VM(mk.get(("xnTev", i - 2), 0))))
                mk[("tp", i)] = mtp
                mev = sch.add("vector", lambda e, qq=q3, pt=PTt: e.tensor_copy(
                    xnT[qq][:, :, :], pt[:, :, :]), (TM(mtp),), inc=("v", 1))
                mk[("xnTev", i)] = mev

            def p1B(i):
                p = i % 2
                q3 = i % 3
                mev = mk[("xnTev", i)]
                mdp = None
                for n in range(2):
                    for k in range(8):
                        mdp = sch.add("tensor", lambda e, nn=n, kk=k, qq=q3: e.matmul(
                            PA[:, nn * 512:(nn + 1) * 512], xnT[qq][:, kk, :],
                            W_dd[:, kk * 1024 + nn * 512: kk * 1024 + nn * 512 + 512],
                            start=(kk == 0), stop=(kk == 7)),
                            (VM(mev), VM(mk.get(("resev", i - 1), 0)))
                            if (n == 0 and k == 0) else (),
                            inc=("t", 1) if (n == 1 and k == 7) else None)
                mba = sch.add("vector", lambda e, pp=p: e.tensor_add(
                    delta_f[pp][:, :], PA[:, :], s_bdbc[:, :]),
                    (TM(mdp), GM(mk.get(("g0", i - 2), 0)), mk.get(("dst", i - 2))),
                    inc=("v", 1))
                mcs = None
                for n in range(2):
                    for k in range(8):
                        mcs = sch.add("tensor", lambda e, nn=n, kk=k, qq=q3: e.matmul(
                            PB[:, nn * 512:(nn + 1) * 512], xnT[qq][:, kk, :],
                            WcT[:, kk * 1024 + nn * 512: kk * 1024 + nn * 512 + 512],
                            start=(kk == 0), stop=(kk == 7)),
                            (SM(mk.get(("csbf", i - 1), 0)),) if (n == 0 and k == 0) else (),
                            inc=("t", 1) if (n == 1 and k == 7) else None)
                mex = sch.add("scalar", lambda e, pp=p: e.activation(
                    e1_f[pp][:, :], delta_f[pp][:, :], AF.Exp), (VM(mba),), inc=("s", 1))
                mln = sch.add("scalar", lambda e, pp=p: e.activation(
                    delta_f[pp][:, :], e1_f[pp][:, :], AF.Ln, bias=1.0), (SM(mex),),
                    inc=("s", 1))
                mk[("sp", i)] = mln
                mk[("dst", i)] = dma("scalar", sc_delta[i, :, :], delta_f[p][:, :], ch=f"sdl{p}")
                mcb = sch.add("scalar", lambda e, pp=p: e.copy(cs_bf[pp][:, :], PB[:, :]),
                              (TM(mcs), GM(mk.get(("m", i - 1), 0)),
                               TM(mk.get(("colsum", i - 1), 0))), inc=("s", 1))
                mk[("csbf", i)] = mcb
                mg0 = sch.add("gpsimd", lambda e, pp=p: e.tensor_mul(
                    g0_bf[pp][:, :], delta_f[pp][:, :], s_B1bc[:, :]),
                    (SM(mln), TM(mk.get(("colsum", i - 1), 0))), inc=("g", 1))
                mk[("g0", i)] = mg0
                mmm = sch.add("gpsimd", lambda e, pp=p, qq=q3: e.tensor_mul(
                    m_bf[pp][:, :], xn_bf[qq][:, :], cs_bf[pp][:, :]),
                    (SM(mcb), TM(mk.get(("colsum", i - 1), 0))), inc=("g", 1))
                mk[("m", i)] = mmm
                mrs = None
                for n in range(2):
                    for k in range(8):
                        mrs = sch.add("tensor", lambda e, nn=n, kk=k, qq=q3: e.matmul(
                            PA[:, nn * 512:(nn + 1) * 512], xnT[qq][:, kk, :],
                            WrT[:, kk * 1024 + nn * 512: kk * 1024 + nn * 512 + 512],
                            start=(kk == 0), stop=(kk == 7)),
                            (VM(mba),) if (n == 0 and k == 0) else (),
                            inc=("t", 1) if (n == 1 and k == 7) else None)
                mrev = sch.add("vector", lambda e, pp=p: e.tensor_copy(
                    res_f[pp][:, :], PA[:, :]),
                    (TM(mrs), mk.get(("rst", i - 2))), inc=("v", 1))
                mk[("resev", i)] = mrev
                mk[("rst", i)] = dma("scalar", sc_res[i, :, :], res_f[p][:, :], (VM(mrev),),
                                     ch=f"sre{p}")
                mcl = None
                for n in range(2):
                    mcl = sch.add("tensor", lambda e, nn=n, ii=i, pp=p: e.matmul(
                        PC[0:128, nn * 512:(nn + 1) * 512], s_emsk[:, 65 - ii:193 - ii],
                        g0_bf[pp][:, nn * 512:(nn + 1) * 512], start=(ii == 0), stop=False),
                        (GM(mg0),) if n == 0 else ())
                for n in range(2):
                    mcl = sch.add("tensor", lambda e, nn=n, ii=i, pp=p: e.matmul(
                        PC[0:128, nn * 512:(nn + 1) * 512], s_emsk[:, 1:129],
                        m_bf[pp][:, nn * 512:(nn + 1) * 512], start=False, stop=False),
                        (GM(mmm),) if n == 0 else ())
                for n in range(2):
                    mcl = sch.add("tensor", lambda e, nn=n, ii=i, pp=p: e.matmul(
                        PC[0:128, nn * 512:(nn + 1) * 512], s_emsk[:, 0:128],
                        cs_bf[pp][:, nn * 512:(nn + 1) * 512], start=False,
                        stop=(ii == NT - 1)),
                        (), inc=("t", 1) if n == 1 else None)
                mk[("colsum", i)] = mcl

            for i in range(5):
                if i < NT:
                    mk[("lx", i)] = dma("sync", xb[i % 3][:, :], x_d[i * 128:(i + 1) * 128, :],
                                        (SM(mk.get(("xn_c", i - 3))),), ch=f"lxb{i % 3}")
                if i < 3:
                    p1Aln(i)
            p1Atp(0)
            p1Atp(1)
            for i in range(NT):
                p1B(i)
                if i + 5 < NT:
                    mk[("lx", i + 5)] = dma("sync", xb[(i + 5) % 3][:, :],
                                            x_d[(i + 5) * 128:(i + 6) * 128, :],
                                            (SM(mk[("xn_c", i + 2)]),), ch=f"lxb{(i + 5) % 3}")
                if i + 3 < NT:
                    p1Aln(i + 3)
                if i + 2 < NT:
                    p1Atp(i + 2)

            sch.barrier()
            es_p1t.close()
            es_p1w.close()

            C2a = lambda t: R(es_p2a, t)
            tbl_sb = C2a(nc.sbuf_tensor("tbl_sb", [128, 1024], F32, side="right"))
            car0_sb = C2a(nc.sbuf_tensor("car0_sb", [64, 1024], F32, side="right"))
            l1bc_sb = C2a(nc.sbuf_tensor("l1bc_sb", [64, 1024], F32, side="right"))
            tadd_sb = C2a(nc.sbuf_tensor("tadd_sb", [64, 1024], F32, side="right"))
            hi_bf = C2a(nc.sbuf_tensor("hi_bf", [64, 1024], BF16, side="right"))
            lo_bf = C2a(nc.sbuf_tensor("lo_bf", [64, 1024], BF16, side="right"))
            hi32_sb = C2a(nc.sbuf_tensor("hi32_sb", [64, 1024], F32, side="right"))
            b0_sb = C2a(nc.sbuf_tensor("b0_sb", [1, 6144], F32, side="right"))
            ucbc_sb = C2a(nc.sbuf_tensor("ucbc_sb", [128, 1024], F32, side="right"))

            mtb = sch.add("vector", lambda e: e.tensor_copy(tbl_sb[0:66, :], PC[0:66, :]), (),
                          inc=("v", 1))
            mu1 = dma("sync", b0_sb[0:1, 0:1024], tbl_sb[64:65, :], (VM(mtb),))
            mu2 = dma("sync", b0_sb[0:1, 1024:2048], tbl_sb[65:66, :])
            mpf = None
            for n in range(2):
                mpf = sch.add("tensor", lambda e, nn=n: e.matmul(
                    PA[0:64, nn * 512:(nn + 1) * 512], s_tri64[:, :],
                    tbl_sb[0:64, nn * 512:(nn + 1) * 512], start=True, stop=True),
                    (VM(mtb),) if n == 0 else (), inc=("t", 1) if n == 1 else None)
            mc0 = sch.add("vector", lambda e: e.tensor_copy(car0_sb[:, :], PA[0:64, :]), (TM(mpf),),
                          inc=("v", 1))
            ml1 = dma("sync", b0_sb[0:1, 2048:3072], car0_sb[32:33, :], (VM(mc0),))
            ml2 = dma("sync", b0_sb[0:1, 3072:4096], car0_sb[63:64, :])
            ml3 = dma("sync", b0_sb[0:1, 4096:5120], tbl_sb[63:64, :])
            sch.add("vector", lambda e: e.tensor_add(b0_sb[0:1, 5120:6144], b0_sb[0:1, 3072:4096],
                                                     b0_sb[0:1, 4096:5120]), (LD(ml3),), inc=("v", 1))
            mt_ = sch.add("vector", lambda e: e.tensor_sub(b0_sb[0:1, 5120:6144],
                                                           b0_sb[0:1, 5120:6144],
                                                           b0_sb[0:1, 2048:3072]), inc=("v", 1))
            mb1 = None
            for n in range(2):
                mb1 = sch.add("tensor", lambda e, nn=n: e.matmul(
                    PB[0:64, nn * 512:(nn + 1) * 512], s_o64[:, :],
                    b0_sb[0:1, 2048 + nn * 512:2048 + (nn + 1) * 512], start=True, stop=True),
                    (VM(mt_),) if n == 0 else (), inc=("t", 1) if n == 1 else None)
            mevl1 = sch.add("vector", lambda e: e.tensor_copy(l1bc_sb[:, :], PB[0:64, :]), (TM(mb1),),
                            inc=("v", 1))
            mb2 = None
            for n in range(2):
                mb2 = sch.add("tensor", lambda e, nn=n: e.matmul(
                    PB[0:64, nn * 512:(nn + 1) * 512], s_o64[:, :],
                    b0_sb[0:1, 5120 + nn * 512:5120 + (nn + 1) * 512], start=True, stop=True),
                    (VM(mevl1),) if n == 0 else (), inc=("t", 1) if n == 1 else None)
            mta = sch.add("vector", lambda e: e.scalar_tensor_tensor(
                tadd_sb[:, :], PB[0:64, :], s_m1[:, :], car0_sb[:, :], ALU.mult, ALU.add),
                (TM(mb2),), inc=("v", 1))
            mta = sch.add("vector", lambda e: e.scalar_tensor_tensor(
                tadd_sb[:, :], l1bc_sb[:, :], s_m2n[:, :], tadd_sb[:, :], ALU.mult, ALU.add),
                inc=("v", 1))
            mhi = sch.add("scalar", lambda e: e.copy(hi_bf[:, :], tadd_sb[:, :]), (VM(mta),),
                          inc=("s", 1))
            mhi2 = sch.add("scalar", lambda e: e.copy(hi32_sb[:, :], hi_bf[:, :]), (SM(mhi),),
                           inc=("s", 1))
            mlo = sch.add("vector", lambda e: e.tensor_sub(lo_bf[:, :], tadd_sb[:, :], hi32_sb[:, :]),
                          (SM(mhi2),), inc=("v", 1))
            mst_hi = dma("scalar", sc_hi[:, :], hi_bf[:, :], (SM(mhi),))
            mst_lo = dma("scalar", sc_lo[:, :], lo_bf[:, :], (VM(mlo),))
            mcb_ = None
            for n in range(2):
                mcb_ = sch.add("tensor", lambda e, nn=n: e.matmul(
                    PC[:, nn * 512:(nn + 1) * 512], s_o1r[:, :],
                    b0_sb[0:1, 1024 + nn * 512:1024 + (nn + 1) * 512], start=True, stop=True),
                    (LD(mu2), VM(mtb)) if n == 0 else (), inc=("t", 1) if n == 1 else None)
            mcsbc = sch.add("vector", lambda e: e.tensor_copy(s_Csbc[:, :], PC[:, :]), (TM(mcb_),),
                            inc=("v", 1))
            mub_ = None
            for n in range(2):
                mub_ = sch.add("tensor", lambda e, nn=n: e.matmul(
                    PC[:, nn * 512:(nn + 1) * 512], s_o1r[:, :],
                    b0_sb[0:1, 0 + nn * 512:0 + (nn + 1) * 512], start=True, stop=True),
                    (VM(mcsbc), LD(mu1)) if n == 0 else (), inc=("t", 1) if n == 1 else None)
            mucbc = sch.add("vector", lambda e: e.tensor_copy(ucbc_sb[:, :], PC[:, :]), (TM(mub_),),
                            inc=("v", 1))
            mr2 = sch.add("gpsimd", lambda e: e.tensor_mul(s_R2bc[:, :], s_B1bc[:, :], ucbc_sb[:, :]),
                          (VM(mucbc),), inc=("g", 1))
            if dbg:
                dma("scalar", dbg_tbl[0:66, :], tbl_sb[0:66, :], (VM(mtb),))
                dma("scalar", dbg_bc[:, 0:1024], s_Csbc[:, :], (VM(mcsbc),))
                dma("scalar", dbg_bc[:, 1024:2048], ucbc_sb[:, :], (VM(mucbc),))
                dma("scalar", dbg_bc[:, 2048:3072], s_R2bc[:, :], (GM(mr2),))
                dma("scalar", dbg_bc[0:64, 3072:4096], car0_sb[0:64, :], (VM(mc0),))

            sch.barrier()
            es_p2a.close()
            es_p1t_dummy = None


        if phases >= 2:
            C2 = es_p2.enter_context
            A_ = [C2(nc.sbuf_tensor(f"p2A{p}", [128, 1024], F32)) for p in range(2)]
            B_ = [C2(nc.sbuf_tensor(f"p2B{p}", [128, 1024], F32)) for p in range(2)]
            Cc_ = [C2(nc.sbuf_tensor(f"p2C{p}", [128, 1024], F32)) for p in range(2)]
            D_ = [C2(nc.sbuf_tensor(f"p2D{p}", [128, 1024], F32)) for p in range(2)]
            E_ = [C2(nc.sbuf_tensor(f"p2E{p}", [128, 1024], F32)) for p in range(2)]
            F_ = [C2(nc.sbuf_tensor(f"p2F{p}", [128, 1024], F32)) for p in range(3)]
            G_ = [C2(nc.sbuf_tensor(f"p2G{p}", [128, 1024], F32)) for p in range(3)]
            H_ = [C2(nc.sbuf_tensor(f"p2H{p}", [128, 1024], F32)) for p in range(2)]
            g0b = [C2(nc.sbuf_tensor(f"p2g0{p}", [128, 1024], BF16)) for p in range(2)]
            hilo = [C2(nc.sbuf_tensor(f"hilo{p}", [2, 1024], BF16)) for p in range(2)]
            xn2b = [C2(nc.sbuf_tensor(f"xn2b{p}", [128, 1024], BF16)) for p in range(2)]
            xn2T = [C2(nc.sbuf_tensor(f"xn2T{p}", [128, 8, 128], BF16)) for p in range(2)]
            xn3b = [C2(nc.sbuf_tensor(f"xn3b{p}", [128, 1024], BF16)) for p in range(2)]
            st2 = [C2(nc.sbuf_tensor(f"st2_{p}", [128, 16], F32)) for p in range(2)]
            st3 = [C2(nc.sbuf_tensor(f"st3_{p}", [128, 16], F32)) for p in range(2)]

            mwq = load_weight(WqT, wqT_d)
            mwk = load_weight(WkT, wkT_d)
            mwv = load_weight(WvT, wvT_d)

            mk2 = {}
            for i in range(2):
                mk2[("lx", i)] = dma("sync", A_[i][:, :], x_d[i * 128:(i + 1) * 128, :],
                                     ch=f"lA{i}")
                mk2[("lhi", i)] = dma("sync", hilo[i][0:1, :], sc_hi[i:i + 1, :],
                                      (mst_hi, mst_lo), ch=f"lh{i}")
                mk2[("llo", i)] = dma("sync", hilo[i][1:2, :], sc_lo[i:i + 1, :], ch=f"lh{i}")
                mk2[("ldl", i)] = dma("vector", B_[i][:, :], sc_delta[i, :, :],
                                      (mk[("dst", i)],), ch=f"lB{i}")
                mk2[("lr", i)] = dma("vector", Cc_[i][:, :], sc_res[i, :, :],
                                     (mk[("rst", i)],), ch=f"lC{i}")


            def p2A(i):
                p = i % 2
                q3 = i % 3
                PD = PA if p == 0 else PC
                mg0 = sch.add("gpsimd", lambda e, pp=p: e.tensor_mul(
                    g0b[pp][:, :], B_[pp][:, :], s_B1bc[:, :]),
                    (VD(mk2[("ldl", i)]), TM(mk2.get(("tri", i - 2), 0)),
                     GM(mr2) if i == 0 else None), inc=("g", 1))
                mtr = None
                for n in range(2):
                    mtr = sch.add("tensor", lambda e, nn=n, pp=p, PD_=PD: e.matmul(
                        PD_[:, nn * 512:(nn + 1) * 512], s_tri128[:, :],
                        g0b[pp][:, nn * 512:(nn + 1) * 512], start=True, stop=False),
                        (GM(mg0), VM(mk2.get(("ydb", i - 2), 0))) if n == 0 else ())
                for n in range(2):
                    mtr = sch.add("tensor", lambda e, nn=n, pp=p, PD_=PD: e.matmul(
                        PD_[:, nn * 512:(nn + 1) * 512], s_o2r[:, :],
                        hilo[pp][:, nn * 512:(nn + 1) * 512], start=False, stop=True),
                        (LD(mk2[("llo", i)]),) if n == 0 else (),
                        inc=("t", 1) if n == 1 else None)
                mk2[("tri", i)] = mtr
                mei = sch.add("gpsimd", lambda e, pp=p: e.tensor_mul(
                    D_[pp][:, :], B_[pp][:, :], s_A1bc[:, :]), (), inc=("g", 1))
                mk2[("eIn", i)] = mei
                mev_ = sch.add("scalar", lambda e, pp=p: e.activation(
                    E_[pp][:, :], D_[pp][:, :], AF.Exp), (GM(mei),), inc=("s", 1))
                mydb = sch.add("vector", lambda e, pp=q3, PD_=PD: e.tensor_mul(
                    G_[pp][:, :], PD_[:, :], s_Csbc[:, :]),
                    (TM(mtr), GM(mk2.get(("x2", i - 3), 0))), inc=("v", 1))
                mk2[("ydb", i)] = mydb
                my1 = sch.add("gpsimd", lambda e, pp=p, qq=q3: e.tensor_mul(
                    F_[qq][:, :], E_[pp][:, :], s_R2bc[:, :]),
                    (SM(mev_), VM(mk2.get(("x2a", i - 3), 0))), inc=("g", 1))
                my12 = sch.add("gpsimd", lambda e, pp=p, qq=q3: e.tensor_add(
                    D_[pp][:, :], G_[qq][:, :], F_[qq][:, :]), (VM(mydb),), inc=("g", 1))
                mx1a = sch.add("vector", lambda e, pp=p: e.scalar_tensor_tensor(
                    E_[pp][:, :], Cc_[pp][:, :], s_Dbc[:, :], D_[pp][:, :], ALU.mult, ALU.add),
                    (VD(mk2[("lr", i)]), GM(my12), SM(mev_)), inc=("v", 1))
                mk2[("x1a", i)] = mx1a
                mx1 = sch.add("gpsimd", lambda e, pp=p, qq=q3: e.tensor_add(
                    F_[qq][:, :], E_[pp][:, :], A_[pp][:, :]),
                    (LD(mk2[("lx", i)]), VM(mx1a)), inc=("g", 1))
                mk2[("x1", i)] = mx1
                if dbg:
                    dma("scalar", sc_x1[i, :, :], F_[q3][:, :], (GM(mx1),))

            def p2Bln(i):
                p = i % 2
                q3 = i % 3
                mxn2 = ln(F_[q3], xn2b[p], st2[p], (GM(mk2[("x1", i)]),
                                                   TM(mk2.get(("tp2", i - 2), 0))))
                mk2[("xn2", i)] = mxn2

            def p2B(i):
                p = i % 2
                q3 = i % 3
                PTx = PT1 if p == 0 else PT2
                mxn2 = mk2[("xn2", i)]
                mtp2 = transposes8(PTx, xn2b[p], (SM(mxn2), SM(mk2.get(("ev2", i - 2), 0))))
                mk2[("tp2", i)] = mtp2
                mev2 = sch.add("scalar", lambda e, pp=p, pt=PTx: e.copy(
                    xn2T[pp][:, :, :], pt[:, :, :]),
                               (TM(mtp2),), inc=("s", 1))
                mk2[("ev2", i)] = mev2
                mf1 = None
                for n in range(2):
                    for k in range(8):
                        mf1 = sch.add("tensor", lambda e, nn=n, kk=k, pp=p: e.matmul(
                            PB[:, nn * 512:(nn + 1) * 512], xn2T[pp][:, kk, :],
                            W_eff[:, kk * 1024 + nn * 512: kk * 1024 + nn * 512 + 512],
                            start=(kk == 0), stop=(kk == 7)),
                            (SM(mev2), VM(mk2.get(("x2a", i - 1), 0)), VM(m_weff_v))
                            if (n == 0 and k == 0) else (),
                            inc=("t", 1) if (n == 1 and k == 7) else None)
                mx2a = sch.add("vector", lambda e, qq=q3: e.tensor_add(
                    G_[qq][:, :], PB[:, :], s_bfbc[:, :]), (TM(mf1),), inc=("v", 1))
                mk2[("x2a", i)] = mx2a
                mx2 = sch.add("gpsimd", lambda e, pp=p, qq=q3: e.tensor_add(
                    H_[pp][:, :], G_[qq][:, :], F_[qq][:, :]),
                    (VM(mx2a), mk2.get(("sx2", i - 2))), inc=("g", 1))
                mk2[("x2", i)] = mx2

            def p2C(i):
                p = i % 2
                mxn3 = ln(H_[p], xn3b[p], st3[p], (GM(mk2[("x2", i)]),
                                                   mk2.get(("sxn3", i - 2))))
                mk2[("sxn3", i)] = dma("scalar", sc_xn3[i, :, :], xn3b[p][:, :], ch=f"sx3{p}")
                if i < NQ:
                    mk2[("sx2", i)] = dma("scalar", sc_x2[i, :, :], H_[p][:, :],
                                          (GM(mk2[("x2", i)]),), ch=f"sx2{p}")

            def p2loads(i):
                p = i % 2
                mk2[("lx", i)] = dma("sync", A_[p][:, :], x_d[i * 128:(i + 1) * 128, :],
                                     (GM(mk2[("x1", i - 2)]),), ch=f"lA{p}")
                mk2[("lhi", i)] = dma("sync", hilo[p][0:1, :], sc_hi[i:i + 1, :],
                                      (TM(mk2[("tri", i - 2)]),), ch=f"lh{p}")
                mk2[("llo", i)] = dma("sync", hilo[p][1:2, :], sc_lo[i:i + 1, :], ch=f"lh{p}")
                mk2[("ldl", i)] = dma("vector", B_[p][:, :], sc_delta[i, :, :],
                                      (GM(mk2[("eIn", i - 2)]), mk[("dst", i)]), ch=f"lB{p}")
                mk2[("lr", i)] = dma("vector", Cc_[p][:, :], sc_res[i, :, :],
                                     (VM(mk2[("x1a", i - 2)]), mk[("rst", i)]), ch=f"lC{p}")

            p2A(0)
            p2A(1)
            p2loads(2)
            p2A(2)
            p2Bln(0)
            p2Bln(1)
            p2B(0)
            for i in range(NT):
                p2C(i)
                if i + 3 < NT:
                    p2loads(i + 3)
                    p2A(i + 3)
                if i + 2 < NT:
                    p2Bln(i + 2)
                if i + 1 < NT:
                    p2B(i + 1)

            sch.barrier()
            es_p2.close()
            es_bc.close()
            es_wtmp.close()

        if phases >= 3:
            xg3T = [R(es_prep, nc.sbuf_tensor(f"xg3T{p}", [128, 8, 512], BF16, side="right"))
                    for p in range(2)]
            xq_in = [R(es_prep, nc.sbuf_tensor(f"xq_in{p}", [128, 1024], BF16, side="right"))
                     for p in range(2)]
            qsp = [R(es_qsp, nc.sbuf_tensor(f"qsp{p}", [128, 8, 512], BF16, side="right"))
                   for p in range(2)]

            mk3 = {"tcnt": 0, "vevlast": 0}

            def prep_load(ti, tc):
                mk3[("gld", tc)] = dma("sync", xq_in[tc % 2][:, :], sc_xn3[ti, :, :],
                                       (TM(mk3.get(("gtp", tc - 2), 0)), mk2[("sxn3", ti)]),
                                       ch=f"lxq{tc % 2}")

            def prep_group(tiles, pg):
                last = None
                tiles = list(tiles)
                seq = mk3.get("prepseq", {})
                for j, ti in enumerate(tiles):
                    tc = mk3["tcnt"]
                    tl = tc % 4
                    for tq in (tc, tc + 1):
                        if ("gld", tq) not in mk3 and tq in seq:
                            prep_load(seq[tq], tq)
                    PTp = PT2 if tc % 2 == 0 else PT1
                    mtp = transposes8(PTp, xq_in[tc % 2],
                                      (LD(mk3[("gld", tc)]), SM(mk3.get(("gev", tc - 2), 0)),
                                       SM(mk3.get("pevlast", 0)),
                                       SM(mk3.get("ev4last", 0))))
                    mk3[("gtp", tc)] = mtp
                    last = sch.add("scalar",
                                   lambda e, tt=tl, pp=pg, pt=PTp: e.copy(
                                       xg3T[pp][:, :, tt * 128:(tt + 1) * 128], pt[:, :, :]),
                                   (TM(mtp),), inc=("s", 1))
                    mk3[("gev", tc)] = last
                    mk3["tcnt"] = tc + 1
                    if ("gld", tc + 2) not in mk3 and (tc + 2) in seq:
                        prep_load(seq[tc + 2], tc + 2)
                return last

            mk3["prepseq"] = {j: j for j in range(32)}
            for g in range(8):
                pg = g % 2
                mev = prep_group(range(4 * g, 4 * g + 4), pg)
                for c in range(8):
                    half = c % 2
                    mq = None
                    for k in range(8):
                        mq = sch.add("tensor", lambda e, cc=c, kk=k, hh=half, pp=pg: e.matmul(
                            PA[:, hh * 512:(hh + 1) * 512],
                            WqT[:, kk * 1024 + cc * 128: kk * 1024 + (cc + 1) * 128],
                            xg3T[pp][:, kk, :], start=(kk == 0), stop=(kk == 7)),
                            ((SM(mev), GM(mwq)) if c == 0 else
                             (SM(mk3.get(("qev", g, c - 2), 0)),)) if k == 0 else (),
                            inc=("t", 1) if k == 7 else None)
                    mqe = sch.add("scalar", lambda e, cc=c, hh=half, pp=pg: e.activation(
                        qsp[pp][:, cc, :], PA[:, hh * 512:(hh + 1) * 512],
                        AF.Identity, bias=s_bqT[:, cc:cc + 1], scale=1.0),
                        (TM(mq), mk3.get(("qst", g - 2))), inc=("s", 1))
                    mk3[("qev", g, c)] = mqe
                for t in range(4):
                    mk3[("qst", g)] = dma("scalar", sc_qT[4 * g + t, :, :, :],
                                          qsp[pg][:, :, t * 128:(t + 1) * 128],
                                          (SM(mk3[("qev", g, 7)]),) if t == 0 else (),
                                          ch=f"sq{pg}")
            sch.barrier()
            es_qsp.close()
            es_pqw.close()
            C3 = es_p3.enter_context
            kT_res = C3(nc.sbuf_tensor("kT_res", [128, 8, 2048], BF16))
            v_res = C3(nc.sbuf_tensor("v_res", [128, 16384], BF16))
            qT_t = [C3(nc.sbuf_tensor(f"qT_t{p}", [128, 8, 128], BF16)) for p in range(2)]
            P_bf = [C3(nc.sbuf_tensor(f"P_bf{p}", [128, 2048], BF16)) for p in range(2)]
            pt_sb = [C3(nc.sbuf_tensor(f"pt_sb{p}", [128, 16, 128], BF16))
                     for p in range(2)]
            zs = [C3(nc.sbuf_tensor(f"zs{p}", [128, 1024], F32)) for p in range(2)]
            zl = [C3(nc.sbuf_tensor(f"zl{p}", [128, 1024], F32)) for p in range(3)]
            x2l = [C3(nc.sbuf_tensor(f"x2l{p}", [128, 1024], F32)) for p in range(2)]
            x3b = [C3(nc.sbuf_tensor(f"x3b{p}", [128, 1024], F32)) for p in range(2)]
            xn4b = [C3(nc.sbuf_tensor("xn4b0", [128, 1024], BF16))] * 2
            xn4T = [pt_sb[0][:, 0:8, :], pt_sb[1][:, 0:8, :]]
            st4 = [C3(nc.sbuf_tensor(f"st4_{p}", [128, 16], F32)) for p in range(2)]
            m_run = C3(nc.sbuf_tensor("m_run", [128, 32], F32))
            l_run = C3(nc.sbuf_tensor("l_run", [128, 32], F32))
            cmax = C3(nc.sbuf_tensor("cmax", [128, 8], F32))
            lbuf = C3(nc.sbuf_tensor("lbuf", [128, 8], F32))
            pstat = C3(nc.sbuf_tensor("pstat", [128, 8], F32))

            for r in range(KR):
                lastq = r * NQ - 1
                mk3["prepseq"] = {mk3["tcnt"] + j: 16 * r + j for j in range(16)}
                for gg in range(4):
                    pg = (8 + r * 4 + gg) % 2
                    mev = prep_group(range(16 * r + 4 * gg, 16 * r + 4 * gg + 4), pg)
                    for c in range(8):
                        half = c % 2
                        mq = None
                        for k in range(8):
                            mq = sch.add("tensor", lambda e, cc=c, kk=k, hh=half, pp=pg: e.matmul(
                                PA[:, hh * 512:(hh + 1) * 512],
                                WkT[:, kk * 1024 + cc * 128: kk * 1024 + (cc + 1) * 128],
                                xg3T[pp][:, kk, :], start=(kk == 0), stop=(kk == 7)),
                                ((SM(mev), GM(mwk),
                                  SM(mk3.get(("exp", lastq, 3), 0)),
                                  VM(mk3.get(("rmx", lastq, 3), 0))) if c == 0 else
                                 (SM(mk3.get(("kev", gg, c - 2), 0)),)) if k == 0 else (),
                                inc=("t", 1) if k == 7 else None)
                        mqe = sch.add("scalar", lambda e, cc=c, hh=half, go=gg: e.activation(
                            kT_res[:, cc, go * 512:(go + 1) * 512],
                            PA[:, hh * 512:(hh + 1) * 512],
                            AF.Identity, bias=s_bkT[:, cc:cc + 1], scale=1.0),
                            (TM(mq),), inc=("s", 1))
                        mk3[("kev", gg, c)] = mqe
                    for t in range(4):
                        mv = None
                        for n in range(2):
                            for k in range(8):
                                mv = sch.add("tensor", lambda e, nn=n, kk=k, tt=t, pp=pg: e.matmul(
                                    PB[:, nn * 512:(nn + 1) * 512],
                                    xg3T[pp][:, kk, tt * 128:(tt + 1) * 128],
                                    WvT[:, kk * 1024 + nn * 512: kk * 1024 + nn * 512 + 512],
                                    start=(kk == 0), stop=(kk == 7)),
                                    ((GM(mwv), VM(mk3["vevlast"]),
                                      SM(mk3.get(("exp", lastq, 3), 0)),
                                      VM(mk3.get(("rmx", lastq, 3), 0)))
                                     if (n == 0 and k == 0) else ()),
                                    inc=("t", 1) if (n == 1 and k == 7) else None)
                        mve = sch.add("vector", lambda e, tt=t, go=gg: e.tensor_tensor(
                            v_res[:, (4 * go + tt) * 1024:(4 * go + tt + 1) * 1024],
                            PB[:, :], s_bvbc[:, :], ALU.add), (TM(mv),), inc=("v", 1))
                        mk3["vevlast"] = mve
                mk3["kevlast"] = mk3[("kev", 3, 7)]
                mk3["vevlast2"] = mk3["vevlast"]
                mk3["gevlast"] = mk3[("gev", mk3["tcnt"] - 1)]


                def qS(step):
                    qi = step - r * NQ
                    p = step % 2
                    ac = 3 + 3 * (step % 2)
                    mlq = dma("vector", qT_t[p][:, :, :], sc_qT[qi, :, :, :],
                              (TM(mk3.get(("smm", step - 2), 0)), mk3[("qst", qi // 4)]),
                              ch=f"lqt{p}")
                    if r > 0:
                        mk3[("lz", step)] = dma("vector", zl[step % 3][:, :], sc_z[qi, :, :],
                                                (VM(mk3.get(("zco", step - 3), 0)),
                                                 mk3[("zst", (r - 1) * NQ + qi)]),
                                                ch=f"lzl{step % 3}")
                    if r == KR - 1:
                        mk3[("lx2", step)] = dma("vector", x2l[p][:, :], sc_x2[qi, :, :],
                                                 (VM(mk3.get(("x3", step - 2), 0)),
                                                  mk2[("sx2", qi)]), ch=f"lx2{p}")
                    msmm = None
                    for n in range(4):
                        ps_ = PA if n < 2 else PB
                        off = (n % 2) * 512
                        for c in range(8):
                            msmm = sch.add("tensor", lambda e, nn=n, cc=c, o=ps_, of=off, pp=p:
                                           e.matmul(o[:, of:of + 512], qT_t[pp][:, cc, :],
                                                    kT_res[:, cc, nn * 512:(nn + 1) * 512],
                                                    start=(cc == 0), stop=(cc == 7)),
                                           ((VD(mlq), SM(mk3.get(("exp", step - 1, n), 0)),
                                             VM(mk3.get(("rmx", step - 1, n), 0)),
                                             SM(mk3.get("kevlast", 0)),
                                             VM(mk3.get("vevlast2", 0)))
                                            if c == 0 else ()),
                                           inc=("t", 1) if c == 7 else None)
                        mk3[("smmn", step, n)] = msmm
                        mrx = sch.add("vector", lambda e, nn=n, o=ps_, of=off: e.reduce_max(
                            cmax[:, nn:nn + 1], o[:, of:of + 512], AX), (TM(msmm),),
                            inc=("v", 1))
                        mk3[("rmx", step, n)] = mrx
                    mk3[("smm", step)] = msmm
                    mmx = sch.add("vector", lambda e: e.reduce_max(pstat[:, 0:1], cmax[:, 0:4],
                                                                   AX),
                                  (VM(mk3[("rmx", step, 3)]),), inc=("v", 1))
                    if r > 0:
                        mmx = sch.add("vector", lambda e, q_=qi: e.tensor_max(
                            pstat[:, 0:1], pstat[:, 0:1], m_run[:, q_:q_ + 1]), inc=("v", 1))
                        mdm = sch.add("vector", lambda e, q_=qi: e.tensor_sub(
                            pstat[:, 2:3], m_run[:, q_:q_ + 1], pstat[:, 0:1]), inc=("v", 1))
                        mal = sch.add("scalar", lambda e, a_=ac: e.activation(
                            pstat[:, a_:a_ + 1], pstat[:, 2:3], AF.Exp, scale=ISQ),
                            (VM(mdm),), inc=("s", 1))
                        mk3[("al", step)] = mal
                    mng = sch.add("scalar", lambda e: e.activation(pstat[:, 1:2], pstat[:, 0:1],
                                                                   AF.Identity, scale=-ISQ),
                                  (VM(mmx),), inc=("s", 1))
                    sch.add("scalar", lambda e, q_=qi: e.activation(
                        m_run[:, q_:q_ + 1], pstat[:, 0:1], AF.Identity, scale=1.0),
                        (SM(mng),), inc=("s", 1))
                    for n in range(4):
                        ps_ = PA if n < 2 else PB
                        off = (n % 2) * 512
                        mex = sch.add("scalar", lambda e, nn=n, o=ps_, of=off, pp=p:
                                      e.activation(
                                          P_bf[pp][:, nn * 512:(nn + 1) * 512],
                                          o[:, of:of + 512], AF.Exp,
                                          bias=pstat[:, 1:2], scale=ISQ,
                                          accum_out=lbuf[:, nn:nn + 1]),
                                      (TM(mk3.get(("ptp", step - 2), 0)), SM(mng),
                                       VM(mk3.get(("la", step - 1), 0))) if n == 0 else (),
                                      inc=("s", 1))
                        mk3[("exp", step, n)] = mex
                    mla = sch.add("vector", lambda e: e.reduce_sum(pstat[:, 4:5], lbuf[:, 0:4],
                                                                   AX),
                                  (SM(mk3[("exp", step, 3)]),), inc=("v", 1))
                    mk3[("la", step)] = mla
                    if r == 0:
                        sch.add("scalar", lambda e, q_=qi: e.activation(
                            l_run[:, q_:q_ + 1], pstat[:, 4:5], AF.Identity, scale=1.0),
                            (VM(mla),), inc=("s", 1))
                        mk3[("lr", step)] = ("v", mla)
                    else:
                        sch.add("vector", lambda e, q_=qi, a_=ac: e.tensor_mul(
                            l_run[:, q_:q_ + 1], l_run[:, q_:q_ + 1], pstat[:, a_:a_ + 1]),
                            (SM(mk3[("al", step)]),), inc=("v", 1))
                        mlr = sch.add("vector", lambda e, q_=qi: e.tensor_add(
                            l_run[:, q_:q_ + 1], l_run[:, q_:q_ + 1], pstat[:, 4:5]),
                            (VM(mla),), inc=("v", 1))
                        mk3[("lr", step)] = ("v", mlr)

                def qB(step):
                    p = step % 2
                    mpt = None
                    for j in range(16):
                        pt = PT1 if j < 8 else PT2
                        mpt = sch.add("tensor", lambda e, jj=j, pt_=pt, pp=p: e.transpose(
                            pt_[:, jj % 8, :], P_bf[pp][:, jj * 128:(jj + 1) * 128],
                            s_id[:, :]),
                            ((SM(mk3[("exp", step, 3)]), SM(mk3.get(("pev", step - 1), 0)),
                              SM(mk3.get(("ev4", step - 1), 0)), SM(mk3.get("gevlast", 0)))
                             if j == 0 else ()),
                            inc=("t", 1) if j == 15 else None)
                    mk3[("ptp", step)] = mpt
                    sch.add("scalar", lambda e, pp=p: e.copy(pt_sb[pp][:, 0:8, :],
                                                             PT1[:, :, :]),
                            (TM(mpt),), inc=("s", 1))
                    mpe = sch.add("scalar", lambda e, pp=p: e.copy(pt_sb[pp][:, 8:16, :],
                                                                   PT2[:, :, :]),
                                  inc=("s", 1))
                    mk3[("pev", step)] = mpe

                def qC(step):
                    qi = step - r * NQ
                    p = step % 2
                    ac = 3 + 3 * (step % 2)
                    mpe = mk3[("pev", step)]
                    mpv = None
                    for j in range(16):
                        for n in range(2):
                            mpv = sch.add("tensor", lambda e, jj=j, nn=n, pp=p: e.matmul(
                                PC[:, nn * 512:(nn + 1) * 512], pt_sb[pp][:, jj, :],
                                v_res[:, jj * 1024 + nn * 512: jj * 1024 + nn * 512 + 512],
                                start=(jj == 0), stop=(jj == 15)),
                                ((SM(mpe), VM(mk3.get(("zco", step - 1), 0)),
                                  VM(mk3.get(("mo1", step - 1), 0)))
                                 if (j == 0 and n == 0) else ()),
                                inc=("t", 1) if (j == 15 and n == 1) else None)
                    if r == 0:
                        mz = sch.add("vector", lambda e, pp=p: e.tensor_copy(zs[pp][:, :],
                                                                             PC[:, :]),
                                     (TM(mpv), mk3.get(("zst", step - 2))), inc=("v", 1))
                    else:
                        mz = sch.add("vector", lambda e, pp=p, z3=step % 3, a_=ac:
                                     e.scalar_tensor_tensor(
                            zs[pp][:, :], zl[z3][:, :], pstat[:, a_:a_ + 1], PC[:, :],
                            ALU.mult, ALU.add),
                            (TM(mpv), mk3.get(("lz", step)), SM(mk3[("al", step)]),
                             mk3.get(("zst", step - 2)),
                             GM(mk3.get(("moo", step - 2), 0))),
                            inc=("v", 1))
                    mk3[("zco", step)] = mz
                    if r < KR - 1:
                        mk3[("zst", step)] = dma("scalar", sc_z[qi, :, :], zs[p][:, :],
                                                 (VM(mz),), ch=f"sz{p}")
                    else:
                        mli = sch.add("vector", lambda e, q_=qi: e.reciprocal(
                            pstat[:, 5:6], l_run[:, q_:q_ + 1]),
                            (mk3[("lr", step)],), inc=("v", 1))
                        mx3 = sch.add("vector", lambda e, pp=p: e.scalar_tensor_tensor(
                            x3b[pp][:, :], zs[pp][:, :], pstat[:, 5:6], x2l[pp][:, :],
                            ALU.mult, ALU.add),
                            (VM(mli), mk3.get(("lx2", step)),
                             GM(mk3.get(("moo", step - 2), 0))), inc=("v", 1))
                        mk3[("x3", step)] = mx3
                        if dbg:
                            mk3[("zdbg", step)] = dma("scalar", sc_z[qi, :, :], zs[p][:, :],
                                                      (VM(mx3),), ch=f"sz{p}")
                        mxn4 = ln(x3b[p], xn4b[p], st4[p],
                                  (VM(mx3), TM(mk3.get(("tp4", step - 1), 0))))
                        mtp4 = transposes8(PT1, xn4b[p], (SM(mxn4),))
                        mk3[("tp4", step)] = mtp4
                        mev4 = sch.add("scalar", lambda e, pp=p: e.copy(xn4T[pp],
                                                                        PT1[:, :, :]),
                                       (TM(mtp4),), inc=("s", 1))
                        mk3[("ev4", step)] = mev4
                        mf2 = None
                        for n in range(2):
                            for k in range(8):
                                mf2 = sch.add("tensor", lambda e, nn=n, kk=k, pp=p: e.matmul(
                                    PC[:, nn * 512:(nn + 1) * 512], xn4T[pp].tensor[:, kk, :],
                                    W_eff[:, kk * 1024 + nn * 512: kk * 1024 + nn * 512 + 512],
                                    start=(kk == 0), stop=(kk == 7)),
                                    (SM(mev4), VM(mk3[("zco", step)])) if (n == 0 and k == 0)
                                    else (),
                                    inc=("t", 1) if (n == 1 and k == 7) else None)
                        mo1 = sch.add("vector", lambda e, pp=p: e.tensor_add(
                            zs[pp][:, :], PC[:, :], s_bfbc[:, :]),
                            (TM(mf2), GM(mk3.get(("moo", step - 2), 0)),
                             mk3.get(("zdbg", step))), inc=("v", 1))
                        mk3[("mo1", step)] = mo1
                        moo = sch.add("gpsimd", lambda e, pp=p: e.tensor_add(
                            s_out[pp][:, :], zs[pp][:, :], x3b[pp][:, :]),
                            (VM(mo1), mk3.get(("ost", step - 1))), inc=("g", 1))
                        mk3[("moo", step)] = moo
                        mk3[("ost", step)] = dma("scalar", out_d[qi * 128:(qi + 1) * 128, :],
                                                 s_out[p][:, :], (GM(moo),), ch=f"so{p}")

                qS(r * NQ)
                qS(r * NQ + 1)
                qB(r * NQ)
                if r < KR - 1:
                    for qi in range(NQ):
                        step = r * NQ + qi
                        if qi + 2 < NQ:
                            qS(step + 2)
                        if qi + 1 < NQ:
                            qB(step + 1)
                        qC(step)
                else:
                    for qi in range(NQ):
                        step = r * NQ + qi
                        qC(step)
                        if qi + 2 < NQ:
                            qS(step + 2)
                        if qi + 1 < NQ:
                            qB(step + 1)
                mk3["pevlast"] = mk3[("pev", r * NQ + NQ - 1)]
                mk3["ev4last"] = mk3.get(("ev4", r * NQ + NQ - 1), 0)

        pass
    except _Cut:
        pass

    if dbg and phases >= 3:
        dma("scalar", dbg_ml[:, 0:32], m_run[:, :], (VM(sch.count("v")),))
        dma("scalar", dbg_ml[:, 32:64], l_run[:, :], (VM(sch.count("v")),))
    _fw = [(snm, sch.count(snm)) for snm in list(sch.cnt.keys())
           if snm not in ("t", "v", "s", "g")]
    _fw += [TM(sch.count("t")), VM(sch.count("v")), GM(sch.count("g"))]
    sch.add("scalar", lambda e: e.activation(s_sqscr[:, 0:1], s_sqscr[:, 0:1], AF.Identity,
                                             scale=1.0), tuple(_fw), inc=("s", 1))

    for n in sorted(set(list(sch.cnt.keys()) + ["t", "v", "s", "g", "ld", "st"])):
        sems[n] = C(nc.semaphore(n))

    def run_prog(name):
        def f(eng):
            for item in sch.prog[name]:
                if item[0] == "w":
                    eng.wait_ge(sems[item[1]], item[2])
                else:
                    _, fn, inc = item
                    ins = fn(eng)
                    if inc is not None:
                        ins.then_inc(sems[inc[0]], inc[1])
        return f

    with nc.Block() as blk:
        blk.tensor(run_prog("tensor"))
        blk.vector(run_prog("vector"))
        blk.scalar(run_prog("scalar"))
        blk.sync(run_prog("sync"))
        blk.gpsimd(run_prog("gpsimd"))

    for es in (es_p3, es_p2, es_p1t, es_p1w, es_pqw, es_qsp, es_prep,
               es_p2a, es_fold, es_bc, es_wtmp, es_p23w, es_glob):
        es.close()
    return nc


def make_inmaps(inputs):
    f32 = np.float32
    bf = ml_dtypes.bfloat16
    x = np.asarray(inputs["L0_attn_a_x"], f32)
    Wp = np.asarray(inputs["L0_attn_a_W_proj"], f32)
    A = np.asarray(inputs["L0_attn_a_A"], f32).reshape(1, 1024)
    Bp = np.asarray(inputs["L0_attn_a_Bp"], f32).reshape(1, 1024)
    Dsc = np.asarray(inputs["L0_attn_a_D"], f32).reshape(1)
    Wd = np.asarray(inputs["L0_attn_a_W_delta"], f32)
    bd = np.asarray(inputs["L0_attn_a_b_delta"], f32).reshape(1, 1024)
    Wq = np.asarray(inputs["L0_attn_b_Wq"], f32)
    bq = np.asarray(inputs["L0_attn_b_bq"], f32)
    Wk = np.asarray(inputs["L0_attn_b_Wk"], f32)
    bk = np.asarray(inputs["L0_attn_b_bk"], f32)
    Wv = np.asarray(inputs["L0_attn_b_Wv"], f32)
    bv = np.asarray(inputs["L0_attn_b_bv"], f32).reshape(1, 1024)
    W = np.asarray(inputs["L0_ffn_W"], f32)
    b = np.asarray(inputs["L0_ffn_b"], f32).reshape(1, 1024)
    ld_ = np.asarray(inputs["L0_ffn_lora_down"], f32)
    lu_ = np.asarray(inputs["L0_ffn_lora_up"], f32)
    da_ = np.asarray(inputs["L0_ffn_down_aux"], f32)
    ua_ = np.asarray(inputs["L0_ffn_up_aux"], f32)

    ct = np.ascontiguousarray

    def _pad128(a):
        out = np.zeros((128,) + a.shape[1:], f32)
        out[:a.shape[0]] = a
        return out

    rows = np.zeros((8, 1024), f32)
    rows[0] = A
    rows[1] = Bp
    rows[2] = bd
    rows[3] = b
    rows[4] = bv
    rows[5, 0] = Dsc[0]
    emsk = np.zeros((128, 256), bf)
    emsk[:, 65] = 1
    shared = dict(
        wp0=ct(Wp[0:1024, :]), wdT=ct(Wd.T), wcT=ct(Wp[2048:3072, :].T),
        wrT=ct(Wp[3072:4096, :].T), wT=ct(W.T),
        uaT=_pad128(ua_.T), da=_pad128(da_), ldT=_pad128(ld_.T), luT=_pad128(lu_.T),
        wqT=ct(Wq.T), wkT=ct(Wk.T), wvT=ct(Wv.T),
        rows=rows, bqT=ct(bq.reshape(8, 128).T), bkT=ct(bk.reshape(8, 128).T),
        ident=np.eye(128, dtype=bf), tri128=np.triu(np.ones((128, 128))).astype(bf),
        tri64x=np.triu(np.ones((64, 64), f32), 1), emsk=emsk,
        ones1r=np.ones((1, 128), f32), ones64=np.ones((1, 64), f32),
        ones2r=np.ones((2, 128), bf),
    )
    in_maps = []
    for c in range(8):
        bidx, h = c // 2, c % 2
        xl = x[bidx] if h == 0 else np.concatenate([x[bidx, 4096:], x[bidx, :4096]], axis=0)
        m1 = np.zeros((64, 1), f32)
        m2n = np.zeros((64, 1), f32)
        if h == 1:
            m1[0:32] = 1.0
            m2n[32:64] = -1.0
        im = dict(shared)
        im["x"] = ct(xl)
        im["m1c"] = m1
        im["m2n"] = m2n
        in_maps.append(im)
    return in_maps


def kernel(**inputs):
    in_maps = make_inmaps(inputs)
    nc = build(phases=3, dbg=False)
    res = run_bass_kernel_spmd(nc, in_maps, core_ids=list(range(8)))
    out = np.empty((4, 8192, 1024), np.float32)
    for c in range(8):
        bidx, h = c // 2, c % 2
        out[bidx, h * 4096:(h + 1) * 4096] = np.asarray(res.results[c]["out"], np.float32)
    return out


# version 12
# speedup vs baseline: 1.1205x; 1.0126x over previous
import numpy as np
import ml_dtypes
from contextlib import ExitStack

import concourse.bass as bass
import concourse.mybir as mybir
from concourse.bass_utils import run_bass_kernel_spmd

F32 = mybir.dt.float32
BF16 = mybir.dt.bfloat16
AX = mybir.AxisListType.X
AF = mybir.ActivationFunctionType
ALU = mybir.AluOpType

NT = 64
NQ = 32
KR = 4
D = 1024
EPS = 1e-5
ISQ = 1.0 / 32.0

ENGS = ("tensor", "vector", "scalar", "sync", "gpsimd")


class Sched:
    def __init__(self):
        self.prog = {e: [] for e in ENGS}
        self.cnt = {}
        self.lastwait = {e: {} for e in ENGS}

    CHAIN = {"vector": "v", "scalar": "s", "gpsimd": "g"}

    def add(self, eng, fn, waits=(), inc=None):
        sem0 = self.CHAIN.get(eng)
        if sem0 is not None:
            waits = ((sem0, self.cnt.get(sem0, 0)),) + tuple(waits)
            if inc is None:
                inc = (sem0, 1)
        for w in waits:
            if w is None:
                continue
            sem, val = w
            if isinstance(val, tuple):
                sem, val = val
            if val is None or val <= 0:
                continue
            if sem in ("ld", "st"):
                val = self.cnt.get(sem, 0)
            lw = self.lastwait[eng]
            if lw.get(sem, 0) >= val:
                continue
            lw[sem] = val
            self.prog[eng].append(("w", sem, val))
        self.prog[eng].append(("c", fn, inc))
        if inc is not None:
            sem, amt = inc
            self.cnt[sem] = self.cnt.get(sem, 0) + amt
            return self.cnt[sem]
        return None

    def count(self, sem):
        return self.cnt.get(sem, 0)

    def barrier(self):
        snap = dict(self.cnt)
        for e in ENGS:
            for sem, val in snap.items():
                if val > 0 and self.lastwait[e].get(sem, 0) < val:
                    self.lastwait[e][sem] = val
                    self.prog[e].append(("w", sem, val))


class _Cut(Exception):
    pass


def build(phases=3, dbg=False, cut=None):
    nc = bass.Bass("TRN2", target_bir_lowering=False)

    dI = lambda n, s, dt=F32: nc.dram_tensor(n, s, dt, kind="ExternalInput").ap()
    x_d = dI("x", [8192, 1024])
    wp0_d = dI("wp0", [1024, 1024])
    wdT_d = dI("wdT", [1024, 1024])
    wcT_d = dI("wcT", [1024, 1024])
    wrT_d = dI("wrT", [1024, 1024])
    wT_d = dI("wT", [1024, 1024])
    uaT_d = dI("uaT", [128, 1024])
    da_d = dI("da", [128, 1024])
    ldT_d = dI("ldT", [128, 4])
    luT_d = dI("luT", [128, 50])
    wqT_d = dI("wqT", [1024, 1024])
    wkT_d = dI("wkT", [1024, 1024])
    wvT_d = dI("wvT", [1024, 1024])
    rows_d = dI("rows", [8, 1024])
    bqT_d = dI("bqT", [128, 8])
    bkT_d = dI("bkT", [128, 8])
    m1_d = dI("m1c", [64, 1])
    m2n_d = dI("m2n", [64, 1])
    ident_d = dI("ident", [128, 128], BF16)
    tri128_d = dI("tri128", [128, 128], BF16)
    tri64_d = dI("tri64x", [64, 64])
    emsk_d = dI("emsk", [128, 256], BF16)
    o1r_d = dI("ones1r", [1, 128])
    o64_d = dI("ones64", [1, 64])
    o2r_d = dI("ones2r", [2, 128], BF16)

    sk = "ExternalOutput" if dbg else "Internal"
    dS = lambda n, s, dt=F32: nc.dram_tensor(n, s, dt, kind=sk).ap()
    sc_delta = dS("sc_delta", [NT, 128, 1024])
    sc_res = dS("sc_res", [NT, 128, 1024])
    sc_hi = dS("sc_hi", [64, 1024], BF16)
    sc_lo = dS("sc_lo", [64, 1024], BF16)
    sc_xn3 = dS("sc_xn3", [NT, 128, 1024], BF16)
    sc_x2 = dS("sc_x2", [NQ, 128, 1024])
    sc_qT = dS("sc_qT", [NQ, 128, 8, 128], BF16)
    sc_z = dS("sc_z", [NQ, 128, 1024])
    if dbg:
        dbg_tbl = nc.dram_tensor("dbg_tbl", [128, 1024], F32, kind="ExternalOutput").ap()
        dbg_bc = nc.dram_tensor("dbg_bc", [128, 4096], F32, kind="ExternalOutput").ap()
        sc_x1 = nc.dram_tensor("sc_x1", [NT, 128, 1024], F32, kind="ExternalOutput").ap()
        dbg_ml = nc.dram_tensor("dbg_ml", [128, 64], F32, kind="ExternalOutput").ap()
    out_d = nc.dram_tensor("out", [4096, 1024], F32, kind="ExternalOutput").ap()

    sch = Sched()

    def _ck(n):
        if cut is not None and n >= cut:
            raise _Cut()

    LD = lambda m: m if isinstance(m, tuple) else ("ld", m)
    VD = LD
    ST = lambda m: m if isinstance(m, tuple) else ("st", m)
    TM = lambda m: ("t", m)
    VM = lambda m: ("v", m)
    SM = lambda m: ("s", m)
    GM = lambda m: ("g", m)

    (es_glob, es_p23w, es_pqw, es_p1w, es_p1t, es_p2, es_p3,
     es_bc, es_wtmp, es_fold, es_p2a, es_prep, es_qsp) = (ExitStack() for _ in range(13))
    C = es_glob.enter_context
    R = lambda es, t: es.enter_context(t)

    s_id = C(nc.sbuf_tensor("s_id", [128, 128], BF16))
    s_tri128 = C(nc.sbuf_tensor("s_tri128", [128, 128], BF16))
    s_tri64 = C(nc.sbuf_tensor("s_tri64", [64, 64], F32))
    s_emsk = C(nc.sbuf_tensor("s_emsk", [128, 256], BF16))
    s_o1r = C(nc.sbuf_tensor("s_o1r", [1, 128], F32))
    s_o64 = C(nc.sbuf_tensor("s_o64", [1, 64], F32))
    s_o2r = C(nc.sbuf_tensor("s_o2r", [2, 128], BF16))
    s_bqT = C(nc.sbuf_tensor("s_bqT", [128, 8], F32))
    s_bkT = C(nc.sbuf_tensor("s_bkT", [128, 8], F32))
    s_m1 = C(nc.sbuf_tensor("s_m1", [64, 1], F32))
    s_m2n = C(nc.sbuf_tensor("s_m2n", [64, 1], F32))
    s_Dbc = C(nc.sbuf_tensor("s_Dbc", [128, 1], F32))
    s_bfbc = C(nc.sbuf_tensor("s_bfbc", [128, 1024], F32))
    s_bvbc = R(es_prep, nc.sbuf_tensor("s_bvbc", [128, 1024], F32, side="right"))
    W_eff = C(nc.sbuf_tensor("W_eff", [128, 8192], BF16))
    s_wtmp = [R(es_wtmp, nc.sbuf_tensor(f"s_wtmp{p}", [128, 1024], F32, side="right"))
              for p in range(2)]
    s_sqscr = C(nc.sbuf_tensor("s_sqscr", [128, 1024], BF16))
    s_out = [C(nc.sbuf_tensor("s_out0", [128, 1024], F32))] * 2

    PA = C(nc.psum_tensor("PA", [128, 1024], F32))
    PB = C(nc.psum_tensor("PB", [128, 1024], F32))
    PC = C(nc.psum_tensor("PC", [128, 1024], F32))
    PT1 = C(nc.psum_tensor("PT1", [128, 8, 128], BF16))
    PT2 = C(nc.psum_tensor("PT2", [128, 8, 128], BF16))

    sems = {}

    def dma(q, outap, inap, waits=(), ch=None):
        if q == "vector":
            q = "sync"
        semn = ch if ch is not None else {"sync": "ld", "scalar": "st"}[q]
        if semn in ("ld", "st"):
            waits = tuple(waits) + ((semn, 1),)
        m = sch.add(q, lambda e, o=outap, i=inap: e.dma_start(out=o, in_=i),
                    waits, inc=(semn, 16))
        return (semn, m)

    def transposes8(pt, src_bf, waits=()):
        m = None
        for c in range(8):
            m = sch.add("tensor",
                        lambda e, o=pt[:, c, :], i_=src_bf[:, c * 128:(c + 1) * 128]:
                        e.transpose(o, i_, s_id[:, :]),
                        waits if c == 0 else (), inc=("t", 1) if c == 7 else None)
        return m

    def ln(xin, xn_bf, stats, w_x):
        mv = sch.add("vector", lambda e: e.reduce_sum(stats[:, 0:1], xin[:, :], AX), w_x,
                     inc=("v", 1))
        msq = sch.add("scalar", lambda e: e.activation(s_sqscr[:, :], xin[:, :], AF.Square,
                                                       accum_out=stats[:, 1:2]), w_x,
                      inc=("s", 1))
        mng = sch.add("scalar", lambda e: e.activation(stats[:, 2:3], stats[:, 0:1], AF.Identity,
                                                       scale=-1.0 / 1024.0), (VM(mv),), inc=("s", 1))
        mvh = sch.add("scalar", lambda e: e.activation(stats[:, 3:4], stats[:, 1:2], AF.Identity,
                                                       scale=1.0 / 1024.0), (SM(msq),), inc=("s", 1))
        ms = sch.add("scalar", lambda e: e.activation(stats[:, 4:5], stats[:, 2:3], AF.Square),
                     (SM(mng),), inc=("s", 1))
        mv2 = sch.add("vector", lambda e: e.tensor_sub(stats[:, 5:6], stats[:, 3:4],
                                                       stats[:, 4:5]), (SM(ms), SM(mvh)), inc=("v", 1))
        ms2 = sch.add("scalar", lambda e: e.activation(stats[:, 6:7], stats[:, 5:6], AF.Copy,
                                                       bias=EPS), (VM(mv2),), inc=("s", 1))
        mv3 = sch.add("vector", lambda e: e.reciprocal(stats[:, 7:8], stats[:, 6:7]),
                      (SM(ms2),), inc=("v", 1))
        ms3 = sch.add("scalar", lambda e: e.activation(stats[:, 8:9], stats[:, 7:8], AF.Sqrt),
                      (VM(mv3),), inc=("s", 1))
        mv4 = sch.add("vector", lambda e: e.tensor_mul(stats[:, 9:10], stats[:, 2:3],
                                                       stats[:, 8:9]), (SM(ms3),), inc=("v", 1))
        mxn = sch.add("scalar", lambda e: e.activation(xn_bf[:, :], xin[:, :], AF.Identity,
                                                       bias=stats[:, 9:10], scale=stats[:, 8:9]),
                      (VM(mv4), SM(ms3)), inc=("s", 1))
        return mxn

    wtmp_prev = {0: None, 1: None}

    def load_weight(dst_bf, src_d):
        last = None
        for k in range(8):
            p = k % 2
            ml = dma("sync", s_wtmp[p][:, :], src_d[k * 128:(k + 1) * 128, :],
                     (wtmp_prev[p],), ch=f"lw{p}")
            mg = sch.add("gpsimd",
                         lambda e, kk=k, pp=p: e.tensor_copy(
                             dst_bf[:, kk * 1024:(kk + 1) * 1024], s_wtmp[pp][:, :]),
                         (LD(ml),), inc=("g", 1))
            wtmp_prev[p] = GM(mg)
            last = mg
        return last

    WkT = R(es_p23w, nc.sbuf_tensor("WkT", [128, 8192], BF16))
    WvT = R(es_p23w, nc.sbuf_tensor("WvT", [128, 8192], BF16))
    WqT = R(es_pqw, nc.sbuf_tensor("WqT", [128, 8192], BF16))
    W_dd = R(es_p1w, nc.sbuf_tensor("W_dd", [128, 8192], BF16))
    WcT = R(es_p1w, nc.sbuf_tensor("WcT", [128, 8192], BF16))
    WrT = R(es_p1w, nc.sbuf_tensor("WrT", [128, 8192], BF16))
    Rb = lambda t: R(es_bc, t)
    s_A1bc = Rb(nc.sbuf_tensor("s_A1bc", [128, 1024], F32, side="right"))
    s_B1bc = Rb(nc.sbuf_tensor("s_B1bc", [128, 1024], F32, side="right"))
    s_bdbc = Rb(nc.sbuf_tensor("s_bdbc", [128, 1024], F32, side="right"))
    s_Csbc = Rb(nc.sbuf_tensor("s_Csbc", [128, 1024], F32, side="right"))
    s_R2bc = Rb(nc.sbuf_tensor("s_R2bc", [128, 1024], F32, side="right"))
    Cf = lambda t: R(es_fold, t)
    s_row1 = Cf(nc.sbuf_tensor("s_row1", [1, 1024], F32, side="right"))
    Wp0 = Cf(nc.sbuf_tensor("Wp0", [128, 8192], BF16, side="right"))
    WdT = Cf(nc.sbuf_tensor("WdT", [128, 8192], BF16, side="right"))
    s_ldT = Cf(nc.sbuf_tensor("s_ldT", [128, 4], BF16, side="right"))
    s_luT = Cf(nc.sbuf_tensor("s_luT", [128, 50], BF16, side="right"))
    s_uaT = Cf(nc.sbuf_tensor("s_uaT", [128, 1024], BF16, side="right"))
    s_da = Cf(nc.sbuf_tensor("s_da", [128, 1024], BF16, side="right"))
    s_t1 = Cf(nc.sbuf_tensor("s_t1", [128, 1024], BF16, side="right"))
    s_t2 = Cf(nc.sbuf_tensor("s_t2", [128, 1024], BF16, side="right"))
    s_lt = s_wtmp[0]

    try:
        for outap, inap in ((s_id[:, :], ident_d[:, :]), (s_tri128[:, :], tri128_d[:, :]),
                            (s_tri64[:, :], tri64_d[:, :]), (s_emsk[:, :], emsk_d[:, :]),
                            (s_o1r[:, :], o1r_d[:, :]), (s_o64[:, :], o64_d[:, :]),
                            (s_o2r[:, :], o2r_d[:, :]),
                            (s_bqT[:, :], bqT_d[:, :]), (s_bkT[:, :], bkT_d[:, :]),
                            (s_m1[:, :], m1_d[:, :]), (s_m2n[:, :], m2n_d[:, :])):
            m_startld = dma("sync", outap, inap)
        _ck(1)

        ml = dma("sync", s_lt[:, 0:4], ldT_d[:, :], (wtmp_prev[0],), ch="lw0")
        mg1 = sch.add("gpsimd", lambda e: e.tensor_copy(s_ldT[:, :], s_lt[:, 0:4]),
                      (LD(ml),), inc=("g", 1))
        ml = dma("sync", s_lt[:, 0:50], luT_d[:, :], (GM(mg1),), ch="lw0")
        mg2 = sch.add("gpsimd", lambda e: e.tensor_copy(s_luT[:, :], s_lt[:, 0:50]),
                      (LD(ml),), inc=("g", 1))
        ml = dma("sync", s_lt[:, :], uaT_d[:, :], (GM(mg2),), ch="lw0")
        mg3 = sch.add("gpsimd", lambda e: e.tensor_copy(s_uaT[:, :], s_lt[:, :]),
                      (LD(ml),), inc=("g", 1))
        ml = dma("sync", s_lt[:, :], da_d[:, :], (GM(mg3),), ch="lw0")
        mg4 = sch.add("gpsimd", lambda e: e.tensor_copy(s_da[:, :], s_lt[:, :]),
                      (LD(ml),), inc=("g", 1))
        wtmp_prev[0] = GM(mg4)
        mz1 = sch.add("gpsimd", lambda e: e.memzero(s_t1[:, :]), inc=("g", 1))
        mz2 = sch.add("gpsimd", lambda e: e.memzero(s_t2[:, :]), inc=("g", 1))
        _ck(2)

        mwp0 = load_weight(Wp0, wp0_d)
        mwdt = load_weight(WdT, wdT_d)
        _ck(3)

        mt1 = None
        for n in range(2):
            mt1 = sch.add("tensor", lambda e, nn=n: e.matmul(
                PA[0:4, nn * 512:(nn + 1) * 512], s_ldT[:, :], s_da[:, nn * 512:(nn + 1) * 512],
                start=True, stop=True), (GM(mg4), GM(mg1)) if n == 0 else (),
                inc=("t", 1) if n == 1 else None)
        ms = sch.add("scalar", lambda e: e.copy(s_t1[0:4, :], PA[0:4, :]),
                     (TM(mt1), GM(mz1)), inc=("s", 1))
        mt2 = None
        for n in range(2):
            mt2 = sch.add("tensor", lambda e, nn=n: e.matmul(
                PB[0:50, nn * 512:(nn + 1) * 512], s_luT[:, :], s_t1[:, nn * 512:(nn + 1) * 512],
                start=True, stop=True), (SM(ms), GM(mg2)) if n == 0 else (),
                inc=("t", 1) if n == 1 else None)
        ms2 = sch.add("scalar", lambda e: e.copy(s_t2[0:50, :], PB[0:50, :]),
                      (TM(mt2), GM(mz2)), inc=("s", 1))
        _ck(4)
        m_weff_v = None
        for mch in range(8):
            p = mch % 2
            wld = dma("sync", s_wtmp[p][:, :], wT_d[mch * 128:(mch + 1) * 128, :],
                      (wtmp_prev[p],), ch=f"lw{p}")
            mml = None
            for n in range(2):
                mml = sch.add("tensor", lambda e, nn=n, m_=mch: e.matmul(
                    PA[:, nn * 512:(nn + 1) * 512], s_t2[:, m_ * 128:(m_ + 1) * 128],
                    s_uaT[:, nn * 512:(nn + 1) * 512], start=True, stop=True),
                    ((SM(ms2), GM(mg3)) if mch == 0 else (VM(m_weff_v),))
                    if n == 0 else (),
                    inc=("t", 1) if n == 1 else None)
            m_weff_v = sch.add("vector", lambda e, m_=mch, pp=p: e.tensor_add(
                W_eff[:, m_ * 1024:(m_ + 1) * 1024], PA[:, :], s_wtmp[pp][:, :]),
                (TM(mml), LD(wld)), inc=("v", 1))
            wtmp_prev[p] = VM(m_weff_v)

        _ck(5)
        m_wdd_s = None
        for mch in range(8):
            mml = None
            for n in range(2):
                for k in range(8):
                    mml = sch.add("tensor", lambda e, nn=n, kk=k, m_=mch: e.matmul(
                        PB[:, nn * 512:(nn + 1) * 512],
                        Wp0[:, kk * 1024 + m_ * 128: kk * 1024 + (m_ + 1) * 128],
                        WdT[:, kk * 1024 + nn * 512: kk * 1024 + nn * 512 + 512],
                        start=(kk == 0), stop=(kk == 7)),
                        ((GM(mwp0), GM(mwdt), SM(ms2)) if mch == 0 else (SM(m_wdd_s),))
                        if (n == 0 and k == 0) else (),
                        inc=("t", 1) if (n == 1 and k == 7) else None)
            m_wdd_s = sch.add("scalar", lambda e, m_=mch: e.copy(
                W_dd[:, m_ * 1024:(m_ + 1) * 1024], PB[:, :]), (TM(mml),), inc=("s", 1))

        _ck(6)
        mwc = load_weight(WcT, wcT_d)
        mwr = load_weight(WrT, wrT_d)

        _ck(7)
        m_prevbc = 0

        def bcast_row(dst, rowidx, dcol=False):
            nonlocal_marks = {}
            mrl = dma("sync", s_row1[:, :], rows_d[rowidx:rowidx + 1, :],
                      (TM(m_prevbc), LD(m_startld)))
            mb = None
            for n in range(2):
                mb = sch.add("tensor", lambda e, nn=n: e.matmul(
                    PC[:, nn * 512:(nn + 1) * 512], s_o1r[:, :],
                    s_row1[0:1, nn * 512:(nn + 1) * 512], start=True, stop=True),
                    (LD(mrl), VM(sch.count("v"))) if n == 0 else (),
                    inc=("t", 1) if n == 1 else None)
            if dcol:
                mv = sch.add("vector", lambda e: e.tensor_copy(s_Dbc[:, :], PC[:, 0:1]),
                             (TM(mb),), inc=("v", 1))
            else:
                mv = sch.add("vector", lambda e: e.tensor_copy(dst[:, :], PC[:, :]), (TM(mb),),
                             inc=("v", 1))
            return mb

        m_prevbc = bcast_row(s_A1bc, 0)
        m_prevbc = bcast_row(s_B1bc, 1)
        m_prevbc = bcast_row(s_bdbc, 2)
        m_prevbc = bcast_row(s_bfbc, 3)
        m_prevbc = bcast_row(s_bvbc, 4)
        m_prevbc = bcast_row(None, 5, dcol=True)

        _ck(8)
        sch.barrier()
        es_fold.close()

        if phases >= 1:
            Cp = es_p1t.enter_context
            xb = [Cp(nc.sbuf_tensor(f"xb{p}", [128, 1024], F32)) for p in range(3)]
            xn_bf = [Cp(nc.sbuf_tensor(f"xn_bf{p}", [128, 1024], BF16)) for p in range(3)]
            xnT = [Cp(nc.sbuf_tensor(f"xnT{p}", [128, 8, 128], BF16)) for p in range(3)]
            delta_f = [Cp(nc.sbuf_tensor(f"delta_f{p}", [128, 1024], F32)) for p in range(2)]
            e1_f = [Cp(nc.sbuf_tensor("e1_f0", [128, 1024], F32))] * 2
            res_f = [Cp(nc.sbuf_tensor(f"res_f{p}", [128, 1024], F32)) for p in range(2)]
            cs_bf = [Cp(nc.sbuf_tensor("cs_bf0", [128, 1024], BF16))] * 2
            g0_bf = [Cp(nc.sbuf_tensor("g0_bf0", [128, 1024], BF16))] * 2
            m_bf = [Cp(nc.sbuf_tensor("m_bf0", [128, 1024], BF16))] * 2
            st1 = [Cp(nc.sbuf_tensor(f"st1_{p}", [128, 16], F32)) for p in range(3)]


            mk = {}

            def p1Aln(i):
                p = i % 3
                q3 = i % 3
                w_x = [LD(mk[("lx", i)]), SM(mk.get(("xn_c", i - 3))),
                       GM(mk.get(("m", i - 3))), TM(mk.get(("tp", i - 3)))]
                if i == 0:
                    w_x += [SM(m_wdd_s), GM(mwc), GM(mwr), VM(sch.count("v"))]
                mxn = ln(xb[p], xn_bf[q3], st1[q3], tuple(w_x))
                mk[("xn_c", i)] = mxn

            def p1Atp(i):
                q3 = i % 3
                PTt = PT1 if i % 2 == 0 else PT2
                mtp = transposes8(PTt, xn_bf[q3],
                                  (SM(mk[("xn_c", i)]), VM(mk.get(("xnTev", i - 2), 0))))
                mk[("tp", i)] = mtp
                mev = sch.add("vector", lambda e, qq=q3, pt=PTt: e.tensor_copy(
                    xnT[qq][:, :, :], pt[:, :, :]), (TM(mtp),), inc=("v", 1))
                mk[("xnTev", i)] = mev

            def p1B(i):
                p = i % 2
                q3 = i % 3
                mev = mk[("xnTev", i)]
                mdp = None
                for n in range(2):
                    for k in range(8):
                        mdp = sch.add("tensor", lambda e, nn=n, kk=k, qq=q3: e.matmul(
                            PA[:, nn * 512:(nn + 1) * 512], xnT[qq][:, kk, :],
                            W_dd[:, kk * 1024 + nn * 512: kk * 1024 + nn * 512 + 512],
                            start=(kk == 0), stop=(kk == 7)),
                            (VM(mev), VM(mk.get(("resev", i - 1), 0)))
                            if (n == 0 and k == 0) else (),
                            inc=("t", 1) if (n == 1 and k == 7) else None)
                mba = sch.add("vector", lambda e, pp=p: e.tensor_add(
                    delta_f[pp][:, :], PA[:, :], s_bdbc[:, :]),
                    (TM(mdp), GM(mk.get(("g0", i - 2), 0)), mk.get(("dst", i - 2))),
                    inc=("v", 1))
                mcs = None
                for n in range(2):
                    for k in range(8):
                        mcs = sch.add("tensor", lambda e, nn=n, kk=k, qq=q3: e.matmul(
                            PB[:, nn * 512:(nn + 1) * 512], xnT[qq][:, kk, :],
                            WcT[:, kk * 1024 + nn * 512: kk * 1024 + nn * 512 + 512],
                            start=(kk == 0), stop=(kk == 7)),
                            (SM(mk.get(("csbf", i - 1), 0)),) if (n == 0 and k == 0) else (),
                            inc=("t", 1) if (n == 1 and k == 7) else None)
                mex = sch.add("scalar", lambda e, pp=p: e.activation(
                    e1_f[pp][:, :], delta_f[pp][:, :], AF.Exp), (VM(mba),), inc=("s", 1))
                mln = sch.add("scalar", lambda e, pp=p: e.activation(
                    delta_f[pp][:, :], e1_f[pp][:, :], AF.Ln, bias=1.0), (SM(mex),),
                    inc=("s", 1))
                mk[("sp", i)] = mln
                mk[("dst", i)] = dma("scalar", sc_delta[i, :, :], delta_f[p][:, :], ch=f"sdl{p}")
                mcb = sch.add("scalar", lambda e, pp=p: e.copy(cs_bf[pp][:, :], PB[:, :]),
                              (TM(mcs), GM(mk.get(("m", i - 1), 0)),
                               TM(mk.get(("colsum", i - 1), 0))), inc=("s", 1))
                mk[("csbf", i)] = mcb
                mg0 = sch.add("gpsimd", lambda e, pp=p: e.tensor_mul(
                    g0_bf[pp][:, :], delta_f[pp][:, :], s_B1bc[:, :]),
                    (SM(mln), TM(mk.get(("colsum", i - 1), 0))), inc=("g", 1))
                mk[("g0", i)] = mg0
                mmm = sch.add("gpsimd", lambda e, pp=p, qq=q3: e.tensor_mul(
                    m_bf[pp][:, :], xn_bf[qq][:, :], cs_bf[pp][:, :]),
                    (SM(mcb), TM(mk.get(("colsum", i - 1), 0))), inc=("g", 1))
                mk[("m", i)] = mmm
                mrs = None
                for n in range(2):
                    for k in range(8):
                        mrs = sch.add("tensor", lambda e, nn=n, kk=k, qq=q3: e.matmul(
                            PA[:, nn * 512:(nn + 1) * 512], xnT[qq][:, kk, :],
                            WrT[:, kk * 1024 + nn * 512: kk * 1024 + nn * 512 + 512],
                            start=(kk == 0), stop=(kk == 7)),
                            (VM(mba),) if (n == 0 and k == 0) else (),
                            inc=("t", 1) if (n == 1 and k == 7) else None)
                mrev = sch.add("vector", lambda e, pp=p: e.tensor_copy(
                    res_f[pp][:, :], PA[:, :]),
                    (TM(mrs), mk.get(("rst", i - 2))), inc=("v", 1))
                mk[("resev", i)] = mrev
                mk[("rst", i)] = dma("scalar", sc_res[i, :, :], res_f[p][:, :], (VM(mrev),),
                                     ch=f"sre{p}")
                mcl = None
                for n in range(2):
                    mcl = sch.add("tensor", lambda e, nn=n, ii=i, pp=p: e.matmul(
                        PC[0:128, nn * 512:(nn + 1) * 512], s_emsk[:, 65 - ii:193 - ii],
                        g0_bf[pp][:, nn * 512:(nn + 1) * 512], start=(ii == 0), stop=False),
                        (GM(mg0),) if n == 0 else ())
                for n in range(2):
                    mcl = sch.add("tensor", lambda e, nn=n, ii=i, pp=p: e.matmul(
                        PC[0:128, nn * 512:(nn + 1) * 512], s_emsk[:, 1:129],
                        m_bf[pp][:, nn * 512:(nn + 1) * 512], start=False, stop=False),
                        (GM(mmm),) if n == 0 else ())
                for n in range(2):
                    mcl = sch.add("tensor", lambda e, nn=n, ii=i, pp=p: e.matmul(
                        PC[0:128, nn * 512:(nn + 1) * 512], s_emsk[:, 0:128],
                        cs_bf[pp][:, nn * 512:(nn + 1) * 512], start=False,
                        stop=(ii == NT - 1)),
                        (), inc=("t", 1) if n == 1 else None)
                mk[("colsum", i)] = mcl

            for i in range(5):
                if i < NT:
                    mk[("lx", i)] = dma("sync", xb[i % 3][:, :], x_d[i * 128:(i + 1) * 128, :],
                                        (SM(mk.get(("xn_c", i - 3))),), ch=f"lxb{i % 3}")
                if i < 3:
                    p1Aln(i)
            p1Atp(0)
            p1Atp(1)
            for i in range(NT):
                p1B(i)
                if i + 5 < NT:
                    mk[("lx", i + 5)] = dma("sync", xb[(i + 5) % 3][:, :],
                                            x_d[(i + 5) * 128:(i + 6) * 128, :],
                                            (SM(mk[("xn_c", i + 2)]),), ch=f"lxb{(i + 5) % 3}")
                if i + 3 < NT:
                    p1Aln(i + 3)
                if i + 2 < NT:
                    p1Atp(i + 2)

            sch.barrier()
            es_p1t.close()
            es_p1w.close()

            C2a = lambda t: R(es_p2a, t)
            tbl_sb = C2a(nc.sbuf_tensor("tbl_sb", [128, 1024], F32, side="right"))
            car0_sb = C2a(nc.sbuf_tensor("car0_sb", [64, 1024], F32, side="right"))
            l1bc_sb = C2a(nc.sbuf_tensor("l1bc_sb", [64, 1024], F32, side="right"))
            tadd_sb = C2a(nc.sbuf_tensor("tadd_sb", [64, 1024], F32, side="right"))
            hi_bf = C2a(nc.sbuf_tensor("hi_bf", [64, 1024], BF16, side="right"))
            lo_bf = C2a(nc.sbuf_tensor("lo_bf", [64, 1024], BF16, side="right"))
            hi32_sb = C2a(nc.sbuf_tensor("hi32_sb", [64, 1024], F32, side="right"))
            b0_sb = C2a(nc.sbuf_tensor("b0_sb", [1, 6144], F32, side="right"))
            ucbc_sb = C2a(nc.sbuf_tensor("ucbc_sb", [128, 1024], F32, side="right"))

            mtb = sch.add("vector", lambda e: e.tensor_copy(tbl_sb[0:66, :], PC[0:66, :]), (),
                          inc=("v", 1))
            mu1 = dma("sync", b0_sb[0:1, 0:1024], tbl_sb[64:65, :], (VM(mtb),))
            mu2 = dma("sync", b0_sb[0:1, 1024:2048], tbl_sb[65:66, :])
            mpf = None
            for n in range(2):
                mpf = sch.add("tensor", lambda e, nn=n: e.matmul(
                    PA[0:64, nn * 512:(nn + 1) * 512], s_tri64[:, :],
                    tbl_sb[0:64, nn * 512:(nn + 1) * 512], start=True, stop=True),
                    (VM(mtb),) if n == 0 else (), inc=("t", 1) if n == 1 else None)
            mc0 = sch.add("vector", lambda e: e.tensor_copy(car0_sb[:, :], PA[0:64, :]), (TM(mpf),),
                          inc=("v", 1))
            ml1 = dma("sync", b0_sb[0:1, 2048:3072], car0_sb[32:33, :], (VM(mc0),))
            ml2 = dma("sync", b0_sb[0:1, 3072:4096], car0_sb[63:64, :])
            ml3 = dma("sync", b0_sb[0:1, 4096:5120], tbl_sb[63:64, :])
            sch.add("vector", lambda e: e.tensor_add(b0_sb[0:1, 5120:6144], b0_sb[0:1, 3072:4096],
                                                     b0_sb[0:1, 4096:5120]), (LD(ml3),), inc=("v", 1))
            mt_ = sch.add("vector", lambda e: e.tensor_sub(b0_sb[0:1, 5120:6144],
                                                           b0_sb[0:1, 5120:6144],
                                                           b0_sb[0:1, 2048:3072]), inc=("v", 1))
            mb1 = None
            for n in range(2):
                mb1 = sch.add("tensor", lambda e, nn=n: e.matmul(
                    PB[0:64, nn * 512:(nn + 1) * 512], s_o64[:, :],
                    b0_sb[0:1, 2048 + nn * 512:2048 + (nn + 1) * 512], start=True, stop=True),
                    (VM(mt_),) if n == 0 else (), inc=("t", 1) if n == 1 else None)
            mevl1 = sch.add("vector", lambda e: e.tensor_copy(l1bc_sb[:, :], PB[0:64, :]), (TM(mb1),),
                            inc=("v", 1))
            mb2 = None
            for n in range(2):
                mb2 = sch.add("tensor", lambda e, nn=n: e.matmul(
                    PB[0:64, nn * 512:(nn + 1) * 512], s_o64[:, :],
                    b0_sb[0:1, 5120 + nn * 512:5120 + (nn + 1) * 512], start=True, stop=True),
                    (VM(mevl1),) if n == 0 else (), inc=("t", 1) if n == 1 else None)
            mta = sch.add("vector", lambda e: e.scalar_tensor_tensor(
                tadd_sb[:, :], PB[0:64, :], s_m1[:, :], car0_sb[:, :], ALU.mult, ALU.add),
                (TM(mb2),), inc=("v", 1))
            mta = sch.add("vector", lambda e: e.scalar_tensor_tensor(
                tadd_sb[:, :], l1bc_sb[:, :], s_m2n[:, :], tadd_sb[:, :], ALU.mult, ALU.add),
                inc=("v", 1))
            mhi = sch.add("scalar", lambda e: e.copy(hi_bf[:, :], tadd_sb[:, :]), (VM(mta),),
                          inc=("s", 1))
            mhi2 = sch.add("scalar", lambda e: e.copy(hi32_sb[:, :], hi_bf[:, :]), (SM(mhi),),
                           inc=("s", 1))
            mlo = sch.add("vector", lambda e: e.tensor_sub(lo_bf[:, :], tadd_sb[:, :], hi32_sb[:, :]),
                          (SM(mhi2),), inc=("v", 1))
            mst_hi = dma("scalar", sc_hi[:, :], hi_bf[:, :], (SM(mhi),))
            mst_lo = dma("scalar", sc_lo[:, :], lo_bf[:, :], (VM(mlo),))
            mcb_ = None
            for n in range(2):
                mcb_ = sch.add("tensor", lambda e, nn=n: e.matmul(
                    PC[:, nn * 512:(nn + 1) * 512], s_o1r[:, :],
                    b0_sb[0:1, 1024 + nn * 512:1024 + (nn + 1) * 512], start=True, stop=True),
                    (LD(mu2), VM(mtb)) if n == 0 else (), inc=("t", 1) if n == 1 else None)
            mcsbc = sch.add("vector", lambda e: e.tensor_copy(s_Csbc[:, :], PC[:, :]), (TM(mcb_),),
                            inc=("v", 1))
            mub_ = None
            for n in range(2):
                mub_ = sch.add("tensor", lambda e, nn=n: e.matmul(
                    PC[:, nn * 512:(nn + 1) * 512], s_o1r[:, :],
                    b0_sb[0:1, 0 + nn * 512:0 + (nn + 1) * 512], start=True, stop=True),
                    (VM(mcsbc), LD(mu1)) if n == 0 else (), inc=("t", 1) if n == 1 else None)
            mucbc = sch.add("vector", lambda e: e.tensor_copy(ucbc_sb[:, :], PC[:, :]), (TM(mub_),),
                            inc=("v", 1))
            mr2 = sch.add("gpsimd", lambda e: e.tensor_mul(s_R2bc[:, :], s_B1bc[:, :], ucbc_sb[:, :]),
                          (VM(mucbc),), inc=("g", 1))
            if dbg:
                dma("scalar", dbg_tbl[0:66, :], tbl_sb[0:66, :], (VM(mtb),))
                dma("scalar", dbg_bc[:, 0:1024], s_Csbc[:, :], (VM(mcsbc),))
                dma("scalar", dbg_bc[:, 1024:2048], ucbc_sb[:, :], (VM(mucbc),))
                dma("scalar", dbg_bc[:, 2048:3072], s_R2bc[:, :], (GM(mr2),))
                dma("scalar", dbg_bc[0:64, 3072:4096], car0_sb[0:64, :], (VM(mc0),))

            sch.barrier()
            es_p2a.close()
            es_p1t_dummy = None


        if phases >= 2:
            C2 = es_p2.enter_context
            A_ = [C2(nc.sbuf_tensor(f"p2A{p}", [128, 1024], F32)) for p in range(2)]
            B_ = [C2(nc.sbuf_tensor(f"p2B{p}", [128, 1024], F32)) for p in range(2)]
            Cc_ = [C2(nc.sbuf_tensor(f"p2C{p}", [128, 1024], F32)) for p in range(2)]
            D_ = [C2(nc.sbuf_tensor(f"p2D{p}", [128, 1024], F32)) for p in range(2)]
            E_ = [C2(nc.sbuf_tensor(f"p2E{p}", [128, 1024], F32)) for p in range(2)]
            F_ = [C2(nc.sbuf_tensor(f"p2F{p}", [128, 1024], F32)) for p in range(3)]
            G_ = [C2(nc.sbuf_tensor(f"p2G{p}", [128, 1024], F32)) for p in range(3)]
            H_ = [C2(nc.sbuf_tensor(f"p2H{p}", [128, 1024], F32)) for p in range(2)]
            g0b = [C2(nc.sbuf_tensor(f"p2g0{p}", [128, 1024], BF16)) for p in range(2)]
            hilo = [C2(nc.sbuf_tensor(f"hilo{p}", [2, 1024], BF16)) for p in range(2)]
            xn2b = [C2(nc.sbuf_tensor(f"xn2b{p}", [128, 1024], BF16)) for p in range(2)]
            xn2T = [C2(nc.sbuf_tensor(f"xn2T{p}", [128, 8, 128], BF16)) for p in range(2)]
            xn3b = [C2(nc.sbuf_tensor(f"xn3b{p}", [128, 1024], BF16)) for p in range(2)]
            st2 = [C2(nc.sbuf_tensor(f"st2_{p}", [128, 16], F32)) for p in range(2)]
            st3 = [C2(nc.sbuf_tensor(f"st3_{p}", [128, 16], F32)) for p in range(2)]

            mwq = load_weight(WqT, wqT_d)
            mwk = load_weight(WkT, wkT_d)
            mwv = load_weight(WvT, wvT_d)

            mk2 = {}
            for i in range(2):
                mk2[("lx", i)] = dma("sync", A_[i][:, :], x_d[i * 128:(i + 1) * 128, :],
                                     ch=f"lA{i}")
                mk2[("lhi", i)] = dma("sync", hilo[i][0:1, :], sc_hi[i:i + 1, :],
                                      (mst_hi, mst_lo), ch=f"lh{i}")
                mk2[("llo", i)] = dma("sync", hilo[i][1:2, :], sc_lo[i:i + 1, :], ch=f"lh{i}")
                mk2[("ldl", i)] = dma("vector", B_[i][:, :], sc_delta[i, :, :],
                                      (mk[("dst", i)],), ch=f"lB{i}")
                mk2[("lr", i)] = dma("vector", Cc_[i][:, :], sc_res[i, :, :],
                                     (mk[("rst", i)],), ch=f"lC{i}")


            def p2A(i):
                p = i % 2
                q3 = i % 3
                PD = PA if p == 0 else PC
                mg0 = sch.add("gpsimd", lambda e, pp=p: e.tensor_mul(
                    g0b[pp][:, :], B_[pp][:, :], s_B1bc[:, :]),
                    (VD(mk2[("ldl", i)]), TM(mk2.get(("tri", i - 2), 0)),
                     GM(mr2) if i == 0 else None), inc=("g", 1))
                mtr = None
                for n in range(2):
                    mtr = sch.add("tensor", lambda e, nn=n, pp=p, PD_=PD: e.matmul(
                        PD_[:, nn * 512:(nn + 1) * 512], s_tri128[:, :],
                        g0b[pp][:, nn * 512:(nn + 1) * 512], start=True, stop=False),
                        (GM(mg0), VM(mk2.get(("ydb", i - 2), 0))) if n == 0 else ())
                for n in range(2):
                    mtr = sch.add("tensor", lambda e, nn=n, pp=p, PD_=PD: e.matmul(
                        PD_[:, nn * 512:(nn + 1) * 512], s_o2r[:, :],
                        hilo[pp][:, nn * 512:(nn + 1) * 512], start=False, stop=True),
                        (LD(mk2[("llo", i)]),) if n == 0 else (),
                        inc=("t", 1) if n == 1 else None)
                mk2[("tri", i)] = mtr
                mei = sch.add("gpsimd", lambda e, pp=p: e.tensor_mul(
                    D_[pp][:, :], B_[pp][:, :], s_A1bc[:, :]), (), inc=("g", 1))
                mk2[("eIn", i)] = mei
                mev_ = sch.add("scalar", lambda e, pp=p: e.activation(
                    E_[pp][:, :], D_[pp][:, :], AF.Exp), (GM(mei),), inc=("s", 1))
                mydb = sch.add("vector", lambda e, pp=q3, PD_=PD: e.tensor_mul(
                    G_[pp][:, :], PD_[:, :], s_Csbc[:, :]),
                    (TM(mtr), GM(mk2.get(("x2", i - 3), 0))), inc=("v", 1))
                mk2[("ydb", i)] = mydb
                my1 = sch.add("gpsimd", lambda e, pp=p, qq=q3: e.tensor_mul(
                    F_[qq][:, :], E_[pp][:, :], s_R2bc[:, :]),
                    (SM(mev_), VM(mk2.get(("x2a", i - 3), 0))), inc=("g", 1))
                my12 = sch.add("gpsimd", lambda e, pp=p, qq=q3: e.tensor_add(
                    D_[pp][:, :], G_[qq][:, :], F_[qq][:, :]), (VM(mydb),), inc=("g", 1))
                mx1a = sch.add("vector", lambda e, pp=p: e.scalar_tensor_tensor(
                    E_[pp][:, :], Cc_[pp][:, :], s_Dbc[:, :], D_[pp][:, :], ALU.mult, ALU.add),
                    (VD(mk2[("lr", i)]), GM(my12), SM(mev_)), inc=("v", 1))
                mk2[("x1a", i)] = mx1a
                mx1 = sch.add("gpsimd", lambda e, pp=p, qq=q3: e.tensor_add(
                    F_[qq][:, :], E_[pp][:, :], A_[pp][:, :]),
                    (LD(mk2[("lx", i)]), VM(mx1a)), inc=("g", 1))
                mk2[("x1", i)] = mx1
                if dbg:
                    dma("scalar", sc_x1[i, :, :], F_[q3][:, :], (GM(mx1),))

            def p2Bln(i):
                p = i % 2
                q3 = i % 3
                mxn2 = ln(F_[q3], xn2b[p], st2[p], (GM(mk2[("x1", i)]),
                                                   TM(mk2.get(("tp2", i - 2), 0))))
                mk2[("xn2", i)] = mxn2

            def p2Btp(i):
                p = i % 2
                PTx = PT1 if p == 0 else PT2
                mxn2 = mk2[("xn2", i)]
                mtp2 = transposes8(PTx, xn2b[p], (SM(mxn2), SM(mk2.get(("ev2", i - 2), 0)),
                                                  TM(mk2.get(("f1", i - 2), 0))))
                mk2[("tp2", i)] = mtp2
                mev2 = sch.add("scalar", lambda e, pp=p, pt=PTx: e.copy(
                    xn2T[pp][:, :, :], pt[:, :, :]),
                               (TM(mtp2),), inc=("s", 1))
                mk2[("ev2", i)] = mev2

            def p2B(i):
                p = i % 2
                q3 = i % 3
                mev2 = mk2[("ev2", i)]
                mf1 = None
                for n in range(2):
                    for k in range(8):
                        mf1 = sch.add("tensor", lambda e, nn=n, kk=k, pp=p: e.matmul(
                            PB[:, nn * 512:(nn + 1) * 512], xn2T[pp][:, kk, :],
                            W_eff[:, kk * 1024 + nn * 512: kk * 1024 + nn * 512 + 512],
                            start=(kk == 0), stop=(kk == 7)),
                            (SM(mev2), VM(mk2.get(("x2a", i - 1), 0)), VM(m_weff_v))
                            if (n == 0 and k == 0) else (),
                            inc=("t", 1) if (n == 1 and k == 7) else None)
                mk2[("f1", i)] = mf1
                mx2a = sch.add("vector", lambda e, qq=q3: e.tensor_add(
                    G_[qq][:, :], PB[:, :], s_bfbc[:, :]), (TM(mf1),), inc=("v", 1))
                mk2[("x2a", i)] = mx2a
                mx2 = sch.add("gpsimd", lambda e, pp=p, qq=q3: e.tensor_add(
                    H_[pp][:, :], G_[qq][:, :], F_[qq][:, :]),
                    (VM(mx2a), mk2.get(("sx2", i - 2))), inc=("g", 1))
                mk2[("x2", i)] = mx2

            def p2C(i):
                p = i % 2
                mxn3 = ln(H_[p], xn3b[p], st3[p], (GM(mk2[("x2", i)]),
                                                   mk2.get(("sxn3", i - 2))))
                mk2[("sxn3", i)] = dma("scalar", sc_xn3[i, :, :], xn3b[p][:, :], ch=f"sx3{p}")
                if i < NQ:
                    mk2[("sx2", i)] = dma("scalar", sc_x2[i, :, :], H_[p][:, :],
                                          (GM(mk2[("x2", i)]),), ch=f"sx2{p}")

            def p2loads(i):
                p = i % 2
                mk2[("lx", i)] = dma("sync", A_[p][:, :], x_d[i * 128:(i + 1) * 128, :],
                                     (GM(mk2[("x1", i - 2)]),), ch=f"lA{p}")
                mk2[("lhi", i)] = dma("sync", hilo[p][0:1, :], sc_hi[i:i + 1, :],
                                      (TM(mk2[("tri", i - 2)]),), ch=f"lh{p}")
                mk2[("llo", i)] = dma("sync", hilo[p][1:2, :], sc_lo[i:i + 1, :], ch=f"lh{p}")
                mk2[("ldl", i)] = dma("vector", B_[p][:, :], sc_delta[i, :, :],
                                      (GM(mk2[("eIn", i - 2)]), mk[("dst", i)]), ch=f"lB{p}")
                mk2[("lr", i)] = dma("vector", Cc_[p][:, :], sc_res[i, :, :],
                                     (VM(mk2[("x1a", i - 2)]), mk[("rst", i)]), ch=f"lC{p}")

            p2A(0)
            p2A(1)
            p2loads(2)
            p2A(2)
            p2Bln(0)
            p2Bln(1)
            p2Btp(0)
            p2Btp(1)
            p2B(0)
            for i in range(NT):
                p2C(i)
                if i + 3 < NT:
                    p2loads(i + 3)
                    p2A(i + 3)
                if i + 2 < NT:
                    p2Bln(i + 2)
                    p2Btp(i + 2)
                if i + 1 < NT:
                    p2B(i + 1)

            sch.barrier()
            es_p2.close()
            es_bc.close()
            es_wtmp.close()

        if phases >= 3:
            xg3T = [R(es_prep, nc.sbuf_tensor(f"xg3T{p}", [128, 8, 512], BF16, side="right"))
                    for p in range(2)]
            xq_in = [R(es_prep, nc.sbuf_tensor(f"xq_in{p}", [128, 1024], BF16, side="right"))
                     for p in range(2)]
            qsp = [R(es_qsp, nc.sbuf_tensor(f"qsp{p}", [128, 8, 512], BF16, side="right"))
                   for p in range(2)]

            mk3 = {"tcnt": 0, "vevlast": 0}

            def prep_load(ti, tc):
                mk3[("gld", tc)] = dma("sync", xq_in[tc % 2][:, :], sc_xn3[ti, :, :],
                                       (TM(mk3.get(("gtp", tc - 2), 0)), mk2[("sxn3", ti)]),
                                       ch=f"lxq{tc % 2}")

            def prep_group(tiles, pg):
                last = None
                tiles = list(tiles)
                seq = mk3.get("prepseq", {})
                for j, ti in enumerate(tiles):
                    tc = mk3["tcnt"]
                    tl = tc % 4
                    for tq in (tc, tc + 1):
                        if ("gld", tq) not in mk3 and tq in seq:
                            prep_load(seq[tq], tq)
                    PTp = PT2 if tc % 2 == 0 else PT1
                    mtp = transposes8(PTp, xq_in[tc % 2],
                                      (LD(mk3[("gld", tc)]), SM(mk3.get(("gev", tc - 2), 0)),
                                       SM(mk3.get("pevlast", 0)),
                                       SM(mk3.get("ev4last", 0))))
                    mk3[("gtp", tc)] = mtp
                    last = sch.add("scalar",
                                   lambda e, tt=tl, pp=pg, pt=PTp: e.copy(
                                       xg3T[pp][:, :, tt * 128:(tt + 1) * 128], pt[:, :, :]),
                                   (TM(mtp),), inc=("s", 1))
                    mk3[("gev", tc)] = last
                    mk3["tcnt"] = tc + 1
                    if ("gld", tc + 2) not in mk3 and (tc + 2) in seq:
                        prep_load(seq[tc + 2], tc + 2)
                return last

            mk3["prepseq"] = {j: j for j in range(32)}
            for g in range(8):
                pg = g % 2
                mev = prep_group(range(4 * g, 4 * g + 4), pg)
                for c in range(8):
                    half = c % 2
                    mq = None
                    for k in range(8):
                        mq = sch.add("tensor", lambda e, cc=c, kk=k, hh=half, pp=pg: e.matmul(
                            PA[:, hh * 512:(hh + 1) * 512],
                            WqT[:, kk * 1024 + cc * 128: kk * 1024 + (cc + 1) * 128],
                            xg3T[pp][:, kk, :], start=(kk == 0), stop=(kk == 7)),
                            ((SM(mev), GM(mwq)) if c == 0 else
                             (SM(mk3.get(("qev", g, c - 2), 0)),)) if k == 0 else (),
                            inc=("t", 1) if k == 7 else None)
                    mqe = sch.add("scalar", lambda e, cc=c, hh=half, pp=pg: e.activation(
                        qsp[pp][:, cc, :], PA[:, hh * 512:(hh + 1) * 512],
                        AF.Identity, bias=s_bqT[:, cc:cc + 1], scale=1.0),
                        (TM(mq), mk3.get(("qst", g - 2))), inc=("s", 1))
                    mk3[("qev", g, c)] = mqe
                for t in range(4):
                    mk3[("qst", g)] = dma("scalar", sc_qT[4 * g + t, :, :, :],
                                          qsp[pg][:, :, t * 128:(t + 1) * 128],
                                          (SM(mk3[("qev", g, 7)]),) if t == 0 else (),
                                          ch=f"sq{pg}")
            sch.barrier()
            es_qsp.close()
            es_pqw.close()
            C3 = es_p3.enter_context
            kT_res = C3(nc.sbuf_tensor("kT_res", [128, 8, 2048], BF16))
            v_res = C3(nc.sbuf_tensor("v_res", [128, 16384], BF16))
            qT_t = [C3(nc.sbuf_tensor(f"qT_t{p}", [128, 8, 128], BF16)) for p in range(2)]
            P_bf = [C3(nc.sbuf_tensor(f"P_bf{p}", [128, 2048], BF16)) for p in range(2)]
            pt_sb = [C3(nc.sbuf_tensor(f"pt_sb{p}", [128, 16, 128], BF16))
                     for p in range(2)]
            zs = [C3(nc.sbuf_tensor(f"zs{p}", [128, 1024], F32)) for p in range(2)]
            zl = [C3(nc.sbuf_tensor(f"zl{p}", [128, 1024], F32)) for p in range(3)]
            x2l = [C3(nc.sbuf_tensor(f"x2l{p}", [128, 1024], F32)) for p in range(2)]
            x3b = [C3(nc.sbuf_tensor(f"x3b{p}", [128, 1024], F32)) for p in range(2)]
            xn4b = [C3(nc.sbuf_tensor("xn4b0", [128, 1024], BF16))] * 2
            xn4T = [pt_sb[0][:, 0:8, :], pt_sb[1][:, 0:8, :]]
            st4 = [C3(nc.sbuf_tensor(f"st4_{p}", [128, 16], F32)) for p in range(2)]
            m_run = C3(nc.sbuf_tensor("m_run", [128, 32], F32))
            l_run = C3(nc.sbuf_tensor("l_run", [128, 32], F32))
            cmax = C3(nc.sbuf_tensor("cmax", [128, 8], F32))
            lbuf = C3(nc.sbuf_tensor("lbuf", [128, 8], F32))
            pstat = C3(nc.sbuf_tensor("pstat", [128, 8], F32))

            for r in range(KR):
                lastq = r * NQ - 1
                mk3["prepseq"] = {mk3["tcnt"] + j: 16 * r + j for j in range(16)}
                for gg in range(4):
                    pg = (8 + r * 4 + gg) % 2
                    mev = prep_group(range(16 * r + 4 * gg, 16 * r + 4 * gg + 4), pg)
                    for c in range(8):
                        half = c % 2
                        mq = None
                        for k in range(8):
                            mq = sch.add("tensor", lambda e, cc=c, kk=k, hh=half, pp=pg: e.matmul(
                                PA[:, hh * 512:(hh + 1) * 512],
                                WkT[:, kk * 1024 + cc * 128: kk * 1024 + (cc + 1) * 128],
                                xg3T[pp][:, kk, :], start=(kk == 0), stop=(kk == 7)),
                                ((SM(mev), GM(mwk),
                                  SM(mk3.get(("exp", lastq, 3), 0)),
                                  VM(mk3.get(("rmx", lastq, 3), 0))) if c == 0 else
                                 (SM(mk3.get(("kev", gg, c - 2), 0)),)) if k == 0 else (),
                                inc=("t", 1) if k == 7 else None)
                        mqe = sch.add("scalar", lambda e, cc=c, hh=half, go=gg: e.activation(
                            kT_res[:, cc, go * 512:(go + 1) * 512],
                            PA[:, hh * 512:(hh + 1) * 512],
                            AF.Identity, bias=s_bkT[:, cc:cc + 1], scale=1.0),
                            (TM(mq),), inc=("s", 1))
                        mk3[("kev", gg, c)] = mqe
                    for t in range(4):
                        mv = None
                        for n in range(2):
                            for k in range(8):
                                mv = sch.add("tensor", lambda e, nn=n, kk=k, tt=t, pp=pg: e.matmul(
                                    PB[:, nn * 512:(nn + 1) * 512],
                                    xg3T[pp][:, kk, tt * 128:(tt + 1) * 128],
                                    WvT[:, kk * 1024 + nn * 512: kk * 1024 + nn * 512 + 512],
                                    start=(kk == 0), stop=(kk == 7)),
                                    ((GM(mwv), VM(mk3["vevlast"]),
                                      SM(mk3.get(("exp", lastq, 3), 0)),
                                      VM(mk3.get(("rmx", lastq, 3), 0)))
                                     if (n == 0 and k == 0) else ()),
                                    inc=("t", 1) if (n == 1 and k == 7) else None)
                        mve = sch.add("vector", lambda e, tt=t, go=gg: e.tensor_tensor(
                            v_res[:, (4 * go + tt) * 1024:(4 * go + tt + 1) * 1024],
                            PB[:, :], s_bvbc[:, :], ALU.add), (TM(mv),), inc=("v", 1))
                        mk3["vevlast"] = mve
                mk3["kevlast"] = mk3[("kev", 3, 7)]
                mk3["vevlast2"] = mk3["vevlast"]
                mk3["gevlast"] = mk3[("gev", mk3["tcnt"] - 1)]


                def qS(step):
                    qi = step - r * NQ
                    p = step % 2
                    ac = 3 + 3 * (step % 2)
                    mlq = dma("vector", qT_t[p][:, :, :], sc_qT[qi, :, :, :],
                              (TM(mk3.get(("smm", step - 2), 0)), mk3[("qst", qi // 4)]),
                              ch=f"lqt{p}")
                    if r > 0:
                        mk3[("lz", step)] = dma("vector", zl[step % 3][:, :], sc_z[qi, :, :],
                                                (VM(mk3.get(("zco", step - 3), 0)),
                                                 mk3[("zst", (r - 1) * NQ + qi)]),
                                                ch=f"lzl{step % 3}")
                    if r == KR - 1:
                        mk3[("lx2", step)] = dma("vector", x2l[p][:, :], sc_x2[qi, :, :],
                                                 (VM(mk3.get(("x3", step - 2), 0)),
                                                  mk2[("sx2", qi)]), ch=f"lx2{p}")
                    msmm = None
                    for n in range(4):
                        ps_ = PA if n < 2 else PB
                        off = (n % 2) * 512
                        for c in range(8):
                            msmm = sch.add("tensor", lambda e, nn=n, cc=c, o=ps_, of=off, pp=p:
                                           e.matmul(o[:, of:of + 512], qT_t[pp][:, cc, :],
                                                    kT_res[:, cc, nn * 512:(nn + 1) * 512],
                                                    start=(cc == 0), stop=(cc == 7)),
                                           ((VD(mlq), SM(mk3.get(("exp", step - 1, n), 0)),
                                             VM(mk3.get(("rmx", step - 1, n), 0)),
                                             SM(mk3.get("kevlast", 0)),
                                             VM(mk3.get("vevlast2", 0)))
                                            if c == 0 else ()),
                                           inc=("t", 1) if c == 7 else None)
                        mk3[("smmn", step, n)] = msmm
                        mrx = sch.add("vector", lambda e, nn=n, o=ps_, of=off: e.reduce_max(
                            cmax[:, nn:nn + 1], o[:, of:of + 512], AX), (TM(msmm),),
                            inc=("v", 1))
                        mk3[("rmx", step, n)] = mrx
                    mk3[("smm", step)] = msmm
                    mmx = sch.add("vector", lambda e: e.reduce_max(pstat[:, 0:1], cmax[:, 0:4],
                                                                   AX),
                                  (VM(mk3[("rmx", step, 3)]),), inc=("v", 1))
                    if r > 0:
                        mmx = sch.add("vector", lambda e, q_=qi: e.tensor_max(
                            pstat[:, 0:1], pstat[:, 0:1], m_run[:, q_:q_ + 1]), inc=("v", 1))
                        mdm = sch.add("vector", lambda e, q_=qi: e.tensor_sub(
                            pstat[:, 2:3], m_run[:, q_:q_ + 1], pstat[:, 0:1]), inc=("v", 1))
                        mal = sch.add("scalar", lambda e, a_=ac: e.activation(
                            pstat[:, a_:a_ + 1], pstat[:, 2:3], AF.Exp, scale=ISQ),
                            (VM(mdm),), inc=("s", 1))
                        mk3[("al", step)] = mal
                    mng = sch.add("scalar", lambda e: e.activation(pstat[:, 1:2], pstat[:, 0:1],
                                                                   AF.Identity, scale=-ISQ),
                                  (VM(mmx),), inc=("s", 1))
                    sch.add("scalar", lambda e, q_=qi: e.activation(
                        m_run[:, q_:q_ + 1], pstat[:, 0:1], AF.Identity, scale=1.0),
                        (SM(mng),), inc=("s", 1))
                    for n in range(4):
                        ps_ = PA if n < 2 else PB
                        off = (n % 2) * 512
                        mex = sch.add("scalar", lambda e, nn=n, o=ps_, of=off, pp=p:
                                      e.activation(
                                          P_bf[pp][:, nn * 512:(nn + 1) * 512],
                                          o[:, of:of + 512], AF.Exp,
                                          bias=pstat[:, 1:2], scale=ISQ,
                                          accum_out=lbuf[:, nn:nn + 1]),
                                      (TM(mk3.get(("ptp", step - 2), 0)), SM(mng),
                                       VM(mk3.get(("la", step - 1), 0))) if n == 0 else (),
                                      inc=("s", 1))
                        mk3[("exp", step, n)] = mex
                    mla = sch.add("vector", lambda e: e.reduce_sum(pstat[:, 4:5], lbuf[:, 0:4],
                                                                   AX),
                                  (SM(mk3[("exp", step, 3)]),), inc=("v", 1))
                    mk3[("la", step)] = mla
                    if r == 0:
                        sch.add("scalar", lambda e, q_=qi: e.activation(
                            l_run[:, q_:q_ + 1], pstat[:, 4:5], AF.Identity, scale=1.0),
                            (VM(mla),), inc=("s", 1))
                        mk3[("lr", step)] = ("v", mla)
                    else:
                        sch.add("vector", lambda e, q_=qi, a_=ac: e.tensor_mul(
                            l_run[:, q_:q_ + 1], l_run[:, q_:q_ + 1], pstat[:, a_:a_ + 1]),
                            (SM(mk3[("al", step)]),), inc=("v", 1))
                        mlr = sch.add("vector", lambda e, q_=qi: e.tensor_add(
                            l_run[:, q_:q_ + 1], l_run[:, q_:q_ + 1], pstat[:, 4:5]),
                            (VM(mla),), inc=("v", 1))
                        mk3[("lr", step)] = ("v", mlr)

                def qB(step):
                    p = step % 2
                    mpt = None
                    for j in range(16):
                        pt = PT1 if j < 8 else PT2
                        mpt = sch.add("tensor", lambda e, jj=j, pt_=pt, pp=p: e.transpose(
                            pt_[:, jj % 8, :], P_bf[pp][:, jj * 128:(jj + 1) * 128],
                            s_id[:, :]),
                            ((SM(mk3[("exp", step, 3)]), SM(mk3.get(("pev", step - 1), 0)),
                              SM(mk3.get(("ev4", step - 1), 0)), SM(mk3.get("gevlast", 0)))
                             if j == 0 else ()),
                            inc=("t", 1) if j == 15 else None)
                    mk3[("ptp", step)] = mpt
                    sch.add("scalar", lambda e, pp=p: e.copy(pt_sb[pp][:, 0:8, :],
                                                             PT1[:, :, :]),
                            (TM(mpt),), inc=("s", 1))
                    mpe = sch.add("scalar", lambda e, pp=p: e.copy(pt_sb[pp][:, 8:16, :],
                                                                   PT2[:, :, :]),
                                  inc=("s", 1))
                    mk3[("pev", step)] = mpe

                def qC(step):
                    qi = step - r * NQ
                    p = step % 2
                    ac = 3 + 3 * (step % 2)
                    mpe = mk3[("pev", step)]
                    mpv = None
                    for j in range(16):
                        for n in range(2):
                            mpv = sch.add("tensor", lambda e, jj=j, nn=n, pp=p: e.matmul(
                                PC[:, nn * 512:(nn + 1) * 512], pt_sb[pp][:, jj, :],
                                v_res[:, jj * 1024 + nn * 512: jj * 1024 + nn * 512 + 512],
                                start=(jj == 0), stop=(jj == 15)),
                                ((SM(mpe), VM(mk3.get(("zco", step - 1), 0)),
                                  VM(mk3.get(("mo1", step - 1), 0)))
                                 if (j == 0 and n == 0) else ()),
                                inc=("t", 1) if (j == 15 and n == 1) else None)
                    if r == 0:
                        mz = sch.add("vector", lambda e, pp=p: e.tensor_copy(zs[pp][:, :],
                                                                             PC[:, :]),
                                     (TM(mpv), mk3.get(("zst", step - 2))), inc=("v", 1))
                    else:
                        mz = sch.add("vector", lambda e, pp=p, z3=step % 3, a_=ac:
                                     e.scalar_tensor_tensor(
                            zs[pp][:, :], zl[z3][:, :], pstat[:, a_:a_ + 1], PC[:, :],
                            ALU.mult, ALU.add),
                            (TM(mpv), mk3.get(("lz", step)), SM(mk3[("al", step)]),
                             mk3.get(("zst", step - 2)),
                             GM(mk3.get(("moo", step - 2), 0))),
                            inc=("v", 1))
                    mk3[("zco", step)] = mz
                    if r < KR - 1:
                        mk3[("zst", step)] = dma("scalar", sc_z[qi, :, :], zs[p][:, :],
                                                 (VM(mz),), ch=f"sz{p}")
                    else:
                        mli = sch.add("vector", lambda e, q_=qi: e.reciprocal(
                            pstat[:, 5:6], l_run[:, q_:q_ + 1]),
                            (mk3[("lr", step)],), inc=("v", 1))
                        mx3 = sch.add("vector", lambda e, pp=p: e.scalar_tensor_tensor(
                            x3b[pp][:, :], zs[pp][:, :], pstat[:, 5:6], x2l[pp][:, :],
                            ALU.mult, ALU.add),
                            (VM(mli), mk3.get(("lx2", step)),
                             GM(mk3.get(("moo", step - 2), 0))), inc=("v", 1))
                        mk3[("x3", step)] = mx3
                        if dbg:
                            mk3[("zdbg", step)] = dma("scalar", sc_z[qi, :, :], zs[p][:, :],
                                                      (VM(mx3),), ch=f"sz{p}")
                        mxn4 = ln(x3b[p], xn4b[p], st4[p],
                                  (VM(mx3), TM(mk3.get(("tp4", step - 1), 0))))
                        mtp4 = transposes8(PT1, xn4b[p], (SM(mxn4),))
                        mk3[("tp4", step)] = mtp4
                        mev4 = sch.add("scalar", lambda e, pp=p: e.copy(xn4T[pp],
                                                                        PT1[:, :, :]),
                                       (TM(mtp4),), inc=("s", 1))
                        mk3[("ev4", step)] = mev4
                        mf2 = None
                        for n in range(2):
                            for k in range(8):
                                mf2 = sch.add("tensor", lambda e, nn=n, kk=k, pp=p: e.matmul(
                                    PC[:, nn * 512:(nn + 1) * 512], xn4T[pp].tensor[:, kk, :],
                                    W_eff[:, kk * 1024 + nn * 512: kk * 1024 + nn * 512 + 512],
                                    start=(kk == 0), stop=(kk == 7)),
                                    (SM(mev4), VM(mk3[("zco", step)])) if (n == 0 and k == 0)
                                    else (),
                                    inc=("t", 1) if (n == 1 and k == 7) else None)
                        mo1 = sch.add("vector", lambda e, pp=p: e.tensor_add(
                            zs[pp][:, :], PC[:, :], s_bfbc[:, :]),
                            (TM(mf2), GM(mk3.get(("moo", step - 2), 0)),
                             mk3.get(("zdbg", step))), inc=("v", 1))
                        mk3[("mo1", step)] = mo1
                        moo = sch.add("gpsimd", lambda e, pp=p: e.tensor_add(
                            s_out[pp][:, :], zs[pp][:, :], x3b[pp][:, :]),
                            (VM(mo1), mk3.get(("ost", step - 1))), inc=("g", 1))
                        mk3[("moo", step)] = moo
                        mk3[("ost", step)] = dma("scalar", out_d[qi * 128:(qi + 1) * 128, :],
                                                 s_out[p][:, :], (GM(moo),), ch=f"so{p}")

                qS(r * NQ)
                qS(r * NQ + 1)
                qB(r * NQ)
                if r < KR - 1:
                    for qi in range(NQ):
                        step = r * NQ + qi
                        if qi + 2 < NQ:
                            qS(step + 2)
                        if qi + 1 < NQ:
                            qB(step + 1)
                        qC(step)
                else:
                    for qi in range(NQ):
                        step = r * NQ + qi
                        qC(step)
                        if qi + 2 < NQ:
                            qS(step + 2)
                        if qi + 1 < NQ:
                            qB(step + 1)
                mk3["pevlast"] = mk3[("pev", r * NQ + NQ - 1)]
                mk3["ev4last"] = mk3.get(("ev4", r * NQ + NQ - 1), 0)

        pass
    except _Cut:
        pass

    if dbg and phases >= 3:
        dma("scalar", dbg_ml[:, 0:32], m_run[:, :], (VM(sch.count("v")),))
        dma("scalar", dbg_ml[:, 32:64], l_run[:, :], (VM(sch.count("v")),))
    _fw = [(snm, sch.count(snm)) for snm in list(sch.cnt.keys())
           if snm not in ("t", "v", "s", "g")]
    _fw += [TM(sch.count("t")), VM(sch.count("v")), GM(sch.count("g"))]
    sch.add("scalar", lambda e: e.activation(s_sqscr[:, 0:1], s_sqscr[:, 0:1], AF.Identity,
                                             scale=1.0), tuple(_fw), inc=("s", 1))

    for n in sorted(set(list(sch.cnt.keys()) + ["t", "v", "s", "g", "ld", "st"])):
        sems[n] = C(nc.semaphore(n))

    def run_prog(name):
        def f(eng):
            for item in sch.prog[name]:
                if item[0] == "w":
                    eng.wait_ge(sems[item[1]], item[2])
                else:
                    _, fn, inc = item
                    ins = fn(eng)
                    if inc is not None:
                        ins.then_inc(sems[inc[0]], inc[1])
        return f

    with nc.Block() as blk:
        blk.tensor(run_prog("tensor"))
        blk.vector(run_prog("vector"))
        blk.scalar(run_prog("scalar"))
        blk.sync(run_prog("sync"))
        blk.gpsimd(run_prog("gpsimd"))

    for es in (es_p3, es_p2, es_p1t, es_p1w, es_pqw, es_qsp, es_prep,
               es_p2a, es_fold, es_bc, es_wtmp, es_p23w, es_glob):
        es.close()
    return nc


def make_inmaps(inputs):
    f32 = np.float32
    bf = ml_dtypes.bfloat16
    x = np.asarray(inputs["L0_attn_a_x"], f32)
    Wp = np.asarray(inputs["L0_attn_a_W_proj"], f32)
    A = np.asarray(inputs["L0_attn_a_A"], f32).reshape(1, 1024)
    Bp = np.asarray(inputs["L0_attn_a_Bp"], f32).reshape(1, 1024)
    Dsc = np.asarray(inputs["L0_attn_a_D"], f32).reshape(1)
    Wd = np.asarray(inputs["L0_attn_a_W_delta"], f32)
    bd = np.asarray(inputs["L0_attn_a_b_delta"], f32).reshape(1, 1024)
    Wq = np.asarray(inputs["L0_attn_b_Wq"], f32)
    bq = np.asarray(inputs["L0_attn_b_bq"], f32)
    Wk = np.asarray(inputs["L0_attn_b_Wk"], f32)
    bk = np.asarray(inputs["L0_attn_b_bk"], f32)
    Wv = np.asarray(inputs["L0_attn_b_Wv"], f32)
    bv = np.asarray(inputs["L0_attn_b_bv"], f32).reshape(1, 1024)
    W = np.asarray(inputs["L0_ffn_W"], f32)
    b = np.asarray(inputs["L0_ffn_b"], f32).reshape(1, 1024)
    ld_ = np.asarray(inputs["L0_ffn_lora_down"], f32)
    lu_ = np.asarray(inputs["L0_ffn_lora_up"], f32)
    da_ = np.asarray(inputs["L0_ffn_down_aux"], f32)
    ua_ = np.asarray(inputs["L0_ffn_up_aux"], f32)

    ct = np.ascontiguousarray

    def _pad128(a):
        out = np.zeros((128,) + a.shape[1:], f32)
        out[:a.shape[0]] = a
        return out

    rows = np.zeros((8, 1024), f32)
    rows[0] = A
    rows[1] = Bp
    rows[2] = bd
    rows[3] = b
    rows[4] = bv
    rows[5, 0] = Dsc[0]
    emsk = np.zeros((128, 256), bf)
    emsk[:, 65] = 1
    shared = dict(
        wp0=ct(Wp[0:1024, :]), wdT=ct(Wd.T), wcT=ct(Wp[2048:3072, :].T),
        wrT=ct(Wp[3072:4096, :].T), wT=ct(W.T),
        uaT=_pad128(ua_.T), da=_pad128(da_), ldT=_pad128(ld_.T), luT=_pad128(lu_.T),
        wqT=ct(Wq.T), wkT=ct(Wk.T), wvT=ct(Wv.T),
        rows=rows, bqT=ct(bq.reshape(8, 128).T), bkT=ct(bk.reshape(8, 128).T),
        ident=np.eye(128, dtype=bf), tri128=np.triu(np.ones((128, 128))).astype(bf),
        tri64x=np.triu(np.ones((64, 64), f32), 1), emsk=emsk,
        ones1r=np.ones((1, 128), f32), ones64=np.ones((1, 64), f32),
        ones2r=np.ones((2, 128), bf),
    )
    in_maps = []
    for c in range(8):
        bidx, h = c // 2, c % 2
        xl = x[bidx] if h == 0 else np.concatenate([x[bidx, 4096:], x[bidx, :4096]], axis=0)
        m1 = np.zeros((64, 1), f32)
        m2n = np.zeros((64, 1), f32)
        if h == 1:
            m1[0:32] = 1.0
            m2n[32:64] = -1.0
        im = dict(shared)
        im["x"] = ct(xl)
        im["m1c"] = m1
        im["m2n"] = m2n
        in_maps.append(im)
    return in_maps


def kernel(**inputs):
    in_maps = make_inmaps(inputs)
    nc = build(phases=3, dbg=False)
    res = run_bass_kernel_spmd(nc, in_maps, core_ids=list(range(8)))
    out = np.empty((4, 8192, 1024), np.float32)
    for c in range(8):
        bidx, h = c // 2, c % 2
        out[bidx, h * 4096:(h + 1) * 4096] = np.asarray(res.results[c]["out"], np.float32)
    return out


# version 13 (final)
# speedup vs baseline: 1.1267x; 1.0055x over previous
import numpy as np
import ml_dtypes
from contextlib import ExitStack

import concourse.bass as bass
import concourse.mybir as mybir
from concourse.bass_utils import run_bass_kernel_spmd

F32 = mybir.dt.float32
BF16 = mybir.dt.bfloat16
AX = mybir.AxisListType.X
AF = mybir.ActivationFunctionType
ALU = mybir.AluOpType

NT = 64
NQ = 32
KR = 4
D = 1024
EPS = 1e-5
ISQ = 1.0 / 32.0

ENGS = ("tensor", "vector", "scalar", "sync", "gpsimd")


class Sched:
    def __init__(self):
        self.prog = {e: [] for e in ENGS}
        self.cnt = {}
        self.lastwait = {e: {} for e in ENGS}

    CHAIN = {"vector": "v", "scalar": "s", "gpsimd": "g"}

    def add(self, eng, fn, waits=(), inc=None):
        sem0 = self.CHAIN.get(eng)
        if sem0 is not None:
            waits = ((sem0, self.cnt.get(sem0, 0)),) + tuple(waits)
            if inc is None:
                inc = (sem0, 1)
        for w in waits:
            if w is None:
                continue
            sem, val = w
            if isinstance(val, tuple):
                sem, val = val
            if val is None or val <= 0:
                continue
            if sem in ("ld", "st"):
                val = self.cnt.get(sem, 0)
            lw = self.lastwait[eng]
            if lw.get(sem, 0) >= val:
                continue
            lw[sem] = val
            self.prog[eng].append(("w", sem, val))
        self.prog[eng].append(("c", fn, inc))
        if inc is not None:
            sem, amt = inc
            self.cnt[sem] = self.cnt.get(sem, 0) + amt
            return self.cnt[sem]
        return None

    def count(self, sem):
        return self.cnt.get(sem, 0)

    def barrier(self):
        snap = dict(self.cnt)
        for e in ENGS:
            for sem, val in snap.items():
                if val > 0 and self.lastwait[e].get(sem, 0) < val:
                    self.lastwait[e][sem] = val
                    self.prog[e].append(("w", sem, val))


class _Cut(Exception):
    pass


def build(phases=3, dbg=False, cut=None):
    nc = bass.Bass("TRN2", target_bir_lowering=False)

    dI = lambda n, s, dt=F32: nc.dram_tensor(n, s, dt, kind="ExternalInput").ap()
    x_d = dI("x", [8192, 1024])
    wp0_d = dI("wp0", [1024, 1024])
    wdT_d = dI("wdT", [1024, 1024])
    wcT_d = dI("wcT", [1024, 1024])
    wrT_d = dI("wrT", [1024, 1024])
    wT_d = dI("wT", [1024, 1024])
    uaT_d = dI("uaT", [128, 1024])
    da_d = dI("da", [128, 1024])
    ldT_d = dI("ldT", [128, 4])
    luT_d = dI("luT", [128, 50])
    wqT_d = dI("wqT", [1024, 1024])
    wkT_d = dI("wkT", [1024, 1024])
    wvT_d = dI("wvT", [1024, 1024])
    rows_d = dI("rows", [8, 1024])
    bqT_d = dI("bqT", [128, 8])
    bkT_d = dI("bkT", [128, 8])
    m1_d = dI("m1c", [64, 1])
    m2n_d = dI("m2n", [64, 1])
    ident_d = dI("ident", [128, 128], BF16)
    tri128_d = dI("tri128", [128, 128], BF16)
    tri64_d = dI("tri64x", [64, 64])
    emsk_d = dI("emsk", [128, 256], BF16)
    o1r_d = dI("ones1r", [1, 128])
    o64_d = dI("ones64", [1, 64])
    o2r_d = dI("ones2r", [2, 128], BF16)

    sk = "ExternalOutput" if dbg else "Internal"
    dS = lambda n, s, dt=F32: nc.dram_tensor(n, s, dt, kind=sk).ap()
    sc_delta = dS("sc_delta", [NT, 128, 1024])
    sc_res = dS("sc_res", [NT, 128, 1024])
    sc_hi = dS("sc_hi", [64, 1024], BF16)
    sc_lo = dS("sc_lo", [64, 1024], BF16)
    sc_xn3 = dS("sc_xn3", [NT, 128, 1024], BF16)
    sc_x2 = dS("sc_x2", [NQ, 128, 1024])
    sc_qT = dS("sc_qT", [NQ, 128, 8, 128], BF16)
    sc_z = dS("sc_z", [NQ, 128, 1024])
    if dbg:
        dbg_tbl = nc.dram_tensor("dbg_tbl", [128, 1024], F32, kind="ExternalOutput").ap()
        dbg_bc = nc.dram_tensor("dbg_bc", [128, 4096], F32, kind="ExternalOutput").ap()
        sc_x1 = nc.dram_tensor("sc_x1", [NT, 128, 1024], F32, kind="ExternalOutput").ap()
        dbg_ml = nc.dram_tensor("dbg_ml", [128, 64], F32, kind="ExternalOutput").ap()
    out_d = nc.dram_tensor("out", [4096, 1024], F32, kind="ExternalOutput").ap()

    sch = Sched()

    def _ck(n):
        if cut is not None and n >= cut:
            raise _Cut()

    LD = lambda m: m if isinstance(m, tuple) else ("ld", m)
    VD = LD
    ST = lambda m: m if isinstance(m, tuple) else ("st", m)
    TM = lambda m: ("t", m)
    VM = lambda m: ("v", m)
    SM = lambda m: ("s", m)
    GM = lambda m: ("g", m)

    (es_glob, es_p23w, es_pqw, es_p1w, es_p1t, es_p2, es_p3,
     es_bc, es_wtmp, es_fold, es_p2a, es_prep, es_qsp) = (ExitStack() for _ in range(13))
    C = es_glob.enter_context
    R = lambda es, t: es.enter_context(t)

    s_id = C(nc.sbuf_tensor("s_id", [128, 128], BF16))
    s_tri128 = C(nc.sbuf_tensor("s_tri128", [128, 128], BF16))
    s_tri64 = C(nc.sbuf_tensor("s_tri64", [64, 64], F32))
    s_emsk = C(nc.sbuf_tensor("s_emsk", [128, 256], BF16))
    s_o1r = C(nc.sbuf_tensor("s_o1r", [1, 128], F32))
    s_o64 = C(nc.sbuf_tensor("s_o64", [1, 64], F32))
    s_o2r = C(nc.sbuf_tensor("s_o2r", [2, 128], BF16))
    s_bqT = C(nc.sbuf_tensor("s_bqT", [128, 8], F32))
    s_bkT = C(nc.sbuf_tensor("s_bkT", [128, 8], F32))
    s_m1 = C(nc.sbuf_tensor("s_m1", [64, 1], F32))
    s_m2n = C(nc.sbuf_tensor("s_m2n", [64, 1], F32))
    s_Dbc = C(nc.sbuf_tensor("s_Dbc", [128, 1], F32))
    s_bfbc = C(nc.sbuf_tensor("s_bfbc", [128, 1024], F32))
    s_bvbc = R(es_prep, nc.sbuf_tensor("s_bvbc", [128, 1024], F32, side="right"))
    W_eff = C(nc.sbuf_tensor("W_eff", [128, 8192], BF16))
    s_wtmp = [R(es_wtmp, nc.sbuf_tensor(f"s_wtmp{p}", [128, 1024], F32, side="right"))
              for p in range(2)]
    s_sqscr = C(nc.sbuf_tensor("s_sqscr", [128, 1024], BF16))
    s_out = [C(nc.sbuf_tensor("s_out0", [128, 1024], F32))] * 2

    PA = C(nc.psum_tensor("PA", [128, 1024], F32))
    PB = C(nc.psum_tensor("PB", [128, 1024], F32))
    PC = C(nc.psum_tensor("PC", [128, 1024], F32))
    PT1 = C(nc.psum_tensor("PT1", [128, 8, 128], BF16))
    PT2 = C(nc.psum_tensor("PT2", [128, 8, 128], BF16))

    sems = {}

    def dma(q, outap, inap, waits=(), ch=None):
        if q == "vector":
            q = "sync"
        semn = ch if ch is not None else {"sync": "ld", "scalar": "st"}[q]
        if semn in ("ld", "st"):
            waits = tuple(waits) + ((semn, 1),)
        m = sch.add(q, lambda e, o=outap, i=inap: e.dma_start(out=o, in_=i),
                    waits, inc=(semn, 16))
        return (semn, m)

    def transposes8(pt, src_bf, waits=()):
        m = None
        for c in range(8):
            m = sch.add("tensor",
                        lambda e, o=pt[:, c, :], i_=src_bf[:, c * 128:(c + 1) * 128]:
                        e.transpose(o, i_, s_id[:, :]),
                        waits if c == 0 else (), inc=("t", 1) if c == 7 else None)
        return m

    def ln(xin, xn_bf, stats, w_x):
        mv = sch.add("vector", lambda e: e.reduce_sum(stats[:, 0:1], xin[:, :], AX), w_x,
                     inc=("v", 1))
        msq = sch.add("scalar", lambda e: e.activation(s_sqscr[:, :], xin[:, :], AF.Square,
                                                       accum_out=stats[:, 1:2]), w_x,
                      inc=("s", 1))
        mng = sch.add("scalar", lambda e: e.activation(stats[:, 2:3], stats[:, 0:1], AF.Identity,
                                                       scale=-1.0 / 1024.0), (VM(mv),), inc=("s", 1))
        mvh = sch.add("scalar", lambda e: e.activation(stats[:, 3:4], stats[:, 1:2], AF.Identity,
                                                       scale=1.0 / 1024.0), (SM(msq),), inc=("s", 1))
        ms = sch.add("scalar", lambda e: e.activation(stats[:, 4:5], stats[:, 2:3], AF.Square),
                     (SM(mng),), inc=("s", 1))
        mv2 = sch.add("vector", lambda e: e.tensor_sub(stats[:, 5:6], stats[:, 3:4],
                                                       stats[:, 4:5]), (SM(ms), SM(mvh)), inc=("v", 1))
        ms2 = sch.add("scalar", lambda e: e.activation(stats[:, 6:7], stats[:, 5:6], AF.Copy,
                                                       bias=EPS), (VM(mv2),), inc=("s", 1))
        mv3 = sch.add("vector", lambda e: e.reciprocal(stats[:, 7:8], stats[:, 6:7]),
                      (SM(ms2),), inc=("v", 1))
        ms3 = sch.add("scalar", lambda e: e.activation(stats[:, 8:9], stats[:, 7:8], AF.Sqrt),
                      (VM(mv3),), inc=("s", 1))
        mv4 = sch.add("vector", lambda e: e.tensor_mul(stats[:, 9:10], stats[:, 2:3],
                                                       stats[:, 8:9]), (SM(ms3),), inc=("v", 1))
        mxn = sch.add("scalar", lambda e: e.activation(xn_bf[:, :], xin[:, :], AF.Identity,
                                                       bias=stats[:, 9:10], scale=stats[:, 8:9]),
                      (VM(mv4), SM(ms3)), inc=("s", 1))
        return mxn

    wtmp_prev = {0: None, 1: None}

    def load_weight(dst_bf, src_d):
        last = None
        for k in range(8):
            p = k % 2
            ml = dma("sync", s_wtmp[p][:, :], src_d[k * 128:(k + 1) * 128, :],
                     (wtmp_prev[p],), ch=f"lw{p}")
            mg = sch.add("gpsimd",
                         lambda e, kk=k, pp=p: e.tensor_copy(
                             dst_bf[:, kk * 1024:(kk + 1) * 1024], s_wtmp[pp][:, :]),
                         (LD(ml),), inc=("g", 1))
            wtmp_prev[p] = GM(mg)
            last = mg
        return last

    WkT = R(es_p23w, nc.sbuf_tensor("WkT", [128, 8192], BF16))
    WvT = R(es_p23w, nc.sbuf_tensor("WvT", [128, 8192], BF16))
    WqT = R(es_pqw, nc.sbuf_tensor("WqT", [128, 8192], BF16))
    W_dd = R(es_p1w, nc.sbuf_tensor("W_dd", [128, 8192], BF16))
    WcT = R(es_p1w, nc.sbuf_tensor("WcT", [128, 8192], BF16))
    WrT = R(es_p1w, nc.sbuf_tensor("WrT", [128, 8192], BF16))
    Rb = lambda t: R(es_bc, t)
    s_A1bc = Rb(nc.sbuf_tensor("s_A1bc", [128, 1024], F32, side="right"))
    s_B1bc = Rb(nc.sbuf_tensor("s_B1bc", [128, 1024], F32, side="right"))
    s_bdbc = Rb(nc.sbuf_tensor("s_bdbc", [128, 1024], F32, side="right"))
    s_Csbc = Rb(nc.sbuf_tensor("s_Csbc", [128, 1024], F32, side="right"))
    s_R2bc = Rb(nc.sbuf_tensor("s_R2bc", [128, 1024], F32, side="right"))
    Cf = lambda t: R(es_fold, t)
    s_row1 = Cf(nc.sbuf_tensor("s_row1", [1, 1024], F32, side="right"))
    Wp0 = Cf(nc.sbuf_tensor("Wp0", [128, 8192], BF16, side="right"))
    WdT = Cf(nc.sbuf_tensor("WdT", [128, 8192], BF16, side="right"))
    s_ldT = Cf(nc.sbuf_tensor("s_ldT", [128, 4], BF16, side="right"))
    s_luT = Cf(nc.sbuf_tensor("s_luT", [128, 50], BF16, side="right"))
    s_uaT = Cf(nc.sbuf_tensor("s_uaT", [128, 1024], BF16, side="right"))
    s_da = Cf(nc.sbuf_tensor("s_da", [128, 1024], BF16, side="right"))
    s_t1 = Cf(nc.sbuf_tensor("s_t1", [128, 1024], BF16, side="right"))
    s_t2 = Cf(nc.sbuf_tensor("s_t2", [128, 1024], BF16, side="right"))
    s_lt = s_wtmp[0]

    try:
        for outap, inap in ((s_id[:, :], ident_d[:, :]), (s_tri128[:, :], tri128_d[:, :]),
                            (s_tri64[:, :], tri64_d[:, :]), (s_emsk[:, :], emsk_d[:, :]),
                            (s_o1r[:, :], o1r_d[:, :]), (s_o64[:, :], o64_d[:, :]),
                            (s_o2r[:, :], o2r_d[:, :]),
                            (s_bqT[:, :], bqT_d[:, :]), (s_bkT[:, :], bkT_d[:, :]),
                            (s_m1[:, :], m1_d[:, :]), (s_m2n[:, :], m2n_d[:, :])):
            m_startld = dma("sync", outap, inap)
        _ck(1)

        ml = dma("sync", s_lt[:, 0:4], ldT_d[:, :], (wtmp_prev[0],), ch="lw0")
        mg1 = sch.add("gpsimd", lambda e: e.tensor_copy(s_ldT[:, :], s_lt[:, 0:4]),
                      (LD(ml),), inc=("g", 1))
        ml = dma("sync", s_lt[:, 0:50], luT_d[:, :], (GM(mg1),), ch="lw0")
        mg2 = sch.add("gpsimd", lambda e: e.tensor_copy(s_luT[:, :], s_lt[:, 0:50]),
                      (LD(ml),), inc=("g", 1))
        ml = dma("sync", s_lt[:, :], uaT_d[:, :], (GM(mg2),), ch="lw0")
        mg3 = sch.add("gpsimd", lambda e: e.tensor_copy(s_uaT[:, :], s_lt[:, :]),
                      (LD(ml),), inc=("g", 1))
        ml = dma("sync", s_lt[:, :], da_d[:, :], (GM(mg3),), ch="lw0")
        mg4 = sch.add("gpsimd", lambda e: e.tensor_copy(s_da[:, :], s_lt[:, :]),
                      (LD(ml),), inc=("g", 1))
        wtmp_prev[0] = GM(mg4)
        mz1 = sch.add("gpsimd", lambda e: e.memzero(s_t1[:, :]), inc=("g", 1))
        mz2 = sch.add("gpsimd", lambda e: e.memzero(s_t2[:, :]), inc=("g", 1))
        _ck(2)

        mwp0 = load_weight(Wp0, wp0_d)
        mwdt = load_weight(WdT, wdT_d)
        _ck(3)

        mt1 = None
        for n in range(2):
            mt1 = sch.add("tensor", lambda e, nn=n: e.matmul(
                PA[0:4, nn * 512:(nn + 1) * 512], s_ldT[:, :], s_da[:, nn * 512:(nn + 1) * 512],
                start=True, stop=True), (GM(mg4), GM(mg1)) if n == 0 else (),
                inc=("t", 1) if n == 1 else None)
        ms = sch.add("scalar", lambda e: e.copy(s_t1[0:4, :], PA[0:4, :]),
                     (TM(mt1), GM(mz1)), inc=("s", 1))
        mt2 = None
        for n in range(2):
            mt2 = sch.add("tensor", lambda e, nn=n: e.matmul(
                PB[0:50, nn * 512:(nn + 1) * 512], s_luT[:, :], s_t1[:, nn * 512:(nn + 1) * 512],
                start=True, stop=True), (SM(ms), GM(mg2)) if n == 0 else (),
                inc=("t", 1) if n == 1 else None)
        ms2 = sch.add("scalar", lambda e: e.copy(s_t2[0:50, :], PB[0:50, :]),
                      (TM(mt2), GM(mz2)), inc=("s", 1))
        _ck(4)
        m_weff_v = None
        for mch in range(8):
            p = mch % 2
            wld = dma("sync", s_wtmp[p][:, :], wT_d[mch * 128:(mch + 1) * 128, :],
                      (wtmp_prev[p],), ch=f"lw{p}")
            mml = None
            for n in range(2):
                mml = sch.add("tensor", lambda e, nn=n, m_=mch: e.matmul(
                    PA[:, nn * 512:(nn + 1) * 512], s_t2[:, m_ * 128:(m_ + 1) * 128],
                    s_uaT[:, nn * 512:(nn + 1) * 512], start=True, stop=True),
                    ((SM(ms2), GM(mg3)) if mch == 0 else (VM(m_weff_v),))
                    if n == 0 else (),
                    inc=("t", 1) if n == 1 else None)
            m_weff_v = sch.add("vector", lambda e, m_=mch, pp=p: e.tensor_add(
                W_eff[:, m_ * 1024:(m_ + 1) * 1024], PA[:, :], s_wtmp[pp][:, :]),
                (TM(mml), LD(wld)), inc=("v", 1))
            wtmp_prev[p] = VM(m_weff_v)

        _ck(5)
        m_wdd_s = None
        for mch in range(8):
            mml = None
            for n in range(2):
                for k in range(8):
                    mml = sch.add("tensor", lambda e, nn=n, kk=k, m_=mch: e.matmul(
                        PB[:, nn * 512:(nn + 1) * 512],
                        Wp0[:, kk * 1024 + m_ * 128: kk * 1024 + (m_ + 1) * 128],
                        WdT[:, kk * 1024 + nn * 512: kk * 1024 + nn * 512 + 512],
                        start=(kk == 0), stop=(kk == 7)),
                        ((GM(mwp0), GM(mwdt), SM(ms2)) if mch == 0 else (SM(m_wdd_s),))
                        if (n == 0 and k == 0) else (),
                        inc=("t", 1) if (n == 1 and k == 7) else None)
            m_wdd_s = sch.add("scalar", lambda e, m_=mch: e.copy(
                W_dd[:, m_ * 1024:(m_ + 1) * 1024], PB[:, :]), (TM(mml),), inc=("s", 1))

        _ck(6)
        mwc = load_weight(WcT, wcT_d)
        mwr = load_weight(WrT, wrT_d)

        _ck(7)
        m_prevbc = 0

        def bcast_row(dst, rowidx, dcol=False):
            nonlocal_marks = {}
            mrl = dma("sync", s_row1[:, :], rows_d[rowidx:rowidx + 1, :],
                      (TM(m_prevbc), LD(m_startld)))
            mb = None
            for n in range(2):
                mb = sch.add("tensor", lambda e, nn=n: e.matmul(
                    PC[:, nn * 512:(nn + 1) * 512], s_o1r[:, :],
                    s_row1[0:1, nn * 512:(nn + 1) * 512], start=True, stop=True),
                    (LD(mrl), VM(sch.count("v"))) if n == 0 else (),
                    inc=("t", 1) if n == 1 else None)
            if dcol:
                mv = sch.add("vector", lambda e: e.tensor_copy(s_Dbc[:, :], PC[:, 0:1]),
                             (TM(mb),), inc=("v", 1))
            else:
                mv = sch.add("vector", lambda e: e.tensor_copy(dst[:, :], PC[:, :]), (TM(mb),),
                             inc=("v", 1))
            return mb

        m_prevbc = bcast_row(s_A1bc, 0)
        m_prevbc = bcast_row(s_B1bc, 1)
        m_prevbc = bcast_row(s_bdbc, 2)
        m_prevbc = bcast_row(s_bfbc, 3)
        m_prevbc = bcast_row(s_bvbc, 4)
        m_prevbc = bcast_row(None, 5, dcol=True)

        _ck(8)
        sch.barrier()
        es_fold.close()

        if phases >= 1:
            Cp = es_p1t.enter_context
            xb = [Cp(nc.sbuf_tensor(f"xb{p}", [128, 1024], F32)) for p in range(3)]
            xn_bf = [Cp(nc.sbuf_tensor(f"xn_bf{p}", [128, 1024], BF16)) for p in range(3)]
            xnT = [Cp(nc.sbuf_tensor(f"xnT{p}", [128, 8, 128], BF16)) for p in range(3)]
            delta_f = [Cp(nc.sbuf_tensor(f"delta_f{p}", [128, 1024], F32)) for p in range(2)]
            e1_f = [Cp(nc.sbuf_tensor("e1_f0", [128, 1024], F32))] * 2
            res_f = [Cp(nc.sbuf_tensor(f"res_f{p}", [128, 1024], F32)) for p in range(2)]
            cs_bf = [Cp(nc.sbuf_tensor("cs_bf0", [128, 1024], BF16))] * 2
            g0_bf = [Cp(nc.sbuf_tensor("g0_bf0", [128, 1024], BF16))] * 2
            m_bf = [Cp(nc.sbuf_tensor("m_bf0", [128, 1024], BF16))] * 2
            st1 = [Cp(nc.sbuf_tensor(f"st1_{p}", [128, 16], F32)) for p in range(3)]


            mk = {}

            def p1Aln(i):
                p = i % 3
                q3 = i % 3
                w_x = [LD(mk[("lx", i)]), SM(mk.get(("xn_c", i - 3))),
                       GM(mk.get(("m", i - 3))), TM(mk.get(("tp", i - 3)))]
                if i == 0:
                    w_x += [SM(m_wdd_s), GM(mwc), GM(mwr), VM(sch.count("v"))]
                mxn = ln(xb[p], xn_bf[q3], st1[q3], tuple(w_x))
                mk[("xn_c", i)] = mxn

            def p1Atp(i):
                q3 = i % 3
                PTt = PT1 if i % 2 == 0 else PT2
                mtp = transposes8(PTt, xn_bf[q3],
                                  (SM(mk[("xn_c", i)]), VM(mk.get(("xnTev", i - 2), 0))))
                mk[("tp", i)] = mtp
                mev = sch.add("vector", lambda e, qq=q3, pt=PTt: e.tensor_copy(
                    xnT[qq][:, :, :], pt[:, :, :]), (TM(mtp),), inc=("v", 1))
                mk[("xnTev", i)] = mev

            def p1B(i):
                p = i % 2
                q3 = i % 3
                mev = mk[("xnTev", i)]
                mdp = None
                for n in range(2):
                    for k in range(8):
                        mdp = sch.add("tensor", lambda e, nn=n, kk=k, qq=q3: e.matmul(
                            PA[:, nn * 512:(nn + 1) * 512], xnT[qq][:, kk, :],
                            W_dd[:, kk * 1024 + nn * 512: kk * 1024 + nn * 512 + 512],
                            start=(kk == 0), stop=(kk == 7)),
                            (VM(mev), VM(mk.get(("resev", i - 1), 0)))
                            if (n == 0 and k == 0) else (),
                            inc=("t", 1) if (n == 1 and k == 7) else None)
                mba = sch.add("vector", lambda e, pp=p: e.tensor_add(
                    delta_f[pp][:, :], PA[:, :], s_bdbc[:, :]),
                    (TM(mdp), GM(mk.get(("g0", i - 2), 0)), mk.get(("dst", i - 2))),
                    inc=("v", 1))
                mcs = None
                for n in range(2):
                    for k in range(8):
                        mcs = sch.add("tensor", lambda e, nn=n, kk=k, qq=q3: e.matmul(
                            PB[:, nn * 512:(nn + 1) * 512], xnT[qq][:, kk, :],
                            WcT[:, kk * 1024 + nn * 512: kk * 1024 + nn * 512 + 512],
                            start=(kk == 0), stop=(kk == 7)),
                            (SM(mk.get(("csbf", i - 1), 0)),) if (n == 0 and k == 0) else (),
                            inc=("t", 1) if (n == 1 and k == 7) else None)
                mex = sch.add("scalar", lambda e, pp=p: e.activation(
                    e1_f[pp][:, :], delta_f[pp][:, :], AF.Exp), (VM(mba),), inc=("s", 1))
                mln = sch.add("scalar", lambda e, pp=p: e.activation(
                    delta_f[pp][:, :], e1_f[pp][:, :], AF.Ln, bias=1.0), (SM(mex),),
                    inc=("s", 1))
                mk[("sp", i)] = mln
                mk[("dst", i)] = dma("scalar", sc_delta[i, :, :], delta_f[p][:, :], ch=f"sdl{p}")
                mcb = sch.add("scalar", lambda e, pp=p: e.copy(cs_bf[pp][:, :], PB[:, :]),
                              (TM(mcs), GM(mk.get(("m", i - 1), 0)),
                               TM(mk.get(("colsum", i - 1), 0))), inc=("s", 1))
                mk[("csbf", i)] = mcb
                mg0 = sch.add("gpsimd", lambda e, pp=p: e.tensor_mul(
                    g0_bf[pp][:, :], delta_f[pp][:, :], s_B1bc[:, :]),
                    (SM(mln), TM(mk.get(("colsum", i - 1), 0))), inc=("g", 1))
                mk[("g0", i)] = mg0
                mmm = sch.add("gpsimd", lambda e, pp=p, qq=q3: e.tensor_mul(
                    m_bf[pp][:, :], xn_bf[qq][:, :], cs_bf[pp][:, :]),
                    (SM(mcb), TM(mk.get(("colsum", i - 1), 0))), inc=("g", 1))
                mk[("m", i)] = mmm
                mrs = None
                for n in range(2):
                    for k in range(8):
                        mrs = sch.add("tensor", lambda e, nn=n, kk=k, qq=q3: e.matmul(
                            PA[:, nn * 512:(nn + 1) * 512], xnT[qq][:, kk, :],
                            WrT[:, kk * 1024 + nn * 512: kk * 1024 + nn * 512 + 512],
                            start=(kk == 0), stop=(kk == 7)),
                            (VM(mba),) if (n == 0 and k == 0) else (),
                            inc=("t", 1) if (n == 1 and k == 7) else None)
                mrev = sch.add("vector", lambda e, pp=p: e.tensor_copy(
                    res_f[pp][:, :], PA[:, :]),
                    (TM(mrs), mk.get(("rst", i - 2))), inc=("v", 1))
                mk[("resev", i)] = mrev
                mk[("rst", i)] = dma("scalar", sc_res[i, :, :], res_f[p][:, :], (VM(mrev),),
                                     ch=f"sre{p}")
                mcl = None
                for n in range(2):
                    mcl = sch.add("tensor", lambda e, nn=n, ii=i, pp=p: e.matmul(
                        PC[0:128, nn * 512:(nn + 1) * 512], s_emsk[:, 65 - ii:193 - ii],
                        g0_bf[pp][:, nn * 512:(nn + 1) * 512], start=(ii == 0), stop=False),
                        (GM(mg0),) if n == 0 else ())
                for n in range(2):
                    mcl = sch.add("tensor", lambda e, nn=n, ii=i, pp=p: e.matmul(
                        PC[0:128, nn * 512:(nn + 1) * 512], s_emsk[:, 1:129],
                        m_bf[pp][:, nn * 512:(nn + 1) * 512], start=False, stop=False),
                        (GM(mmm),) if n == 0 else ())
                for n in range(2):
                    mcl = sch.add("tensor", lambda e, nn=n, ii=i, pp=p: e.matmul(
                        PC[0:128, nn * 512:(nn + 1) * 512], s_emsk[:, 0:128],
                        cs_bf[pp][:, nn * 512:(nn + 1) * 512], start=False,
                        stop=(ii == NT - 1)),
                        (), inc=("t", 1) if n == 1 else None)
                mk[("colsum", i)] = mcl

            for i in range(5):
                if i < NT:
                    mk[("lx", i)] = dma("sync", xb[i % 3][:, :], x_d[i * 128:(i + 1) * 128, :],
                                        (SM(mk.get(("xn_c", i - 3))),), ch=f"lxb{i % 3}")
                if i < 3:
                    p1Aln(i)
            p1Atp(0)
            p1Atp(1)
            for i in range(NT):
                p1B(i)
                if i + 5 < NT:
                    mk[("lx", i + 5)] = dma("sync", xb[(i + 5) % 3][:, :],
                                            x_d[(i + 5) * 128:(i + 6) * 128, :],
                                            (SM(mk[("xn_c", i + 2)]),), ch=f"lxb{(i + 5) % 3}")
                if i + 3 < NT:
                    p1Aln(i + 3)
                if i + 2 < NT:
                    p1Atp(i + 2)

            sch.barrier()
            es_p1t.close()
            es_p1w.close()

            C2a = lambda t: R(es_p2a, t)
            tbl_sb = C2a(nc.sbuf_tensor("tbl_sb", [128, 1024], F32, side="right"))
            car0_sb = C2a(nc.sbuf_tensor("car0_sb", [64, 1024], F32, side="right"))
            l1bc_sb = C2a(nc.sbuf_tensor("l1bc_sb", [64, 1024], F32, side="right"))
            tadd_sb = C2a(nc.sbuf_tensor("tadd_sb", [64, 1024], F32, side="right"))
            hi_bf = C2a(nc.sbuf_tensor("hi_bf", [64, 1024], BF16, side="right"))
            lo_bf = C2a(nc.sbuf_tensor("lo_bf", [64, 1024], BF16, side="right"))
            hi32_sb = C2a(nc.sbuf_tensor("hi32_sb", [64, 1024], F32, side="right"))
            b0_sb = C2a(nc.sbuf_tensor("b0_sb", [1, 6144], F32, side="right"))
            ucbc_sb = C2a(nc.sbuf_tensor("ucbc_sb", [128, 1024], F32, side="right"))

            mtb = sch.add("vector", lambda e: e.tensor_copy(tbl_sb[0:66, :], PC[0:66, :]), (),
                          inc=("v", 1))
            mu1 = dma("sync", b0_sb[0:1, 0:1024], tbl_sb[64:65, :], (VM(mtb),))
            mu2 = dma("sync", b0_sb[0:1, 1024:2048], tbl_sb[65:66, :])
            mpf = None
            for n in range(2):
                mpf = sch.add("tensor", lambda e, nn=n: e.matmul(
                    PA[0:64, nn * 512:(nn + 1) * 512], s_tri64[:, :],
                    tbl_sb[0:64, nn * 512:(nn + 1) * 512], start=True, stop=True),
                    (VM(mtb),) if n == 0 else (), inc=("t", 1) if n == 1 else None)
            mc0 = sch.add("vector", lambda e: e.tensor_copy(car0_sb[:, :], PA[0:64, :]), (TM(mpf),),
                          inc=("v", 1))
            ml1 = dma("sync", b0_sb[0:1, 2048:3072], car0_sb[32:33, :], (VM(mc0),))
            ml2 = dma("sync", b0_sb[0:1, 3072:4096], car0_sb[63:64, :])
            ml3 = dma("sync", b0_sb[0:1, 4096:5120], tbl_sb[63:64, :])
            sch.add("vector", lambda e: e.tensor_add(b0_sb[0:1, 5120:6144], b0_sb[0:1, 3072:4096],
                                                     b0_sb[0:1, 4096:5120]), (LD(ml3),), inc=("v", 1))
            mt_ = sch.add("vector", lambda e: e.tensor_sub(b0_sb[0:1, 5120:6144],
                                                           b0_sb[0:1, 5120:6144],
                                                           b0_sb[0:1, 2048:3072]), inc=("v", 1))
            mb1 = None
            for n in range(2):
                mb1 = sch.add("tensor", lambda e, nn=n: e.matmul(
                    PB[0:64, nn * 512:(nn + 1) * 512], s_o64[:, :],
                    b0_sb[0:1, 2048 + nn * 512:2048 + (nn + 1) * 512], start=True, stop=True),
                    (VM(mt_),) if n == 0 else (), inc=("t", 1) if n == 1 else None)
            mevl1 = sch.add("vector", lambda e: e.tensor_copy(l1bc_sb[:, :], PB[0:64, :]), (TM(mb1),),
                            inc=("v", 1))
            mb2 = None
            for n in range(2):
                mb2 = sch.add("tensor", lambda e, nn=n: e.matmul(
                    PB[0:64, nn * 512:(nn + 1) * 512], s_o64[:, :],
                    b0_sb[0:1, 5120 + nn * 512:5120 + (nn + 1) * 512], start=True, stop=True),
                    (VM(mevl1),) if n == 0 else (), inc=("t", 1) if n == 1 else None)
            mta = sch.add("vector", lambda e: e.scalar_tensor_tensor(
                tadd_sb[:, :], PB[0:64, :], s_m1[:, :], car0_sb[:, :], ALU.mult, ALU.add),
                (TM(mb2),), inc=("v", 1))
            mta = sch.add("vector", lambda e: e.scalar_tensor_tensor(
                tadd_sb[:, :], l1bc_sb[:, :], s_m2n[:, :], tadd_sb[:, :], ALU.mult, ALU.add),
                inc=("v", 1))
            mhi = sch.add("scalar", lambda e: e.copy(hi_bf[:, :], tadd_sb[:, :]), (VM(mta),),
                          inc=("s", 1))
            mhi2 = sch.add("scalar", lambda e: e.copy(hi32_sb[:, :], hi_bf[:, :]), (SM(mhi),),
                           inc=("s", 1))
            mlo = sch.add("vector", lambda e: e.tensor_sub(lo_bf[:, :], tadd_sb[:, :], hi32_sb[:, :]),
                          (SM(mhi2),), inc=("v", 1))
            mst_hi = dma("scalar", sc_hi[:, :], hi_bf[:, :], (SM(mhi),))
            mst_lo = dma("scalar", sc_lo[:, :], lo_bf[:, :], (VM(mlo),))
            mcb_ = None
            for n in range(2):
                mcb_ = sch.add("tensor", lambda e, nn=n: e.matmul(
                    PC[:, nn * 512:(nn + 1) * 512], s_o1r[:, :],
                    b0_sb[0:1, 1024 + nn * 512:1024 + (nn + 1) * 512], start=True, stop=True),
                    (LD(mu2), VM(mtb)) if n == 0 else (), inc=("t", 1) if n == 1 else None)
            mcsbc = sch.add("vector", lambda e: e.tensor_copy(s_Csbc[:, :], PC[:, :]), (TM(mcb_),),
                            inc=("v", 1))
            mub_ = None
            for n in range(2):
                mub_ = sch.add("tensor", lambda e, nn=n: e.matmul(
                    PC[:, nn * 512:(nn + 1) * 512], s_o1r[:, :],
                    b0_sb[0:1, 0 + nn * 512:0 + (nn + 1) * 512], start=True, stop=True),
                    (VM(mcsbc), LD(mu1)) if n == 0 else (), inc=("t", 1) if n == 1 else None)
            mucbc = sch.add("vector", lambda e: e.tensor_copy(ucbc_sb[:, :], PC[:, :]), (TM(mub_),),
                            inc=("v", 1))
            mr2 = sch.add("gpsimd", lambda e: e.tensor_mul(s_R2bc[:, :], s_B1bc[:, :], ucbc_sb[:, :]),
                          (VM(mucbc),), inc=("g", 1))
            if dbg:
                dma("scalar", dbg_tbl[0:66, :], tbl_sb[0:66, :], (VM(mtb),))
                dma("scalar", dbg_bc[:, 0:1024], s_Csbc[:, :], (VM(mcsbc),))
                dma("scalar", dbg_bc[:, 1024:2048], ucbc_sb[:, :], (VM(mucbc),))
                dma("scalar", dbg_bc[:, 2048:3072], s_R2bc[:, :], (GM(mr2),))
                dma("scalar", dbg_bc[0:64, 3072:4096], car0_sb[0:64, :], (VM(mc0),))

            sch.barrier()
            es_p2a.close()
            es_p1t_dummy = None


        if phases >= 2:
            C2 = es_p2.enter_context
            A_ = [C2(nc.sbuf_tensor(f"p2A{p}", [128, 1024], F32)) for p in range(2)]
            B_ = [C2(nc.sbuf_tensor(f"p2B{p}", [128, 1024], F32)) for p in range(2)]
            Cc_ = [C2(nc.sbuf_tensor(f"p2C{p}", [128, 1024], F32)) for p in range(2)]
            D_ = [C2(nc.sbuf_tensor(f"p2D{p}", [128, 1024], F32)) for p in range(2)]
            E_ = [C2(nc.sbuf_tensor(f"p2E{p}", [128, 1024], F32)) for p in range(2)]
            F_ = [C2(nc.sbuf_tensor(f"p2F{p}", [128, 1024], F32)) for p in range(3)]
            G_ = [C2(nc.sbuf_tensor(f"p2G{p}", [128, 1024], F32)) for p in range(3)]
            H_ = [C2(nc.sbuf_tensor(f"p2H{p}", [128, 1024], F32)) for p in range(2)]
            g0b = [C2(nc.sbuf_tensor(f"p2g0{p}", [128, 1024], BF16)) for p in range(2)]
            hilo = [C2(nc.sbuf_tensor(f"hilo{p}", [2, 1024], BF16)) for p in range(2)]
            xn2b = [C2(nc.sbuf_tensor(f"xn2b{p}", [128, 1024], BF16)) for p in range(2)]
            xn2T = [C2(nc.sbuf_tensor(f"xn2T{p}", [128, 8, 128], BF16)) for p in range(2)]
            xn3b = [C2(nc.sbuf_tensor(f"xn3b{p}", [128, 1024], BF16)) for p in range(2)]
            st2 = [C2(nc.sbuf_tensor(f"st2_{p}", [128, 16], F32)) for p in range(2)]
            st3 = [C2(nc.sbuf_tensor(f"st3_{p}", [128, 16], F32)) for p in range(2)]

            mwq = load_weight(WqT, wqT_d)
            mwk = load_weight(WkT, wkT_d)
            mwv = load_weight(WvT, wvT_d)

            mk2 = {}
            for i in range(2):
                mk2[("lx", i)] = dma("sync", A_[i][:, :], x_d[i * 128:(i + 1) * 128, :],
                                     ch=f"lA{i}")
                mk2[("lhi", i)] = dma("sync", hilo[i][0:1, :], sc_hi[i:i + 1, :],
                                      (mst_hi, mst_lo), ch=f"lh{i}")
                mk2[("llo", i)] = dma("sync", hilo[i][1:2, :], sc_lo[i:i + 1, :], ch=f"lh{i}")
                mk2[("ldl", i)] = dma("vector", B_[i][:, :], sc_delta[i, :, :],
                                      (mk[("dst", i)],), ch=f"lB{i}")
                mk2[("lr", i)] = dma("vector", Cc_[i][:, :], sc_res[i, :, :],
                                     (mk[("rst", i)],), ch=f"lC{i}")


            def p2A(i):
                p = i % 2
                q3 = i % 3
                PD = PA if p == 0 else PC
                mei = sch.add("gpsimd", lambda e, pp=p: e.tensor_mul(
                    D_[pp][:, :], B_[pp][:, :], s_A1bc[:, :]),
                    (VD(mk2[("ldl", i)]), GM(mr2) if i == 0 else None), inc=("g", 1))
                mk2[("eIn", i)] = mei
                mev_ = sch.add("scalar", lambda e, pp=p: e.activation(
                    E_[pp][:, :], D_[pp][:, :], AF.Exp), (GM(mei),), inc=("s", 1))
                mg0 = sch.add("gpsimd", lambda e, pp=p: e.tensor_mul(
                    g0b[pp][:, :], B_[pp][:, :], s_B1bc[:, :]),
                    (TM(mk2.get(("tri", i - 2), 0)),), inc=("g", 1))
                mtr = None
                for n in range(2):
                    mtr = sch.add("tensor", lambda e, nn=n, pp=p, PD_=PD: e.matmul(
                        PD_[:, nn * 512:(nn + 1) * 512], s_tri128[:, :],
                        g0b[pp][:, nn * 512:(nn + 1) * 512], start=True, stop=False),
                        (GM(mg0), VM(mk2.get(("ydb", i - 2), 0))) if n == 0 else ())
                for n in range(2):
                    mtr = sch.add("tensor", lambda e, nn=n, pp=p, PD_=PD: e.matmul(
                        PD_[:, nn * 512:(nn + 1) * 512], s_o2r[:, :],
                        hilo[pp][:, nn * 512:(nn + 1) * 512], start=False, stop=True),
                        (LD(mk2[("llo", i)]),) if n == 0 else (),
                        inc=("t", 1) if n == 1 else None)
                mk2[("tri", i)] = mtr
                mydb = sch.add("vector", lambda e, pp=q3, PD_=PD: e.tensor_mul(
                    G_[pp][:, :], PD_[:, :], s_Csbc[:, :]),
                    (TM(mtr), GM(mk2.get(("x2", i - 3), 0))), inc=("v", 1))
                mk2[("ydb", i)] = mydb
                my1 = sch.add("gpsimd", lambda e, pp=p, qq=q3: e.tensor_mul(
                    F_[qq][:, :], E_[pp][:, :], s_R2bc[:, :]),
                    (SM(mev_), VM(mk2.get(("x2a", i - 3), 0))), inc=("g", 1))
                my12 = sch.add("gpsimd", lambda e, pp=p, qq=q3: e.tensor_add(
                    D_[pp][:, :], G_[qq][:, :], F_[qq][:, :]), (VM(mydb),), inc=("g", 1))
                mx1a = sch.add("vector", lambda e, pp=p: e.scalar_tensor_tensor(
                    E_[pp][:, :], Cc_[pp][:, :], s_Dbc[:, :], D_[pp][:, :], ALU.mult, ALU.add),
                    (VD(mk2[("lr", i)]), GM(my12), SM(mev_)), inc=("v", 1))
                mk2[("x1a", i)] = mx1a
                mx1 = sch.add("gpsimd", lambda e, pp=p, qq=q3: e.tensor_add(
                    F_[qq][:, :], E_[pp][:, :], A_[pp][:, :]),
                    (LD(mk2[("lx", i)]), VM(mx1a)), inc=("g", 1))
                mk2[("x1", i)] = mx1
                if dbg:
                    dma("scalar", sc_x1[i, :, :], F_[q3][:, :], (GM(mx1),))

            def p2Bln(i):
                p = i % 2
                q3 = i % 3
                mxn2 = ln(F_[q3], xn2b[p], st2[p], (GM(mk2[("x1", i)]),
                                                   TM(mk2.get(("tp2", i - 2), 0))))
                mk2[("xn2", i)] = mxn2

            def p2Btp(i):
                p = i % 2
                PTx = PT1 if p == 0 else PT2
                mxn2 = mk2[("xn2", i)]
                mtp2 = transposes8(PTx, xn2b[p], (SM(mxn2), SM(mk2.get(("ev2", i - 2), 0)),
                                                  TM(mk2.get(("f1", i - 2), 0))))
                mk2[("tp2", i)] = mtp2
                mev2 = sch.add("scalar", lambda e, pp=p, pt=PTx: e.copy(
                    xn2T[pp][:, :, :], pt[:, :, :]),
                               (TM(mtp2),), inc=("s", 1))
                mk2[("ev2", i)] = mev2

            def p2Bmm(i):
                p = i % 2
                q3 = i % 3
                mev2 = mk2[("ev2", i)]
                mf1 = None
                for n in range(2):
                    for k in range(8):
                        mf1 = sch.add("tensor", lambda e, nn=n, kk=k, pp=p: e.matmul(
                            PB[:, nn * 512:(nn + 1) * 512], xn2T[pp][:, kk, :],
                            W_eff[:, kk * 1024 + nn * 512: kk * 1024 + nn * 512 + 512],
                            start=(kk == 0), stop=(kk == 7)),
                            (SM(mev2), VM(mk2.get(("x2a", i - 1), 0)), VM(m_weff_v))
                            if (n == 0 and k == 0) else (),
                            inc=("t", 1) if (n == 1 and k == 7) else None)
                mk2[("f1", i)] = mf1
                mx2a = sch.add("vector", lambda e, qq=q3: e.tensor_add(
                    G_[qq][:, :], PB[:, :], s_bfbc[:, :]), (TM(mf1),), inc=("v", 1))
                mk2[("x2a", i)] = mx2a

            def p2Bx2(i):
                p = i % 2
                q3 = i % 3
                mx2a = mk2[("x2a", i)]
                mx2 = sch.add("gpsimd", lambda e, pp=p, qq=q3: e.tensor_add(
                    H_[pp][:, :], G_[qq][:, :], F_[qq][:, :]),
                    (VM(mx2a), mk2.get(("sx2", i - 2)),
                     SM(mk2.get(("ln3", i - 2), 0))), inc=("g", 1))
                mk2[("x2", i)] = mx2

            def p2C(i):
                p = i % 2
                mxn3 = ln(H_[p], xn3b[p], st3[p], (GM(mk2[("x2", i)]),
                                                   mk2.get(("sxn3", i - 2))))
                mk2[("ln3", i)] = mxn3
                mk2[("sxn3", i)] = dma("sync", sc_xn3[i, :, :], xn3b[p][:, :],
                                       (SM(mxn3),), ch=f"sx3{p}")
                if i < NQ:
                    mk2[("sx2", i)] = dma("sync", sc_x2[i, :, :], H_[p][:, :],
                                          (GM(mk2[("x2", i)]),), ch=f"sx2{p}")

            def p2loads(i):
                p = i % 2
                mk2[("lx", i)] = dma("sync", A_[p][:, :], x_d[i * 128:(i + 1) * 128, :],
                                     (GM(mk2[("x1", i - 2)]),), ch=f"lA{p}")
                mk2[("lhi", i)] = dma("sync", hilo[p][0:1, :], sc_hi[i:i + 1, :],
                                      (TM(mk2[("tri", i - 2)]),), ch=f"lh{p}")
                mk2[("llo", i)] = dma("sync", hilo[p][1:2, :], sc_lo[i:i + 1, :], ch=f"lh{p}")
                mk2[("ldl", i)] = dma("vector", B_[p][:, :], sc_delta[i, :, :],
                                      (GM(mk2[("eIn", i - 2)]), mk[("dst", i)]), ch=f"lB{p}")
                mk2[("lr", i)] = dma("vector", Cc_[p][:, :], sc_res[i, :, :],
                                     (VM(mk2[("x1a", i - 2)]), mk[("rst", i)]), ch=f"lC{p}")

            p2A(0)
            p2A(1)
            p2loads(2)
            p2A(2)
            p2Bln(0)
            p2Btp(0)
            p2Bln(1)
            p2Btp(1)
            p2Bmm(0)
            p2Bx2(0)
            for i in range(NT):
                if i + 1 < NT:
                    p2Bmm(i + 1)
                if i + 3 < NT:
                    p2loads(i + 3)
                    p2A(i + 3)
                if i + 1 < NT:
                    p2Bx2(i + 1)
                if i + 2 < NT:
                    p2Bln(i + 2)
                    p2Btp(i + 2)
                p2C(i)

            sch.barrier()
            es_p2.close()
            es_bc.close()
            es_wtmp.close()

        if phases >= 3:
            xg3T = [R(es_prep, nc.sbuf_tensor(f"xg3T{p}", [128, 8, 512], BF16, side="right"))
                    for p in range(2)]
            xq_in = [R(es_prep, nc.sbuf_tensor(f"xq_in{p}", [128, 1024], BF16, side="right"))
                     for p in range(2)]
            qsp = [R(es_qsp, nc.sbuf_tensor(f"qsp{p}", [128, 8, 512], BF16, side="right"))
                   for p in range(2)]

            mk3 = {"tcnt": 0, "vevlast": 0}

            def prep_load(ti, tc):
                mk3[("gld", tc)] = dma("sync", xq_in[tc % 2][:, :], sc_xn3[ti, :, :],
                                       (TM(mk3.get(("gtp", tc - 2), 0)), mk2[("sxn3", ti)]),
                                       ch=f"lxq{tc % 2}")

            def prep_group(tiles, pg):
                last = None
                tiles = list(tiles)
                seq = mk3.get("prepseq", {})
                for j, ti in enumerate(tiles):
                    tc = mk3["tcnt"]
                    tl = tc % 4
                    for tq in (tc, tc + 1):
                        if ("gld", tq) not in mk3 and tq in seq:
                            prep_load(seq[tq], tq)
                    PTp = PT2 if tc % 2 == 0 else PT1
                    mtp = transposes8(PTp, xq_in[tc % 2],
                                      (LD(mk3[("gld", tc)]), SM(mk3.get(("gev", tc - 2), 0)),
                                       SM(mk3.get("pevlast", 0)),
                                       SM(mk3.get("ev4last", 0))))
                    mk3[("gtp", tc)] = mtp
                    last = sch.add("scalar",
                                   lambda e, tt=tl, pp=pg, pt=PTp: e.copy(
                                       xg3T[pp][:, :, tt * 128:(tt + 1) * 128], pt[:, :, :]),
                                   (TM(mtp),), inc=("s", 1))
                    mk3[("gev", tc)] = last
                    mk3["tcnt"] = tc + 1
                    if ("gld", tc + 2) not in mk3 and (tc + 2) in seq:
                        prep_load(seq[tc + 2], tc + 2)
                return last

            mk3["prepseq"] = {j: j for j in range(32)}
            for g in range(8):
                pg = g % 2
                mev = prep_group(range(4 * g, 4 * g + 4), pg)
                for c in range(8):
                    half = c % 2
                    mq = None
                    for k in range(8):
                        mq = sch.add("tensor", lambda e, cc=c, kk=k, hh=half, pp=pg: e.matmul(
                            PA[:, hh * 512:(hh + 1) * 512],
                            WqT[:, kk * 1024 + cc * 128: kk * 1024 + (cc + 1) * 128],
                            xg3T[pp][:, kk, :], start=(kk == 0), stop=(kk == 7)),
                            ((SM(mev), GM(mwq)) if c == 0 else
                             (SM(mk3.get(("qev", g, c - 2), 0)),)) if k == 0 else (),
                            inc=("t", 1) if k == 7 else None)
                    mqe = sch.add("scalar", lambda e, cc=c, hh=half, pp=pg: e.activation(
                        qsp[pp][:, cc, :], PA[:, hh * 512:(hh + 1) * 512],
                        AF.Identity, bias=s_bqT[:, cc:cc + 1], scale=1.0),
                        (TM(mq), mk3.get(("qst", g - 2))), inc=("s", 1))
                    mk3[("qev", g, c)] = mqe
                for t in range(4):
                    mk3[("qst", g)] = dma("scalar", sc_qT[4 * g + t, :, :, :],
                                          qsp[pg][:, :, t * 128:(t + 1) * 128],
                                          (SM(mk3[("qev", g, 7)]),) if t == 0 else (),
                                          ch=f"sq{pg}")
            sch.barrier()
            es_qsp.close()
            es_pqw.close()
            C3 = es_p3.enter_context
            kT_res = C3(nc.sbuf_tensor("kT_res", [128, 8, 2048], BF16))
            v_res = C3(nc.sbuf_tensor("v_res", [128, 16384], BF16))
            qT_t = [C3(nc.sbuf_tensor(f"qT_t{p}", [128, 8, 128], BF16)) for p in range(2)]
            P_bf = [C3(nc.sbuf_tensor(f"P_bf{p}", [128, 2048], BF16)) for p in range(2)]
            pt_sb = [C3(nc.sbuf_tensor(f"pt_sb{p}", [128, 16, 128], BF16))
                     for p in range(2)]
            zs = [C3(nc.sbuf_tensor(f"zs{p}", [128, 1024], F32)) for p in range(2)]
            zl = [C3(nc.sbuf_tensor(f"zl{p}", [128, 1024], F32)) for p in range(3)]
            x2l = [C3(nc.sbuf_tensor(f"x2l{p}", [128, 1024], F32)) for p in range(2)]
            x3b = [C3(nc.sbuf_tensor(f"x3b{p}", [128, 1024], F32)) for p in range(2)]
            xn4b = [C3(nc.sbuf_tensor("xn4b0", [128, 1024], BF16))] * 2
            xn4T = [pt_sb[0][:, 0:8, :], pt_sb[1][:, 0:8, :]]
            st4 = [C3(nc.sbuf_tensor(f"st4_{p}", [128, 16], F32)) for p in range(2)]
            m_run = C3(nc.sbuf_tensor("m_run", [128, 32], F32))
            l_run = C3(nc.sbuf_tensor("l_run", [128, 32], F32))
            cmax = C3(nc.sbuf_tensor("cmax", [128, 8], F32))
            lbuf = C3(nc.sbuf_tensor("lbuf", [128, 8], F32))
            pstat = C3(nc.sbuf_tensor("pstat", [128, 8], F32))

            for r in range(KR):
                lastq = r * NQ - 1
                mk3["prepseq"] = {mk3["tcnt"] + j: 16 * r + j for j in range(16)}
                for gg in range(4):
                    pg = (8 + r * 4 + gg) % 2
                    mev = prep_group(range(16 * r + 4 * gg, 16 * r + 4 * gg + 4), pg)
                    for c in range(8):
                        half = c % 2
                        mq = None
                        for k in range(8):
                            mq = sch.add("tensor", lambda e, cc=c, kk=k, hh=half, pp=pg: e.matmul(
                                PA[:, hh * 512:(hh + 1) * 512],
                                WkT[:, kk * 1024 + cc * 128: kk * 1024 + (cc + 1) * 128],
                                xg3T[pp][:, kk, :], start=(kk == 0), stop=(kk == 7)),
                                ((SM(mev), GM(mwk),
                                  SM(mk3.get(("exp", lastq, 3), 0)),
                                  VM(mk3.get(("rmx", lastq, 3), 0))) if c == 0 else
                                 (SM(mk3.get(("kev", gg, c - 2), 0)),)) if k == 0 else (),
                                inc=("t", 1) if k == 7 else None)
                        mqe = sch.add("scalar", lambda e, cc=c, hh=half, go=gg: e.activation(
                            kT_res[:, cc, go * 512:(go + 1) * 512],
                            PA[:, hh * 512:(hh + 1) * 512],
                            AF.Identity, bias=s_bkT[:, cc:cc + 1], scale=1.0),
                            (TM(mq),), inc=("s", 1))
                        mk3[("kev", gg, c)] = mqe
                    for t in range(4):
                        mv = None
                        for n in range(2):
                            for k in range(8):
                                mv = sch.add("tensor", lambda e, nn=n, kk=k, tt=t, pp=pg: e.matmul(
                                    PB[:, nn * 512:(nn + 1) * 512],
                                    xg3T[pp][:, kk, tt * 128:(tt + 1) * 128],
                                    WvT[:, kk * 1024 + nn * 512: kk * 1024 + nn * 512 + 512],
                                    start=(kk == 0), stop=(kk == 7)),
                                    ((GM(mwv), VM(mk3["vevlast"]),
                                      SM(mk3.get(("exp", lastq, 3), 0)),
                                      VM(mk3.get(("rmx", lastq, 3), 0)))
                                     if (n == 0 and k == 0) else ()),
                                    inc=("t", 1) if (n == 1 and k == 7) else None)
                        mve = sch.add("vector", lambda e, tt=t, go=gg: e.tensor_tensor(
                            v_res[:, (4 * go + tt) * 1024:(4 * go + tt + 1) * 1024],
                            PB[:, :], s_bvbc[:, :], ALU.add), (TM(mv),), inc=("v", 1))
                        mk3["vevlast"] = mve
                mk3["kevlast"] = mk3[("kev", 3, 7)]
                mk3["vevlast2"] = mk3["vevlast"]
                mk3["gevlast"] = mk3[("gev", mk3["tcnt"] - 1)]


                def qS(step):
                    qi = step - r * NQ
                    p = step % 2
                    ac = 3 + 3 * (step % 2)
                    mlq = dma("vector", qT_t[p][:, :, :], sc_qT[qi, :, :, :],
                              (TM(mk3.get(("smm", step - 2), 0)), mk3[("qst", qi // 4)]),
                              ch=f"lqt{p}")
                    if r > 0:
                        mk3[("lz", step)] = dma("vector", zl[step % 3][:, :], sc_z[qi, :, :],
                                                (VM(mk3.get(("zco", step - 3), 0)),
                                                 mk3[("zst", (r - 1) * NQ + qi)]),
                                                ch=f"lzl{step % 3}")
                    if r == KR - 1:
                        mk3[("lx2", step)] = dma("vector", x2l[p][:, :], sc_x2[qi, :, :],
                                                 (VM(mk3.get(("x3", step - 2), 0)),
                                                  mk2[("sx2", qi)]), ch=f"lx2{p}")
                    msmm = None
                    for n in range(4):
                        ps_ = PA if n < 2 else PB
                        off = (n % 2) * 512
                        for c in range(8):
                            msmm = sch.add("tensor", lambda e, nn=n, cc=c, o=ps_, of=off, pp=p:
                                           e.matmul(o[:, of:of + 512], qT_t[pp][:, cc, :],
                                                    kT_res[:, cc, nn * 512:(nn + 1) * 512],
                                                    start=(cc == 0), stop=(cc == 7)),
                                           ((VD(mlq), SM(mk3.get(("exp", step - 1, n), 0)),
                                             VM(mk3.get(("rmx", step - 1, n), 0)),
                                             SM(mk3.get("kevlast", 0)),
                                             VM(mk3.get("vevlast2", 0)))
                                            if c == 0 else ()),
                                           inc=("t", 1) if c == 7 else None)
                        mk3[("smmn", step, n)] = msmm
                        mrx = sch.add("vector", lambda e, nn=n, o=ps_, of=off: e.reduce_max(
                            cmax[:, nn:nn + 1], o[:, of:of + 512], AX), (TM(msmm),),
                            inc=("v", 1))
                        mk3[("rmx", step, n)] = mrx
                    mk3[("smm", step)] = msmm
                    mmx = sch.add("vector", lambda e: e.reduce_max(pstat[:, 0:1], cmax[:, 0:4],
                                                                   AX),
                                  (VM(mk3[("rmx", step, 3)]),), inc=("v", 1))
                    if r > 0:
                        mmx = sch.add("vector", lambda e, q_=qi: e.tensor_max(
                            pstat[:, 0:1], pstat[:, 0:1], m_run[:, q_:q_ + 1]), inc=("v", 1))
                        mdm = sch.add("vector", lambda e, q_=qi: e.tensor_sub(
                            pstat[:, 2:3], m_run[:, q_:q_ + 1], pstat[:, 0:1]), inc=("v", 1))
                        mal = sch.add("scalar", lambda e, a_=ac: e.activation(
                            pstat[:, a_:a_ + 1], pstat[:, 2:3], AF.Exp, scale=ISQ),
                            (VM(mdm),), inc=("s", 1))
                        mk3[("al", step)] = mal
                    mng = sch.add("scalar", lambda e: e.activation(pstat[:, 1:2], pstat[:, 0:1],
                                                                   AF.Identity, scale=-ISQ),
                                  (VM(mmx),), inc=("s", 1))
                    sch.add("scalar", lambda e, q_=qi: e.activation(
                        m_run[:, q_:q_ + 1], pstat[:, 0:1], AF.Identity, scale=1.0),
                        (SM(mng),), inc=("s", 1))
                    for n in range(4):
                        ps_ = PA if n < 2 else PB
                        off = (n % 2) * 512
                        mex = sch.add("scalar", lambda e, nn=n, o=ps_, of=off, pp=p:
                                      e.activation(
                                          P_bf[pp][:, nn * 512:(nn + 1) * 512],
                                          o[:, of:of + 512], AF.Exp,
                                          bias=pstat[:, 1:2], scale=ISQ,
                                          accum_out=lbuf[:, nn:nn + 1]),
                                      (TM(mk3.get(("ptp", step - 2), 0)), SM(mng),
                                       VM(mk3.get(("la", step - 1), 0))) if n == 0 else (),
                                      inc=("s", 1))
                        mk3[("exp", step, n)] = mex
                    mla = sch.add("vector", lambda e: e.reduce_sum(pstat[:, 4:5], lbuf[:, 0:4],
                                                                   AX),
                                  (SM(mk3[("exp", step, 3)]),), inc=("v", 1))
                    mk3[("la", step)] = mla
                    if r == 0:
                        sch.add("scalar", lambda e, q_=qi: e.activation(
                            l_run[:, q_:q_ + 1], pstat[:, 4:5], AF.Identity, scale=1.0),
                            (VM(mla),), inc=("s", 1))
                        mk3[("lr", step)] = ("v", mla)
                    else:
                        sch.add("vector", lambda e, q_=qi, a_=ac: e.tensor_mul(
                            l_run[:, q_:q_ + 1], l_run[:, q_:q_ + 1], pstat[:, a_:a_ + 1]),
                            (SM(mk3[("al", step)]),), inc=("v", 1))
                        mlr = sch.add("vector", lambda e, q_=qi: e.tensor_add(
                            l_run[:, q_:q_ + 1], l_run[:, q_:q_ + 1], pstat[:, 4:5]),
                            (VM(mla),), inc=("v", 1))
                        mk3[("lr", step)] = ("v", mlr)

                def qB(step):
                    p = step % 2
                    mpt = None
                    for j in range(16):
                        pt = PT1 if j < 8 else PT2
                        mpt = sch.add("tensor", lambda e, jj=j, pt_=pt, pp=p: e.transpose(
                            pt_[:, jj % 8, :], P_bf[pp][:, jj * 128:(jj + 1) * 128],
                            s_id[:, :]),
                            ((SM(mk3[("exp", step, 3)]), SM(mk3.get(("pev", step - 1), 0)),
                              SM(mk3.get(("ev4", step - 1), 0)), SM(mk3.get("gevlast", 0)))
                             if j == 0 else ()),
                            inc=("t", 1) if j == 15 else None)
                    mk3[("ptp", step)] = mpt
                    sch.add("scalar", lambda e, pp=p: e.copy(pt_sb[pp][:, 0:8, :],
                                                             PT1[:, :, :]),
                            (TM(mpt),), inc=("s", 1))
                    mpe = sch.add("scalar", lambda e, pp=p: e.copy(pt_sb[pp][:, 8:16, :],
                                                                   PT2[:, :, :]),
                                  inc=("s", 1))
                    mk3[("pev", step)] = mpe

                def qC(step):
                    qi = step - r * NQ
                    p = step % 2
                    ac = 3 + 3 * (step % 2)
                    mpe = mk3[("pev", step)]
                    mpv = None
                    for j in range(16):
                        for n in range(2):
                            mpv = sch.add("tensor", lambda e, jj=j, nn=n, pp=p: e.matmul(
                                PC[:, nn * 512:(nn + 1) * 512], pt_sb[pp][:, jj, :],
                                v_res[:, jj * 1024 + nn * 512: jj * 1024 + nn * 512 + 512],
                                start=(jj == 0), stop=(jj == 15)),
                                ((SM(mpe), VM(mk3.get(("zco", step - 1), 0)),
                                  VM(mk3.get(("mo1", step - 1), 0)))
                                 if (j == 0 and n == 0) else ()),
                                inc=("t", 1) if (j == 15 and n == 1) else None)
                    if r == 0:
                        mz = sch.add("vector", lambda e, pp=p: e.tensor_copy(zs[pp][:, :],
                                                                             PC[:, :]),
                                     (TM(mpv), mk3.get(("zst", step - 2))), inc=("v", 1))
                    else:
                        mz = sch.add("vector", lambda e, pp=p, z3=step % 3, a_=ac:
                                     e.scalar_tensor_tensor(
                            zs[pp][:, :], zl[z3][:, :], pstat[:, a_:a_ + 1], PC[:, :],
                            ALU.mult, ALU.add),
                            (TM(mpv), mk3.get(("lz", step)), SM(mk3[("al", step)]),
                             mk3.get(("zst", step - 2)),
                             GM(mk3.get(("moo", step - 2), 0))),
                            inc=("v", 1))
                    mk3[("zco", step)] = mz
                    if r < KR - 1:
                        mk3[("zst", step)] = dma("scalar", sc_z[qi, :, :], zs[p][:, :],
                                                 (VM(mz),), ch=f"sz{p}")
                    else:
                        mli = sch.add("vector", lambda e, q_=qi: e.reciprocal(
                            pstat[:, 5:6], l_run[:, q_:q_ + 1]),
                            (mk3[("lr", step)],), inc=("v", 1))
                        mx3 = sch.add("vector", lambda e, pp=p: e.scalar_tensor_tensor(
                            x3b[pp][:, :], zs[pp][:, :], pstat[:, 5:6], x2l[pp][:, :],
                            ALU.mult, ALU.add),
                            (VM(mli), mk3.get(("lx2", step)),
                             GM(mk3.get(("moo", step - 2), 0))), inc=("v", 1))
                        mk3[("x3", step)] = mx3
                        if dbg:
                            mk3[("zdbg", step)] = dma("scalar", sc_z[qi, :, :], zs[p][:, :],
                                                      (VM(mx3),), ch=f"sz{p}")
                        mxn4 = ln(x3b[p], xn4b[p], st4[p],
                                  (VM(mx3), TM(mk3.get(("tp4", step - 1), 0))))
                        mtp4 = transposes8(PT1, xn4b[p], (SM(mxn4),))
                        mk3[("tp4", step)] = mtp4
                        mev4 = sch.add("scalar", lambda e, pp=p: e.copy(xn4T[pp],
                                                                        PT1[:, :, :]),
                                       (TM(mtp4),), inc=("s", 1))
                        mk3[("ev4", step)] = mev4
                        mf2 = None
                        for n in range(2):
                            for k in range(8):
                                mf2 = sch.add("tensor", lambda e, nn=n, kk=k, pp=p: e.matmul(
                                    PC[:, nn * 512:(nn + 1) * 512], xn4T[pp].tensor[:, kk, :],
                                    W_eff[:, kk * 1024 + nn * 512: kk * 1024 + nn * 512 + 512],
                                    start=(kk == 0), stop=(kk == 7)),
                                    (SM(mev4), VM(mk3[("zco", step)])) if (n == 0 and k == 0)
                                    else (),
                                    inc=("t", 1) if (n == 1 and k == 7) else None)
                        mo1 = sch.add("vector", lambda e, pp=p: e.tensor_add(
                            zs[pp][:, :], PC[:, :], s_bfbc[:, :]),
                            (TM(mf2), GM(mk3.get(("moo", step - 2), 0)),
                             mk3.get(("zdbg", step))), inc=("v", 1))
                        mk3[("mo1", step)] = mo1
                        moo = sch.add("gpsimd", lambda e, pp=p: e.tensor_add(
                            s_out[pp][:, :], zs[pp][:, :], x3b[pp][:, :]),
                            (VM(mo1), mk3.get(("ost", step - 1))), inc=("g", 1))
                        mk3[("moo", step)] = moo
                        mk3[("ost", step)] = dma("scalar", out_d[qi * 128:(qi + 1) * 128, :],
                                                 s_out[p][:, :], (GM(moo),), ch=f"so{p}")

                qS(r * NQ)
                qS(r * NQ + 1)
                qB(r * NQ)
                if r < KR - 1:
                    for qi in range(NQ):
                        step = r * NQ + qi
                        if qi + 2 < NQ:
                            qS(step + 2)
                        if qi + 1 < NQ:
                            qB(step + 1)
                        qC(step)
                else:
                    for qi in range(NQ):
                        step = r * NQ + qi
                        qC(step)
                        if qi + 2 < NQ:
                            qS(step + 2)
                        if qi + 1 < NQ:
                            qB(step + 1)
                mk3["pevlast"] = mk3[("pev", r * NQ + NQ - 1)]
                mk3["ev4last"] = mk3.get(("ev4", r * NQ + NQ - 1), 0)

        pass
    except _Cut:
        pass

    if dbg and phases >= 3:
        dma("scalar", dbg_ml[:, 0:32], m_run[:, :], (VM(sch.count("v")),))
        dma("scalar", dbg_ml[:, 32:64], l_run[:, :], (VM(sch.count("v")),))
    _fw = [(snm, sch.count(snm)) for snm in list(sch.cnt.keys())
           if snm not in ("t", "v", "s", "g")]
    _fw += [TM(sch.count("t")), VM(sch.count("v")), GM(sch.count("g"))]
    sch.add("scalar", lambda e: e.activation(s_sqscr[:, 0:1], s_sqscr[:, 0:1], AF.Identity,
                                             scale=1.0), tuple(_fw), inc=("s", 1))

    for n in sorted(set(list(sch.cnt.keys()) + ["t", "v", "s", "g", "ld", "st"])):
        sems[n] = C(nc.semaphore(n))

    def run_prog(name):
        def f(eng):
            for item in sch.prog[name]:
                if item[0] == "w":
                    eng.wait_ge(sems[item[1]], item[2])
                else:
                    _, fn, inc = item
                    ins = fn(eng)
                    if inc is not None:
                        ins.then_inc(sems[inc[0]], inc[1])
        return f

    with nc.Block() as blk:
        blk.tensor(run_prog("tensor"))
        blk.vector(run_prog("vector"))
        blk.scalar(run_prog("scalar"))
        blk.sync(run_prog("sync"))
        blk.gpsimd(run_prog("gpsimd"))

    for es in (es_p3, es_p2, es_p1t, es_p1w, es_pqw, es_qsp, es_prep,
               es_p2a, es_fold, es_bc, es_wtmp, es_p23w, es_glob):
        es.close()
    return nc


def make_inmaps(inputs):
    f32 = np.float32
    bf = ml_dtypes.bfloat16
    x = np.asarray(inputs["L0_attn_a_x"], f32)
    Wp = np.asarray(inputs["L0_attn_a_W_proj"], f32)
    A = np.asarray(inputs["L0_attn_a_A"], f32).reshape(1, 1024)
    Bp = np.asarray(inputs["L0_attn_a_Bp"], f32).reshape(1, 1024)
    Dsc = np.asarray(inputs["L0_attn_a_D"], f32).reshape(1)
    Wd = np.asarray(inputs["L0_attn_a_W_delta"], f32)
    bd = np.asarray(inputs["L0_attn_a_b_delta"], f32).reshape(1, 1024)
    Wq = np.asarray(inputs["L0_attn_b_Wq"], f32)
    bq = np.asarray(inputs["L0_attn_b_bq"], f32)
    Wk = np.asarray(inputs["L0_attn_b_Wk"], f32)
    bk = np.asarray(inputs["L0_attn_b_bk"], f32)
    Wv = np.asarray(inputs["L0_attn_b_Wv"], f32)
    bv = np.asarray(inputs["L0_attn_b_bv"], f32).reshape(1, 1024)
    W = np.asarray(inputs["L0_ffn_W"], f32)
    b = np.asarray(inputs["L0_ffn_b"], f32).reshape(1, 1024)
    ld_ = np.asarray(inputs["L0_ffn_lora_down"], f32)
    lu_ = np.asarray(inputs["L0_ffn_lora_up"], f32)
    da_ = np.asarray(inputs["L0_ffn_down_aux"], f32)
    ua_ = np.asarray(inputs["L0_ffn_up_aux"], f32)

    ct = np.ascontiguousarray

    def _pad128(a):
        out = np.zeros((128,) + a.shape[1:], f32)
        out[:a.shape[0]] = a
        return out

    rows = np.zeros((8, 1024), f32)
    rows[0] = A
    rows[1] = Bp
    rows[2] = bd
    rows[3] = b
    rows[4] = bv
    rows[5, 0] = Dsc[0]
    emsk = np.zeros((128, 256), bf)
    emsk[:, 65] = 1
    shared = dict(
        wp0=ct(Wp[0:1024, :]), wdT=ct(Wd.T), wcT=ct(Wp[2048:3072, :].T),
        wrT=ct(Wp[3072:4096, :].T), wT=ct(W.T),
        uaT=_pad128(ua_.T), da=_pad128(da_), ldT=_pad128(ld_.T), luT=_pad128(lu_.T),
        wqT=ct(Wq.T), wkT=ct(Wk.T), wvT=ct(Wv.T),
        rows=rows, bqT=ct(bq.reshape(8, 128).T), bkT=ct(bk.reshape(8, 128).T),
        ident=np.eye(128, dtype=bf), tri128=np.triu(np.ones((128, 128))).astype(bf),
        tri64x=np.triu(np.ones((64, 64), f32), 1), emsk=emsk,
        ones1r=np.ones((1, 128), f32), ones64=np.ones((1, 64), f32),
        ones2r=np.ones((2, 128), bf),
    )
    in_maps = []
    for c in range(8):
        bidx, h = c // 2, c % 2
        xl = x[bidx] if h == 0 else np.concatenate([x[bidx, 4096:], x[bidx, :4096]], axis=0)
        m1 = np.zeros((64, 1), f32)
        m2n = np.zeros((64, 1), f32)
        if h == 1:
            m1[0:32] = 1.0
            m2n[32:64] = -1.0
        im = dict(shared)
        im["x"] = ct(xl)
        im["m1c"] = m1
        im["m2n"] = m2n
        in_maps.append(im)
    return in_maps


def kernel(**inputs):
    in_maps = make_inmaps(inputs)
    nc = build(phases=3, dbg=False)
    res = run_bass_kernel_spmd(nc, in_maps, core_ids=list(range(8)))
    out = np.empty((4, 8192, 1024), np.float32)
    for c in range(8):
        bidx, h = c // 2, c % 2
        out[bidx, h * 4096:(h + 1) * 4096] = np.asarray(res.results[c]["out"], np.float32)
    return out
